# version 13
# speedup vs baseline: 1.2436x; 1.1660x over previous
import math
import numpy as np
import ml_dtypes
import concourse.bass as bass
import concourse.mybir as mybir
from concourse.bass_utils import run_bass_kernel_spmd
from contextlib import ExitStack

from concourse.bass import IndirectOffsetOnAxis

F32 = mybir.dt.float32
BF16 = mybir.dt.bfloat16
U32 = mybir.dt.uint32
AF = mybir.ActivationFunctionType
ALU = mybir.AluOpType
AX = mybir.AxisListType
BF = ml_dtypes.bfloat16

W = 8
S = 8192
D = 4096
T = S // W
NB = T // 128

QL, KVL = 1536, 512
NOPE0, ROPE0 = 128, 64
QH0 = NOPE0 + ROPE0
VH0 = 128
NH0, HPC0 = 32, 4
NH1, HD1, ROT1 = 16, 256, 64
MASKVAL = -1.0e9 / 16.0


class Sem:
    def __init__(self, handle):
        self.h = handle
        self.n = 0

    def inc(self, k=16):
        self.n += k
        return self.n


class Prog:
    ENGINES = ("gpsimd", "sync", "tensor", "vector", "scalar")

    def __init__(self, nc):
        self.nc = nc
        self.items = []

    def emit(self, eng, fn):
        self.items.append((eng, fn))

    def dma(self, eng, out, in_, sem):
        v = sem.inc(16)
        self.emit(eng, lambda e, o=out, i=in_, s=sem.h: e.dma_start(out=o, in_=i).then_inc(s, 16))
        return v

    def dma_ind(self, out, in_, idx, sem):
        v = sem.inc(16)
        self.emit("gpsimd", lambda e, o=out, i=in_, x=idx, s=sem.h:
                  e.indirect_dma_start(
                      out=o, in_=i,
                      in_offset=IndirectOffsetOnAxis(ap=x, axis=0),
                      out_offset=None).then_inc(s, 16))
        return v

    def wait(self, eng, sem, v):
        if v > 0:
            self.emit(eng, lambda e, s=sem.h, vv=v: e.wait_ge(s, vv))

    def op(self, eng, builder, sem=None):
        if sem is None:
            self.emit(eng, builder)
            return None
        v = sem.inc(1)
        self.emit(eng, lambda e, b=builder, s=sem.h: b(e).then_inc(s, 1))
        return v

    def replay(self, block):
        for key in self.ENGINES:
            deco = getattr(block, key)

            def make(k):
                def body(eng):
                    for ek, fn in self.items:
                        if ek == k:
                            fn(eng)
                return body
            deco(make(key))


def ceil_div(a, b):
    return (a + b - 1) // b


def colvec_dma(p, nc, ctx, name, dram_ap, n, s_ld):
    m = ceil_div(n, 128)
    nfull = n // 128
    tail = n - nfull * 128
    E = ctx.enter_context
    st = {}
    if nfull:
        st["stage"] = E(nc.sbuf_tensor(name + "_stg", [nfull, 128], F32))
        p.dma("gpsimd", st["stage"][:, :], dram_ap[0:nfull * 128].rearrange("(p f) -> p f", f=128), s_ld)
    if tail:
        st["stage2"] = E(nc.sbuf_tensor(name + "_stg2", [1, tail], F32))
        p.dma("gpsimd", st["stage2"][:, :], dram_ap[nfull * 128:n].rearrange("(p f) -> p f", f=tail), s_ld)
    st["col"] = E(nc.sbuf_tensor(name, [128, m], F32))
    st["n"] = n
    return st


def colvec_compute(p, st, idf_sb, ps_cv, col_off, s_pe, s_cv, scale=None):
    n = st["n"]
    m = ceil_div(n, 128)
    nfull = n // 128
    tail = n - nfull * 128
    col = st["col"]
    p.wait("tensor", s_cv, s_cv.n)
    if nfull:
        p.op("tensor", lambda e, o=ps_cv[0:128, col_off:col_off + nfull], i=st["stage"][:, :],
             d=idf_sb[0:nfull, 0:nfull]: e.transpose(o, i, d), sem=s_pe)
    if tail:
        p.op("tensor", lambda e, o=ps_cv[0:tail, col_off + nfull:col_off + nfull + 1], i=st["stage2"][:, :],
             d=idf_sb[0:1, 0:1]: e.transpose(o, i, d), sem=s_pe)
    p.wait("scalar", s_pe, s_pe.n)
    sc = 1.0 if scale is None else scale
    if nfull:
        p.op("scalar", lambda e, o=col[:, 0:nfull], i=ps_cv[:, col_off:col_off + nfull], s=sc:
             e.activation(o, i, AF.Copy, bias=0.0, scale=s), sem=None if tail else s_cv)
    if tail:
        p.op("scalar", lambda e, o=col[0:tail, nfull:nfull + 1],
             i=ps_cv[0:tail, col_off + nfull:col_off + nfull + 1], s=sc:
             e.activation(o, i, AF.Copy, bias=0.0, scale=s), sem=s_cv)
    return col



SEM_NAMES = ["ld_g", "ld_s", "w0", "w1", "x0", "x1", "sp0", "sp1", "st0", "st1",
             "stv0", "stv1", "k0", "k1", "k2", "k3", "v0", "v1", "v2", "v3", "m0", "m1",
             "pe", "act", "dve", "cv", "gp",
             "cc0", "cc1", "cc2", "cc3"]


def mk_sems(nc, ctx):
    return {n: Sem(ctx.enter_context(nc.semaphore("sm_" + n))) for n in SEM_NAMES}


def barrier(p, sm, skip=()):
    for eng in Prog.ENGINES:
        for name, s in sm.items():
            if name not in skip:
                p.wait(eng, s, s.n)


NOUT_A = QL + KVL + ROPE0
NOC_A = ceil_div(NOUT_A, 128)


def mk_shared(p, nc, ctx, sm, io):
    E = ctx.enter_context
    sh = {}
    sh["idf_sb"] = E(nc.sbuf_tensor("sh_idf", [128, 128], F32))
    sh["idb_sb"] = E(nc.sbuf_tensor("sh_idb", [128, 128], BF16))
    sh["ones_r"] = E(nc.sbuf_tensor("sh_onr", [1, 128], F32))
    sh["ones_rb"] = E(nc.sbuf_tensor("sh_onrb", [1, 128], BF16))
    sh["ones_cb"] = E(nc.sbuf_tensor("sh_oncb", [128, 1], BF16))
    sh["eps12"] = E(nc.sbuf_tensor("sh_eps12", [1, 1], F32))
    sh["eps5"] = E(nc.sbuf_tensor("sh_eps5", [1, 1], F32))
    s_ld = sm["ld_g"]
    p.dma("gpsimd", sh["idf_sb"][:, :], io["idf"][:, :], s_ld)
    p.dma("gpsimd", sh["idb_sb"][:, :], io["idf"][:, :], s_ld)
    p.dma("gpsimd", sh["ones_r"][:, :], io["onesf"][:, :], s_ld)
    p.dma("gpsimd", sh["ones_rb"][:, :], io["onesf"][:, :], s_ld)
    p.dma("gpsimd", sh["ones_cb"][:, :], io["onescf"][:, :], s_ld)
    p.op("vector", lambda e, o=sh["eps12"]: e.memset(o[:, :], 1e-12))
    v_eps = p.op("vector", lambda e, o=sh["eps5"]: e.memset(o[:, :], 1e-5), sem=sm["dve"])
    v_ld = s_ld.n
    for eng in ("tensor", "vector", "scalar", "sync"):
        p.wait(eng, s_ld, v_ld)
        p.wait(eng, sm["dve"], v_eps)
    p.wait("gpsimd", s_ld, v_ld)
    return sh


def emit_a(p, nc, ctx, sm, sh, io):
    E = ctx.enter_context
    xT, wpk, bqc, bkvc, out = io["xT"], io["wpk_a"], io["bqc"], io["bkvc"], io["qckvT"]
    idf_sb = sh["idf_sb"]
    xbf = E(nc.sbuf_tensor("a_xbf", [128, 32 * T], BF16))
    wbuf = [E(nc.sbuf_tensor(f"a_wbuf{i}", [128, 32 * 128], BF16)) for i in range(2)]
    evac = [E(nc.sbuf_tensor(f"a_evac{i}", [128, T], BF16)) for i in range(2)]
    ps_cv = E(nc.psum_tensor("a_ps_cv", [128, 32], F32))
    ps = [E(nc.psum_tensor(f"a_ps{i}", [128, 512], F32)) for i in range(4)]
    s_ld = sm["ld_g"]
    s_w = [sm["w0"], sm["w1"]]
    s_pe, s_cv, s_ev = sm["pe"], sm["cv"], sm["act"]
    s_st = [sm["st0"], sm["st1"]]

    for g in range(8):
        p.dma("gpsimd", xbf[:, :].rearrange("p (c t) -> p c t", t=T)[:, g * 4:(g + 1) * 4, :],
              xT.rearrange("(c p) t -> p c t", p=128)[:, g * 4:(g + 1) * 4, :], s_ld)
    cv_bqc = colvec_dma(p, nc, ctx, "a_bqc_col", bqc, QL, s_ld)
    cv_bkv = colvec_dma(p, nc, ctx, "a_bkv_col", bkvc, KVL + ROPE0, s_ld)
    v_ld = s_ld.n
    p.wait("tensor", s_ld, v_ld)
    bqc_col = colvec_compute(p, cv_bqc, idf_sb, ps_cv, 0, s_pe, s_cv)
    bkv_col = colvec_compute(p, cv_bkv, idf_sb, ps_cv, 12, s_pe, s_cv)
    v_cv0 = s_cv.n

    ev_of = {}
    st_of = {}
    pe_of = {}
    for oc in range(NOC_A):
        rows = min(128, NOUT_A - oc * 128)
        buf = wbuf[oc % 2]
        ebuf = evac[oc % 2]
        if oc >= 2:
            p.wait("gpsimd", s_pe, pe_of[(oc - 2, 1)])
        vw = p.dma("gpsimd", buf[:, :], wpk[oc], s_w[oc % 2])
        p.wait("tensor", s_w[oc % 2], vw)
        for th in range(2):
            k = oc * 2 + th
            pst = ps[k % 4]
            if k >= 4:
                p.wait("tensor", s_ev, ev_of[((k - 4) // 2, (k - 4) % 2)])
            for ic in range(32):
                last = ic == 31
                pe_v = p.op("tensor", lambda e, o=pst[:, :], w=buf[:, ic * 128:(ic + 1) * 128],
                            x=xbf[:, ic * T + th * 512: ic * T + th * 512 + 512],
                            st=(ic == 0), sp=last: e.matmul(o, w, x, start=st, stop=sp),
                            sem=s_pe if last else None)
            pe_of[(oc, th)] = pe_v
            col, ccol = (bqc_col, oc) if oc < 12 else (bkv_col, oc - 12)
            if oc >= 2 and th == 0:
                p.wait("scalar", s_st[oc % 2], st_of[(oc - 2, 1)])
            p.wait("scalar", s_pe, pe_of[(oc, th)])
            if oc == 0 and th == 0:
                p.wait("scalar", s_cv, v_cv0)
            ev_of[(oc, th)] = p.op("scalar",
                lambda e, o=ebuf[0:rows, th * 512:(th + 1) * 512], i=pst[0:rows, :],
                b=col[0:rows, ccol:ccol + 1]: e.activation(o, i, AF.Identity, bias=b, scale=1.0),
                sem=s_ev)
            p.wait("sync", s_ev, ev_of[(oc, th)])
            st_of[(oc, th)] = p.dma("sync", out[oc * 128: oc * 128 + rows, th * 512:(th + 1) * 512],
                                    ebuf[0:rows, th * 512:(th + 1) * 512], s_st[oc % 2])
    p.wait("sync", s_st[0], s_st[0].n)
    p.wait("sync", s_st[1], s_st[1].n)


def build_l0_pre():
    nc = bass.Bass("TRN2", target_bir_lowering=False)
    io = dict(
        xT=nc.dram_tensor("xT", [D, T], F32, kind="ExternalInput").ap(),
        wpk_a=nc.dram_tensor("wpk_a", [NOC_A, 128, 32 * 128], F32, kind="ExternalInput").ap(),
        bqc=nc.dram_tensor("bqc", [QL], F32, kind="ExternalInput").ap(),
        bkvc=nc.dram_tensor("bkvc", [KVL + ROPE0], F32, kind="ExternalInput").ap(),
        idf=nc.dram_tensor("idf", [128, 128], F32, kind="ExternalInput").ap(),
        onesf=nc.dram_tensor("onesf", [1, 128], F32, kind="ExternalInput").ap(),
        onescf=nc.dram_tensor("onescf", [128, 1], F32, kind="ExternalInput").ap(),
        qckvT=nc.dram_tensor("qckvT", [NOUT_A, T], BF16, kind="ExternalOutput").ap(),
    )
    with ExitStack() as ctx:
        sm = mk_sems(nc, ctx)
        block = ctx.enter_context(nc.Block())
        p = Prog(nc)
        sh = mk_shared(p, nc, ctx, sm, io)
        emit_a(p, nc, ctx, sm, sh, io)
        p.replay(block)
    return nc


def emit_b(p, nc, ctx, sm, sh, io, SQ=S, nheads=HPC0):
    TQ = SQ // W
    MT = SQ // 512
    NQB = SQ // 128
    NKT = SQ // 128
    QG = 2
    SC0 = 1.0 / math.sqrt(QH0)
    qckv, out = io["qckv"], io["osum_part"]
    wqn, wqr, wkk, wkv = io["wqn"], io["wqr"], io["wkk"], io["wkv"]
    bqn_c, bqr_c, bkk_c, bkv_r = io["bqn_c"], io["bqr_c"], io["bkk_c"], io["bkv_r"]
    if True:
        E = ctx.enter_context
        qn_sb = E(nc.sbuf_tensor("qn_sb", [128, SQ], BF16))
        kn_sb = E(nc.sbuf_tensor("kn_sb", [128, SQ], BF16))
        qr_sb = E(nc.sbuf_tensor("qr_sb", [128, SQ], BF16))
        kr_sb = E(nc.sbuf_tensor("kr_sb", [128, SQ], BF16))
        vbuf = E(nc.sbuf_tensor("vbuf", [128, SQ], BF16))
        pbuf = [E(nc.sbuf_tensor(f"pbuf{i}", [128, SQ], BF16)) for i in range(3)]
        osum = E(nc.sbuf_tensor("osum", [128, SQ], F32))
        qcsl = [E(nc.sbuf_tensor(f"qcsl{i}", [128, 12 * 512], BF16)) for i in range(2)]
        kvsl = [E(nc.sbuf_tensor(f"kvsl{i}", [128, 4 * 512], BF16)) for i in range(2)]
        pt_sb = [E(nc.sbuf_tensor(f"pt_sb{i}", [128, 1024], BF16)) for i in range(2)]
        whb = [dict(qn=E(nc.sbuf_tensor("wqnb", [128, 12 * 128], BF16)),
                    qr=E(nc.sbuf_tensor("wqrb", [128, 12 * 64], BF16)),
                    kk=E(nc.sbuf_tensor("wkkb", [128, 4 * 128], BF16)),
                    kv=E(nc.sbuf_tensor("wkvb", [128, 4 * 128], BF16)))]
        bqn = E(nc.sbuf_tensor("bqn", [128, nheads], F32))
        bqr = E(nc.sbuf_tensor("bqr", [128, nheads], F32))
        bkk = E(nc.sbuf_tensor("bkk", [128, nheads], F32))
        bvr = [E(nc.sbuf_tensor(f"bvr{h}", [1, 128], BF16)) for h in range(nheads)]
        mcol2 = [E(nc.sbuf_tensor(f"mcol{i}", [128, MT], F32)) for i in range(2)]
        lcol2 = [E(nc.sbuf_tensor(f"lcol{i}", [128, MT], F32)) for i in range(2)]
        negM = E(nc.sbuf_tensor("negM", [128, 1], F32))
        Ccol = E(nc.sbuf_tensor("Ccol", [128, MT], F32))
        CLc = E(nc.sbuf_tensor("CLc", [128, MT], F32))
        lsum = E(nc.sbuf_tensor("lsum", [128, 1], F32))
        linv = E(nc.sbuf_tensor("linv", [128, 1], F32))
        Fcol = E(nc.sbuf_tensor("Fcol", [128, MT], F32))
        ost = [E(nc.sbuf_tensor(f"ost{i}", [128, 128], F32)) for i in range(2)]

        ps_a = E(nc.psum_tensor("ps_a", [128, 512], F32))
        ps_b = E(nc.psum_tensor("ps_b", [128, 512], F32))
        ps_c = E(nc.psum_tensor("ps_c", [128, 512], F32))
        ps_d = E(nc.psum_tensor("ps_d", [128, 512], F32))
        ps_t = [E(nc.psum_tensor(f"ps_t{i}", [128, 1024], BF16)) for i in range(2)]
        ps_o = E(nc.psum_tensor("ps_o", [128, 128], F32))

        s_ld, s_ld2 = sm["ld_g"], sm["ld_s"]
        s_wh = [sm["w0"], sm["w1"]]
        s_sl = [sm["x0"], sm["x1"]]
        s_pe, s_act, s_dve = sm["pe"], sm["act"], sm["dve"]
        s_st = [sm["st0"], sm["st1"]]
        ones_sb = sh["ones_rb"]
        idf_sb = sh["idf_sb"]
        idb_sb = sh["idb_sb"]

        p.dma("gpsimd", bqn[:, :], bqn_c[:, :], s_ld)
        p.dma("gpsimd", bqr[:, :], bqr_c[:, :], s_ld)
        p.dma("gpsimd", bkk[:, :], bkk_c[:, :], s_ld)
        for h in range(nheads):
            p.dma("gpsimd", bvr[h][:, :], bkv_r[h:h + 1, :], s_ld)
        for c in range(W):
            p.dma("sync", kr_sb[0:64, c * TQ:(c + 1) * TQ], qckv[c, QL + KVL:NOUT_A, :], s_ld2)
        p.op("vector", lambda e: e.memset(qr_sb[64:128, :], 0.0))
        v_ms = p.op("vector", lambda e: e.memset(kr_sb[64:128, :], 0.0), sem=s_dve)
        p.wait("tensor", s_dve, v_ms)
        v_ld = s_ld.n
        v_ld2 = s_ld2.n
        p.wait("scalar", s_ld, v_ld)
        p.op("scalar", lambda e: e.mul(bqn[:, :], bqn[:, :], SC0), sem=s_act)
        v_bias = p.op("scalar", lambda e: e.mul(bqr[:, :], bqr[:, :], SC0), sem=s_act)
        p.wait("scalar", s_act, v_bias)
        p.wait("tensor", s_ld, v_ld)
        p.wait("tensor", s_ld2, v_ld2)
        p.wait("vector", s_ld, v_ld)

        vCe_prev = [0]
        last_evac = {"a": 0, "b": 0, "c": 0, "d": 0}
        last_exp = {"a": 0, "b": 0, "c": 0, "d": 0}
        s1_done_pe = {}
        sl_consumed = {}
        trans_last = 0
        trans_done = {}
        oev_prev = (None, 0)
        osum_written = {}
        st_of = {}

        for h in range(nheads):
            wb = whb[0]
            if h >= 1:
                p.wait("gpsimd", s_pe, s1_done_pe[h - 1])
            p.dma("gpsimd", wb["qn"][:, :], wqn[h], s_wh[0])
            p.dma("gpsimd", wb["qr"][:, :], wqr[h], s_wh[0])
            p.dma("gpsimd", wb["kk"][:, :], wkk[h], s_wh[0])
            vwh = p.dma("gpsimd", wb["kv"][:, :], wkv[h], s_wh[0])

            p.wait("tensor", s_wh[0], vwh)
            for m in range(MT):
                c, tb = m // (TQ // 512), m % (TQ // 512)
                mp = m % 2
                gm = h * MT + m
                if gm >= 2:
                    p.wait("sync", s_pe, sl_consumed[gm - 2])
                p.dma("sync", qcsl[mp][:, :].rearrange("p (i t) -> p i t", t=512),
                      qckv[c, 0:QL, tb * 512:(tb + 1) * 512].rearrange("(i p) t -> p i t", p=128),
                      s_sl[mp])
                vsl = p.dma("sync", kvsl[mp][:, :].rearrange("p (i t) -> p i t", t=512),
                            qckv[c, QL:QL + KVL, tb * 512:(tb + 1) * 512].rearrange("(i p) t -> p i t", p=128),
                            s_sl[mp])
                p.wait("tensor", s_sl[mp], vsl)
                p.wait("tensor", s_act, last_evac["a"])
                p.wait("tensor", s_act, last_exp["a"])
                for ic in range(12):
                    p.op("tensor", lambda e, o=ps_a[:, :], w=wb["qn"][:, ic * 128:(ic + 1) * 128],
                         x=qcsl[mp][:, ic * 512:(ic + 1) * 512], st=(ic == 0), sp=(ic == 11):
                         e.matmul(o, w, x, start=st, stop=sp))
                p.wait("tensor", s_act, last_evac["b"])
                p.wait("tensor", s_act, last_exp["b"])
                for ic in range(12):
                    p.op("tensor", lambda e, o=ps_b[0:64, :], w=wb["qr"][:, ic * 64:(ic + 1) * 64],
                         x=qcsl[mp][:, ic * 512:(ic + 1) * 512], st=(ic == 0), sp=(ic == 11):
                         e.matmul(o, w, x, start=st, stop=sp))
                p.wait("tensor", s_act, last_evac["c"])
                p.wait("tensor", s_act, last_exp["c"])
                for ic in range(4):
                    p.op("tensor", lambda e, o=ps_c[:, :], w=wb["kk"][:, ic * 128:(ic + 1) * 128],
                         x=kvsl[mp][:, ic * 512:(ic + 1) * 512], st=(ic == 0), sp=(ic == 3):
                         e.matmul(o, w, x, start=st, stop=sp))
                p.wait("tensor", s_act, last_evac["d"])
                vmm = 0
                for ts in range(4):
                    for ic in range(4):
                        p.op("tensor", lambda e, o=ps_d[:, ts * 128:(ts + 1) * 128],
                             w=kvsl[mp][:, ic * 512 + ts * 128: ic * 512 + ts * 128 + 128],
                             x=wb["kv"][:, ic * 128:(ic + 1) * 128], st=(ic == 0):
                             e.matmul(o, w, x, start=st, stop=False))
                    vmm = p.op("tensor", lambda e, o=ps_d[:, ts * 128:(ts + 1) * 128], w=ones_sb[:, :],
                               x=bvr[h][:, :]: e.matmul(o, w, x, start=False, stop=True),
                               sem=s_pe if ts == 3 else None)
                sl_consumed[gm] = vmm
                p.wait("scalar", s_pe, vmm)
                last_evac["a"] = p.op("scalar", lambda e, o=qn_sb[:, m * 512:(m + 1) * 512],
                                      i=ps_a[:, :], b=bqn[:, h:h + 1], s=SC0:
                                      e.activation(o, i, AF.Identity, bias=b, scale=s), sem=s_act)
                last_evac["b"] = p.op("scalar", lambda e, o=qr_sb[0:64, m * 512:(m + 1) * 512],
                                      i=ps_b[0:64, :], b=bqr[0:64, h:h + 1], s=SC0:
                                      e.activation(o, i, AF.Identity, bias=b, scale=s), sem=s_act)
                last_evac["c"] = p.op("scalar", lambda e, o=kn_sb[:, m * 512:(m + 1) * 512],
                                      i=ps_c[:, :], b=bkk[:, h:h + 1]:
                                      e.activation(o, i, AF.Identity, bias=b, scale=1.0), sem=s_act)
                last_evac["d"] = p.op("scalar", lambda e, o=vbuf[:, m * 512:(m + 1) * 512],
                                      i=ps_d[:, :]: e.copy(o, i), sem=s_act)
            s1_done_pe[h] = sl_consumed[h * MT + MT - 1]
            s1_evacs_done = last_evac["d"]

            p.wait("tensor", s_act, s1_evacs_done)
            SBANKS = [("a", ps_a), ("b", ps_b), ("c", ps_c), ("d", ps_d)]
            s_gp = sm["gp"]

            def score_chunk(qb, kc):
                pb = pbuf[qb % 3]
                mcol = mcol2[qb % 2]
                lcol = lcol2[qb % 2]
                bank, pss = SBANKS[kc % 4]
                if kc % 4 == 0:
                    p.wait("tensor", s_act, last_exp[SBANKS[3][0]])
                    if kc < 4:
                        p.wait("tensor", s_act, max(last_evac.values()))
                p.op("tensor", lambda e, o=pss[:, :], w=qn_sb[:, qb * 128:(qb + 1) * 128],
                     x=kn_sb[:, kc * 512:(kc + 1) * 512]: e.matmul(o, w, x, start=True, stop=False))
                vs = p.op("tensor", lambda e, o=pss[:, :], w=qr_sb[:, qb * 128:(qb + 1) * 128],
                          x=kr_sb[:, kc * 512:(kc + 1) * 512]:
                          e.matmul(o, w, x, start=False, stop=True), sem=s_pe)
                if kc == 0:
                    p.wait("vector", s_pe, vs)
                    vm = p.op("vector", lambda e, o=mcol[:, 0:1], i=pss[:, :]:
                              e.reduce_max(o, i, axis=AX.X, negate=True), sem=s_dve)
                    p.wait("scalar", s_dve, vm)
                else:
                    p.wait("scalar", s_pe, vs)
                last_exp[bank] = p.op("scalar", lambda e, o=pb[:, kc * 512:(kc + 1) * 512],
                                      i=pss[:, :], b=mcol[:, 0:1], a=lcol[:, kc:kc + 1]:
                                      e.activation(o, i, AF.Exp, bias=b, scale=1.0, accum_out=a),
                                      sem=s_act)

            def emit_stats(qb):
                lcol = lcol2[qb % 2]
                p.wait("vector", s_act, qb_exp_done[qb])
                v2 = p.op("vector", lambda e, o=lsum[:, :], i=lcol[:, :]:
                          e.reduce_sum(o, i, axis=AX.X), sem=s_dve)
                p.wait("vector", s_dve, v2)
                v3 = p.op("vector", lambda e, o=linv[:, :], i=lsum[:, :]: e.reciprocal(o, i),
                          sem=s_dve)
                p.wait("vector", s_dve, v3)
                p.wait("vector", s_pe, trans_done.get(qb % 3, 0))

            def emit_corr(qb, kc):
                pb = pbuf[qb % 3]
                return p.op("vector", lambda e, o=pb[:, kc * 512:(kc + 1) * 512],
                            sc=linv[:, :]: e.tensor_scalar(o, o, sc, None, ALU.mult),
                            sem=s_dve)

            def av_step(gq, k8, st_av):
                if k8 == 0:
                    p.wait("tensor", s_dve, corr_done[gq])
                    eng0, val0 = oev_prev2[0]
                    if eng0 == "scalar":
                        p.wait("tensor", s_act, val0)
                    elif eng0 == "vector":
                        p.wait("tensor", s_dve, val0)
                if k8 > 0:
                    kprev = k8 - 1
                    eng0, val0 = st_av["ptev"][kprev]
                    p.wait("tensor", s_act if eng0 == "scalar" else s_dve, val0)
                    for sub in range(8):
                        kt = 8 * kprev + sub
                        avv = p.op("tensor", lambda e, o=ps_o[:, :],
                                   w=vbuf[:, kt * 128:(kt + 1) * 128],
                                   x=pt_sb[kprev % 2][:, sub * 128:(sub + 1) * 128],
                                   st=(kt == 0), sp=(kt == NKT - 1):
                                   e.matmul(o, w, x, start=st, stop=sp),
                                   sem=s_pe if kt == NKT - 1 else None)
                if k8 < NK8:
                    tb2 = k8 % 2
                    if k8 >= 2:
                        eng0, val0 = st_av["ptev"][k8 - 2]
                        p.wait("tensor", s_act if eng0 == "scalar" else s_dve, val0)
                    tv = 0
                    for sub in range(8):
                        kt = 8 * k8 + sub
                        tv = p.op("tensor", lambda e,
                                  o=ps_t[tb2][:, sub * 128:(sub + 1) * 128],
                                  i=pbuf[gq % 3][:, kt * 128:(kt + 1) * 128], d=idb_sb[:, :]:
                                  e.transpose(o, i, d),
                                  sem=s_pe if sub == 7 else None)
                    eng = "vector" if k8 % 2 == 0 else "scalar"
                    p.wait(eng, s_pe, tv)
                    if eng == "vector":
                        pv = p.op("vector", lambda e, o=pt_sb[tb2][:, :], i=ps_t[tb2][:, :]:
                                  e.tensor_copy(o, i), sem=s_dve)
                    else:
                        pv = p.op("scalar", lambda e, o=pt_sb[tb2][:, :], i=ps_t[tb2][:, :]:
                                  e.copy(o, i), sem=s_act)
                    st_av["ptev"][k8] = (eng, pv)
                else:
                    avv_final = s_pe.n
                    trans_done[gq % 3] = avv_final
                    if h == 0:
                        p.wait("scalar", s_pe, avv_final)
                        ov = p.op("scalar", lambda e, o=osum[:, gq * 128:(gq + 1) * 128],
                                  i=ps_o[:, :]: e.copy(o, i), sem=s_act)
                        osum_written[gq] = ("scalar", ov)
                        oev_prev2[0] = ("scalar", ov)
                    else:
                        p.wait("vector", s_pe, avv_final)
                        peng, pval = osum_written[gq]
                        p.wait("vector", s_act if peng == "scalar" else s_dve, pval)
                        ov = p.op("vector", lambda e, o=osum[:, gq * 128:(gq + 1) * 128],
                                  i0=ps_o[:, :], i1=osum[:, gq * 128:(gq + 1) * 128]:
                                  e.tensor_tensor(o, i0, i1, ALU.add), sem=s_dve)
                        osum_written[gq] = ("vector", ov)
                        oev_prev2[0] = ("vector", ov)

            corr_done = {}
            qb_exp_done = {}
            oev_prev2 = [oev_prev]
            st_av = {"ptev": {}}
            av_q = []
            NK8 = NKT // 8
            slot = [0]

            def pump_av(force=False):
                if av_q and (force or av_q[0][2] <= slot[0]):
                    g0, k8, _ = av_q[0]
                    if k8 == 0:
                        st_av["ptev"] = {}
                    av_step(g0, k8, st_av)
                    if k8 + 1 > NK8:
                        av_q.pop(0)
                    else:
                        av_q[0][1] = k8 + 1

            NCS = (MT + 1) // 2
            for qb in range(NQB):
                pbq = qb % 3
                for kc in range(MT):
                    if kc == 0:
                        p.wait("scalar", s_pe, trans_done.get(pbq, 0))
                    score_chunk(qb, kc)
                    if kc == MT - 1:
                        qb_exp_done[qb] = last_exp[SBANKS[(MT - 1) % 4][0]]
                    if qb >= 1:
                        if kc == 0:
                            emit_stats(qb - 1)
                        elif kc <= NCS:
                            for c2 in (2 * (kc - 1), 2 * (kc - 1) + 1):
                                if c2 < MT:
                                    cv = emit_corr(qb - 1, c2)
                            if kc == NCS:
                                corr_done[qb - 1] = cv
                                av_q.append([qb - 1, 0, slot[0] + 1])
                    if kc % 2 == 1 or kc == 0:
                        pump_av()
                    slot[0] += 1
            emit_stats(NQB - 1)
            for kc in range(MT):
                cv = emit_corr(NQB - 1, kc)
            corr_done[NQB - 1] = cv
            av_q.append([NQB - 1, 0, 0])
            while av_q:
                pump_av(force=True)
            oev_prev = oev_prev2[0]
            trans_last = max(trans_done.values()) if trans_done else trans_last

        fin_eng, fin_val = osum_written[NQB // QG - 1]
        p.wait("tensor", s_dve if fin_eng == "vector" else s_act, fin_val)
        for g in sorted(osum_written):
            eng, val = osum_written[g]
            p.wait("tensor", s_dve if eng == "vector" else s_act, val)
        ev3 = {}
        for tt in range(NKT):
            pst = ps_c if tt % 2 == 0 else ps_d
            if tt >= 2:
                p.wait("tensor", s_act, ev3[tt - 2])
            tv = p.op("tensor", lambda e, o=pst[:, 0:128], i=osum[:, tt * 128:(tt + 1) * 128],
                      d=idf_sb[:, :]: e.transpose(o, i, d), sem=s_pe)
            p.wait("scalar", s_pe, tv)
            if tt >= 2:
                p.wait("scalar", s_st[tt % 2], st_of[tt - 2])
            ev3[tt] = p.op("scalar", lambda e, o=ost[tt % 2][:, :], i=pst[:, 0:128]: e.copy(o, i),
                           sem=s_act)
            p.wait("sync", s_act, ev3[tt])
            st_of[tt] = p.dma("sync", out[tt * 128:(tt + 1) * 128, :], ost[tt % 2][:, :], s_st[tt % 2])
        p.wait("sync", s_st[0], s_st[0].n)
        p.wait("sync", s_st[1], s_st[1].n)


def build_l0_attn(SQ=S, nheads=HPC0):
    TQ = SQ // W
    nc = bass.Bass("TRN2", target_bir_lowering=False)
    io = dict(
        qckv=nc.dram_tensor("qckv", [W, NOUT_A, TQ], BF16, kind="ExternalInput").ap(),
        wqn=nc.dram_tensor("wqn", [nheads, 128, 12 * 128], F32, kind="ExternalInput").ap(),
        wqr=nc.dram_tensor("wqr", [nheads, 128, 12 * 64], F32, kind="ExternalInput").ap(),
        wkk=nc.dram_tensor("wkk", [nheads, 128, 4 * 128], F32, kind="ExternalInput").ap(),
        wkv=nc.dram_tensor("wkv", [nheads, 128, 4 * 128], F32, kind="ExternalInput").ap(),
        bqn_c=nc.dram_tensor("bqn_c", [128, nheads], F32, kind="ExternalInput").ap(),
        bqr_c=nc.dram_tensor("bqr_c", [128, nheads], F32, kind="ExternalInput").ap(),
        bkk_c=nc.dram_tensor("bkk_c", [128, nheads], F32, kind="ExternalInput").ap(),
        bkv_r=nc.dram_tensor("bkv_r", [nheads, 128], F32, kind="ExternalInput").ap(),
        idf=nc.dram_tensor("idf", [128, 128], F32, kind="ExternalInput").ap(),
        onesf=nc.dram_tensor("onesf", [1, 128], F32, kind="ExternalInput").ap(),
        onescf=nc.dram_tensor("onescf", [128, 1], F32, kind="ExternalInput").ap(),
        osum_part=nc.dram_tensor("osum_part", [SQ, VH0], F32, kind="ExternalOutput").ap(),
    )
    with ExitStack() as ctx:
        sm = mk_sems(nc, ctx)
        block = ctx.enter_context(nc.Block())
        p = Prog(nc)
        sh = mk_shared(p, nc, ctx, sm, io)
        emit_b(p, nc, ctx, sm, sh, io, SQ=SQ, nheads=nheads)
        p.replay(block)
    return nc


def emit_c(p, nc, ctx, sm, sh, io, rs_mode=False):
    NE, RK = 4, 8
    ZC = NE * RK + NE
    oparts, out = io["oparts"], io["x2T"]
    woT, bo_d, xT = io["woT"], io["bo"], io["xT"]
    wb_pk, bb_d, az_rhs, bcT, lp_br = io["wb_pk"], io["b_base"], io["az_rhs"], io["bcT"], io["lp_br"]
    x1psp = nc.dram_tensor("x1psp", [32, 128, T], F32)
    x1nsp = nc.dram_tensor("x1nsp", [32, 128, T], F32)
    x2psp = nc.dram_tensor("x2psp", [32, 128, T], F32)
    if True:
        E = ctx.enter_context
        idf_sb = sh["idf_sb"]
        idb_sb = sh["idb_sb"]
        ones_r = sh["ones_r"]
        ones_cb = sh["ones_cb"]
        eps12 = sh["eps12"]
        eps5 = sh["eps5"]
        obuf = E(nc.sbuf_tensor("obuf", [128, (1 if rs_mode else W) * T], F32))
        oacc = E(nc.sbuf_tensor("oacc", [128, T], F32))
        oacb = E(nc.sbuf_tensor("oacb", [128, T], BF16))
        oT_bf = E(nc.sbuf_tensor("oT_bf", [128, T], BF16))
        wo_sb = E(nc.sbuf_tensor("wo_sb", [128, D], BF16))
        wbuf = [E(nc.sbuf_tensor(f"wbufc{i}", [128, 32 * 128], BF16)) for i in range(2)]
        xc = [E(nc.sbuf_tensor(f"xc{i}", [128, T], F32)) for i in range(2)]
        t1 = [E(nc.sbuf_tensor(f"t1_{i}", [128, T], F32)) for i in range(2)]
        tbf = [E(nc.sbuf_tensor(f"tbf{i}", [128, T], BF16)) for i in range(2)]
        sqb = [E(nc.sbuf_tensor(f"sqb{i}", [128, T], BF16)) for i in range(2)]
        x1bf = E(nc.sbuf_tensor("x1bf", [128, 32 * T], BF16))
        mrow = E(nc.sbuf_tensor("mrow", [1, T], F32))
        qrow = E(nc.sbuf_tensor("qrow", [1, T], F32))
        vrow = E(nc.sbuf_tensor("vrow", [1, T], F32))
        rrow = E(nc.sbuf_tensor("rrow", [1, T], F32))
        mb = E(nc.sbuf_tensor("mb", [128, T], F32))
        rb = E(nc.sbuf_tensor("rb", [128, T], F32))
        zb = E(nc.sbuf_tensor("zb", [128, NB * ZC], F32))
        zrhs = E(nc.sbuf_tensor("zrhs", [128, 32 * ZC], BF16))
        zwb = E(nc.sbuf_tensor("zwb", [128, NB * NE * RK], BF16))
        zwT = E(nc.sbuf_tensor("zwT", [NE * RK, T], BF16))
        bcT_sb = E(nc.sbuf_tensor("bcT_sb", [NE * RK, D], BF16))
        lpbr = E(nc.sbuf_tensor("lpbr", [1, NE], F32))
        brow = E(nc.sbuf_tensor("brow", [1, NE], F32))
        lrow = E(nc.sbuf_tensor("lrow", [1, NE], F32))
        lrow_b = E(nc.sbuf_tensor("lrow_b", [128, NE], F32))
        a4 = E(nc.sbuf_tensor("a4", [128, NE], F32))
        e4 = E(nc.sbuf_tensor("e4", [128, NE], F32))
        am = E(nc.sbuf_tensor("am", [128, 1], F32))
        al = E(nc.sbuf_tensor("al", [128, 1], F32))
        ali = E(nc.sbuf_tensor("ali", [128, 1], F32))
        ostg = [E(nc.sbuf_tensor(f"ostg{i}", [128, T], F32)) for i in range(2)]

        ps_1 = E(nc.psum_tensor("ps_1", [128, 512], F32))
        ps_2 = E(nc.psum_tensor("ps_2", [128, 512], F32))
        ps_bt = [E(nc.psum_tensor(f"ps_bt{i}", [128, 128], BF16)) for i in range(2)]
        ps_s = [E(nc.psum_tensor(f"ps_s{i}", [1, 512], F32)) for i in range(2)]
        ps_q = [E(nc.psum_tensor(f"ps_q{i}", [1, 512], F32)) for i in range(2)]

        s_ld, s_ld2 = sm["ld_g"], sm["ld_s"]
        s_w = [sm["w0"], sm["w1"]]
        s_x = [sm["x0"], sm["x1"]]
        s_pe, s_act, s_dve = sm["pe"], sm["act"], sm["dve"]
        s_sp = [sm["sp0"], sm["sp1"]]
        s_spl = [sm["k0"], sm["k1"]]
        s_st = [sm["st0"], sm["st1"]]

        p.dma("gpsimd", wo_sb[:, :], woT[:, :], s_ld)
        p.dma("gpsimd", zrhs[:, :].rearrange("p (i z) -> p i z", z=ZC),
              az_rhs.rearrange("(i p) z -> p i z", p=128), s_ld)
        p.dma("gpsimd", bcT_sb[:, :], bcT[:, :], s_ld)
        p.dma("gpsimd", lpbr[:, :], lp_br[0:1, :], s_ld)
        p.dma("gpsimd", brow[:, :], lp_br[1:2, :], s_ld)
        for c in range(1 if rs_mode else W):
            p.dma("sync", obuf[:, c * T:(c + 1) * T].rearrange("p (i v) -> p i v", v=VH0),
                  (oparts if rs_mode else oparts[c]).rearrange("(i p) v -> p i v", p=128), s_ld2)
        cv_bo = colvec_dma(p, nc, ctx, "bo_colc", bo_d, D, s_ld)
        cv_bb = colvec_dma(p, nc, ctx, "bb_colc", bb_d, D, s_ld)
        v_ld, v_ld2 = s_ld.n, s_ld2.n
        for eng in ("tensor", "vector", "scalar"):
            p.wait(eng, s_ld, v_ld)
            p.wait(eng, s_ld2, v_ld2)
        bo_c = colvec_compute(p, cv_bo, idf_sb, ps_1, 0, s_pe, s_act)
        bb_c = colvec_compute(p, cv_bb, idf_sb, ps_1, 32, s_pe, s_act)
        v_cv = s_act.n
        p.wait("scalar", s_act, v_cv)

        va = p.op("scalar", lambda e, o=lrow[:, :], i=lpbr[:, :], b=eps12[:, :]:
                  e.activation(o, i, AF.Ln, bias=b, scale=1.0), sem=s_act)
        p.wait("vector", s_act, va)
        vb = p.op("vector", lambda e, o=lrow[:, :], i0=lrow[:, :], i1=brow[:, :]:
                  e.tensor_tensor(o, i0, i1, ALU.add), sem=s_dve)
        p.wait("tensor", s_dve, vb)
        p.wait("tensor", s_act, v_cv)
        vbc = p.op("tensor", lambda e, o=ps_1[0:128, 0:NE], w=ones_r[:, :], x=lrow[:, :]:
                   e.matmul(o, w, x, start=True, stop=True), sem=s_pe)
        p.wait("vector", s_pe, vbc)
        v_lrb = p.op("vector", lambda e, o=lrow_b[:, :], i=ps_1[0:128, 0:NE]:
                     e.tensor_copy(o, i), sem=s_dve)

        if rs_mode:
            vcb = p.op("vector", lambda e, o=oacb[:, :], i=obuf[:, 0:T]: e.tensor_copy(o, i),
                       sem=s_dve)
        else:
            vsum = 0
            for c in range(1, W):
                src = obuf[:, 0:T] if c == 1 else oacc[:, :]
                if c > 1:
                    p.wait("vector", s_dve, vsum)
                vsum = p.op("vector", lambda e, o=oacc[:, :], i0=obuf[:, c * T:(c + 1) * T], i1=src:
                            e.tensor_tensor(o, i0, i1, ALU.add), sem=s_dve)
            p.wait("vector", s_dve, vsum)
            vcb = p.op("vector", lambda e, o=oacb[:, :], i=oacc[:, :]: e.tensor_copy(o, i), sem=s_dve)
        p.wait("tensor", s_dve, vcb)
        vet = 0
        for tt in range(NB):
            if tt >= 2:
                p.wait("tensor", s_act, ev_bt[tt - 2])
            vtr = p.op("tensor", lambda e, o=ps_bt[tt % 2][:, :],
                       i=oacb[:, tt * 128:(tt + 1) * 128], d=idb_sb[:, :]:
                       e.transpose(o, i, d), sem=s_pe)
            p.wait("scalar", s_pe, vtr)
            vet = p.op("scalar", lambda e, o=oT_bf[:, tt * 128:(tt + 1) * 128],
                       i=ps_bt[tt % 2][:, :]: e.copy(o, i), sem=s_act)
            ev_bt = ev_bt if tt > 0 else {}
            ev_bt[tt] = vet
        v_oT = vet

        def pass1(tag, emit_mms, bias_col, res_src, spill, wload=None, pre_wait=None):
            ev = {}
            addv = {}
            spv = {0: 0, 1: 0}
            stat_of = {}
            vsq_of = {}
            vtb_of = {}

            def emit_statmms(oc2):
                par2 = oc2 % 2
                p.wait("tensor", s_dve, vtb_of[oc2])
                p.wait("tensor", s_act, vsq_of[oc2])
                sm = 0
                for th2 in range(2):
                    p.op("tensor", lambda e, o=ps_s[th2][:, :], w=ones_cb[:, :],
                         x=tbf[par2][:, th2 * 512:(th2 + 1) * 512], st=(oc2 == 0), sp=(oc2 == 31):
                         e.matmul(o, w, x, start=st, stop=sp))
                    sm = p.op("tensor", lambda e, o=ps_q[th2][:, :], w=ones_cb[:, :],
                              x=sqb[par2][:, th2 * 512:(th2 + 1) * 512], st=(oc2 == 0), sp=(oc2 == 31):
                              e.matmul(o, w, x, start=st, stop=sp),
                              sem=s_pe if th2 == 1 else None)
                stat_of[oc2] = sm

            for oc in range(32):
                par = oc % 2
                if wload is not None:
                    wload(oc)
                if oc == 0 and pre_wait is not None:
                    for sm_, vv_ in pre_wait:
                        p.wait("sync", sm_, vv_)
                if oc >= 2:
                    p.wait("sync", s_dve, addv[oc - 2])
                p.dma("sync", xc[par][:, :], res_src(oc), s_x[par])
                vx = s_x[par].n
                for th in range(2):
                    k = oc * 2 + th
                    pst = ps_1 if k % 2 == 0 else ps_2
                    if k >= 2:
                        p.wait("tensor", s_act, ev[k - 2])
                    vk = emit_mms(oc, th, pst)
                    p.wait("scalar", s_pe, vk)
                    if oc >= 2 and th == 0:
                        p.wait("scalar", s_spl[par], spv[par])
                    ev[k] = p.op("scalar", lambda e, o=t1[par][:, th * 512:(th + 1) * 512],
                                 i=pst[:, 0:512], b=bias_col[:, oc:oc + 1]:
                                 e.activation(o, i, AF.Identity, bias=b, scale=1.0), sem=s_act)
                p.wait("vector", s_act, ev[oc * 2 + 1])
                p.wait("vector", s_x[par], vx)
                addv[oc] = p.op("vector", lambda e, o=t1[par][:, :], i0=t1[par][:, :],
                                i1=xc[par][:, :]: e.tensor_tensor(o, i0, i1, ALU.add), sem=s_dve)
                p.wait("scalar", s_dve, addv[oc])
                if oc >= 2:
                    p.wait("scalar", s_pe, stat_of.get(oc - 2, 0))
                vsq_of[oc] = p.op("scalar", lambda e, o=sqb[par][:, :], i=t1[par][:, :]:
                                  e.square(o, i), sem=s_act)
                p.wait("vector", s_dve, addv[oc])
                if oc >= 2:
                    p.wait("vector", s_pe, stat_of.get(oc - 2, 0))
                vtb_of[oc] = p.op("vector", lambda e, o=tbf[par][:, :], i=t1[par][:, :]:
                                  e.tensor_copy(o, i), sem=s_dve)
                if oc >= 1:
                    emit_statmms(oc - 1)
                p.wait("sync", s_dve, addv[oc])
                spv[par] = p.dma("sync", spill[oc], t1[par][:, :], s_spl[par])
            emit_statmms(31)
            return spv, stat_of[31]

        def ln_finalize(stat_mark):
            p.wait("scalar", s_pe, stat_mark)
            c1 = p.op("scalar", lambda e, o=mrow[0:1, 0:512], i=ps_s[0][:, :]:
                      e.activation(o, i, AF.Copy, bias=0.0, scale=1.0 / D), sem=s_act)
            c2 = p.op("scalar", lambda e, o=mrow[0:1, 512:1024], i=ps_s[1][:, :]:
                      e.activation(o, i, AF.Copy, bias=0.0, scale=1.0 / D), sem=s_act)
            p.wait("scalar", s_act, c2)
            c3 = p.op("scalar", lambda e, o=qrow[:, :], i=mrow[:, :]: e.square(o, i), sem=s_act)
            p.wait("vector", s_act, c3)
            v1 = 0
            for th in range(2):
                v1 = p.op("vector", lambda e, o=vrow[0:1, th * 512:(th + 1) * 512],
                          i0=ps_q[th][:, :], i1=qrow[0:1, th * 512:(th + 1) * 512]:
                          e.scalar_tensor_tensor(o, i0, 1.0 / D, i1, ALU.mult, ALU.subtract),
                          sem=s_dve)
            p.wait("scalar", s_dve, v1)
            c4 = p.op("scalar", lambda e, o=vrow[:, :], i=vrow[:, :], b=eps5[:, :]:
                      e.activation(o, i, AF.Sqrt, bias=b, scale=1.0), sem=s_act)
            p.wait("vector", s_act, c4)
            v2 = p.op("vector", lambda e, o=rrow[:, :], i=vrow[:, :]: e.reciprocal(o, i), sem=s_dve)
            p.wait("tensor", s_dve, v2)
            p.wait("tensor", s_act, c2)
            vmb = vrb = 0
            for th in range(2):
                if th == 1:
                    p.wait("tensor", s_dve, vmb)
                vb1 = p.op("tensor", lambda e, o=ps_1[:, :], w=ones_r[:, :],
                           x=mrow[0:1, th * 512:(th + 1) * 512]:
                           e.matmul(o, w, x, start=True, stop=True), sem=s_pe)
                p.wait("vector", s_pe, vb1)
                vmb = p.op("vector", lambda e, o=mb[:, th * 512:(th + 1) * 512], i=ps_1[:, :]:
                           e.tensor_copy(o, i), sem=s_dve)
                if th == 1:
                    p.wait("tensor", s_dve, vrb)
                vb2 = p.op("tensor", lambda e, o=ps_2[:, :], w=ones_r[:, :],
                           x=rrow[0:1, th * 512:(th + 1) * 512]:
                           e.matmul(o, w, x, start=True, stop=True), sem=s_pe)
                p.wait("vector", s_pe, vb2)
                vrb = p.op("vector", lambda e, o=rb[:, th * 512:(th + 1) * 512], i=ps_2[:, :]:
                           e.tensor_copy(o, i), sem=s_dve)
            return vrb

        def pass2(tag, spill_in, sink, vready, pre_wait=None):
            nv = {}
            for oc in range(32):
                par = oc % 2
                if oc == 0 and pre_wait is not None:
                    for sm_, vv_ in pre_wait:
                        p.wait("sync", sm_, vv_)
                if oc >= 2:
                    p.wait("sync", s_dve, nv[oc - 2])
                p.dma("sync", xc[par][:, :], spill_in[oc], s_x[par])
                vx = s_x[par].n
                p.wait("vector", s_x[par], vx)
                if oc == 0:
                    p.wait("vector", s_dve, vready)
                va1 = p.op("vector", lambda e, o=ostg[par][:, :], i0=xc[par][:, :], i1=mb[:, :]:
                           e.tensor_tensor(o, i0, i1, ALU.subtract), sem=s_dve)
                p.wait("vector", s_dve, va1)
                nv[oc] = p.op("vector", lambda e, o=ostg[par][:, :], i0=ostg[par][:, :], i1=rb[:, :]:
                              e.tensor_tensor(o, i0, i1, ALU.mult), sem=s_dve)
                sink(oc, par, nv[oc])
            return nv

        p.wait("tensor", s_act, v_oT)

        def mm_outproj(oc, th, pst):
            return p.op("tensor", lambda e, o=pst[:, 0:512],
                        w=wo_sb[:, oc * 128:(oc + 1) * 128],
                        x=oT_bf[:, th * 512:(th + 1) * 512]:
                        e.matmul(o, w, x, start=True, stop=True), sem=s_pe)

        spv1, stat1 = pass1("x1", mm_outproj, bo_c, lambda oc: xT[oc * 128:(oc + 1) * 128, :],
                            x1psp)
        vr1 = ln_finalize(stat1)
        pw1 = [(s_spl[0], s_spl[0].n), (s_spl[1], s_spl[1].n)]

        sp1n = {0: 0, 1: 0}

        def sink1(oc, par, vn):
            p.wait("scalar", s_dve, vn)
            vcp = p.op("scalar", lambda e, o=x1bf[:, oc * T:(oc + 1) * T], i=ostg[par][:, :]:
                       e.copy(o, i), sem=s_act)
            p.wait("gpsimd", s_dve, vn)
            sp1n[par] = p.dma("gpsimd", x1nsp[oc], ostg[par][:, :], s_sp[par])
            if oc < 30:
                p.wait("vector", s_sp[par], sp1n[par])
                p.wait("vector", s_act, vcp)

        nv1 = pass2("x1n", x1psp, sink1, vr1, pre_wait=pw1)
        v_x1bf = s_act.n
        pw2 = [(s_sp[0], s_sp[0].n), (s_sp[1], s_sp[1].n)]

        p.wait("tensor", s_act, v_x1bf)
        zev = {}
        ztr = {}
        for ts in range(NB):
            if ts >= 1:
                p.wait("tensor", s_act, zev[ts - 1])
            for ic in range(32):
                vz = p.op("tensor", lambda e, o=ps_2[:, 0:ZC], w=x1bf[:, ic * T + ts * 128: ic * T + ts * 128 + 128],
                          x=zrhs[:, ic * ZC:(ic + 1) * ZC], st=(ic == 0), sp=(ic == 31):
                          e.matmul(o, w, x, start=st, stop=sp), sem=s_pe if ic == 31 else None)
            p.wait("scalar", s_pe, vz)
            zev[ts] = p.op("scalar", lambda e, o=zb[:, ts * ZC:(ts + 1) * ZC], i=ps_2[:, 0:ZC]:
                           e.copy(o, i), sem=s_act)
            p.wait("vector", s_act, zev[ts])
            if ts == 0:
                p.wait("vector", s_dve, v_lrb)
            w1 = p.op("vector", lambda e, o=a4[:, :], i0=zb[:, ts * ZC + NE * RK:(ts + 1) * ZC],
                      i1=lrow_b[:, :]: e.tensor_tensor(o, i0, i1, ALU.add), sem=s_dve)
            p.wait("vector", s_dve, w1)
            w2 = p.op("vector", lambda e, o=am[:, :], i=a4[:, :]:
                      e.reduce_max(o, i, axis=AX.X, negate=True), sem=s_dve)
            p.wait("scalar", s_dve, w2)
            w3 = p.op("scalar", lambda e, o=e4[:, :], i=a4[:, :], b=am[:, :], a=al[:, :]:
                      e.activation(o, i, AF.Exp, bias=b, scale=1.0, accum_out=a), sem=s_act)
            p.wait("vector", s_act, w3)
            w4 = p.op("vector", lambda e, o=ali[:, :], i=al[:, :]: e.reciprocal(o, i), sem=s_dve)
            p.wait("scalar", s_dve, w4)
            w5 = p.op("scalar", lambda e, o=ali[:, :], i=ali[:, :]: e.mul(o, i, 2.0), sem=s_act)
            p.wait("vector", s_act, w5)
            w6 = p.op("vector", lambda e, o=e4[:, :], i=e4[:, :], sc=ali[:, :]:
                      e.tensor_scalar(o, i, sc, None, ALU.mult), sem=s_dve)
            p.wait("vector", s_dve, w6)
            w7 = 0
            for ee in range(NE):
                w7 = p.op("vector", lambda e, o=zwb[:, ts * NE * RK + ee * RK: ts * NE * RK + (ee + 1) * RK],
                          i=zb[:, ts * ZC + ee * RK: ts * ZC + (ee + 1) * RK], sc=e4[:, ee:ee + 1]:
                          e.tensor_scalar(o, i, sc, None, ALU.mult),
                          sem=s_dve if ee == NE - 1 else None)
            p.wait("tensor", s_dve, w7)
            if ts >= 2:
                p.wait("tensor", s_act, ztr[ts - 2])
            vt = p.op("tensor", lambda e, o=ps_bt[ts % 2][0:NE * RK, :],
                      i=zwb[:, ts * NE * RK:(ts + 1) * NE * RK], d=idb_sb[:, :]:
                      e.transpose(o, i, d), sem=s_pe)
            p.wait("scalar", s_pe, vt)
            ztr[ts] = p.op("scalar", lambda e, o=zwT[:, ts * 128:(ts + 1) * 128],
                           i=ps_bt[ts % 2][0:NE * RK, :]:
                           e.copy(o, i), sem=s_act)
        v_zwT = ztr[NB - 1]

        p.wait("tensor", s_act, v_zwT)
        wv_of = {}

        def wloadC(oc):
            par = oc % 2
            if oc >= 2:
                p.wait("gpsimd", s_pe, wv_of[oc - 2])
            p.dma("gpsimd", wbuf[par][:, :], wb_pk[oc], s_w[par])

        def mm_ffn(oc, th, pst):
            par = oc % 2
            if th == 0:
                p.wait("tensor", s_w[par], s_w[par].n)
            for ic in range(32):
                p.op("tensor", lambda e, o=pst[:, 0:512], w=wbuf[par][:, ic * 128:(ic + 1) * 128],
                     x=x1bf[:, ic * T + th * 512: ic * T + th * 512 + 512], st=(ic == 0):
                     e.matmul(o, w, x, start=st, stop=False))
            vd = p.op("tensor", lambda e, o=pst[:, 0:512], w=bcT_sb[:, oc * 128:(oc + 1) * 128],
                      x=zwT[:, th * 512:(th + 1) * 512]:
                      e.matmul(o, w, x, start=False, stop=True), sem=s_pe)
            wv_of[oc] = vd
            return vd

        spv2, stat2 = pass1("x2", mm_ffn, bb_c, lambda oc: x1nsp[oc], x2psp, wload=wloadC,
                            pre_wait=pw2)
        vr2 = ln_finalize(stat2)
        pw3 = [(s_spl[0], s_spl[0].n), (s_spl[1], s_spl[1].n)]

        stv = {0: 0, 1: 0}

        def sink2(oc, par, vn):
            p.wait("sync", s_dve, vn)
            stv[par] = p.dma("sync", out[oc * 128:(oc + 1) * 128, :], ostg[par][:, :], s_st[par])
            if oc < 30:
                p.wait("vector", s_st[par], stv[par])
        pass2("out", x2psp, sink2, vr2, pre_wait=pw3)
        p.wait("sync", s_st[0], s_st[0].n)
        p.wait("sync", s_st[1], s_st[1].n)


def build_l0_post():
    NE, RK = 4, 8
    ZC = NE * RK + NE
    nc = bass.Bass("TRN2", target_bir_lowering=False)
    io = dict(
        oparts=nc.dram_tensor("oparts", [W, T, VH0], F32, kind="ExternalInput").ap(),
        woT=nc.dram_tensor("woT", [VH0, D], F32, kind="ExternalInput").ap(),
        bo=nc.dram_tensor("bo", [D], F32, kind="ExternalInput").ap(),
        xT=nc.dram_tensor("xT", [D, T], F32, kind="ExternalInput").ap(),
        wb_pk=nc.dram_tensor("wb_pk", [32, 128, 32 * 128], F32, kind="ExternalInput").ap(),
        b_base=nc.dram_tensor("b_base", [D], F32, kind="ExternalInput").ap(),
        az_rhs=nc.dram_tensor("az_rhs", [D, ZC], F32, kind="ExternalInput").ap(),
        bcT=nc.dram_tensor("bcT", [NE * RK, D], F32, kind="ExternalInput").ap(),
        lp_br=nc.dram_tensor("lp_br", [2, NE], F32, kind="ExternalInput").ap(),
        idf=nc.dram_tensor("idf", [128, 128], F32, kind="ExternalInput").ap(),
        onesf=nc.dram_tensor("onesf", [1, 128], F32, kind="ExternalInput").ap(),
        onescf=nc.dram_tensor("onescf", [128, 1], F32, kind="ExternalInput").ap(),
        x2T=nc.dram_tensor("x2T", [D, T], F32, kind="ExternalOutput").ap(),
    )
    with ExitStack() as ctx:
        sm = mk_sems(nc, ctx)
        block = ctx.enter_context(nc.Block())
        p = Prog(nc)
        sh = mk_shared(p, nc, ctx, sm, io)
        emit_c(p, nc, ctx, sm, sh, io)
        p.replay(block)
    return nc


def emit_d(p, nc, ctx, sm, sh, io):
    x2T, cosd, sins = io["x2T"], io["cosd"], io["sins"]
    wq_pk, wk_pk, wv_pk = io["wq_pk"], io["wk_pk"], io["wv_pk"]
    qT, kT, vT = io["qT"], io["kT"], io["vT"]
    if True:
        E = ctx.enter_context
        xbf = E(nc.sbuf_tensor("xbfd", [128, 32 * T], BF16))
        idb_sb = sh["idb_sb"]
        cos_sb = E(nc.sbuf_tensor("cos_sb", [ROT1, T], F32))
        sin_sb = E(nc.sbuf_tensor("sin_sb", [ROT1, T], F32))
        wbuf = [E(nc.sbuf_tensor(f"wbufd{i}", [128, 32 * 128], BF16)) for i in range(2)]
        stg = [E(nc.sbuf_tensor(f"stgd{i}", [128, T], F32)) for i in range(2)]
        rtmp = E(nc.sbuf_tensor("rtmp", [ROT1, T], F32))
        obf = [E(nc.sbuf_tensor(f"obfd{i}", [128, T], BF16)) for i in range(2)]
        ps = [E(nc.psum_tensor(f"psd{i}", [128, 512], F32)) for i in range(4)]
        s_ld = sm["ld_g"]
        s_w = [sm["w0"], sm["w1"]]
        s_pe, s_act, s_dve = sm["pe"], sm["act"], sm["dve"]
        s_st = [sm["st0"], sm["st1"]]

        p.dma("gpsimd", cos_sb[:, :], cosd[:, :], s_ld)
        p.dma("gpsimd", sin_sb[:, :], sins[:, :], s_ld)
        for g in range(8):
            p.dma("gpsimd", xbf[:, :].rearrange("p (c t) -> p c t", t=T)[:, g * 4:(g + 1) * 4, :],
                  x2T.rearrange("(c p) t -> p c t", p=128)[:, g * 4:(g + 1) * 4, :], s_ld)
        v_ld = s_ld.n
        for eng in ("tensor", "vector", "scalar"):
            p.wait(eng, s_ld, v_ld)

        SHUF = [(i ^ 1) for i in range(32)]
        ev = {}
        st_of = {}
        rope_v = {}
        wld = {}
        kk = 0
        post_pack = io.get("post_pack")
        for wi, (wpk, outt) in enumerate([(wq_pk, qT), (wk_pk, kT), (wv_pk, vT)]):
            for oc in range(32):
                par = kk % 2
                if kk >= 2:
                    p.wait("gpsimd", s_pe, wld[kk - 2])
                p.dma("gpsimd", wbuf[par][:, :], wpk[oc], s_w[par])
                vw = s_w[par].n
                p.wait("tensor", s_w[par], vw)
                vmm = 0
                for th in range(2):
                    pst = ps[(kk * 2 + th) % 4]
                    if kk * 2 + th >= 4:
                        p.wait("tensor", s_act, ev[kk * 2 + th - 4])
                    for ic in range(32):
                        vmm = p.op("tensor", lambda e, o=pst[:, :], w=wbuf[par][:, ic * 128:(ic + 1) * 128],
                                   x=xbf[:, ic * T + th * 512: ic * T + th * 512 + 512],
                                   st=(ic == 0), sp=(ic == 31): e.matmul(o, w, x, start=st, stop=sp),
                                   sem=s_pe if ic == 31 else None)
                wld[kk] = vmm
                for th in range(2):
                    pst = ps[(kk * 2 + th) % 4]
                    p.wait("scalar", s_pe, wld[kk])
                    if kk >= 2 and th == 0:
                        if st_of.get(kk - 2, 0):
                            p.wait("scalar", s_st[par], st_of[kk - 2])
                        if (kk - 2) in rope_v:
                            p.wait("scalar", s_dve, rope_v[kk - 2])
                    sc = (1.0 / 16.0) if wi == 0 else 1.0
                    ev[kk * 2 + th] = p.op("scalar", lambda e, o=stg[par][:, th * 512:(th + 1) * 512],
                                           i=pst[:, :], s=sc:
                                           e.activation(o, i, AF.Copy, bias=0.0, scale=s), sem=s_act)
                if True:
                    p.wait("vector", s_act, ev[kk * 2 + 1])
                    vr = 0
                    if wi < 2 and oc % 2 == 0:
                        vr = p.op("vector", lambda e, o=rtmp[:, :], i=stg[par][0:ROT1, :], m=SHUF:
                                  e.stream_shuffle(o, i, m), sem=s_dve)
                        p.wait("vector", s_dve, vr)
                        vr = p.op("vector", lambda e, o=rtmp[:, :], i0=rtmp[:, :], i1=sin_sb[:, :]:
                                  e.tensor_tensor(o, i0, i1, ALU.mult), sem=s_dve)
                        p.wait("vector", s_dve, vr)
                        vr = p.op("vector", lambda e, o=stg[par][0:ROT1, :], i0=stg[par][0:ROT1, :],
                                  i1=cos_sb[:, :]: e.tensor_tensor(o, i0, i1, ALU.mult), sem=s_dve)
                        p.wait("vector", s_dve, vr)
                        vr = p.op("vector", lambda e, o=stg[par][0:ROT1, :], i0=stg[par][0:ROT1, :],
                                  i1=rtmp[:, :]: e.tensor_tensor(o, i0, i1, ALU.add), sem=s_dve)
                        p.wait("vector", s_dve, vr)
                    vcb = p.op("vector", lambda e, o=obf[par][:, :], i=stg[par][:, :]:
                               e.tensor_copy(o, i), sem=s_dve)
                    rope_v[kk] = vcb
                    p.wait("sync", s_dve, vcb)
                    st_of[kk] = p.dma("sync", outt[oc * 128:(oc + 1) * 128, :], obf[par][:, :],
                                      s_st[par])
                kk += 1
            if post_pack is not None:
                p.wait("gpsimd", s_st[0], s_st[0].n)
                p.wait("gpsimd", s_st[1], s_st[1].n)
                post_pack[wi]()
        p.wait("sync", s_st[0], s_st[0].n)
        p.wait("sync", s_st[1], s_st[1].n)


def build_l1_qkv():
    nc = bass.Bass("TRN2", target_bir_lowering=False)
    io = dict(
        x2T=nc.dram_tensor("x2T", [D, T], F32, kind="ExternalInput").ap(),
        wq_pk=nc.dram_tensor("wq_pk", [32, 128, 32 * 128], F32, kind="ExternalInput").ap(),
        wk_pk=nc.dram_tensor("wk_pk", [32, 128, 32 * 128], F32, kind="ExternalInput").ap(),
        wv_pk=nc.dram_tensor("wv_pk", [32, 128, 32 * 128], F32, kind="ExternalInput").ap(),
        cosd=nc.dram_tensor("cosd", [ROT1, T], F32, kind="ExternalInput").ap(),
        sins=nc.dram_tensor("sins", [ROT1, T], F32, kind="ExternalInput").ap(),
        idf=nc.dram_tensor("idf", [128, 128], F32, kind="ExternalInput").ap(),
        onesf=nc.dram_tensor("onesf", [1, 128], F32, kind="ExternalInput").ap(),
        onescf=nc.dram_tensor("onescf", [128, 1], F32, kind="ExternalInput").ap(),
        qT=nc.dram_tensor("qT", [D, T], BF16, kind="ExternalOutput").ap(),
        kT=nc.dram_tensor("kT", [D, T], BF16, kind="ExternalOutput").ap(),
        vT=nc.dram_tensor("vT", [D, T], BF16, kind="ExternalOutput").ap(),
    )
    with ExitStack() as ctx:
        sm = mk_sems(nc, ctx)
        block = ctx.enter_context(nc.Block())
        p = Prog(nc)
        sh = mk_shared(p, nc, ctx, sm, io)
        emit_d(p, nc, ctx, sm, sh, io)
        p.replay(block)
    return nc


def emit_e(p, nc, ctx, sm, sh, io, SQ=S):
    NBq = T // 128
    qT_d, k_full, v_full, mask3, out = io["qT"], io["k_full"], io["v_full"], io["mask3"], io["oT"]
    E = ctx.enter_context
    idb_sb = sh["idb_sb"]
    q_sb = E(nc.sbuf_tensor("q_sbe", [128, 32 * T], BF16))
    msk_sb = [E(nc.sbuf_tensor(f"msk_sbe{i}", [128, 3 * 512], F32)) for i in range(2)]
    pbuf = [E(nc.sbuf_tensor(f"pbufe{i}", [128, SQ], BF16)) for i in range(3)]
    kstg = [E(nc.sbuf_tensor(f"kstg{i}", [128, 2 * 512], BF16)) for i in range(4)]
    vstg = [E(nc.sbuf_tensor(f"vstge{i}", [128, 1024], BF16)) for i in range(4)]
    pt_sb = [E(nc.sbuf_tensor(f"ptste{i}", [128, 1024], BF16)) for i in range(2)]
    otst = [E(nc.sbuf_tensor(f"otste{i}", [128, 256], BF16)) for i in range(2)]
    mcol2 = [E(nc.sbuf_tensor(f"mcole{i}", [128, 16], F32)) for i in range(2)]
    lcol2 = [E(nc.sbuf_tensor(f"lcole{i}", [128, 16], F32)) for i in range(2)]
    negM = E(nc.sbuf_tensor("negMe", [128, 1], F32))
    Ccol = E(nc.sbuf_tensor("Ccole", [128, 16], F32))
    CLc = E(nc.sbuf_tensor("CLce", [128, 16], F32))
    lsum = E(nc.sbuf_tensor("lsume", [128, 1], F32))
    linv = E(nc.sbuf_tensor("linve", [128, 1], F32))
    Fcol = E(nc.sbuf_tensor("Fcole", [128, 16], F32))
    ps_a = E(nc.psum_tensor("ps_ae", [128, 512], F32))
    ps_b = E(nc.psum_tensor("ps_be", [128, 512], F32))
    ps_c = E(nc.psum_tensor("ps_ce", [128, 512], F32))
    ps_d = E(nc.psum_tensor("ps_de", [128, 512], F32))
    ps_t = [E(nc.psum_tensor(f"ps_te{i}", [128, 1024], BF16)) for i in range(2)]
    ps_o = [E(nc.psum_tensor(f"ps_oe{i}", [128, 128], F32)) for i in range(2)]

    s_ld, s_ldq = sm["ld_g"], sm["ld_s"]
    s_m = [sm["m0"], sm["m1"]]
    s_k = [sm["w0"], sm["w1"], sm["sp0"], sm["sp1"]]
    s_v = [sm["v0"], sm["v1"], sm["v2"], sm["v3"]]
    s_pe, s_act, s_dve = sm["pe"], sm["act"], sm["dve"]
    s_st = [sm["st0"], sm["st1"]]

    for g in range(8):
        p.dma("sync", q_sb[:, :].rearrange("p (c t) -> p c t", t=T)[:, g * 4:(g + 1) * 4, :],
              qT_d.rearrange("(c p) t -> p c t", p=128)[:, g * 4:(g + 1) * 4, :], s_ldq)
    v_ldq = s_ldq.n
    for eng in ("tensor", "vector", "scalar"):
        p.wait(eng, s_ldq, v_ldq)

    SBANKS = [ps_a, ps_b, ps_c, ps_d]
    last_exp = [0, 0, 0, 0]
    trans_done = {}
    corr_done = {}
    h_exp_done = {}
    osum_written = {}
    oev_prev = ("", 0)
    kld = {}
    kcons = {}
    vld = {}
    vcons = {}
    kq = [0]
    vq = [0]
    stv = {0: 0, 1: 0}
    mcons = {0: 0, 1: 0}

    def kpos(b):
        return (b % W) * T + (b // W) * 128

    def load_kgroup(j, h, gr):
        ring = kq[0] % 4
        kq[0] += 1
        par = ring
        b0 = gr * 4
        c0, j2 = b0 % W, b0 // W
        if kq[0] > 4:
            p.wait("gpsimd", s_pe, kcons.get(kq[0] - 4, 0))
        for ch in range(2):
            p.dma("gpsimd", kstg[ring][:, ch * 512:(ch + 1) * 512].rearrange("p (b t) -> p b t", t=128),
                  k_full[c0:c0 + 4, (2 * h + ch) * 128:(2 * h + ch + 1) * 128,
                         j2 * 128:(j2 + 1) * 128].rearrange("c p t -> p c t"), s_k[par])
        kld[kq[0]] = (ring, par, s_k[par].n)
        return kq[0]

    def load_vgroup(j, h, g4):
        ring = vq[0] % 4
        vq[0] += 1
        par = ring
        b0 = g4 * 4
        c0, j2 = b0 % W, b0 // W
        if vq[0] > 4:
            p.wait("sync", s_pe, vcons.get(vq[0] - 4, 0))
        p.dma("sync", vstg[ring][:, :].rearrange("p (b t) -> p b t", t=256),
              v_full[c0:c0 + 4, j2 * 128:(j2 + 1) * 128, h * 256:(h + 1) * 256].rearrange("c p t -> p c t"),
              s_v[par])
        vld[vq[0]] = (ring, par, s_v[par].n)
        return vq[0]

    for j in range(NBq):
        nkt = W * j + W
        ngr = nkt // 4
        NK8 = nkt // 8
        mpar = j % 2
        p.wait("gpsimd", s_dve, mcons[mpar])
        for s3 in range(3):
            vmsk = p.dma("gpsimd", msk_sb[mpar][:, s3 * 512:(s3 + 1) * 512], mask3[j, s3],
                         s_m[mpar])

        st_av = {"ptev": {}, "kid": {}, "gk": 0}
        av_q = []

        def score_group(h, gr):
            mcol = mcol2[h % 2]
            lcol = lcol2[h % 2]
            pss = SBANKS[gr % 4]
            kid = st_av["kid"].pop((h, gr))
            ring, par, vkl = kld[kid]
            if gr % 4 == 0:
                p.wait("tensor", s_act, max(last_exp))
            p.wait("tensor", s_k[par], vkl)
            p.op("tensor", lambda e, o=pss[:, :], w=q_sb[:, 2 * h * T + j * 128: 2 * h * T + j * 128 + 128],
                 x=kstg[ring][:, 0:512]: e.matmul(o, w, x, start=True, stop=False))
            vs = p.op("tensor", lambda e, o=pss[:, :],
                      w=q_sb[:, (2 * h + 1) * T + j * 128:(2 * h + 1) * T + j * 128 + 128],
                      x=kstg[ring][:, 512:1024]:
                      e.matmul(o, w, x, start=False, stop=True), sem=s_pe)
            kcons[kid] = vs
            p.wait("vector", s_pe, vs)
            s3 = gr - (ngr - 3)
            if s3 >= 0:
                if h == 0 and gr == max(0, ngr - 3):
                    p.wait("vector", s_m[mpar], vmsk)
                vs2 = p.op("vector", lambda e, o=pss[:, :], i0=pss[:, :],
                           i1=msk_sb[mpar][:, s3 * 512:(s3 + 1) * 512]:
                           e.tensor_tensor(o, i0, i1, ALU.add), sem=s_dve)
                p.wait("vector", s_dve, vs2)
                if h == NH1 - 1 and gr == ngr - 1:
                    mcons[mpar] = vs2
            vm = p.op("vector", lambda e, o=mcol[:, gr:gr + 1], i=pss[:, :]:
                      e.reduce_max(o, i, axis=AX.X, negate=True), sem=s_dve)
            p.wait("scalar", s_dve, vm)
            if gr == 0:
                p.wait("scalar", s_pe, trans_done.get(h % 3, 0))
            last_exp[gr % 4] = p.op("scalar", lambda e, o=pbuf[h % 3][:, gr * 512:(gr + 1) * 512],
                                    i=pss[:, :], b=mcol[:, gr:gr + 1], a=lcol[:, gr:gr + 1]:
                                    e.activation(o, i, AF.Exp, bias=b, scale=1.0, accum_out=a),
                                    sem=s_act)
            if gr == ngr - 1:
                h_exp_done[h] = last_exp[gr % 4]

        def emit_stats(h):
            mcol = mcol2[h % 2]
            lcol = lcol2[h % 2]
            p.wait("vector", s_act, h_exp_done[h])
            vC = p.op("vector", lambda e, o=negM[:, :], i=mcol[:, 0:ngr]:
                      e.tensor_reduce(o, i, axis=AX.X, op=ALU.min), sem=s_dve)
            p.wait("scalar", s_dve, vC)
            vCe = p.op("scalar", lambda e, o=Ccol[:, 0:ngr], i=mcol[:, 0:ngr], b=negM[:, :]:
                       e.activation(o, i, AF.Exp, bias=b, scale=-1.0), sem=s_act)
            p.wait("vector", s_act, vCe)
            v1 = p.op("vector", lambda e, o=CLc[:, 0:ngr], i0=Ccol[:, 0:ngr], i1=lcol[:, 0:ngr]:
                      e.tensor_tensor(o, i0, i1, ALU.mult), sem=s_dve)
            p.wait("vector", s_dve, v1)
            v2 = p.op("vector", lambda e, o=lsum[:, :], i=CLc[:, 0:ngr]:
                      e.reduce_sum(o, i, axis=AX.X), sem=s_dve)
            p.wait("vector", s_dve, v2)
            v3 = p.op("vector", lambda e, o=linv[:, :], i=lsum[:, :]: e.reciprocal(o, i), sem=s_dve)
            p.wait("vector", s_dve, v3)
            v4 = p.op("vector", lambda e, o=Fcol[:, 0:ngr], i=Ccol[:, 0:ngr], sc=linv[:, :]:
                      e.tensor_scalar(o, i, sc, None, ALU.mult), sem=s_dve)
            p.wait("vector", s_dve, v4)
            p.wait("vector", s_pe, trans_done.get(h % 3, 0))
            cv = 0
            for gr in range(ngr):
                cv = p.op("vector", lambda e, o=pbuf[h % 3][:, gr * 512:(gr + 1) * 512],
                          sc=Fcol[:, gr:gr + 1]: e.tensor_scalar(o, o, sc, None, ALU.mult),
                          sem=s_dve if gr == ngr - 1 else None)
            corr_done[h] = cv

        def av_step(h, k8):
            if k8 == 0:
                p.wait("tensor", s_dve, corr_done[h])
                eng0, val0 = oev_prev2[0]
                if eng0 == "scalar":
                    p.wait("tensor", s_act, val0)
                elif eng0 == "vector":
                    p.wait("tensor", s_dve, val0)
            if k8 > 0:
                kprev = k8 - 1
                eng0, val0, tbp = st_av[("pt", h, kprev)]
                p.wait("tensor", s_act if eng0 == "scalar" else s_dve, val0)
                for g4 in (2 * kprev, 2 * kprev + 1):
                    vid = st_av[("v", h, g4)]
                    ring, par, vvl = vld[vid]
                    p.wait("tensor", s_v[par], vvl)
                    avv = 0
                    for sub in range(4):
                        kt = 4 * g4 + sub
                        for ch in range(2):
                            avv = p.op("tensor", lambda e, o=ps_o[ch][:, :],
                                       w=vstg[ring][:, sub * 256 + ch * 128: sub * 256 + ch * 128 + 128],
                                       x=pt_sb[tbp][:, (kt - 8 * kprev) * 128:(kt - 8 * kprev + 1) * 128],
                                       st=(kt == 0), sp=(kt == nkt - 1):
                                       e.matmul(o, w, x, start=st, stop=sp),
                                       sem=s_pe if (sub == 3 and ch == 1) else None)
                    vcons[vid] = avv
            if k8 < NK8:
                gk = st_av["gk"]
                st_av["gk"] = gk + 1
                tb2 = gk % 2
                if gk >= 2:
                    eng0, val0, _ = st_av[("ptg", gk - 2)]
                    p.wait("tensor", s_act if eng0 == "scalar" else s_dve, val0)
                tv = 0
                for sub in range(8):
                    kt = 8 * k8 + sub
                    tv = p.op("tensor", lambda e, o=ps_t[tb2][:, sub * 128:(sub + 1) * 128],
                              i=pbuf[h % 3][:, kt * 128:(kt + 1) * 128], d=idb_sb[:, :]:
                              e.transpose(o, i, d), sem=s_pe if sub == 7 else None)
                eng = "vector" if gk % 2 == 0 else "scalar"
                p.wait(eng, s_pe, tv)
                if eng == "vector":
                    pv = p.op("vector", lambda e, o=pt_sb[tb2][:, :], i=ps_t[tb2][:, :]:
                              e.tensor_copy(o, i), sem=s_dve)
                else:
                    pv = p.op("scalar", lambda e, o=pt_sb[tb2][:, :], i=ps_t[tb2][:, :]:
                              e.copy(o, i), sem=s_act)
                st_av[("pt", h, k8)] = (eng, pv, tb2)
                st_av[("ptg", gk)] = (eng, pv, tb2)
            else:
                avf = s_pe.n
                trans_done[h % 3] = avf
                spar = h % 2
                p.wait("scalar", s_pe, avf)
                if stv[spar]:
                    p.wait("scalar", s_st[spar], stv[spar])
                p.op("scalar", lambda e, o=otst[spar][:, 0:128], i=ps_o[0][:, :]: e.copy(o, i))
                ov = p.op("scalar", lambda e, o=otst[spar][:, 128:256], i=ps_o[1][:, :]:
                          e.copy(o, i), sem=s_act)
                oev_prev2[0] = ("scalar", ov)
                p.wait("sync", s_act, ov)
                for ch in range(2):
                    stv[spar] = p.dma("sync", out[(2 * h + ch) * 128:(2 * h + ch + 1) * 128,
                                                  j * 128:(j + 1) * 128],
                                      otst[spar][:, ch * 128:(ch + 1) * 128], s_st[spar])

        oev_prev2 = [oev_prev]

        def pump():
            if av_q:
                h0, k8 = av_q[0]
                if k8 < NK8:
                    for g4 in (2 * k8, 2 * k8 + 1):
                        if ("v", h0, g4) not in st_av:
                            st_av[("v", h0, g4)] = load_vgroup(j, h0, g4)
                av_step(h0, k8)
                if k8 + 1 > NK8:
                    av_q.pop(0)
                else:
                    av_q[0][1] = k8 + 1

        slots = [(h, gr) for h in range(NH1) for gr in range(ngr)]
        for ci in range(min(2, len(slots))):
            hh, gg = slots[ci]
            st_av["kid"][(hh, gg)] = load_kgroup(j, hh, gg)
        for si, (h, gr) in enumerate(slots):
            if si + 2 < len(slots):
                hh, gg = slots[si + 2]
                st_av["kid"][(hh, gg)] = load_kgroup(j, hh, gg)
            score_group(h, gr)
            pump()
            if gr == 0:
                pump()
            if gr == ngr - 1:
                emit_stats(h)
                av_q.append([h, 0])
        while av_q:
            pump()
    p.wait("sync", s_st[0], s_st[0].n)
    p.wait("sync", s_st[1], s_st[1].n)


def build_l1_attn(SQ=S):
    nc = bass.Bass("TRN2", target_bir_lowering=False)
    io = dict(
        qT=nc.dram_tensor("qT", [D, T], BF16, kind="ExternalInput").ap(),
        k_full=nc.dram_tensor("k_full", [W, D, T], BF16, kind="ExternalInput").ap(),
        v_full=nc.dram_tensor("v_full", [W, T, D], BF16, kind="ExternalInput").ap(),
        mask3=nc.dram_tensor("mask3", [NB, 3, 128, 512], F32, kind="ExternalInput").ap(),
        idf=nc.dram_tensor("idf", [128, 128], F32, kind="ExternalInput").ap(),
        onesf=nc.dram_tensor("onesf", [1, 128], F32, kind="ExternalInput").ap(),
        onescf=nc.dram_tensor("onescf", [128, 1], F32, kind="ExternalInput").ap(),
        oT=nc.dram_tensor("oT", [D, T], BF16, kind="ExternalOutput").ap(),
    )
    with ExitStack() as ctx:
        sm = mk_sems(nc, ctx)
        block = ctx.enter_context(nc.Block())
        p = Prog(nc)
        sh = mk_shared(p, nc, ctx, sm, io)
        emit_e(p, nc, ctx, sm, sh, io, SQ=SQ)
        p.replay(block)
    return nc


def emit_e2(p, nc, ctx, sm, sh, io):
    NQB2 = S // 128
    q_full, k_full, vT_full = io["q_full"], io["k_full"], io["vT_full"]
    trimask, hix, out = io["trimask"], io["hix"], io["o_own"]
    E = ctx.enter_context
    idb_sb = sh["idb_sb"]
    qh = [E(nc.sbuf_tensor(f"qh{i}", [128, S], BF16)) for i in range(2)]
    kh = [E(nc.sbuf_tensor(f"kh{i}", [128, S], BF16)) for i in range(2)]
    vhf = [E(nc.sbuf_tensor(f"vhf{i}", [128, S], BF16)) for i in range(2)]
    vhT = E(nc.sbuf_tensor("vhT", [128, S], BF16))
    kstg2 = [E(nc.sbuf_tensor(f"kstg2{i}", [128, T], BF16)) for i in range(2)]
    pbuf = [E(nc.sbuf_tensor(f"pbufe{i}", [128, S], BF16)) for i in range(3)]
    pt_sb = [E(nc.sbuf_tensor(f"ptste{i}", [128, 1024], BF16)) for i in range(2)]
    msk_sb = E(nc.sbuf_tensor("msk_sb2", [128, 4 * 512], F32))
    hix_sb = E(nc.sbuf_tensor("hix_sb", [128, 32], U32))
    otst = [E(nc.sbuf_tensor(f"otste{i}", [128, 256], BF16)) for i in range(2)]
    mcol2 = [E(nc.sbuf_tensor(f"mcole{i}", [128, 16], F32)) for i in range(2)]
    lcol2 = [E(nc.sbuf_tensor(f"lcole{i}", [128, 16], F32)) for i in range(2)]
    negM = E(nc.sbuf_tensor("negMe", [128, 1], F32))
    Ccol = E(nc.sbuf_tensor("Ccole", [128, 16], F32))
    CLc = E(nc.sbuf_tensor("CLce", [128, 16], F32))
    lsum = E(nc.sbuf_tensor("lsume", [128, 1], F32))
    linv = E(nc.sbuf_tensor("linve", [128, 1], F32))
    Fcol = E(nc.sbuf_tensor("Fcole", [128, 16], F32))
    ps_a = E(nc.psum_tensor("ps_ae", [128, 512], F32))
    ps_b = E(nc.psum_tensor("ps_be", [128, 512], F32))
    ps_c = E(nc.psum_tensor("ps_ce", [128, 512], F32))
    ps_d = E(nc.psum_tensor("ps_de", [128, 512], F32))
    ps_t = [E(nc.psum_tensor(f"ps_te{i}", [128, 1024], BF16)) for i in range(2)]
    ps_o2 = [E(nc.psum_tensor(f"ps_o2e{i}", [128, 128], F32)) for i in range(2)]

    s_ld, s_ldq = sm["ld_g"], sm["ld_s"]
    s_kg = [sm["m0"], sm["m1"]]
    s_rd = [sm["k0"], sm["k1"]]
    s_pe, s_act, s_dve = sm["pe"], sm["act"], sm["dve"]
    s_st = [sm["st0"], sm["st1"]]

    def kpos(b):
        return (b % W) * T + (b // W) * 128

    p.dma("gpsimd", msk_sb[:, :].rearrange("p (v t) -> p v t", t=512),
          trimask.rearrange("v p t -> p v t"), s_ld)
    v_hx = p.dma("gpsimd", hix_sb[:, :], hix[:, :], s_ld)
    p.wait("gpsimd", s_ld, v_hx)

    qk_flat = [q_full.rearrange("c d t -> (c d) t"),
               k_full.rearrange("c d t -> (c d) t"),
               vT_full.rearrange("c d t -> (c d) t")]

    SBANKS = [ps_a, ps_b, ps_c, ps_d]
    last_exp = [0, 0, 0, 0]
    trans_done = {}
    stv = {0: 0, 1: 0}
    oev = [0]
    vCe_prev = [0]
    scnt = [0]
    ptfree = {0: ("scalar", 0), 1: ("scalar", 0)}

    cc_qkv = io.get("cc_qkv")
    post_head = io.get("post_head")
    for hh in range(2):
        if hh == 1:
            p.wait("gpsimd", s_pe, s_pe.n)
        if hh == 0 and cc_qkv is not None:
            p.wait("gpsimd", cc_qkv[0], cc_qkv[0].n)
        qv = 0
        for c in range(W):
            qv = p.dma_ind(qh[0][:, c * T:(c + 1) * T], qk_flat[0],
                           hix_sb[:, (hh * 16 + c):(hh * 16 + c) + 1], s_ld)
            qv = p.dma_ind(qh[1][:, c * T:(c + 1) * T], qk_flat[0],
                           hix_sb[:, (hh * 16 + 8 + c):(hh * 16 + 8 + c) + 1], s_ld)
        rd = {0: 0, 1: 0}
        kv = 0
        for ti in (1, 2):
            if hh == 0 and cc_qkv is not None:
                p.wait("gpsimd", cc_qkv[ti], cc_qkv[ti].n)
            for ch in range(2):
                if ti == 2 and ch == 1:
                    p.wait("sync", s_pe, s_pe.n)
                for c in range(W):
                    sl = c % 2
                    p.wait("gpsimd", s_rd[sl], rd[sl])
                    gv = p.dma_ind(kstg2[sl][:, :], qk_flat[ti],
                                   hix_sb[:, (hh * 16 + ch * 8 + c):(hh * 16 + ch * 8 + c) + 1],
                                   s_kg[sl])
                    p.wait("sync", s_kg[sl], gv)
                    dst = kh[ch] if ti == 1 else vhT
                    kv = p.dma("sync",
                               dst[:, :].rearrange("p (m cu) -> p m cu", cu=1024)
                               [:, :, c * 128:(c + 1) * 128],
                               kstg2[sl][:, :].rearrange("p (m u) -> p m u", u=128),
                               s_rd[sl])
                    rd[sl] = kv
                if ti == 2:
                    p.wait("tensor", s_rd[0], rd[0])
                    p.wait("tensor", s_rd[1], rd[1])
                    for g8 in range(8):
                        tb2 = g8 % 2
                        e0, v0 = ptfree[tb2]
                        p.wait("tensor", s_act if e0 == "scalar" else s_dve, v0)
                        tv = 0
                        for sub in range(8):
                            bk = g8 * 8 + sub
                            tv = p.op("tensor", lambda e, o=ps_t[tb2][:, sub * 128:(sub + 1) * 128],
                                      i=vhT[:, bk * 128:(bk + 1) * 128], d=idb_sb[:, :]:
                                      e.transpose(o, i, d), sem=s_pe if sub == 7 else None)
                        eng = "vector" if g8 % 2 == 0 else "scalar"
                        p.wait(eng, s_pe, tv)
                        if eng == "vector":
                            pv = p.op("vector", lambda e, o=vhf[ch][:, g8 * 1024:(g8 + 1) * 1024],
                                      i=ps_t[tb2][:, :]: e.tensor_copy(o, i), sem=s_dve)
                        else:
                            pv = p.op("scalar", lambda e, o=vhf[ch][:, g8 * 1024:(g8 + 1) * 1024],
                                      i=ps_t[tb2][:, :]: e.copy(o, i), sem=s_act)
                        ptfree[tb2] = (eng, pv)
        v_gth = s_ld.n
        for eng in ("tensor", "vector", "scalar"):
            p.wait(eng, s_ld, v_gth)
            p.wait(eng, s_rd[0], rd[0])
            p.wait(eng, s_rd[1], rd[1])

        st_av = {"ptev": {}}
        av_q = []
        corr_done = {}
        qb_exp_done = {}

        def score_chunk(b, cc):
            mcol = mcol2[b % 2]
            lcol = lcol2[b % 2]
            pss = SBANKS[scnt[0] % 4]
            bank = scnt[0] % 4
            if scnt[0] % 4 == 0 and scnt[0] > 0:
                p.wait("tensor", s_act, last_exp[3])
            scnt[0] += 1
            p.op("tensor", lambda e, o=pss[:, :], w=qh[0][:, kpos(b):kpos(b) + 128],
                 x=kh[0][:, cc * 512:(cc + 1) * 512]: e.matmul(o, w, x, start=True, stop=False))
            vs = p.op("tensor", lambda e, o=pss[:, :], w=qh[1][:, kpos(b):kpos(b) + 128],
                      x=kh[1][:, cc * 512:(cc + 1) * 512]:
                      e.matmul(o, w, x, start=False, stop=True), sem=s_pe)
            if cc == b // 4:
                p.wait("vector", s_pe, vs)
                va = p.op("vector", lambda e, o=pss[:, :], i0=pss[:, :],
                          i1=msk_sb[:, (b % 4) * 512:(b % 4 + 1) * 512]:
                          e.tensor_tensor(o, i0, i1, ALU.add), sem=s_dve)
                p.wait("vector", s_dve, va)
                if cc > 0:
                    p.wait("scalar", s_dve, va)
            if cc == 0:
                p.wait("vector", s_pe, vs)
                vm = p.op("vector", lambda e, o=mcol[:, 0:1], i=pss[:, :]:
                          e.reduce_max(o, i, axis=AX.X, negate=True), sem=s_dve)
                p.wait("scalar", s_dve, vm)
            elif cc != b // 4:
                p.wait("scalar", s_pe, vs)
            last_exp[bank] = p.op("scalar", lambda e, o=pbuf[b % 3][:, cc * 512:(cc + 1) * 512],
                                  i=pss[:, :], b2=mcol[:, 0:1], a=lcol[:, cc:cc + 1]:
                                  e.activation(o, i, AF.Exp, bias=b2, scale=1.0, accum_out=a),
                                  sem=s_act)

        def emit_stats(b):
            ncc = b // 4 + 1
            lcol = lcol2[b % 2]
            p.wait("vector", s_act, qb_exp_done[b])
            v2 = p.op("vector", lambda e, o=lsum[:, :], i=lcol[:, 0:ncc]:
                      e.reduce_sum(o, i, axis=AX.X), sem=s_dve)
            p.wait("vector", s_dve, v2)
            v3 = p.op("vector", lambda e, o=linv[:, :], i=lsum[:, :]: e.reciprocal(o, i),
                      sem=s_dve)
            p.wait("vector", s_dve, v3)
            p.wait("vector", s_pe, trans_done.get(b % 3, 0))

        def emit_corr(b, cc):
            return p.op("vector", lambda e, o=pbuf[b % 3][:, cc * 512:(cc + 1) * 512],
                        sc=linv[:, :]: e.tensor_scalar(o, o, sc, None, ALU.mult),
                        sem=s_dve)

        def av_step(gq, k8):
            NKT_g = gq + 1
            NK8_g = (gq // 8) + 1
            if k8 == 0:
                p.wait("tensor", s_dve, corr_done[gq])
                p.wait("tensor", s_act, oev[0])
            if k8 > 0:
                kprev = k8 - 1
                e0, v0 = st_av["ptev"][kprev]
                p.wait("tensor", s_act if e0 == "scalar" else s_dve, v0)
                n_in = min(8, NKT_g - 8 * kprev)
                for sub in range(n_in):
                    kt = 8 * kprev + sub
                    for ch in range(2):
                        p.op("tensor", lambda e, o=ps_o2[ch][:, :],
                             w=vhf[ch][:, kt * 128:(kt + 1) * 128],
                             x=pt_sb[kprev % 2][:, sub * 128:(sub + 1) * 128],
                             st=(kt == 0), sp=(kt == NKT_g - 1):
                             e.matmul(o, w, x, start=st, stop=sp),
                             sem=s_pe if (kt == NKT_g - 1 and ch == 1) else None)
            if k8 < NK8_g:
                tb2 = k8 % 2
                e0, v0 = ptfree[tb2]
                p.wait("tensor", s_act if e0 == "scalar" else s_dve, v0)
                n_t = min(8, NKT_g - 8 * k8)
                tv = 0
                for sub in range(n_t):
                    kt = 8 * k8 + sub
                    tv = p.op("tensor", lambda e, o=ps_t[tb2][:, sub * 128:(sub + 1) * 128],
                              i=pbuf[gq % 3][:, kt * 128:(kt + 1) * 128], d=idb_sb[:, :]:
                              e.transpose(o, i, d), sem=s_pe if sub == n_t - 1 else None)
                eng = "vector" if k8 % 2 == 0 else "scalar"
                p.wait(eng, s_pe, tv)
                if eng == "vector":
                    pv = p.op("vector", lambda e, o=pt_sb[tb2][:, 0:n_t * 128],
                              i=ps_t[tb2][:, 0:n_t * 128]: e.tensor_copy(o, i), sem=s_dve)
                else:
                    pv = p.op("scalar", lambda e, o=pt_sb[tb2][:, 0:n_t * 128],
                              i=ps_t[tb2][:, 0:n_t * 128]: e.copy(o, i), sem=s_act)
                st_av["ptev"][k8] = (eng, pv)
                ptfree[tb2] = (eng, pv)
            else:
                avv_final = s_pe.n
                trans_done[gq % 3] = avv_final
                par = gq % 2
                p.wait("scalar", s_pe, avv_final)
                p.wait("scalar", s_st[par], stv[par])
                p.op("scalar", lambda e, o=otst[par][:, 0:128], i=ps_o2[0][:, :]: e.copy(o, i))
                ov = p.op("scalar", lambda e, o=otst[par][:, 128:256], i=ps_o2[1][:, :]:
                          e.copy(o, i), sem=s_act)
                oev[0] = ov
                p.wait("sync", s_act, ov)
                kp = kpos(gq)
                p.dma("sync", out[(hh * 2) * 128:(hh * 2 + 1) * 128, kp:kp + 128],
                      otst[par][:, 0:128], s_st[par])
                stv[par] = p.dma("sync", out[(hh * 2 + 1) * 128:(hh * 2 + 2) * 128, kp:kp + 128],
                                 otst[par][:, 128:256], s_st[par])

        slot = [0]

        def pump_av(force=False):
            if av_q and (force or av_q[0][2] <= slot[0]):
                g0, k8, _ = av_q[0]
                if k8 == 0:
                    st_av["ptev"] = {}
                av_step(g0, k8)
                if k8 + 1 > (g0 // 8) + 1:
                    av_q.pop(0)
                else:
                    av_q[0][1] = k8 + 1

        for b in range(NQB2):
            ncc = b // 4 + 1
            while av_q and av_q[0][0] <= b - 3:
                pump_av(force=True)
            for cc in range(ncc):
                if cc == 0:
                    p.wait("scalar", s_pe, trans_done.get(b % 3, 0))
                score_chunk(b, cc)
                if cc == ncc - 1:
                    qb_exp_done[b] = last_exp[(scnt[0] - 1) % 4]
                if b >= 1:
                    ncp = (b - 1) // 4 + 1
                    if ncc == 1:
                        if cc == 0:
                            emit_stats(b - 1)
                            cv = emit_corr(b - 1, 0)
                            corr_done[b - 1] = cv
                            av_q.append([b - 1, 0, slot[0] + 1])
                    else:
                        NCS = ceil_div(ncp, 2)
                        if cc == 0:
                            emit_stats(b - 1)
                        elif cc <= NCS:
                            for c2 in (2 * (cc - 1), 2 * (cc - 1) + 1):
                                if c2 < ncp:
                                    cv = emit_corr(b - 1, c2)
                            if cc == NCS:
                                corr_done[b - 1] = cv
                                av_q.append([b - 1, 0, slot[0] + 1])
                pump_av()
                slot[0] += 1
        emit_stats(NQB2 - 1)
        for cc in range(NQB2 // 4 + 0):
            cv = emit_corr(NQB2 - 1, cc)
        corr_done[NQB2 - 1] = cv
        av_q.append([NQB2 - 1, 0, 0])
        while av_q:
            pump_av(force=True)
        if post_head is not None:
            p.wait("gpsimd", s_st[0], s_st[0].n)
            p.wait("gpsimd", s_st[1], s_st[1].n)
            post_head[hh]()
    p.wait("sync", s_st[0], s_st[0].n)
    p.wait("sync", s_st[1], s_st[1].n)


def build_l1_attn2():
    nc = bass.Bass("TRN2", target_bir_lowering=False)
    io = dict(
        q_full=nc.dram_tensor("q_full", [W, D, T], BF16, kind="ExternalInput").ap(),
        k_full=nc.dram_tensor("k_full", [W, D, T], BF16, kind="ExternalInput").ap(),
        vT_full=nc.dram_tensor("vT_full", [W, D, T], BF16, kind="ExternalInput").ap(),
        trimask=nc.dram_tensor("trimask", [4, 128, 512], F32, kind="ExternalInput").ap(),
        hix=nc.dram_tensor("hix", [128, 32], U32, kind="ExternalInput").ap(),
        idf=nc.dram_tensor("idf", [128, 128], F32, kind="ExternalInput").ap(),
        onesf=nc.dram_tensor("onesf", [1, 128], F32, kind="ExternalInput").ap(),
        onescf=nc.dram_tensor("onescf", [128, 1], F32, kind="ExternalInput").ap(),
        o_own=nc.dram_tensor("o_own", [512, S], BF16, kind="ExternalOutput").ap(),
    )
    with ExitStack() as ctx:
        sm = mk_sems(nc, ctx)
        block = ctx.enter_context(nc.Block())
        p = Prog(nc)
        sh = mk_shared(p, nc, ctx, sm, io)
        emit_e2(p, nc, ctx, sm, sh, io)
        p.replay(block)
    return nc


def emit_f(p, nc, ctx, sm, sh, io):
    NL, RK1 = 16, 32
    ZC1 = NL * RK1 + RK1
    oT_d, x2T, out = io.get("oT"), io["x2T"], io["outT"]
    wo_pk, kn_pk, fb_d, az1, gate_exp = io["wo_pk"], io["kn_pk"], io["fbias"], io["az1"], io["gate_exp"]
    x1psp = nc.dram_tensor("x1psp_f", [32, 128, T], F32)
    x1nsp = nc.dram_tensor("x1nsp_f", [32, 128, T], F32)
    x2psp = nc.dram_tensor("x2psp_f", [32, 128, T], F32)
    if True:
        E = ctx.enter_context
        idf_sb = sh["idf_sb"]
        idb_sb = sh["idb_sb"]
        ones_r = sh["ones_r"]
        ones_cb = sh["ones_cb"]
        eps5 = sh["eps5"]
        big = E(nc.sbuf_tensor("bigf", [128, 32 * T], BF16))
        wbuf = [E(nc.sbuf_tensor(f"wbuff{i}", [128, 37 * 128], BF16)) for i in range(2)]
        xc = [E(nc.sbuf_tensor(f"xcf{i}", [128, T], F32)) for i in range(2)]
        t1 = [E(nc.sbuf_tensor(f"t1f{i}", [128, T], F32)) for i in range(2)]
        tbf = [E(nc.sbuf_tensor(f"tbff{i}", [128, T], BF16)) for i in range(2)]
        sqb = [E(nc.sbuf_tensor(f"sqbf{i}", [128, T], BF16)) for i in range(2)]
        mrow = E(nc.sbuf_tensor("mrowf", [1, T], F32))
        qrow = E(nc.sbuf_tensor("qrowf", [1, T], F32))
        vrow = E(nc.sbuf_tensor("vrowf", [1, T], F32))
        rrow = E(nc.sbuf_tensor("rrowf", [1, T], F32))
        mb = E(nc.sbuf_tensor("mbf", [128, T], F32))
        rb = E(nc.sbuf_tensor("rbf", [128, T], F32))
        zrhs = E(nc.sbuf_tensor("zrhsf", [128, 32 * ZC1], BF16))
        zwb = E(nc.sbuf_tensor("zwbf", [128, ZC1], BF16))
        zt128 = E(nc.sbuf_tensor("zt128", [128, 4 * T], BF16))
        zt32 = E(nc.sbuf_tensor("zt32", [RK1, T], BF16))
        gbc = E(nc.sbuf_tensor("gbcf", [128, ZC1], F32))
        grow = [E(nc.sbuf_tensor(f"growf{i}", [1, ZC1], F32)) for i in range(NB)]
        fb_c = None
        ostg = [E(nc.sbuf_tensor(f"ostgf{i}", [128, T], F32)) for i in range(2)]

        ps_1 = E(nc.psum_tensor("ps_1f", [128, 512], F32))
        ps_2 = E(nc.psum_tensor("ps_2f", [128, 512], F32))
        ps_bt = [E(nc.psum_tensor(f"ps_btf{i}", [128, 128], BF16)) for i in range(2)]
        ps_s = [E(nc.psum_tensor(f"ps_sf{i}", [1, 512], F32)) for i in range(2)]
        ps_q = [E(nc.psum_tensor(f"ps_qf{i}", [1, 512], F32)) for i in range(2)]

        s_ld, s_ld2 = sm["ld_g"], sm["ld_s"]
        s_w = [sm["w0"], sm["w1"]]
        s_x = [sm["x0"], sm["x1"]]
        s_pe, s_act, s_dve = sm["pe"], sm["act"], sm["dve"]
        s_sp = [sm["sp0"], sm["sp1"]]
        s_spl = [sm["k0"], sm["k1"]]
        s_st = [sm["st0"], sm["st1"]]

        p.dma("gpsimd", zrhs[:, :].rearrange("p (i z) -> p i z", z=ZC1),
              az1.rearrange("(i p) z -> p i z", p=128), s_ld)
        for jj in range(NB):
            p.dma("gpsimd", grow[jj][:, :], gate_exp[jj:jj + 1, :], s_ld)
        if "oix" in io:
            oix_sb = ctx.enter_context(nc.sbuf_tensor("oix_sb", [128, 32], U32))
            v_ox = p.dma("gpsimd", oix_sb[:, :], io["oix"][:, :], s_ld)
            p.wait("gpsimd", s_ld, v_ox)
            o_flats = [io["o_full_a"].rearrange("c hs (tb t) -> (c hs tb) t", t=T),
                       io["o_full_b"].rearrange("c hs (tb t) -> (c hs tb) t", t=T)]
            cc_o = io.get("cc_o")
            ics = [ic for ic in range(32) if ic % 4 < 2] + \
                  [ic for ic in range(32) if ic % 4 >= 2]
            for n_, ic in enumerate(ics):
                half = 0 if ic % 4 < 2 else 1
                if cc_o is not None and n_ in (0, 16):
                    p.wait("gpsimd", cc_o[half], cc_o[half].n)
                p.dma_ind(big[:, ic * T:(ic + 1) * T], o_flats[half],
                          oix_sb[:, ic:ic + 1], s_ld)
        else:
            for g in range(8):
                p.dma("sync", big[:, :].rearrange("p (c t) -> p c t", t=T)[:, g * 4:(g + 1) * 4, :],
                      oT_d.rearrange("(c p) t -> p c t", p=128)[:, g * 4:(g + 1) * 4, :], s_ld2)
        cv_fb = colvec_dma(p, nc, ctx, "fb_colf", fb_d, D, s_ld)
        v_ld, v_ld2 = s_ld.n, s_ld2.n
        for eng in ("tensor", "vector", "scalar"):
            p.wait(eng, s_ld, v_ld)
            p.wait(eng, s_ld2, v_ld2)
        fb_c = colvec_compute(p, cv_fb, idf_sb, ps_1, 0, s_pe, s_act)
        v_cv = s_act.n
        p.wait("scalar", s_act, v_cv)
        p.wait("tensor", s_act, v_cv)

        def pass1(emit_mms, bias_col, res_src, spill, wload=None, pre_wait=None):
            ev = {}
            addv = {}
            spv = {0: 0, 1: 0}
            stat_of = {}
            vsq_of = {}
            vtb_of = {}

            def emit_statmms(oc2):
                par2 = oc2 % 2
                p.wait("tensor", s_dve, vtb_of[oc2])
                p.wait("tensor", s_act, vsq_of[oc2])
                sm = 0
                for th2 in range(2):
                    p.op("tensor", lambda e, o=ps_s[th2][:, :], w=ones_cb[:, :],
                         x=tbf[par2][:, th2 * 512:(th2 + 1) * 512], st=(oc2 == 0), sp=(oc2 == 31):
                         e.matmul(o, w, x, start=st, stop=sp))
                    sm = p.op("tensor", lambda e, o=ps_q[th2][:, :], w=ones_cb[:, :],
                              x=sqb[par2][:, th2 * 512:(th2 + 1) * 512], st=(oc2 == 0), sp=(oc2 == 31):
                              e.matmul(o, w, x, start=st, stop=sp),
                              sem=s_pe if th2 == 1 else None)
                stat_of[oc2] = sm

            for oc in range(32):
                par = oc % 2
                if wload is not None:
                    wload(oc)
                if oc == 0 and pre_wait is not None:
                    for sm_, vv_ in pre_wait:
                        p.wait("sync", sm_, vv_)
                if oc >= 2:
                    p.wait("sync", s_dve, addv[oc - 2])
                p.dma("sync", xc[par][:, :], res_src(oc), s_x[par])
                vx = s_x[par].n
                for th in range(2):
                    k = oc * 2 + th
                    pst = ps_1 if k % 2 == 0 else ps_2
                    if k >= 2:
                        p.wait("tensor", s_act, ev[k - 2])
                    vk = emit_mms(oc, th, pst)
                    p.wait("scalar", s_pe, vk)
                    if oc >= 2 and th == 0:
                        p.wait("scalar", s_spl[par], spv[par])
                    if bias_col is None:
                        ev[k] = p.op("scalar", lambda e, o=t1[par][:, th * 512:(th + 1) * 512],
                                     i=pst[:, 0:512]: e.copy(o, i), sem=s_act)
                    else:
                        ev[k] = p.op("scalar", lambda e, o=t1[par][:, th * 512:(th + 1) * 512],
                                     i=pst[:, 0:512], b=bias_col[:, oc:oc + 1]:
                                     e.activation(o, i, AF.Identity, bias=b, scale=1.0), sem=s_act)
                p.wait("vector", s_act, ev[oc * 2 + 1])
                p.wait("vector", s_x[par], vx)
                addv[oc] = p.op("vector", lambda e, o=t1[par][:, :], i0=t1[par][:, :],
                                i1=xc[par][:, :]: e.tensor_tensor(o, i0, i1, ALU.add), sem=s_dve)
                p.wait("scalar", s_dve, addv[oc])
                if oc >= 2:
                    p.wait("scalar", s_pe, stat_of.get(oc - 2, 0))
                vsq_of[oc] = p.op("scalar", lambda e, o=sqb[par][:, :], i=t1[par][:, :]:
                                  e.square(o, i), sem=s_act)
                p.wait("vector", s_dve, addv[oc])
                if oc >= 2:
                    p.wait("vector", s_pe, stat_of.get(oc - 2, 0))
                vtb_of[oc] = p.op("vector", lambda e, o=tbf[par][:, :], i=t1[par][:, :]:
                                  e.tensor_copy(o, i), sem=s_dve)
                if oc >= 1:
                    emit_statmms(oc - 1)
                p.wait("sync", s_dve, addv[oc])
                spv[par] = p.dma("sync", spill[oc], t1[par][:, :], s_spl[par])
            emit_statmms(31)
            return spv, stat_of[31]

        def ln_finalize(stat_mark):
            p.wait("scalar", s_pe, stat_mark)
            c1 = p.op("scalar", lambda e, o=mrow[0:1, 0:512], i=ps_s[0][:, :]:
                      e.activation(o, i, AF.Copy, bias=0.0, scale=1.0 / D), sem=s_act)
            c2 = p.op("scalar", lambda e, o=mrow[0:1, 512:1024], i=ps_s[1][:, :]:
                      e.activation(o, i, AF.Copy, bias=0.0, scale=1.0 / D), sem=s_act)
            p.wait("scalar", s_act, c2)
            c3 = p.op("scalar", lambda e, o=qrow[:, :], i=mrow[:, :]: e.square(o, i), sem=s_act)
            p.wait("vector", s_act, c3)
            v1 = 0
            for th in range(2):
                v1 = p.op("vector", lambda e, o=vrow[0:1, th * 512:(th + 1) * 512],
                          i0=ps_q[th][:, :], i1=qrow[0:1, th * 512:(th + 1) * 512]:
                          e.scalar_tensor_tensor(o, i0, 1.0 / D, i1, ALU.mult, ALU.subtract),
                          sem=s_dve)
            p.wait("scalar", s_dve, v1)
            c4 = p.op("scalar", lambda e, o=vrow[:, :], i=vrow[:, :], b=eps5[:, :]:
                      e.activation(o, i, AF.Sqrt, bias=b, scale=1.0), sem=s_act)
            p.wait("vector", s_act, c4)
            v2 = p.op("vector", lambda e, o=rrow[:, :], i=vrow[:, :]: e.reciprocal(o, i), sem=s_dve)
            p.wait("tensor", s_dve, v2)
            p.wait("tensor", s_act, c2)
            vmb = vrb = 0
            for th in range(2):
                if th == 1:
                    p.wait("tensor", s_dve, vmb)
                vb1 = p.op("tensor", lambda e, o=ps_1[:, :], w=ones_r[:, :],
                           x=mrow[0:1, th * 512:(th + 1) * 512]:
                           e.matmul(o, w, x, start=True, stop=True), sem=s_pe)
                p.wait("vector", s_pe, vb1)
                vmb = p.op("vector", lambda e, o=mb[:, th * 512:(th + 1) * 512], i=ps_1[:, :]:
                           e.tensor_copy(o, i), sem=s_dve)
                if th == 1:
                    p.wait("tensor", s_dve, vrb)
                vb2 = p.op("tensor", lambda e, o=ps_2[:, :], w=ones_r[:, :],
                           x=rrow[0:1, th * 512:(th + 1) * 512]:
                           e.matmul(o, w, x, start=True, stop=True), sem=s_pe)
                p.wait("vector", s_pe, vb2)
                vrb = p.op("vector", lambda e, o=rb[:, th * 512:(th + 1) * 512], i=ps_2[:, :]:
                           e.tensor_copy(o, i), sem=s_dve)
            return vrb

        def pass2(spill_in, sink, vready, pre_wait=None):
            nv = {}
            for oc in range(32):
                par = oc % 2
                if oc == 0 and pre_wait is not None:
                    for sm_, vv_ in pre_wait:
                        p.wait("sync", sm_, vv_)
                if oc >= 2:
                    p.wait("sync", s_dve, nv[oc - 2])
                p.dma("sync", xc[par][:, :], spill_in[oc], s_x[par])
                vx = s_x[par].n
                p.wait("vector", s_x[par], vx)
                if oc == 0:
                    p.wait("vector", s_dve, vready)
                va1 = p.op("vector", lambda e, o=ostg[par][:, :], i0=xc[par][:, :], i1=mb[:, :]:
                           e.tensor_tensor(o, i0, i1, ALU.subtract), sem=s_dve)
                p.wait("vector", s_dve, va1)
                nv[oc] = p.op("vector", lambda e, o=ostg[par][:, :], i0=ostg[par][:, :], i1=rb[:, :]:
                              e.tensor_tensor(o, i0, i1, ALU.mult), sem=s_dve)
                sink(oc, par, nv[oc])
            return nv

        wv_of = {}

        def wload1(oc):
            par = oc % 2
            if oc >= 2:
                p.wait("gpsimd", s_pe, wv_of[oc - 2])
            p.dma("gpsimd", wbuf[par][:, 0:32 * 128], wo_pk[oc], s_w[par])

        def mm_outproj(oc, th, pst):
            par = oc % 2
            if th == 0:
                p.wait("tensor", s_w[par], s_w[par].n)
            vd = 0
            for ic in range(32):
                vd = p.op("tensor", lambda e, o=pst[:, 0:512], w=wbuf[par][:, ic * 128:(ic + 1) * 128],
                          x=big[:, ic * T + th * 512: ic * T + th * 512 + 512],
                          st=(ic == 0), sp=(ic == 31): e.matmul(o, w, x, start=st, stop=sp),
                          sem=s_pe if ic == 31 else None)
            wv_of[oc] = vd
            return vd

        spv1, stat1 = pass1(mm_outproj, None, lambda oc: x2T[oc * 128:(oc + 1) * 128, :], x1psp,
                            wload=wload1)
        vr1 = ln_finalize(stat1)
        pw1 = [(s_spl[0], s_spl[0].n), (s_spl[1], s_spl[1].n)]

        sp1n = {0: 0, 1: 0}
        v_oproj_done = stat1

        def sink1(oc, par, vn):
            p.wait("scalar", s_dve, vn)
            if oc == 0:
                p.wait("scalar", s_pe, v_oproj_done)
            vcp = p.op("scalar", lambda e, o=big[:, oc * T:(oc + 1) * T], i=ostg[par][:, :]:
                       e.copy(o, i), sem=s_act)
            p.wait("gpsimd", s_dve, vn)
            sp1n[par] = p.dma("gpsimd", x1nsp[oc], ostg[par][:, :], s_sp[par])
            if oc < 30:
                p.wait("vector", s_sp[par], sp1n[par])
                p.wait("vector", s_act, vcp)

        pass2(x1psp, sink1, vr1, pre_wait=pw1)
        v_x1bf = s_act.n
        pw2 = [(s_sp[0], s_sp[0].n), (s_sp[1], s_sp[1].n)]

        p.wait("tensor", s_act, v_x1bf)
        ztr = {}
        gb_of = {}
        for ts in range(NB):
            p.wait("tensor", s_dve, gb_of.get(ts - 1, 0))
            vg1 = p.op("tensor", lambda e, o=ps_1[:, 0:512], w=ones_r[:, :],
                       x=grow[ts][0:1, 0:512]: e.matmul(o, w, x, start=True, stop=True), sem=s_pe)
            p.wait("vector", s_pe, vg1)
            vgc = p.op("vector", lambda e, o=gbc[:, 0:512], i=ps_1[:, 0:512]:
                       e.tensor_copy(o, i), sem=s_dve)
            p.wait("tensor", s_dve, vgc)
            vg2 = p.op("tensor", lambda e, o=ps_1[:, 0:32], w=ones_r[:, :],
                       x=grow[ts][0:1, 512:544]: e.matmul(o, w, x, start=True, stop=True), sem=s_pe)
            p.wait("vector", s_pe, vg2)
            vgc2 = p.op("vector", lambda e, o=gbc[:, 512:544], i=ps_1[:, 0:32]:
                        e.tensor_copy(o, i), sem=s_dve)
            for grp, (z0, zn) in enumerate([(0, 512), (512, 32)]):
                if grp == 1:
                    p.wait("tensor", s_dve, zw_mark)
                vz = 0
                for ic in range(32):
                    vz = p.op("tensor", lambda e, o=ps_2[:, 0:zn], w=big[:, ic * T + ts * 128: ic * T + ts * 128 + 128],
                              x=zrhs[:, ic * ZC1 + z0: ic * ZC1 + z0 + zn], st=(ic == 0), sp=(ic == 31):
                              e.matmul(o, w, x, start=st, stop=sp), sem=s_pe if ic == 31 else None)
                p.wait("vector", s_pe, vz)
                p.wait("vector", s_dve, vgc2 if grp == 0 else vgc2)
                zw_mark = p.op("vector", lambda e, o=zwb[:, z0:z0 + zn], i0=ps_2[:, 0:zn],
                               i1=gbc[:, z0:z0 + zn]: e.tensor_tensor(o, i0, i1, ALU.mult),
                               sem=s_dve)
                p.wait("vector", s_dve, zw_mark)
            gb_of[ts] = zw_mark
            p.wait("tensor", s_dve, zw_mark)
            for kkk in range(5):
                if (ts * 5 + kkk) >= 2:
                    p.wait("tensor", s_act, ztr[ts * 5 + kkk - 2])
                tb2 = (ts * 5 + kkk) % 2
                if kkk < 4:
                    vt = p.op("tensor", lambda e, o=ps_bt[tb2][:, :],
                              i=zwb[:, kkk * 128:(kkk + 1) * 128], d=idb_sb[:, :]:
                              e.transpose(o, i, d), sem=s_pe)
                    p.wait("scalar", s_pe, vt)
                    ztr[ts * 5 + kkk] = p.op("scalar",
                        lambda e, o=zt128[:, kkk * T + ts * 128: kkk * T + ts * 128 + 128],
                        i=ps_bt[tb2][:, :]: e.copy(o, i), sem=s_act)
                else:
                    vt = p.op("tensor", lambda e, o=ps_bt[tb2][0:RK1, :],
                              i=zwb[:, 512:544], d=idb_sb[:, :]:
                              e.transpose(o, i, d), sem=s_pe)
                    p.wait("scalar", s_pe, vt)
                    ztr[ts * 5 + kkk] = p.op("scalar",
                        lambda e, o=zt32[:, ts * 128:(ts + 1) * 128],
                        i=ps_bt[tb2][0:RK1, :]: e.copy(o, i), sem=s_act)
        v_zwT = ztr[NB * 5 - 1]

        p.wait("tensor", s_act, v_zwT)
        wv2_of = {}

        def wload2(oc):
            par = oc % 2
            if oc >= 2:
                p.wait("gpsimd", s_pe, wv2_of[oc - 2])
            p.dma("gpsimd", wbuf[par][:, :], kn_pk[oc], s_w[par])

        def mm_ffn(oc, th, pst):
            par = oc % 2
            if th == 0:
                p.wait("tensor", s_w[par], s_w[par].n)
            for ic in range(32):
                p.op("tensor", lambda e, o=pst[:, 0:512], w=wbuf[par][:, ic * 128:(ic + 1) * 128],
                     x=big[:, ic * T + th * 512: ic * T + th * 512 + 512], st=(ic == 0):
                     e.matmul(o, w, x, start=st, stop=False))
            for kkk in range(4):
                p.op("tensor", lambda e, o=pst[:, 0:512], w=wbuf[par][:, (32 + kkk) * 128:(33 + kkk) * 128],
                     x=zt128[:, kkk * T + th * 512: kkk * T + th * 512 + 512]:
                     e.matmul(o, w, x, start=False, stop=False))
            vd = p.op("tensor", lambda e, o=pst[:, 0:512], w=wbuf[par][0:RK1, 36 * 128:37 * 128],
                      x=zt32[:, th * 512:(th + 1) * 512]:
                      e.matmul(o, w, x, start=False, stop=True), sem=s_pe)
            wv2_of[oc] = vd
            return vd

        spv2, stat2 = pass1(mm_ffn, fb_c, lambda oc: x1nsp[oc], x2psp, wload=wload2,
                            pre_wait=pw2)
        vr2 = ln_finalize(stat2)
        pw3 = [(s_spl[0], s_spl[0].n), (s_spl[1], s_spl[1].n)]

        stv = {0: 0, 1: 0}

        def sink2(oc, par, vn):
            p.wait("sync", s_dve, vn)
            stv[par] = p.dma("sync", out[oc * 128:(oc + 1) * 128, :], ostg[par][:, :], s_st[par])
            if oc < 30:
                p.wait("vector", s_st[par], stv[par])
        pass2(x2psp, sink2, vr2, pre_wait=pw3)
        p.wait("sync", s_st[0], s_st[0].n)
        p.wait("sync", s_st[1], s_st[1].n)


def build_l1_post():
    NL, RK1 = 16, 32
    ZC1 = NL * RK1 + RK1
    nc = bass.Bass("TRN2", target_bir_lowering=False)
    io = dict(
        oT=nc.dram_tensor("oT", [D, T], BF16, kind="ExternalInput").ap(),
        x2T=nc.dram_tensor("x2T", [D, T], F32, kind="ExternalInput").ap(),
        wo_pk=nc.dram_tensor("wo_pk", [32, 128, 32 * 128], F32, kind="ExternalInput").ap(),
        kn_pk=nc.dram_tensor("kn_pk", [32, 128, 37 * 128], F32, kind="ExternalInput").ap(),
        fbias=nc.dram_tensor("fbias", [D], F32, kind="ExternalInput").ap(),
        az1=nc.dram_tensor("az1", [D, ZC1], F32, kind="ExternalInput").ap(),
        gate_exp=nc.dram_tensor("gate_exp", [NB, ZC1], F32, kind="ExternalInput").ap(),
        idf=nc.dram_tensor("idf", [128, 128], F32, kind="ExternalInput").ap(),
        onesf=nc.dram_tensor("onesf", [1, 128], F32, kind="ExternalInput").ap(),
        onescf=nc.dram_tensor("onescf", [128, 1], F32, kind="ExternalInput").ap(),
        outT=nc.dram_tensor("outT", [D, T], F32, kind="ExternalOutput").ap(),
    )
    with ExitStack() as ctx:
        sm = mk_sems(nc, ctx)
        block = ctx.enter_context(nc.Block())
        p = Prog(nc)
        sh = mk_shared(p, nc, ctx, sm, io)
        emit_f(p, nc, ctx, sm, sh, io)
        p.replay(block)
    return nc


def build_fused():
    NE, RK = 4, 8
    ZC = NE * RK + NE
    NL, RK1 = 16, 32
    ZC1 = NL * RK1 + RK1
    nc = bass.Bass("TRN2", target_bir_lowering=False)

    def inp(name, shape, dt=F32):
        return nc.dram_tensor(name, shape, dt, kind="ExternalInput").ap()

    io = dict(
        xT=inp("xT", [D, T]),
        wpk_a=inp("wpk_a", [NOC_A, 128, 32 * 128]),
        bqc=inp("bqc", [QL]), bkvc=inp("bkvc", [KVL + ROPE0]),
        idf=inp("idf", [128, 128]), onesf=inp("onesf", [1, 128]), onescf=inp("onescf", [128, 1]),
        wqn=inp("wqn", [HPC0, 128, 12 * 128]), wqr=inp("wqr", [HPC0, 128, 12 * 64]),
        wkk=inp("wkk", [HPC0, 128, 4 * 128]), wkv=inp("wkv", [HPC0, 128, 4 * 128]),
        bqn_c=inp("bqn_c", [128, HPC0]), bqr_c=inp("bqr_c", [128, HPC0]),
        bkk_c=inp("bkk_c", [128, HPC0]), bkv_r=inp("bkv_r", [HPC0, 128]),
        woT=inp("woT", [VH0, D]), bo=inp("bo", [D]),
        wb_pk=inp("wb_pk", [32, 128, 32 * 128]), b_base=inp("b_base", [D]),
        az_rhs=inp("az_rhs", [D, ZC]), bcT=inp("bcT", [NE * RK, D]), lp_br=inp("lp_br", [2, NE]),
        wq_pk=inp("wq_pk", [32, 128, 32 * 128]), wk_pk=inp("wk_pk", [32, 128, 32 * 128]),
        wv_pk=inp("wv_pk", [32, 128, 32 * 128]),
        cosd=inp("cosd", [ROT1, T]), sins=inp("sins", [ROT1, T]),
        trimask=inp("trimask", [4, 128, 512]),
        hix=inp("hix", [128, 32], U32), oix=inp("oix", [128, 32], U32),
        wo_pk=inp("wo_pk", [32, 128, 32 * 128]), kn_pk=inp("kn_pk", [32, 128, 37 * 128]),
        fbias=inp("fbias", [D]), az1=inp("az1", [D, ZC1]), gate_exp=inp("gate_exp", [NB, ZC1]),
        outT=nc.dram_tensor("outT", [D, T], F32, kind="ExternalOutput").ap(),
    )
    qckv_own = nc.dram_tensor("i_qckv_own", [NOUT_A, T], BF16)
    qckv_full = nc.dram_tensor("i_qckv_full", [W, NOUT_A, T], BF16, addr_space="Shared")
    osum = nc.dram_tensor("i_osum", [S, VH0], F32)
    o_own = nc.dram_tensor("i_o_own", [T, VH0], F32)
    x2T_i = nc.dram_tensor("i_x2T", [D, T], F32)
    qT_i = nc.dram_tensor("i_qT", [D, T], BF16)
    kT_i = nc.dram_tensor("i_kT", [D, T], BF16)
    vT_i = nc.dram_tensor("i_vT", [D, T], BF16)
    q_full_i = nc.dram_tensor("i_q_full", [W, D, T], BF16, addr_space="Shared")
    k_full_i = nc.dram_tensor("i_k_full", [W, D, T], BF16, addr_space="Shared")
    vT_full_i = nc.dram_tensor("i_vT_full", [W, D, T], BF16, addr_space="Shared")
    o_own_i = nc.dram_tensor("i_oh_own", [512, S], BF16)
    o_full_a = nc.dram_tensor("i_o_full_a", [W, 256, S], BF16, addr_space="Shared")
    o_full_b = nc.dram_tensor("i_o_full_b", [W, 256, S], BF16, addr_space="Shared")
    rg = [list(range(W))]

    def coll(p, sm, kind, op, in_ap, out_ap, ccname):
        s = sm[ccname]
        v = s.inc(1)
        p.emit("gpsimd", lambda e, i=in_ap, o=out_ap, sh_=s.h:
               e.collective_compute(kind, op, replica_groups=rg, ins=[i], outs=[o]).then_inc(sh_, 1))
        for eng in Prog.ENGINES:
            p.wait(eng, s, v)

    with ExitStack() as octx:
        sm = mk_sems(nc, octx)
        block = octx.enter_context(nc.Block())
        p = Prog(nc)
        sh = mk_shared(p, nc, octx, sm, io)
        with ExitStack() as c1:
            emit_a(p, nc, c1, sm, sh, dict(io, qckvT=qckv_own.ap()))
            barrier(p, sm)
        coll(p, sm, "AllGather", ALU.bypass,
             qckv_own.ap(), qckv_full.ap().rearrange("w n t -> (w n) t"), "cc0")
        with ExitStack() as c2:
            emit_b(p, nc, c2, sm, sh, dict(io, qckv=qckv_full.ap(), osum_part=osum.ap()))
            barrier(p, sm)
        coll(p, sm, "ReduceScatter", ALU.add, osum.ap(), o_own.ap(), "cc1")
        with ExitStack() as c3:
            emit_c(p, nc, c3, sm, sh, dict(io, oparts=o_own.ap(), x2T=x2T_i.ap()), rs_mode=True)
            barrier(p, sm)
        def ag_async(in_ap, out_ap, ccname):
            def hook(i=in_ap, o=out_ap, n=ccname):
                s = sm[n]
                s.inc(1)
                p.emit("gpsimd", lambda e, i2=i, o2=o, sh_=s.h:
                       e.collective_compute("AllGather", ALU.bypass, replica_groups=rg,
                                            ins=[i2], outs=[o2]).then_inc(sh_, 1))
            return hook

        with ExitStack() as c4:
            emit_d(p, nc, c4, sm, sh,
                   dict(io, x2T=x2T_i.ap(), qT=qT_i.ap(), kT=kT_i.ap(), vT=vT_i.ap(),
                        post_pack=[
                            ag_async(qT_i.ap(), q_full_i.ap().rearrange("w d t -> (w d) t"), "cc0"),
                            ag_async(kT_i.ap(), k_full_i.ap().rearrange("w d t -> (w d) t"), "cc2"),
                            ag_async(vT_i.ap(), vT_full_i.ap().rearrange("w d t -> (w d) t"), "cc3"),
                        ]))
            barrier(p, sm, skip=("cc0", "cc2", "cc3"))
        with ExitStack() as c5:
            emit_e2(p, nc, c5, sm, sh,
                    dict(io, q_full=q_full_i.ap(), k_full=k_full_i.ap(),
                         vT_full=vT_full_i.ap(), o_own=o_own_i.ap(),
                         cc_qkv=(sm["cc0"], sm["cc2"], sm["cc3"]),
                         post_head=[
                             ag_async(o_own_i.ap()[0:256, :],
                                      o_full_a.ap().rearrange("w h t -> (w h) t"), "stv0"),
                             ag_async(o_own_i.ap()[256:512, :],
                                      o_full_b.ap().rearrange("w h t -> (w h) t"), "stv1"),
                         ]))
            barrier(p, sm, skip=("stv0", "stv1"))
        with ExitStack() as c6:
            emit_f(p, nc, c6, sm, sh, dict(io, o_full_a=o_full_a.ap(), o_full_b=o_full_b.ap(),
                                           x2T=x2T_i.ap(), cc_o=(sm["stv0"], sm["stv1"])))
        p.replay(block)
    return nc


def host_inputs(inp):
    x = np.asarray(inp["L0_attn_hidden_size"])
    hp = {}
    hp["idf"] = np.eye(128, dtype=np.float32)
    hp["onesf"] = np.ones((1, 128), np.float32)
    hp["onescf"] = np.ones((128, 1), np.float32)
    wqcT = _f32(np.asarray(inp["L0_attn_Wq_c"]).T)
    wkvcT = _f32(np.asarray(inp["L0_attn_Wkv_c"]).T)
    hp["wpk_a"] = lhsT_pack(np.concatenate([wqcT, wkvcT], axis=1), pad_out=NOC_A * 128)
    hp["bqc"] = _f32(inp["L0_attn_bq_c"])
    hp["bkvc"] = _f32(inp["L0_attn_bkv_c"])
    act = np.asarray(inp["L0_ffn_active"])
    A = np.asarray(inp["L0_ffn_lora_A"])[act]
    Bm = np.asarray(inp["L0_ffn_lora_B"])[act]
    Wr = np.asarray(inp["L0_ffn_router_W"])[act]
    br = np.asarray(inp["L0_ffn_router_b"])[act]
    prior = np.asarray(inp["L0_ffn_domain_prior"])[act]
    hp["az_rhs"] = _f32(np.concatenate([A.reshape(-1, D).T, Wr.T], axis=1))
    hp["bcT"] = _f32(Bm.transpose(0, 2, 1).reshape(-1, D))
    hp["lp_br"] = _f32(np.stack([prior, br]))
    hp["wb_pk"] = lhsT_pack(_f32(np.asarray(inp["L0_ffn_W_base"]).T))
    hp["woT"] = _f32(np.asarray(inp["L0_attn_Wo"]).T)
    hp["bo"] = _f32(inp["L0_attn_bo"])
    hp["b_base"] = _f32(inp["L0_ffn_b_base"])
    hp["wq_pk"] = lhsT_pack(_f32(np.asarray(inp["L1_attn_w_q"]).T))
    hp["wk_pk"] = lhsT_pack(_f32(np.asarray(inp["L1_attn_w_k"]).T))
    hp["wv_pk"] = lhsT_pack(_f32(np.asarray(inp["L1_attn_w_v"]).T))
    lA = np.asarray(inp["L1_ffn_lora_A"])
    lB = np.asarray(inp["L1_ffn_lora_B"])
    lAs = np.asarray(inp["L1_ffn_lora_A_shared"])
    lBs = np.asarray(inp["L1_ffn_lora_B_shared"])
    gate = _f32(inp["L1_ffn_gate"])
    hp["az1"] = _f32(np.concatenate([lA.reshape(16 * 32, D).T, lAs.T], axis=1))
    bcat = np.concatenate([lB.transpose(0, 2, 1).reshape(16 * 32, D), lBs.T], axis=0)
    kn_pk = np.zeros((32, 128, 37 * 128), np.float32)
    kn_pk[:, :, 0:32 * 128] = lhsT_pack(_f32(np.asarray(inp["L1_ffn_kernel"])))
    for k3 in range(4):
        kn_pk[:, :, (32 + k3) * 128:(33 + k3) * 128] = \
            bcat[k3 * 128:(k3 + 1) * 128].reshape(128, 32, 128).transpose(1, 0, 2)
    kn_pk[:, 0:32, 36 * 128:] = bcat[512:544].reshape(32, 32, 128).transpose(1, 0, 2)
    hp["kn_pk"] = kn_pk
    hp["gate_exp"] = _f32(np.concatenate([np.repeat(gate, 32, axis=1), np.ones((NB, 32))], axis=1))
    hp["wo_pk"] = lhsT_pack(_f32(np.asarray(inp["L1_attn_w_out"]).T))
    hp["fbias"] = _f32(inp["L1_ffn_bias"])
    invf = 1.0 / 10000.0 ** (np.arange(0, ROT1, 2) / ROT1)
    percore = []
    for r in range(W):
        pos = own_token_idx(r).astype(np.float64)
        ang = pos[None, :] * invf[:, None]
        cosd = np.repeat(np.cos(ang), 2, axis=0).astype(np.float32)
        sins_ = np.zeros((ROT1, T), np.float32)
        sins_[0::2] = -np.sin(ang)
        sins_[1::2] = np.sin(ang)
        pk = pack_l0_heads(inp["L0_attn_Wq_up"], inp["L0_attn_bq_up"],
                           inp["L0_attn_Wkv_up"], inp["L0_attn_bkv_up"],
                           [HPC0 * r + h for h in range(HPC0)])
        pk.update(xT=_f32(x[0, own_token_idx(r), :].T), cosd=cosd, sins=sins_,
                  hix=build_hix(r), oix=build_oix(r))
        percore.append(pk)
    hp["trimask"] = build_trimask()
    return hp, percore


def kernel(**inp):
    hp, percore = host_inputs(inp)
    nc = build_fused()
    maps = [dict(hp, **percore[r]) for r in range(W)]
    res = run_bass_kernel_spmd(nc, maps, core_ids=list(range(W)))
    out = np.zeros((1, S, D), np.float32)
    for r in range(W):
        out[0, own_token_idx(r), :] = np.asarray(res.results[r]["outT"]).T
    return out


def build_mask3(r):
    tri = np.triu(np.full((128, 128), MASKVAL, np.float32), 1)
    m = np.zeros((NB, 3, 128, 512), np.float32)
    for j in range(NB):
        g = W * j + r
        ngr = 2 * j + 2
        for s3 in range(3):
            gs = ngr - 3 + s3
            if gs < 0:
                continue
            for bb in range(4):
                b = gs * 4 + bb
                if b == g:
                    m[j, s3, :, bb * 128:(bb + 1) * 128] = tri
                elif b > g:
                    m[j, s3, :, bb * 128:(bb + 1) * 128] = MASKVAL
    return m


def own_token_idx(r):
    return np.concatenate([np.arange((8 * j + r) * 128, (8 * j + r) * 128 + 128) for j in range(NB)])


def build_trimask():
    tri = np.triu(np.full((128, 128), MASKVAL, np.float32), 1)
    m = np.zeros((4, 128, 512), np.float32)
    for d in range(4):
        m[d, :, d * 128:(d + 1) * 128] = tri
        m[d, :, (d + 1) * 128:] = MASKVAL
    return m


def build_hix(r):
    p_ = np.arange(128, dtype=np.uint32)
    m = np.zeros((128, 32), np.uint32)
    for hh in range(2):
        for ch in range(2):
            for c in range(W):
                m[:, hh * 16 + ch * 8 + c] = c * D + (2 * r + hh) * 256 + ch * 128 + p_
    return m


def build_oix(r):
    p_ = np.arange(128, dtype=np.uint32)
    m = np.zeros((128, 32), np.uint32)
    for ic in range(32):
        d = ic * 128 + p_
        m[:, ic] = ((d // 512) * 256 + (d % 256)) * 8 + r
    return m


def _f32(a):
    return np.ascontiguousarray(a, dtype=np.float32)


def lhsT_pack(WT, pad_out=None):
    in_d, out_d = WT.shape
    assert in_d % 128 == 0
    ic = in_d // 128
    od = out_d if pad_out is None else pad_out
    assert od % 128 == 0
    ocn = od // 128
    Wp = np.zeros((in_d, od), np.float32)
    Wp[:, :out_d] = WT
    P = Wp.reshape(ic, 128, ocn, 128).transpose(2, 1, 0, 3)
    return np.ascontiguousarray(P.reshape(ocn, 128, ic * 128))


def pack_l0_heads(wq_up, bq_up, wkv_up, bkv_up, heads):
    nh = len(heads)
    WqT = np.ascontiguousarray(np.asarray(wq_up).T, np.float32)
    WkT = np.ascontiguousarray(np.asarray(wkv_up).T, np.float32)
    Wq = WqT.reshape(12, 128, NH0, QH0)
    Wk = WkT.reshape(4, 128, NH0, NOPE0 + VH0)
    wqn = np.ascontiguousarray(Wq[:, :, heads, :NOPE0].transpose(2, 1, 0, 3).reshape(nh, 128, 12 * 128))
    wqr = np.ascontiguousarray(Wq[:, :, heads, NOPE0:].transpose(2, 1, 0, 3).reshape(nh, 128, 12 * 64))
    wkk = np.ascontiguousarray(Wk[:, :, heads, :NOPE0].transpose(2, 1, 0, 3).reshape(nh, 128, 4 * 128))
    wkv = np.ascontiguousarray(Wk[:, :, heads, NOPE0:].transpose(2, 1, 0, 3).reshape(nh, 128, 4 * 128))
    bq = np.asarray(bq_up).reshape(NH0, QH0)
    bk = np.asarray(bkv_up).reshape(NH0, NOPE0 + VH0)
    bqn_c = np.ascontiguousarray(bq[heads, :NOPE0].T, np.float32)
    bqr_c = np.zeros((128, nh), np.float32)
    bqr_c[0:ROPE0, :] = bq[heads, NOPE0:].T
    bkk_c = np.ascontiguousarray(bk[heads, :NOPE0].T, np.float32)
    bkv_r = np.ascontiguousarray(bk[heads, NOPE0:], np.float32)
    return dict(wqn=wqn, wqr=wqr, wkk=wkk, wkv=wkv,
                bqn_c=bqn_c, bqr_c=bqr_c, bkk_c=bkk_c, bkv_r=bkv_r)


def kernel_multi(**inp):
    x = np.asarray(inp["L0_attn_hidden_size"])
    idf = np.eye(128, dtype=np.float32)
    onesf = np.ones((1, 128), np.float32)
    onescf = np.ones((128, 1), np.float32)
    cores = list(range(W))

    wqcT = _f32(np.asarray(inp["L0_attn_Wq_c"]).T)
    wkvcT = _f32(np.asarray(inp["L0_attn_Wkv_c"]).T)
    wpk_a = lhsT_pack(np.concatenate([wqcT, wkvcT], axis=1), pad_out=NOC_A * 128)
    bqc = _f32(inp["L0_attn_bq_c"])
    bkvc = _f32(inp["L0_attn_bkv_c"])
    nc_a = build_l0_pre()
    xTs = [_f32(x[0, own_token_idx(r), :].T) for r in cores]
    maps = [dict(xT=xTs[r], wpk_a=wpk_a, bqc=bqc, bkvc=bkvc, idf=idf, onesf=onesf, onescf=onescf) for r in cores]
    res = run_bass_kernel_spmd(nc_a, maps, core_ids=cores)
    qckv_st = np.stack([np.asarray(res.results[r]["qckvT"]) for r in cores])

    nc_b = build_l0_attn()
    maps = []
    for r in cores:
        pk = pack_l0_heads(inp["L0_attn_Wq_up"], inp["L0_attn_bq_up"],
                           inp["L0_attn_Wkv_up"], inp["L0_attn_bkv_up"],
                           [HPC0 * r + h for h in range(HPC0)])
        pk.update(qckv=qckv_st, idf=idf, onesf=onesf, onescf=onescf)
        maps.append(pk)
    res = run_bass_kernel_spmd(nc_b, maps, core_ids=cores)
    osum_parts = [np.asarray(res.results[r]["osum_part"]) for r in cores]

    act = np.asarray(inp["L0_ffn_active"])
    A = np.asarray(inp["L0_ffn_lora_A"])[act]
    Bm = np.asarray(inp["L0_ffn_lora_B"])[act]
    Wr = np.asarray(inp["L0_ffn_router_W"])[act]
    br = np.asarray(inp["L0_ffn_router_b"])[act]
    prior = np.asarray(inp["L0_ffn_domain_prior"])[act]
    az = _f32(np.concatenate([A.reshape(-1, D).T, Wr.T], axis=1))
    bcT_h = _f32(Bm.transpose(0, 2, 1).reshape(-1, D))
    lp_br = _f32(np.stack([prior, br]))
    wb_pk = lhsT_pack(_f32(np.asarray(inp["L0_ffn_W_base"]).T))
    woT = _f32(np.asarray(inp["L0_attn_Wo"]).T)
    nc_c = build_l0_post()
    maps = []
    for r in cores:
        op_r = np.stack([osum_parts[c][r * T:(r + 1) * T, :] for c in cores])
        maps.append(dict(oparts=op_r, woT=woT, bo=_f32(inp["L0_attn_bo"]), xT=xTs[r],
                         wb_pk=wb_pk, b_base=_f32(inp["L0_ffn_b_base"]), az_rhs=az,
                         bcT=bcT_h, lp_br=lp_br, idf=idf, onesf=onesf, onescf=onescf))
    res = run_bass_kernel_spmd(nc_c, maps, core_ids=cores)
    x2Ts = [np.asarray(res.results[r]["x2T"]) for r in cores]

    wq_pk = lhsT_pack(_f32(np.asarray(inp["L1_attn_w_q"]).T))
    wk_pk = lhsT_pack(_f32(np.asarray(inp["L1_attn_w_k"]).T))
    wv_pk = lhsT_pack(_f32(np.asarray(inp["L1_attn_w_v"]).T))
    invf = 1.0 / 10000.0 ** (np.arange(0, ROT1, 2) / ROT1)
    nc_d = build_l1_qkv()
    maps = []
    for r in cores:
        pos = own_token_idx(r).astype(np.float64)
        ang = pos[None, :] * invf[:, None]
        cosd = np.repeat(np.cos(ang), 2, axis=0).astype(np.float32)
        sins_ = np.zeros((ROT1, T), np.float32)
        sins_[0::2] = -np.sin(ang)
        sins_[1::2] = np.sin(ang)
        maps.append(dict(x2T=x2Ts[r], wq_pk=wq_pk, wk_pk=wk_pk, wv_pk=wv_pk,
                         cosd=cosd, sins=sins_, idf=idf, onesf=onesf, onescf=onescf))
    res = run_bass_kernel_spmd(nc_d, maps, core_ids=cores)
    qTs = [np.asarray(res.results[r]["qT"]) for r in cores]
    k_full = np.stack([np.asarray(res.results[r]["kT"]) for r in cores])
    v_full = np.stack([np.asarray(res.results[r]["v_own"]) for r in cores])

    nc_e = build_l1_attn()
    maps = [dict(qT=qTs[r], k_full=k_full, v_full=v_full, mask3=build_mask3(r), idf=idf,
                 onesf=onesf, onescf=onescf) for r in cores]
    res = run_bass_kernel_spmd(nc_e, maps, core_ids=cores)
    oTs = [np.asarray(res.results[r]["oT"]) for r in cores]

    lA = np.asarray(inp["L1_ffn_lora_A"])
    lB = np.asarray(inp["L1_ffn_lora_B"])
    lAs = np.asarray(inp["L1_ffn_lora_A_shared"])
    lBs = np.asarray(inp["L1_ffn_lora_B_shared"])
    gate = _f32(inp["L1_ffn_gate"])
    az1 = _f32(np.concatenate([lA.reshape(16 * 32, D).T, lAs.T], axis=1))
    bcat = np.concatenate([lB.transpose(0, 2, 1).reshape(16 * 32, D), lBs.T], axis=0)
    kn_pk = np.zeros((32, 128, 37 * 128), np.float32)
    kn_pk[:, :, 0:32 * 128] = lhsT_pack(_f32(np.asarray(inp["L1_ffn_kernel"])))
    for k3 in range(4):
        kn_pk[:, :, (32 + k3) * 128:(33 + k3) * 128] = \
            bcat[k3 * 128:(k3 + 1) * 128].reshape(128, 32, 128).transpose(1, 0, 2)
    kn_pk[:, 0:32, 36 * 128:] = bcat[512:544].reshape(32, 32, 128).transpose(1, 0, 2)
    gate_exp = _f32(np.concatenate([np.repeat(gate, 32, axis=1), np.ones((NB, 32))], axis=1))
    wo_pk = lhsT_pack(_f32(np.asarray(inp["L1_attn_w_out"]).T))
    nc_f = build_l1_post()
    maps = [dict(oT=oTs[r], x2T=x2Ts[r], wo_pk=wo_pk, kn_pk=kn_pk,
                 fbias=_f32(inp["L1_ffn_bias"]), az1=az1, gate_exp=gate_exp, idf=idf,
                 onesf=onesf, onescf=onescf) for r in cores]
    res = run_bass_kernel_spmd(nc_f, maps, core_ids=cores)

    out = np.zeros((1, S, D), np.float32)
    for r in cores:
        out[0, own_token_idx(r), :] = np.asarray(res.results[r]["outT"]).T
    return out


if __name__ == "__main__":
    pass


# version 15 (final)
# speedup vs baseline: 1.2466x; 1.0024x over previous
import math
import numpy as np
import ml_dtypes
import concourse.bass as bass
import concourse.mybir as mybir
from concourse.bass_utils import run_bass_kernel_spmd
from contextlib import ExitStack

from concourse.bass import IndirectOffsetOnAxis

F32 = mybir.dt.float32
BF16 = mybir.dt.bfloat16
U32 = mybir.dt.uint32
AF = mybir.ActivationFunctionType
ALU = mybir.AluOpType
AX = mybir.AxisListType
BF = ml_dtypes.bfloat16

W = 8
S = 8192
D = 4096
T = S // W
NB = T // 128

QL, KVL = 1536, 512
NOPE0, ROPE0 = 128, 64
QH0 = NOPE0 + ROPE0
VH0 = 128
NH0, HPC0 = 32, 4
NH1, HD1, ROT1 = 16, 256, 64
MASKVAL = -1.0e9 / 16.0


class Sem:
    def __init__(self, handle):
        self.h = handle
        self.n = 0

    def inc(self, k=16):
        self.n += k
        return self.n


class Prog:
    ENGINES = ("gpsimd", "sync", "tensor", "vector", "scalar")

    def __init__(self, nc):
        self.nc = nc
        self.items = []

    def emit(self, eng, fn):
        self.items.append((eng, fn))

    def dma(self, eng, out, in_, sem):
        v = sem.inc(16)
        self.emit(eng, lambda e, o=out, i=in_, s=sem.h: e.dma_start(out=o, in_=i).then_inc(s, 16))
        return v

    def dma_ind(self, out, in_, idx, sem):
        v = sem.inc(16)
        self.emit("gpsimd", lambda e, o=out, i=in_, x=idx, s=sem.h:
                  e.indirect_dma_start(
                      out=o, in_=i,
                      in_offset=IndirectOffsetOnAxis(ap=x, axis=0),
                      out_offset=None).then_inc(s, 16))
        return v

    def wait(self, eng, sem, v):
        if v > 0:
            self.emit(eng, lambda e, s=sem.h, vv=v: e.wait_ge(s, vv))

    def op(self, eng, builder, sem=None):
        if sem is None:
            self.emit(eng, builder)
            return None
        v = sem.inc(1)
        self.emit(eng, lambda e, b=builder, s=sem.h: b(e).then_inc(s, 1))
        return v

    def replay(self, block):
        for key in self.ENGINES:
            deco = getattr(block, key)

            def make(k):
                def body(eng):
                    for ek, fn in self.items:
                        if ek == k:
                            fn(eng)
                return body
            deco(make(key))


def ceil_div(a, b):
    return (a + b - 1) // b


def colvec_dma(p, nc, ctx, name, dram_ap, n, s_ld):
    m = ceil_div(n, 128)
    nfull = n // 128
    tail = n - nfull * 128
    E = ctx.enter_context
    st = {}
    if nfull:
        st["stage"] = E(nc.sbuf_tensor(name + "_stg", [nfull, 128], F32))
        p.dma("gpsimd", st["stage"][:, :], dram_ap[0:nfull * 128].rearrange("(p f) -> p f", f=128), s_ld)
    if tail:
        st["stage2"] = E(nc.sbuf_tensor(name + "_stg2", [1, tail], F32))
        p.dma("gpsimd", st["stage2"][:, :], dram_ap[nfull * 128:n].rearrange("(p f) -> p f", f=tail), s_ld)
    st["col"] = E(nc.sbuf_tensor(name, [128, m], F32))
    st["n"] = n
    return st


def colvec_compute(p, st, idf_sb, ps_cv, col_off, s_pe, s_cv, scale=None):
    n = st["n"]
    m = ceil_div(n, 128)
    nfull = n // 128
    tail = n - nfull * 128
    col = st["col"]
    p.wait("tensor", s_cv, s_cv.n)
    if nfull:
        p.op("tensor", lambda e, o=ps_cv[0:128, col_off:col_off + nfull], i=st["stage"][:, :],
             d=idf_sb[0:nfull, 0:nfull]: e.transpose(o, i, d), sem=s_pe)
    if tail:
        p.op("tensor", lambda e, o=ps_cv[0:tail, col_off + nfull:col_off + nfull + 1], i=st["stage2"][:, :],
             d=idf_sb[0:1, 0:1]: e.transpose(o, i, d), sem=s_pe)
    p.wait("scalar", s_pe, s_pe.n)
    sc = 1.0 if scale is None else scale
    if nfull:
        p.op("scalar", lambda e, o=col[:, 0:nfull], i=ps_cv[:, col_off:col_off + nfull], s=sc:
             e.activation(o, i, AF.Copy, bias=0.0, scale=s), sem=None if tail else s_cv)
    if tail:
        p.op("scalar", lambda e, o=col[0:tail, nfull:nfull + 1],
             i=ps_cv[0:tail, col_off + nfull:col_off + nfull + 1], s=sc:
             e.activation(o, i, AF.Copy, bias=0.0, scale=s), sem=s_cv)
    return col



SEM_NAMES = ["ld_g", "ld_s", "w0", "w1", "x0", "x1", "sp0", "sp1", "st0", "st1",
             "stv0", "stv1", "k0", "k1", "k2", "k3", "v0", "v1", "v2", "v3", "m0", "m1",
             "pe", "act", "dve", "cv", "gp",
             "cc0", "cc1", "cc2", "cc3"]


def mk_sems(nc, ctx):
    return {n: Sem(ctx.enter_context(nc.semaphore("sm_" + n))) for n in SEM_NAMES}


def barrier(p, sm, skip=()):
    for eng in Prog.ENGINES:
        for name, s in sm.items():
            if name not in skip:
                p.wait(eng, s, s.n)


NOUT_A = QL + KVL + ROPE0
NOC_A = ceil_div(NOUT_A, 128)


def mk_shared(p, nc, ctx, sm, io):
    E = ctx.enter_context
    sh = {}
    sh["idf_sb"] = E(nc.sbuf_tensor("sh_idf", [128, 128], F32))
    sh["idb_sb"] = E(nc.sbuf_tensor("sh_idb", [128, 128], BF16))
    sh["ones_r"] = E(nc.sbuf_tensor("sh_onr", [1, 128], F32))
    sh["ones_rb"] = E(nc.sbuf_tensor("sh_onrb", [1, 128], BF16))
    sh["ones_cb"] = E(nc.sbuf_tensor("sh_oncb", [128, 1], BF16))
    sh["eps12"] = E(nc.sbuf_tensor("sh_eps12", [1, 1], F32))
    sh["eps5"] = E(nc.sbuf_tensor("sh_eps5", [1, 1], F32))
    s_ld = sm["ld_g"]
    p.dma("gpsimd", sh["idf_sb"][:, :], io["idf"][:, :], s_ld)
    p.dma("gpsimd", sh["idb_sb"][:, :], io["idf"][:, :], s_ld)
    p.dma("gpsimd", sh["ones_r"][:, :], io["onesf"][:, :], s_ld)
    p.dma("gpsimd", sh["ones_rb"][:, :], io["onesf"][:, :], s_ld)
    p.dma("gpsimd", sh["ones_cb"][:, :], io["onescf"][:, :], s_ld)
    p.op("vector", lambda e, o=sh["eps12"]: e.memset(o[:, :], 1e-12))
    v_eps = p.op("vector", lambda e, o=sh["eps5"]: e.memset(o[:, :], 1e-5), sem=sm["dve"])
    v_ld = s_ld.n
    for eng in ("tensor", "vector", "scalar", "sync"):
        p.wait(eng, s_ld, v_ld)
        p.wait(eng, sm["dve"], v_eps)
    p.wait("gpsimd", s_ld, v_ld)
    return sh


def emit_a(p, nc, ctx, sm, sh, io):
    E = ctx.enter_context
    xT, wpk, bqc, bkvc, out = io["xT"], io["wpk_a"], io["bqc"], io["bkvc"], io["qckvT"]
    idf_sb = sh["idf_sb"]
    xbf = E(nc.sbuf_tensor("a_xbf", [128, 32 * T], BF16))
    wbuf = [E(nc.sbuf_tensor(f"a_wbuf{i}", [128, 32 * 128], BF16)) for i in range(2)]
    evac = [E(nc.sbuf_tensor(f"a_evac{i}", [128, T], BF16)) for i in range(2)]
    ps_cv = E(nc.psum_tensor("a_ps_cv", [128, 32], F32))
    ps = [E(nc.psum_tensor(f"a_ps{i}", [128, 512], F32)) for i in range(4)]
    s_ld = sm["ld_g"]
    s_w = [sm["w0"], sm["w1"]]
    s_pe, s_cv, s_ev = sm["pe"], sm["cv"], sm["act"]
    s_st = [sm["st0"], sm["st1"]]

    for g in range(8):
        p.dma("gpsimd", xbf[:, :].rearrange("p (c t) -> p c t", t=T)[:, g * 4:(g + 1) * 4, :],
              xT.rearrange("(c p) t -> p c t", p=128)[:, g * 4:(g + 1) * 4, :], s_ld)
    cv_bqc = colvec_dma(p, nc, ctx, "a_bqc_col", bqc, QL, s_ld)
    cv_bkv = colvec_dma(p, nc, ctx, "a_bkv_col", bkvc, KVL + ROPE0, s_ld)
    v_ld = s_ld.n
    p.wait("tensor", s_ld, v_ld)
    bqc_col = colvec_compute(p, cv_bqc, idf_sb, ps_cv, 0, s_pe, s_cv)
    bkv_col = colvec_compute(p, cv_bkv, idf_sb, ps_cv, 12, s_pe, s_cv)
    v_cv0 = s_cv.n

    ev_of = {}
    st_of = {}
    pe_of = {}
    for oc in range(NOC_A):
        rows = min(128, NOUT_A - oc * 128)
        buf = wbuf[oc % 2]
        ebuf = evac[oc % 2]
        if oc >= 2:
            p.wait("gpsimd", s_pe, pe_of[(oc - 2, 1)])
        vw = p.dma("gpsimd", buf[:, :], wpk[oc], s_w[oc % 2])
        p.wait("tensor", s_w[oc % 2], vw)
        for th in range(2):
            k = oc * 2 + th
            pst = ps[k % 4]
            if k >= 4:
                p.wait("tensor", s_ev, ev_of[((k - 4) // 2, (k - 4) % 2)])
            for ic in range(32):
                last = ic == 31
                pe_v = p.op("tensor", lambda e, o=pst[:, :], w=buf[:, ic * 128:(ic + 1) * 128],
                            x=xbf[:, ic * T + th * 512: ic * T + th * 512 + 512],
                            st=(ic == 0), sp=last: e.matmul(o, w, x, start=st, stop=sp),
                            sem=s_pe if last else None)
            pe_of[(oc, th)] = pe_v
            col, ccol = (bqc_col, oc) if oc < 12 else (bkv_col, oc - 12)
            if oc >= 2 and th == 0:
                p.wait("scalar", s_st[oc % 2], st_of[(oc - 2, 1)])
            p.wait("scalar", s_pe, pe_of[(oc, th)])
            if oc == 0 and th == 0:
                p.wait("scalar", s_cv, v_cv0)
            ev_of[(oc, th)] = p.op("scalar",
                lambda e, o=ebuf[0:rows, th * 512:(th + 1) * 512], i=pst[0:rows, :],
                b=col[0:rows, ccol:ccol + 1]: e.activation(o, i, AF.Identity, bias=b, scale=1.0),
                sem=s_ev)
            p.wait("sync", s_ev, ev_of[(oc, th)])
            st_of[(oc, th)] = p.dma("sync", out[oc * 128: oc * 128 + rows, th * 512:(th + 1) * 512],
                                    ebuf[0:rows, th * 512:(th + 1) * 512], s_st[oc % 2])
    p.wait("sync", s_st[0], s_st[0].n)
    p.wait("sync", s_st[1], s_st[1].n)


def build_l0_pre():
    nc = bass.Bass("TRN2", target_bir_lowering=False)
    io = dict(
        xT=nc.dram_tensor("xT", [D, T], F32, kind="ExternalInput").ap(),
        wpk_a=nc.dram_tensor("wpk_a", [NOC_A, 128, 32 * 128], F32, kind="ExternalInput").ap(),
        bqc=nc.dram_tensor("bqc", [QL], F32, kind="ExternalInput").ap(),
        bkvc=nc.dram_tensor("bkvc", [KVL + ROPE0], F32, kind="ExternalInput").ap(),
        idf=nc.dram_tensor("idf", [128, 128], F32, kind="ExternalInput").ap(),
        onesf=nc.dram_tensor("onesf", [1, 128], F32, kind="ExternalInput").ap(),
        onescf=nc.dram_tensor("onescf", [128, 1], F32, kind="ExternalInput").ap(),
        qckvT=nc.dram_tensor("qckvT", [NOUT_A, T], BF16, kind="ExternalOutput").ap(),
    )
    with ExitStack() as ctx:
        sm = mk_sems(nc, ctx)
        block = ctx.enter_context(nc.Block())
        p = Prog(nc)
        sh = mk_shared(p, nc, ctx, sm, io)
        emit_a(p, nc, ctx, sm, sh, io)
        p.replay(block)
    return nc


def emit_b(p, nc, ctx, sm, sh, io, SQ=S, nheads=HPC0):
    TQ = SQ // W
    MT = SQ // 512
    NQB = SQ // 128
    NKT = SQ // 128
    QG = 2
    SC0 = 1.0 / math.sqrt(QH0)
    qckv, out = io["qckv"], io["osum_part"]
    wqn, wqr, wkk, wkv = io["wqn"], io["wqr"], io["wkk"], io["wkv"]
    bqn_c, bqr_c, bkk_c, bkv_r = io["bqn_c"], io["bqr_c"], io["bkk_c"], io["bkv_r"]
    if True:
        E = ctx.enter_context
        qn_sb = E(nc.sbuf_tensor("qn_sb", [128, SQ], BF16))
        kn_sb = E(nc.sbuf_tensor("kn_sb", [128, SQ], BF16))
        qr_sb = E(nc.sbuf_tensor("qr_sb", [128, SQ], BF16))
        kr_sb = E(nc.sbuf_tensor("kr_sb", [128, SQ], BF16))
        vbuf = E(nc.sbuf_tensor("vbuf", [128, SQ], BF16))
        pbuf = [E(nc.sbuf_tensor(f"pbuf{i}", [128, SQ], BF16)) for i in range(3)]
        osum = E(nc.sbuf_tensor("osum", [128, SQ], F32))
        qcsl = [E(nc.sbuf_tensor(f"qcsl{i}", [128, 12 * 512], BF16)) for i in range(2)]
        kvsl = [E(nc.sbuf_tensor(f"kvsl{i}", [128, 4 * 512], BF16)) for i in range(2)]
        pt_sb = [E(nc.sbuf_tensor(f"pt_sb{i}", [128, 1024], BF16)) for i in range(2)]
        whb = [dict(qn=E(nc.sbuf_tensor("wqnb", [128, 12 * 128], BF16)),
                    qr=E(nc.sbuf_tensor("wqrb", [128, 12 * 64], BF16)),
                    kk=E(nc.sbuf_tensor("wkkb", [128, 4 * 128], BF16)),
                    kv=E(nc.sbuf_tensor("wkvb", [128, 4 * 128], BF16)))]
        bqn = E(nc.sbuf_tensor("bqn", [128, nheads], F32))
        bqr = E(nc.sbuf_tensor("bqr", [128, nheads], F32))
        bkk = E(nc.sbuf_tensor("bkk", [128, nheads], F32))
        bvr = [E(nc.sbuf_tensor(f"bvr{h}", [1, 128], BF16)) for h in range(nheads)]
        mcol2 = [E(nc.sbuf_tensor(f"mcol{i}", [128, MT], F32)) for i in range(2)]
        lcol2 = [E(nc.sbuf_tensor(f"lcol{i}", [128, MT], F32)) for i in range(2)]
        negM = E(nc.sbuf_tensor("negM", [128, 1], F32))
        Ccol = E(nc.sbuf_tensor("Ccol", [128, MT], F32))
        CLc = E(nc.sbuf_tensor("CLc", [128, MT], F32))
        lsum = E(nc.sbuf_tensor("lsum", [128, 1], F32))
        linv = E(nc.sbuf_tensor("linv", [128, 1], F32))
        Fcol = E(nc.sbuf_tensor("Fcol", [128, MT], F32))
        ost = [E(nc.sbuf_tensor(f"ost{i}", [128, 128], F32)) for i in range(2)]

        ps_a = E(nc.psum_tensor("ps_a", [128, 512], F32))
        ps_b = E(nc.psum_tensor("ps_b", [128, 512], F32))
        ps_c = E(nc.psum_tensor("ps_c", [128, 512], F32))
        ps_d = E(nc.psum_tensor("ps_d", [128, 512], F32))
        ps_t = [E(nc.psum_tensor(f"ps_t{i}", [128, 1024], BF16)) for i in range(2)]
        ps_o = E(nc.psum_tensor("ps_o", [128, 128], F32))

        s_ld, s_ld2 = sm["ld_g"], sm["ld_s"]
        s_wh = [sm["w0"], sm["w1"]]
        s_sl = [sm["x0"], sm["x1"]]
        s_pe, s_act, s_dve = sm["pe"], sm["act"], sm["dve"]
        s_st = [sm["st0"], sm["st1"]]
        ones_sb = sh["ones_rb"]
        idf_sb = sh["idf_sb"]
        idb_sb = sh["idb_sb"]

        p.dma("gpsimd", bqn[:, :], bqn_c[:, :], s_ld)
        p.dma("gpsimd", bqr[:, :], bqr_c[:, :], s_ld)
        p.dma("gpsimd", bkk[:, :], bkk_c[:, :], s_ld)
        for h in range(nheads):
            p.dma("gpsimd", bvr[h][:, :], bkv_r[h:h + 1, :], s_ld)
        for c in range(W):
            p.dma("sync", kr_sb[0:64, c * TQ:(c + 1) * TQ], qckv[c, QL + KVL:NOUT_A, :], s_ld2)
        p.op("vector", lambda e: e.memset(qr_sb[64:128, :], 0.0))
        v_ms = p.op("vector", lambda e: e.memset(kr_sb[64:128, :], 0.0), sem=s_dve)
        p.wait("tensor", s_dve, v_ms)
        v_ld = s_ld.n
        v_ld2 = s_ld2.n
        p.wait("scalar", s_ld, v_ld)
        p.op("scalar", lambda e: e.mul(bqn[:, :], bqn[:, :], SC0), sem=s_act)
        v_bias = p.op("scalar", lambda e: e.mul(bqr[:, :], bqr[:, :], SC0), sem=s_act)
        p.wait("scalar", s_act, v_bias)
        p.wait("tensor", s_ld, v_ld)
        p.wait("tensor", s_ld2, v_ld2)
        p.wait("vector", s_ld, v_ld)

        vCe_prev = [0]
        last_evac = {"a": 0, "b": 0, "c": 0, "d": 0}
        last_exp = {"a": 0, "b": 0, "c": 0, "d": 0}
        s1_done_pe = {}
        sl_consumed = {}
        trans_last = 0
        trans_done = {}
        oev_prev = (None, 0)
        osum_written = {}
        st_of = {}

        for h in range(nheads):
            wb = whb[0]
            if h >= 1:
                p.wait("gpsimd", s_pe, s1_done_pe[h - 1])
            p.dma("gpsimd", wb["qn"][:, :], wqn[h], s_wh[0])
            p.dma("gpsimd", wb["qr"][:, :], wqr[h], s_wh[0])
            p.dma("gpsimd", wb["kk"][:, :], wkk[h], s_wh[0])
            vwh = p.dma("gpsimd", wb["kv"][:, :], wkv[h], s_wh[0])

            p.wait("tensor", s_wh[0], vwh)
            for m in range(MT):
                c, tb = m // (TQ // 512), m % (TQ // 512)
                mp = m % 2
                gm = h * MT + m
                if gm >= 2:
                    p.wait("sync", s_pe, sl_consumed[gm - 2])
                p.dma("sync", qcsl[mp][:, :].rearrange("p (i t) -> p i t", t=512),
                      qckv[c, 0:QL, tb * 512:(tb + 1) * 512].rearrange("(i p) t -> p i t", p=128),
                      s_sl[mp])
                vsl = p.dma("sync", kvsl[mp][:, :].rearrange("p (i t) -> p i t", t=512),
                            qckv[c, QL:QL + KVL, tb * 512:(tb + 1) * 512].rearrange("(i p) t -> p i t", p=128),
                            s_sl[mp])
                p.wait("tensor", s_sl[mp], vsl)
                p.wait("tensor", s_act, last_evac["a"])
                p.wait("tensor", s_act, last_exp["a"])
                for ic in range(12):
                    p.op("tensor", lambda e, o=ps_a[:, :], w=wb["qn"][:, ic * 128:(ic + 1) * 128],
                         x=qcsl[mp][:, ic * 512:(ic + 1) * 512], st=(ic == 0), sp=(ic == 11):
                         e.matmul(o, w, x, start=st, stop=sp))
                p.wait("tensor", s_act, last_evac["b"])
                p.wait("tensor", s_act, last_exp["b"])
                for ic in range(12):
                    p.op("tensor", lambda e, o=ps_b[0:64, :], w=wb["qr"][:, ic * 64:(ic + 1) * 64],
                         x=qcsl[mp][:, ic * 512:(ic + 1) * 512], st=(ic == 0), sp=(ic == 11):
                         e.matmul(o, w, x, start=st, stop=sp))
                p.wait("tensor", s_act, last_evac["c"])
                p.wait("tensor", s_act, last_exp["c"])
                for ic in range(4):
                    p.op("tensor", lambda e, o=ps_c[:, :], w=wb["kk"][:, ic * 128:(ic + 1) * 128],
                         x=kvsl[mp][:, ic * 512:(ic + 1) * 512], st=(ic == 0), sp=(ic == 3):
                         e.matmul(o, w, x, start=st, stop=sp))
                p.wait("tensor", s_act, last_evac["d"])
                vmm = 0
                for ts in range(4):
                    for ic in range(4):
                        p.op("tensor", lambda e, o=ps_d[:, ts * 128:(ts + 1) * 128],
                             w=kvsl[mp][:, ic * 512 + ts * 128: ic * 512 + ts * 128 + 128],
                             x=wb["kv"][:, ic * 128:(ic + 1) * 128], st=(ic == 0):
                             e.matmul(o, w, x, start=st, stop=False))
                    vmm = p.op("tensor", lambda e, o=ps_d[:, ts * 128:(ts + 1) * 128], w=ones_sb[:, :],
                               x=bvr[h][:, :]: e.matmul(o, w, x, start=False, stop=True),
                               sem=s_pe if ts == 3 else None)
                sl_consumed[gm] = vmm
                p.wait("scalar", s_pe, vmm)
                last_evac["a"] = p.op("scalar", lambda e, o=qn_sb[:, m * 512:(m + 1) * 512],
                                      i=ps_a[:, :], b=bqn[:, h:h + 1], s=SC0:
                                      e.activation(o, i, AF.Identity, bias=b, scale=s), sem=s_act)
                last_evac["b"] = p.op("scalar", lambda e, o=qr_sb[0:64, m * 512:(m + 1) * 512],
                                      i=ps_b[0:64, :], b=bqr[0:64, h:h + 1], s=SC0:
                                      e.activation(o, i, AF.Identity, bias=b, scale=s), sem=s_act)
                last_evac["c"] = p.op("scalar", lambda e, o=kn_sb[:, m * 512:(m + 1) * 512],
                                      i=ps_c[:, :], b=bkk[:, h:h + 1]:
                                      e.activation(o, i, AF.Identity, bias=b, scale=1.0), sem=s_act)
                last_evac["d"] = p.op("scalar", lambda e, o=vbuf[:, m * 512:(m + 1) * 512],
                                      i=ps_d[:, :]: e.copy(o, i), sem=s_act)
            s1_done_pe[h] = sl_consumed[h * MT + MT - 1]
            s1_evacs_done = last_evac["d"]

            p.wait("tensor", s_act, s1_evacs_done)
            SBANKS = [("a", ps_a), ("b", ps_b), ("c", ps_c), ("d", ps_d)]
            s_gp = sm["gp"]

            def score_chunk(qb, kc):
                pb = pbuf[qb % 3]
                mcol = mcol2[qb % 2]
                lcol = lcol2[qb % 2]
                bank, pss = SBANKS[kc % 4]
                if kc % 4 == 0:
                    p.wait("tensor", s_act, last_exp[SBANKS[3][0]])
                    if kc < 4:
                        p.wait("tensor", s_act, max(last_evac.values()))
                p.op("tensor", lambda e, o=pss[:, :], w=qn_sb[:, qb * 128:(qb + 1) * 128],
                     x=kn_sb[:, kc * 512:(kc + 1) * 512]: e.matmul(o, w, x, start=True, stop=False))
                vs = p.op("tensor", lambda e, o=pss[:, :], w=qr_sb[:, qb * 128:(qb + 1) * 128],
                          x=kr_sb[:, kc * 512:(kc + 1) * 512]:
                          e.matmul(o, w, x, start=False, stop=True), sem=s_pe)
                if kc == 0:
                    p.wait("vector", s_pe, vs)
                    vm = p.op("vector", lambda e, o=mcol[:, 0:1], i=pss[:, :]:
                              e.reduce_max(o, i, axis=AX.X, negate=True), sem=s_dve)
                    p.wait("scalar", s_dve, vm)
                else:
                    p.wait("scalar", s_pe, vs)
                last_exp[bank] = p.op("scalar", lambda e, o=pb[:, kc * 512:(kc + 1) * 512],
                                      i=pss[:, :], b=mcol[:, 0:1], a=lcol[:, kc:kc + 1]:
                                      e.activation(o, i, AF.Exp, bias=b, scale=1.0, accum_out=a),
                                      sem=s_act)

            def emit_stats(qb):
                lcol = lcol2[qb % 2]
                p.wait("vector", s_act, qb_exp_done[qb])
                v2 = p.op("vector", lambda e, o=lsum[:, :], i=lcol[:, :]:
                          e.reduce_sum(o, i, axis=AX.X), sem=s_dve)
                p.wait("vector", s_dve, v2)
                v3 = p.op("vector", lambda e, o=linv[:, :], i=lsum[:, :]: e.reciprocal(o, i),
                          sem=s_dve)
                p.wait("vector", s_dve, v3)
                p.wait("vector", s_pe, trans_done.get(qb % 3, 0))

            def emit_corr(qb, kc):
                pb = pbuf[qb % 3]
                return p.op("vector", lambda e, o=pb[:, kc * 512:(kc + 1) * 512],
                            sc=linv[:, :]: e.tensor_scalar(o, o, sc, None, ALU.mult),
                            sem=s_dve)

            def av_step(gq, k8, st_av):
                if k8 == 0:
                    p.wait("tensor", s_dve, corr_done[gq])
                    eng0, val0 = oev_prev2[0]
                    if eng0 == "scalar":
                        p.wait("tensor", s_act, val0)
                    elif eng0 == "vector":
                        p.wait("tensor", s_dve, val0)
                if k8 > 0:
                    kprev = k8 - 1
                    eng0, val0 = st_av["ptev"][kprev]
                    p.wait("tensor", s_act if eng0 == "scalar" else s_dve, val0)
                    for sub in range(8):
                        kt = 8 * kprev + sub
                        avv = p.op("tensor", lambda e, o=ps_o[:, :],
                                   w=vbuf[:, kt * 128:(kt + 1) * 128],
                                   x=pt_sb[kprev % 2][:, sub * 128:(sub + 1) * 128],
                                   st=(kt == 0), sp=(kt == NKT - 1):
                                   e.matmul(o, w, x, start=st, stop=sp),
                                   sem=s_pe if kt == NKT - 1 else None)
                if k8 < NK8:
                    tb2 = k8 % 2
                    if k8 >= 2:
                        eng0, val0 = st_av["ptev"][k8 - 2]
                        p.wait("tensor", s_act if eng0 == "scalar" else s_dve, val0)
                    tv = 0
                    for sub in range(8):
                        kt = 8 * k8 + sub
                        tv = p.op("tensor", lambda e,
                                  o=ps_t[tb2][:, sub * 128:(sub + 1) * 128],
                                  i=pbuf[gq % 3][:, kt * 128:(kt + 1) * 128], d=idb_sb[:, :]:
                                  e.transpose(o, i, d),
                                  sem=s_pe if sub == 7 else None)
                    eng = "vector" if k8 % 2 == 0 else "scalar"
                    p.wait(eng, s_pe, tv)
                    if eng == "vector":
                        pv = p.op("vector", lambda e, o=pt_sb[tb2][:, :], i=ps_t[tb2][:, :]:
                                  e.tensor_copy(o, i), sem=s_dve)
                    else:
                        pv = p.op("scalar", lambda e, o=pt_sb[tb2][:, :], i=ps_t[tb2][:, :]:
                                  e.copy(o, i), sem=s_act)
                    st_av["ptev"][k8] = (eng, pv)
                else:
                    avv_final = s_pe.n
                    trans_done[gq % 3] = avv_final
                    if h == 0:
                        p.wait("scalar", s_pe, avv_final)
                        ov = p.op("scalar", lambda e, o=osum[:, gq * 128:(gq + 1) * 128],
                                  i=ps_o[:, :]: e.copy(o, i), sem=s_act)
                        osum_written[gq] = ("scalar", ov)
                        oev_prev2[0] = ("scalar", ov)
                    else:
                        p.wait("vector", s_pe, avv_final)
                        peng, pval = osum_written[gq]
                        p.wait("vector", s_act if peng == "scalar" else s_dve, pval)
                        ov = p.op("vector", lambda e, o=osum[:, gq * 128:(gq + 1) * 128],
                                  i0=ps_o[:, :], i1=osum[:, gq * 128:(gq + 1) * 128]:
                                  e.tensor_tensor(o, i0, i1, ALU.add), sem=s_dve)
                        osum_written[gq] = ("vector", ov)
                        oev_prev2[0] = ("vector", ov)

            corr_done = {}
            qb_exp_done = {}
            oev_prev2 = [oev_prev]
            st_av = {"ptev": {}}
            av_q = []
            NK8 = NKT // 8
            slot = [0]

            def pump_av(force=False):
                if av_q and (force or av_q[0][2] <= slot[0]):
                    g0, k8, _ = av_q[0]
                    if k8 == 0:
                        st_av["ptev"] = {}
                    av_step(g0, k8, st_av)
                    if k8 + 1 > NK8:
                        av_q.pop(0)
                    else:
                        av_q[0][1] = k8 + 1

            NCS = (MT + 1) // 2
            for qb in range(NQB):
                pbq = qb % 3
                for kc in range(MT):
                    if kc == 0:
                        p.wait("scalar", s_pe, trans_done.get(pbq, 0))
                    score_chunk(qb, kc)
                    if kc == MT - 1:
                        qb_exp_done[qb] = last_exp[SBANKS[(MT - 1) % 4][0]]
                    if qb >= 1:
                        if kc == 0:
                            emit_stats(qb - 1)
                        elif kc <= NCS:
                            for c2 in (2 * (kc - 1), 2 * (kc - 1) + 1):
                                if c2 < MT:
                                    cv = emit_corr(qb - 1, c2)
                            if kc == NCS:
                                corr_done[qb - 1] = cv
                                av_q.append([qb - 1, 0, slot[0] + 1])
                    if kc % 2 == 1 or kc == 0:
                        pump_av()
                    slot[0] += 1
            emit_stats(NQB - 1)
            for kc in range(MT):
                cv = emit_corr(NQB - 1, kc)
            corr_done[NQB - 1] = cv
            av_q.append([NQB - 1, 0, 0])
            while av_q:
                pump_av(force=True)
            oev_prev = oev_prev2[0]
            trans_last = max(trans_done.values()) if trans_done else trans_last

        fin_eng, fin_val = osum_written[NQB // QG - 1]
        p.wait("tensor", s_dve if fin_eng == "vector" else s_act, fin_val)
        for g in sorted(osum_written):
            eng, val = osum_written[g]
            p.wait("tensor", s_dve if eng == "vector" else s_act, val)
        ev3 = {}
        for tt in range(NKT):
            pst = ps_c if tt % 2 == 0 else ps_d
            if tt >= 2:
                p.wait("tensor", s_act, ev3[tt - 2])
            tv = p.op("tensor", lambda e, o=pst[:, 0:128], i=osum[:, tt * 128:(tt + 1) * 128],
                      d=idf_sb[:, :]: e.transpose(o, i, d), sem=s_pe)
            p.wait("scalar", s_pe, tv)
            if tt >= 2:
                p.wait("scalar", s_st[tt % 2], st_of[tt - 2])
            ev3[tt] = p.op("scalar", lambda e, o=ost[tt % 2][:, :], i=pst[:, 0:128]: e.copy(o, i),
                           sem=s_act)
            p.wait("sync", s_act, ev3[tt])
            st_of[tt] = p.dma("sync", out[tt * 128:(tt + 1) * 128, :], ost[tt % 2][:, :], s_st[tt % 2])
        p.wait("sync", s_st[0], s_st[0].n)
        p.wait("sync", s_st[1], s_st[1].n)


def build_l0_attn(SQ=S, nheads=HPC0):
    TQ = SQ // W
    nc = bass.Bass("TRN2", target_bir_lowering=False)
    io = dict(
        qckv=nc.dram_tensor("qckv", [W, NOUT_A, TQ], BF16, kind="ExternalInput").ap(),
        wqn=nc.dram_tensor("wqn", [nheads, 128, 12 * 128], F32, kind="ExternalInput").ap(),
        wqr=nc.dram_tensor("wqr", [nheads, 128, 12 * 64], F32, kind="ExternalInput").ap(),
        wkk=nc.dram_tensor("wkk", [nheads, 128, 4 * 128], F32, kind="ExternalInput").ap(),
        wkv=nc.dram_tensor("wkv", [nheads, 128, 4 * 128], F32, kind="ExternalInput").ap(),
        bqn_c=nc.dram_tensor("bqn_c", [128, nheads], F32, kind="ExternalInput").ap(),
        bqr_c=nc.dram_tensor("bqr_c", [128, nheads], F32, kind="ExternalInput").ap(),
        bkk_c=nc.dram_tensor("bkk_c", [128, nheads], F32, kind="ExternalInput").ap(),
        bkv_r=nc.dram_tensor("bkv_r", [nheads, 128], F32, kind="ExternalInput").ap(),
        idf=nc.dram_tensor("idf", [128, 128], F32, kind="ExternalInput").ap(),
        onesf=nc.dram_tensor("onesf", [1, 128], F32, kind="ExternalInput").ap(),
        onescf=nc.dram_tensor("onescf", [128, 1], F32, kind="ExternalInput").ap(),
        osum_part=nc.dram_tensor("osum_part", [SQ, VH0], F32, kind="ExternalOutput").ap(),
    )
    with ExitStack() as ctx:
        sm = mk_sems(nc, ctx)
        block = ctx.enter_context(nc.Block())
        p = Prog(nc)
        sh = mk_shared(p, nc, ctx, sm, io)
        emit_b(p, nc, ctx, sm, sh, io, SQ=SQ, nheads=nheads)
        p.replay(block)
    return nc


def emit_c(p, nc, ctx, sm, sh, io, rs_mode=False):
    NE, RK = 4, 8
    ZC = NE * RK + NE
    oparts, out = io["oparts"], io["x2T"]
    woT, bo_d, xT = io["woT"], io["bo"], io["xT"]
    wb_pk, bb_d, az_rhs, bcT, lp_br = io["wb_pk"], io["b_base"], io["az_rhs"], io["bcT"], io["lp_br"]
    x1psp = nc.dram_tensor("x1psp", [32, 128, T], F32)
    x1nsp = nc.dram_tensor("x1nsp", [32, 128, T], F32)
    x2psp = nc.dram_tensor("x2psp", [32, 128, T], F32)
    if True:
        E = ctx.enter_context
        idf_sb = sh["idf_sb"]
        idb_sb = sh["idb_sb"]
        ones_r = sh["ones_r"]
        ones_cb = sh["ones_cb"]
        eps12 = sh["eps12"]
        eps5 = sh["eps5"]
        obuf = E(nc.sbuf_tensor("obuf", [128, (1 if rs_mode else W) * T], F32))
        oacc = E(nc.sbuf_tensor("oacc", [128, T], F32))
        oacb = E(nc.sbuf_tensor("oacb", [128, T], BF16))
        oT_bf = E(nc.sbuf_tensor("oT_bf", [128, T], BF16))
        wo_sb = E(nc.sbuf_tensor("wo_sb", [128, D], BF16))
        wbuf = [E(nc.sbuf_tensor(f"wbufc{i}", [128, 32 * 128], BF16)) for i in range(2)]
        xc = [E(nc.sbuf_tensor(f"xc{i}", [128, T], F32)) for i in range(2)]
        t1 = [E(nc.sbuf_tensor(f"t1_{i}", [128, T], F32)) for i in range(2)]
        tbf = [E(nc.sbuf_tensor(f"tbf{i}", [128, T], BF16)) for i in range(2)]
        sqb = [E(nc.sbuf_tensor(f"sqb{i}", [128, T], BF16)) for i in range(2)]
        x1bf = E(nc.sbuf_tensor("x1bf", [128, 32 * T], BF16))
        mrow = E(nc.sbuf_tensor("mrow", [1, T], F32))
        qrow = E(nc.sbuf_tensor("qrow", [1, T], F32))
        vrow = E(nc.sbuf_tensor("vrow", [1, T], F32))
        rrow = E(nc.sbuf_tensor("rrow", [1, T], F32))
        mb = E(nc.sbuf_tensor("mb", [128, T], F32))
        rb = E(nc.sbuf_tensor("rb", [128, T], F32))
        zb = E(nc.sbuf_tensor("zb", [128, NB * ZC], F32))
        zrhs = E(nc.sbuf_tensor("zrhs", [128, 32 * ZC], BF16))
        zwb = E(nc.sbuf_tensor("zwb", [128, NB * NE * RK], BF16))
        zwT = E(nc.sbuf_tensor("zwT", [NE * RK, T], BF16))
        bcT_sb = E(nc.sbuf_tensor("bcT_sb", [NE * RK, D], BF16))
        lpbr = E(nc.sbuf_tensor("lpbr", [1, NE], F32))
        brow = E(nc.sbuf_tensor("brow", [1, NE], F32))
        lrow = E(nc.sbuf_tensor("lrow", [1, NE], F32))
        lrow_b = E(nc.sbuf_tensor("lrow_b", [128, NE], F32))
        a4 = E(nc.sbuf_tensor("a4", [128, NE], F32))
        e4 = E(nc.sbuf_tensor("e4", [128, NE], F32))
        am = E(nc.sbuf_tensor("am", [128, 1], F32))
        al = E(nc.sbuf_tensor("al", [128, 1], F32))
        ali = E(nc.sbuf_tensor("ali", [128, 1], F32))
        ostg = [E(nc.sbuf_tensor(f"ostg{i}", [128, T], F32)) for i in range(2)]

        ps_1 = E(nc.psum_tensor("ps_1", [128, 512], F32))
        ps_2 = E(nc.psum_tensor("ps_2", [128, 512], F32))
        ps_bt = [E(nc.psum_tensor(f"ps_bt{i}", [128, 128], BF16)) for i in range(2)]
        ps_s = [E(nc.psum_tensor(f"ps_s{i}", [1, 512], F32)) for i in range(2)]
        ps_q = [E(nc.psum_tensor(f"ps_q{i}", [1, 512], F32)) for i in range(2)]

        s_ld, s_ld2 = sm["ld_g"], sm["ld_s"]
        s_w = [sm["w0"], sm["w1"]]
        s_x = [sm["x0"], sm["x1"]]
        s_pe, s_act, s_dve = sm["pe"], sm["act"], sm["dve"]
        s_sp = [sm["sp0"], sm["sp1"]]
        s_spl = [sm["k0"], sm["k1"]]
        s_st = [sm["st0"], sm["st1"]]

        p.dma("gpsimd", wo_sb[:, :], woT[:, :], s_ld)
        p.dma("gpsimd", zrhs[:, :].rearrange("p (i z) -> p i z", z=ZC),
              az_rhs.rearrange("(i p) z -> p i z", p=128), s_ld)
        p.dma("gpsimd", bcT_sb[:, :], bcT[:, :], s_ld)
        p.dma("gpsimd", lpbr[:, :], lp_br[0:1, :], s_ld)
        p.dma("gpsimd", brow[:, :], lp_br[1:2, :], s_ld)
        for c in range(1 if rs_mode else W):
            p.dma("sync", obuf[:, c * T:(c + 1) * T].rearrange("p (i v) -> p i v", v=VH0),
                  (oparts if rs_mode else oparts[c]).rearrange("(i p) v -> p i v", p=128), s_ld2)
        cv_bo = colvec_dma(p, nc, ctx, "bo_colc", bo_d, D, s_ld)
        cv_bb = colvec_dma(p, nc, ctx, "bb_colc", bb_d, D, s_ld)
        v_ld, v_ld2 = s_ld.n, s_ld2.n
        for eng in ("tensor", "vector", "scalar"):
            p.wait(eng, s_ld, v_ld)
            p.wait(eng, s_ld2, v_ld2)
        bo_c = colvec_compute(p, cv_bo, idf_sb, ps_1, 0, s_pe, s_act)
        bb_c = colvec_compute(p, cv_bb, idf_sb, ps_1, 32, s_pe, s_act)
        v_cv = s_act.n
        p.wait("scalar", s_act, v_cv)

        va = p.op("scalar", lambda e, o=lrow[:, :], i=lpbr[:, :], b=eps12[:, :]:
                  e.activation(o, i, AF.Ln, bias=b, scale=1.0), sem=s_act)
        p.wait("vector", s_act, va)
        vb = p.op("vector", lambda e, o=lrow[:, :], i0=lrow[:, :], i1=brow[:, :]:
                  e.tensor_tensor(o, i0, i1, ALU.add), sem=s_dve)
        p.wait("tensor", s_dve, vb)
        p.wait("tensor", s_act, v_cv)
        vbc = p.op("tensor", lambda e, o=ps_1[0:128, 0:NE], w=ones_r[:, :], x=lrow[:, :]:
                   e.matmul(o, w, x, start=True, stop=True), sem=s_pe)
        p.wait("vector", s_pe, vbc)
        v_lrb = p.op("vector", lambda e, o=lrow_b[:, :], i=ps_1[0:128, 0:NE]:
                     e.tensor_copy(o, i), sem=s_dve)

        if rs_mode:
            vcb = p.op("vector", lambda e, o=oacb[:, :], i=obuf[:, 0:T]: e.tensor_copy(o, i),
                       sem=s_dve)
        else:
            vsum = 0
            for c in range(1, W):
                src = obuf[:, 0:T] if c == 1 else oacc[:, :]
                if c > 1:
                    p.wait("vector", s_dve, vsum)
                vsum = p.op("vector", lambda e, o=oacc[:, :], i0=obuf[:, c * T:(c + 1) * T], i1=src:
                            e.tensor_tensor(o, i0, i1, ALU.add), sem=s_dve)
            p.wait("vector", s_dve, vsum)
            vcb = p.op("vector", lambda e, o=oacb[:, :], i=oacc[:, :]: e.tensor_copy(o, i), sem=s_dve)
        p.wait("tensor", s_dve, vcb)
        vet = 0
        for tt in range(NB):
            if tt >= 2:
                p.wait("tensor", s_act, ev_bt[tt - 2])
            vtr = p.op("tensor", lambda e, o=ps_bt[tt % 2][:, :],
                       i=oacb[:, tt * 128:(tt + 1) * 128], d=idb_sb[:, :]:
                       e.transpose(o, i, d), sem=s_pe)
            p.wait("scalar", s_pe, vtr)
            vet = p.op("scalar", lambda e, o=oT_bf[:, tt * 128:(tt + 1) * 128],
                       i=ps_bt[tt % 2][:, :]: e.copy(o, i), sem=s_act)
            ev_bt = ev_bt if tt > 0 else {}
            ev_bt[tt] = vet
        v_oT = vet

        def pass1(tag, emit_mms, bias_col, res_src, spill, wload=None, pre_wait=None):
            ev = {}
            addv = {}
            spv = {0: 0, 1: 0}
            stat_of = {}
            vsq_of = {}
            vtb_of = {}

            def emit_statmms(oc2):
                par2 = oc2 % 2
                p.wait("tensor", s_dve, vtb_of[oc2])
                p.wait("tensor", s_act, vsq_of[oc2])
                sm = 0
                for th2 in range(2):
                    p.op("tensor", lambda e, o=ps_s[th2][:, :], w=ones_cb[:, :],
                         x=tbf[par2][:, th2 * 512:(th2 + 1) * 512], st=(oc2 == 0), sp=(oc2 == 31):
                         e.matmul(o, w, x, start=st, stop=sp))
                    sm = p.op("tensor", lambda e, o=ps_q[th2][:, :], w=ones_cb[:, :],
                              x=sqb[par2][:, th2 * 512:(th2 + 1) * 512], st=(oc2 == 0), sp=(oc2 == 31):
                              e.matmul(o, w, x, start=st, stop=sp),
                              sem=s_pe if th2 == 1 else None)
                stat_of[oc2] = sm

            for oc in range(32):
                par = oc % 2
                if wload is not None:
                    wload(oc)
                if oc == 0 and pre_wait is not None:
                    for sm_, vv_ in pre_wait:
                        p.wait("sync", sm_, vv_)
                if oc >= 2:
                    p.wait("sync", s_dve, addv[oc - 2])
                p.dma("sync", xc[par][:, :], res_src(oc), s_x[par])
                vx = s_x[par].n
                for th in range(2):
                    k = oc * 2 + th
                    pst = ps_1 if k % 2 == 0 else ps_2
                    if k >= 2:
                        p.wait("tensor", s_act, ev[k - 2])
                    vk = emit_mms(oc, th, pst)
                    p.wait("scalar", s_pe, vk)
                    if oc >= 2 and th == 0:
                        p.wait("scalar", s_spl[par], spv[par])
                    ev[k] = p.op("scalar", lambda e, o=t1[par][:, th * 512:(th + 1) * 512],
                                 i=pst[:, 0:512], b=bias_col[:, oc:oc + 1]:
                                 e.activation(o, i, AF.Identity, bias=b, scale=1.0), sem=s_act)
                p.wait("vector", s_act, ev[oc * 2 + 1])
                p.wait("vector", s_x[par], vx)
                addv[oc] = p.op("vector", lambda e, o=t1[par][:, :], i0=t1[par][:, :],
                                i1=xc[par][:, :]: e.tensor_tensor(o, i0, i1, ALU.add), sem=s_dve)
                p.wait("scalar", s_dve, addv[oc])
                if oc >= 2:
                    p.wait("scalar", s_pe, stat_of.get(oc - 2, 0))
                vsq_of[oc] = p.op("scalar", lambda e, o=sqb[par][:, :], i=t1[par][:, :]:
                                  e.square(o, i), sem=s_act)
                p.wait("vector", s_dve, addv[oc])
                if oc >= 2:
                    p.wait("vector", s_pe, stat_of.get(oc - 2, 0))
                vtb_of[oc] = p.op("vector", lambda e, o=tbf[par][:, :], i=t1[par][:, :]:
                                  e.tensor_copy(o, i), sem=s_dve)
                if oc >= 1:
                    emit_statmms(oc - 1)
                p.wait("sync", s_dve, addv[oc])
                spv[par] = p.dma("sync", spill[oc], t1[par][:, :], s_spl[par])
            emit_statmms(31)
            return spv, stat_of[31]

        def ln_finalize(stat_mark):
            p.wait("scalar", s_pe, stat_mark)
            c1 = p.op("scalar", lambda e, o=mrow[0:1, 0:512], i=ps_s[0][:, :]:
                      e.activation(o, i, AF.Copy, bias=0.0, scale=1.0 / D), sem=s_act)
            c2 = p.op("scalar", lambda e, o=mrow[0:1, 512:1024], i=ps_s[1][:, :]:
                      e.activation(o, i, AF.Copy, bias=0.0, scale=1.0 / D), sem=s_act)
            p.wait("scalar", s_act, c2)
            c3 = p.op("scalar", lambda e, o=qrow[:, :], i=mrow[:, :]: e.square(o, i), sem=s_act)
            p.wait("vector", s_act, c3)
            v1 = 0
            for th in range(2):
                v1 = p.op("vector", lambda e, o=vrow[0:1, th * 512:(th + 1) * 512],
                          i0=ps_q[th][:, :], i1=qrow[0:1, th * 512:(th + 1) * 512]:
                          e.scalar_tensor_tensor(o, i0, 1.0 / D, i1, ALU.mult, ALU.subtract),
                          sem=s_dve)
            p.wait("scalar", s_dve, v1)
            c4 = p.op("scalar", lambda e, o=vrow[:, :], i=vrow[:, :], b=eps5[:, :]:
                      e.activation(o, i, AF.Sqrt, bias=b, scale=1.0), sem=s_act)
            p.wait("vector", s_act, c4)
            v2 = p.op("vector", lambda e, o=rrow[:, :], i=vrow[:, :]: e.reciprocal(o, i), sem=s_dve)
            p.wait("tensor", s_dve, v2)
            p.wait("tensor", s_act, c2)
            vmb = vrb = 0
            for th in range(2):
                if th == 1:
                    p.wait("tensor", s_dve, vmb)
                vb1 = p.op("tensor", lambda e, o=ps_1[:, :], w=ones_r[:, :],
                           x=mrow[0:1, th * 512:(th + 1) * 512]:
                           e.matmul(o, w, x, start=True, stop=True), sem=s_pe)
                p.wait("vector", s_pe, vb1)
                vmb = p.op("vector", lambda e, o=mb[:, th * 512:(th + 1) * 512], i=ps_1[:, :]:
                           e.tensor_copy(o, i), sem=s_dve)
                if th == 1:
                    p.wait("tensor", s_dve, vrb)
                vb2 = p.op("tensor", lambda e, o=ps_2[:, :], w=ones_r[:, :],
                           x=rrow[0:1, th * 512:(th + 1) * 512]:
                           e.matmul(o, w, x, start=True, stop=True), sem=s_pe)
                p.wait("vector", s_pe, vb2)
                vrb = p.op("vector", lambda e, o=rb[:, th * 512:(th + 1) * 512], i=ps_2[:, :]:
                           e.tensor_copy(o, i), sem=s_dve)
            return vrb

        def pass2(tag, spill_in, sink, vready, pre_wait=None):
            nv = {}
            for oc in range(32):
                par = oc % 2
                if oc == 0 and pre_wait is not None:
                    for sm_, vv_ in pre_wait:
                        p.wait("sync", sm_, vv_)
                if oc >= 2:
                    p.wait("sync", s_dve, nv[oc - 2])
                p.dma("sync", xc[par][:, :], spill_in[oc], s_x[par])
                vx = s_x[par].n
                p.wait("vector", s_x[par], vx)
                if oc == 0:
                    p.wait("vector", s_dve, vready)
                va1 = p.op("vector", lambda e, o=ostg[par][:, :], i0=xc[par][:, :], i1=mb[:, :]:
                           e.tensor_tensor(o, i0, i1, ALU.subtract), sem=s_dve)
                p.wait("vector", s_dve, va1)
                nv[oc] = p.op("vector", lambda e, o=ostg[par][:, :], i0=ostg[par][:, :], i1=rb[:, :]:
                              e.tensor_tensor(o, i0, i1, ALU.mult), sem=s_dve)
                sink(oc, par, nv[oc])
            return nv

        p.wait("tensor", s_act, v_oT)

        def mm_outproj(oc, th, pst):
            return p.op("tensor", lambda e, o=pst[:, 0:512],
                        w=wo_sb[:, oc * 128:(oc + 1) * 128],
                        x=oT_bf[:, th * 512:(th + 1) * 512]:
                        e.matmul(o, w, x, start=True, stop=True), sem=s_pe)

        spv1, stat1 = pass1("x1", mm_outproj, bo_c, lambda oc: xT[oc * 128:(oc + 1) * 128, :],
                            x1psp)
        vr1 = ln_finalize(stat1)
        pw1 = [(s_spl[0], s_spl[0].n), (s_spl[1], s_spl[1].n)]

        sp1n = {0: 0, 1: 0}

        def sink1(oc, par, vn):
            p.wait("scalar", s_dve, vn)
            vcp = p.op("scalar", lambda e, o=x1bf[:, oc * T:(oc + 1) * T], i=ostg[par][:, :]:
                       e.copy(o, i), sem=s_act)
            p.wait("gpsimd", s_dve, vn)
            sp1n[par] = p.dma("gpsimd", x1nsp[oc], ostg[par][:, :], s_sp[par])
            if oc < 30:
                p.wait("vector", s_sp[par], sp1n[par])
                p.wait("vector", s_act, vcp)

        nv1 = pass2("x1n", x1psp, sink1, vr1, pre_wait=pw1)
        v_x1bf = s_act.n
        pw2 = [(s_sp[0], s_sp[0].n), (s_sp[1], s_sp[1].n)]

        p.wait("tensor", s_act, v_x1bf)
        zev = {}
        ztr = {}
        for ts in range(NB):
            if ts >= 1:
                p.wait("tensor", s_act, zev[ts - 1])
            for ic in range(32):
                vz = p.op("tensor", lambda e, o=ps_2[:, 0:ZC], w=x1bf[:, ic * T + ts * 128: ic * T + ts * 128 + 128],
                          x=zrhs[:, ic * ZC:(ic + 1) * ZC], st=(ic == 0), sp=(ic == 31):
                          e.matmul(o, w, x, start=st, stop=sp), sem=s_pe if ic == 31 else None)
            p.wait("scalar", s_pe, vz)
            zev[ts] = p.op("scalar", lambda e, o=zb[:, ts * ZC:(ts + 1) * ZC], i=ps_2[:, 0:ZC]:
                           e.copy(o, i), sem=s_act)
            p.wait("vector", s_act, zev[ts])
            if ts == 0:
                p.wait("vector", s_dve, v_lrb)
            w1 = p.op("vector", lambda e, o=a4[:, :], i0=zb[:, ts * ZC + NE * RK:(ts + 1) * ZC],
                      i1=lrow_b[:, :]: e.tensor_tensor(o, i0, i1, ALU.add), sem=s_dve)
            p.wait("vector", s_dve, w1)
            w2 = p.op("vector", lambda e, o=am[:, :], i=a4[:, :]:
                      e.reduce_max(o, i, axis=AX.X, negate=True), sem=s_dve)
            p.wait("scalar", s_dve, w2)
            w3 = p.op("scalar", lambda e, o=e4[:, :], i=a4[:, :], b=am[:, :], a=al[:, :]:
                      e.activation(o, i, AF.Exp, bias=b, scale=1.0, accum_out=a), sem=s_act)
            p.wait("vector", s_act, w3)
            w4 = p.op("vector", lambda e, o=ali[:, :], i=al[:, :]: e.reciprocal(o, i), sem=s_dve)
            p.wait("scalar", s_dve, w4)
            w5 = p.op("scalar", lambda e, o=ali[:, :], i=ali[:, :]: e.mul(o, i, 2.0), sem=s_act)
            p.wait("vector", s_act, w5)
            w6 = p.op("vector", lambda e, o=e4[:, :], i=e4[:, :], sc=ali[:, :]:
                      e.tensor_scalar(o, i, sc, None, ALU.mult), sem=s_dve)
            p.wait("vector", s_dve, w6)
            w7 = 0
            for ee in range(NE):
                w7 = p.op("vector", lambda e, o=zwb[:, ts * NE * RK + ee * RK: ts * NE * RK + (ee + 1) * RK],
                          i=zb[:, ts * ZC + ee * RK: ts * ZC + (ee + 1) * RK], sc=e4[:, ee:ee + 1]:
                          e.tensor_scalar(o, i, sc, None, ALU.mult),
                          sem=s_dve if ee == NE - 1 else None)
            p.wait("tensor", s_dve, w7)
            if ts >= 2:
                p.wait("tensor", s_act, ztr[ts - 2])
            vt = p.op("tensor", lambda e, o=ps_bt[ts % 2][0:NE * RK, :],
                      i=zwb[:, ts * NE * RK:(ts + 1) * NE * RK], d=idb_sb[:, :]:
                      e.transpose(o, i, d), sem=s_pe)
            p.wait("scalar", s_pe, vt)
            ztr[ts] = p.op("scalar", lambda e, o=zwT[:, ts * 128:(ts + 1) * 128],
                           i=ps_bt[ts % 2][0:NE * RK, :]:
                           e.copy(o, i), sem=s_act)
        v_zwT = ztr[NB - 1]

        p.wait("tensor", s_act, v_zwT)
        wv_of = {}

        def wloadC(oc):
            par = oc % 2
            if oc >= 2:
                p.wait("gpsimd", s_pe, wv_of[oc - 2])
            p.dma("gpsimd", wbuf[par][:, :], wb_pk[oc], s_w[par])

        def mm_ffn(oc, th, pst):
            par = oc % 2
            if th == 0:
                p.wait("tensor", s_w[par], s_w[par].n)
            for ic in range(32):
                p.op("tensor", lambda e, o=pst[:, 0:512], w=wbuf[par][:, ic * 128:(ic + 1) * 128],
                     x=x1bf[:, ic * T + th * 512: ic * T + th * 512 + 512], st=(ic == 0):
                     e.matmul(o, w, x, start=st, stop=False))
            vd = p.op("tensor", lambda e, o=pst[:, 0:512], w=bcT_sb[:, oc * 128:(oc + 1) * 128],
                      x=zwT[:, th * 512:(th + 1) * 512]:
                      e.matmul(o, w, x, start=False, stop=True), sem=s_pe)
            wv_of[oc] = vd
            return vd

        spv2, stat2 = pass1("x2", mm_ffn, bb_c, lambda oc: x1nsp[oc], x2psp, wload=wloadC,
                            pre_wait=pw2)
        vr2 = ln_finalize(stat2)
        pw3 = [(s_spl[0], s_spl[0].n), (s_spl[1], s_spl[1].n)]

        stv = {0: 0, 1: 0}

        def sink2(oc, par, vn):
            p.wait("sync", s_dve, vn)
            stv[par] = p.dma("sync", out[oc * 128:(oc + 1) * 128, :], ostg[par][:, :], s_st[par])
            if oc < 30:
                p.wait("vector", s_st[par], stv[par])
        pass2("out", x2psp, sink2, vr2, pre_wait=pw3)
        p.wait("sync", s_st[0], s_st[0].n)
        p.wait("sync", s_st[1], s_st[1].n)


def build_l0_post():
    NE, RK = 4, 8
    ZC = NE * RK + NE
    nc = bass.Bass("TRN2", target_bir_lowering=False)
    io = dict(
        oparts=nc.dram_tensor("oparts", [W, T, VH0], F32, kind="ExternalInput").ap(),
        woT=nc.dram_tensor("woT", [VH0, D], F32, kind="ExternalInput").ap(),
        bo=nc.dram_tensor("bo", [D], F32, kind="ExternalInput").ap(),
        xT=nc.dram_tensor("xT", [D, T], F32, kind="ExternalInput").ap(),
        wb_pk=nc.dram_tensor("wb_pk", [32, 128, 32 * 128], F32, kind="ExternalInput").ap(),
        b_base=nc.dram_tensor("b_base", [D], F32, kind="ExternalInput").ap(),
        az_rhs=nc.dram_tensor("az_rhs", [D, ZC], F32, kind="ExternalInput").ap(),
        bcT=nc.dram_tensor("bcT", [NE * RK, D], F32, kind="ExternalInput").ap(),
        lp_br=nc.dram_tensor("lp_br", [2, NE], F32, kind="ExternalInput").ap(),
        idf=nc.dram_tensor("idf", [128, 128], F32, kind="ExternalInput").ap(),
        onesf=nc.dram_tensor("onesf", [1, 128], F32, kind="ExternalInput").ap(),
        onescf=nc.dram_tensor("onescf", [128, 1], F32, kind="ExternalInput").ap(),
        x2T=nc.dram_tensor("x2T", [D, T], F32, kind="ExternalOutput").ap(),
    )
    with ExitStack() as ctx:
        sm = mk_sems(nc, ctx)
        block = ctx.enter_context(nc.Block())
        p = Prog(nc)
        sh = mk_shared(p, nc, ctx, sm, io)
        emit_c(p, nc, ctx, sm, sh, io)
        p.replay(block)
    return nc


def emit_d(p, nc, ctx, sm, sh, io):
    x2T, cosd, sins = io["x2T"], io["cosd"], io["sins"]
    wq_pk, wk_pk, wv_pk = io["wq_pk"], io["wk_pk"], io["wv_pk"]
    qT, kT, vT = io["qT"], io["kT"], io["vT"]
    if True:
        E = ctx.enter_context
        xbf = E(nc.sbuf_tensor("xbfd", [128, 32 * T], BF16))
        idb_sb = sh["idb_sb"]
        cos_sb = E(nc.sbuf_tensor("cos_sb", [ROT1, T], F32))
        sin_sb = E(nc.sbuf_tensor("sin_sb", [ROT1, T], F32))
        wbuf = [E(nc.sbuf_tensor(f"wbufd{i}", [128, 32 * 128], BF16)) for i in range(2)]
        stg = [E(nc.sbuf_tensor(f"stgd{i}", [128, T], F32)) for i in range(2)]
        rtmp = E(nc.sbuf_tensor("rtmp", [ROT1, T], F32))
        obf = [E(nc.sbuf_tensor(f"obfd{i}", [128, T], BF16)) for i in range(2)]
        ps = [E(nc.psum_tensor(f"psd{i}", [128, 512], F32)) for i in range(4)]
        s_ld = sm["ld_g"]
        s_w = [sm["w0"], sm["w1"]]
        s_pe, s_act, s_dve = sm["pe"], sm["act"], sm["dve"]
        s_st = [sm["st0"], sm["st1"]]

        p.dma("gpsimd", cos_sb[:, :], cosd[:, :], s_ld)
        p.dma("gpsimd", sin_sb[:, :], sins[:, :], s_ld)
        for g in range(8):
            p.dma("gpsimd", xbf[:, :].rearrange("p (c t) -> p c t", t=T)[:, g * 4:(g + 1) * 4, :],
                  x2T.rearrange("(c p) t -> p c t", p=128)[:, g * 4:(g + 1) * 4, :], s_ld)
        v_ld = s_ld.n
        for eng in ("tensor", "vector", "scalar"):
            p.wait(eng, s_ld, v_ld)

        SHUF = [(i ^ 1) for i in range(32)]
        ev = {}
        st_of = {}
        rope_v = {}
        wld = {}
        kk = 0
        post_pack = io.get("post_pack")
        for wi, (wpk, outt) in enumerate([(wq_pk, qT), (wk_pk, kT), (wv_pk, vT)]):
            for oc in range(32):
                par = kk % 2
                if kk >= 2:
                    p.wait("gpsimd", s_pe, wld[kk - 2])
                p.dma("gpsimd", wbuf[par][:, :], wpk[oc], s_w[par])
                vw = s_w[par].n
                p.wait("tensor", s_w[par], vw)
                vmm = 0
                for th in range(2):
                    pst = ps[(kk * 2 + th) % 4]
                    if kk * 2 + th >= 4:
                        p.wait("tensor", s_act, ev[kk * 2 + th - 4])
                    for ic in range(32):
                        vmm = p.op("tensor", lambda e, o=pst[:, :], w=wbuf[par][:, ic * 128:(ic + 1) * 128],
                                   x=xbf[:, ic * T + th * 512: ic * T + th * 512 + 512],
                                   st=(ic == 0), sp=(ic == 31): e.matmul(o, w, x, start=st, stop=sp),
                                   sem=s_pe if ic == 31 else None)
                wld[kk] = vmm
                for th in range(2):
                    pst = ps[(kk * 2 + th) % 4]
                    p.wait("scalar", s_pe, wld[kk])
                    if kk >= 2 and th == 0:
                        if st_of.get(kk - 2, 0):
                            p.wait("scalar", s_st[par], st_of[kk - 2])
                        if (kk - 2) in rope_v:
                            p.wait("scalar", s_dve, rope_v[kk - 2])
                    sc = (1.0 / 16.0) if wi == 0 else 1.0
                    ev[kk * 2 + th] = p.op("scalar", lambda e, o=stg[par][:, th * 512:(th + 1) * 512],
                                           i=pst[:, :], s=sc:
                                           e.activation(o, i, AF.Copy, bias=0.0, scale=s), sem=s_act)
                if True:
                    p.wait("vector", s_act, ev[kk * 2 + 1])
                    vr = 0
                    if wi < 2 and oc % 2 == 0:
                        vr = p.op("vector", lambda e, o=rtmp[:, :], i=stg[par][0:ROT1, :], m=SHUF:
                                  e.stream_shuffle(o, i, m), sem=s_dve)
                        p.wait("vector", s_dve, vr)
                        vr = p.op("vector", lambda e, o=rtmp[:, :], i0=rtmp[:, :], i1=sin_sb[:, :]:
                                  e.tensor_tensor(o, i0, i1, ALU.mult), sem=s_dve)
                        p.wait("vector", s_dve, vr)
                        vr = p.op("vector", lambda e, o=stg[par][0:ROT1, :], i0=stg[par][0:ROT1, :],
                                  i1=cos_sb[:, :]: e.tensor_tensor(o, i0, i1, ALU.mult), sem=s_dve)
                        p.wait("vector", s_dve, vr)
                        vr = p.op("vector", lambda e, o=stg[par][0:ROT1, :], i0=stg[par][0:ROT1, :],
                                  i1=rtmp[:, :]: e.tensor_tensor(o, i0, i1, ALU.add), sem=s_dve)
                        p.wait("vector", s_dve, vr)
                    vcb = p.op("vector", lambda e, o=obf[par][:, :], i=stg[par][:, :]:
                               e.tensor_copy(o, i), sem=s_dve)
                    rope_v[kk] = vcb
                    p.wait("sync", s_dve, vcb)
                    st_of[kk] = p.dma("sync", outt[oc * 128:(oc + 1) * 128, :], obf[par][:, :],
                                      s_st[par])
                kk += 1
                if post_pack is not None and oc == 15:
                    snap = (s_st[0].n, s_st[1].n)
                if post_pack is not None and oc == 17 and len(post_pack[wi]) > 1:
                    p.wait("gpsimd", s_st[0], snap[0])
                    p.wait("gpsimd", s_st[1], snap[1])
                    post_pack[wi][0]()
            if post_pack is not None:
                p.wait("gpsimd", s_st[0], s_st[0].n)
                p.wait("gpsimd", s_st[1], s_st[1].n)
                post_pack[wi][-1]()
        p.wait("sync", s_st[0], s_st[0].n)
        p.wait("sync", s_st[1], s_st[1].n)


def build_l1_qkv():
    nc = bass.Bass("TRN2", target_bir_lowering=False)
    io = dict(
        x2T=nc.dram_tensor("x2T", [D, T], F32, kind="ExternalInput").ap(),
        wq_pk=nc.dram_tensor("wq_pk", [32, 128, 32 * 128], F32, kind="ExternalInput").ap(),
        wk_pk=nc.dram_tensor("wk_pk", [32, 128, 32 * 128], F32, kind="ExternalInput").ap(),
        wv_pk=nc.dram_tensor("wv_pk", [32, 128, 32 * 128], F32, kind="ExternalInput").ap(),
        cosd=nc.dram_tensor("cosd", [ROT1, T], F32, kind="ExternalInput").ap(),
        sins=nc.dram_tensor("sins", [ROT1, T], F32, kind="ExternalInput").ap(),
        idf=nc.dram_tensor("idf", [128, 128], F32, kind="ExternalInput").ap(),
        onesf=nc.dram_tensor("onesf", [1, 128], F32, kind="ExternalInput").ap(),
        onescf=nc.dram_tensor("onescf", [128, 1], F32, kind="ExternalInput").ap(),
        qT=nc.dram_tensor("qT", [D, T], BF16, kind="ExternalOutput").ap(),
        kT=nc.dram_tensor("kT", [D, T], BF16, kind="ExternalOutput").ap(),
        vT=nc.dram_tensor("vT", [D, T], BF16, kind="ExternalOutput").ap(),
    )
    with ExitStack() as ctx:
        sm = mk_sems(nc, ctx)
        block = ctx.enter_context(nc.Block())
        p = Prog(nc)
        sh = mk_shared(p, nc, ctx, sm, io)
        emit_d(p, nc, ctx, sm, sh, io)
        p.replay(block)
    return nc


def emit_e(p, nc, ctx, sm, sh, io, SQ=S):
    NBq = T // 128
    qT_d, k_full, v_full, mask3, out = io["qT"], io["k_full"], io["v_full"], io["mask3"], io["oT"]
    E = ctx.enter_context
    idb_sb = sh["idb_sb"]
    q_sb = E(nc.sbuf_tensor("q_sbe", [128, 32 * T], BF16))
    msk_sb = [E(nc.sbuf_tensor(f"msk_sbe{i}", [128, 3 * 512], F32)) for i in range(2)]
    pbuf = [E(nc.sbuf_tensor(f"pbufe{i}", [128, SQ], BF16)) for i in range(3)]
    kstg = [E(nc.sbuf_tensor(f"kstg{i}", [128, 2 * 512], BF16)) for i in range(4)]
    vstg = [E(nc.sbuf_tensor(f"vstge{i}", [128, 1024], BF16)) for i in range(4)]
    pt_sb = [E(nc.sbuf_tensor(f"ptste{i}", [128, 1024], BF16)) for i in range(2)]
    otst = [E(nc.sbuf_tensor(f"otste{i}", [128, 256], BF16)) for i in range(2)]
    mcol2 = [E(nc.sbuf_tensor(f"mcole{i}", [128, 16], F32)) for i in range(2)]
    lcol2 = [E(nc.sbuf_tensor(f"lcole{i}", [128, 16], F32)) for i in range(2)]
    negM = E(nc.sbuf_tensor("negMe", [128, 1], F32))
    Ccol = E(nc.sbuf_tensor("Ccole", [128, 16], F32))
    CLc = E(nc.sbuf_tensor("CLce", [128, 16], F32))
    lsum = E(nc.sbuf_tensor("lsume", [128, 1], F32))
    linv = E(nc.sbuf_tensor("linve", [128, 1], F32))
    Fcol = E(nc.sbuf_tensor("Fcole", [128, 16], F32))
    ps_a = E(nc.psum_tensor("ps_ae", [128, 512], F32))
    ps_b = E(nc.psum_tensor("ps_be", [128, 512], F32))
    ps_c = E(nc.psum_tensor("ps_ce", [128, 512], F32))
    ps_d = E(nc.psum_tensor("ps_de", [128, 512], F32))
    ps_t = [E(nc.psum_tensor(f"ps_te{i}", [128, 1024], BF16)) for i in range(2)]
    ps_o = [E(nc.psum_tensor(f"ps_oe{i}", [128, 128], F32)) for i in range(2)]

    s_ld, s_ldq = sm["ld_g"], sm["ld_s"]
    s_m = [sm["m0"], sm["m1"]]
    s_k = [sm["w0"], sm["w1"], sm["sp0"], sm["sp1"]]
    s_v = [sm["v0"], sm["v1"], sm["v2"], sm["v3"]]
    s_pe, s_act, s_dve = sm["pe"], sm["act"], sm["dve"]
    s_st = [sm["st0"], sm["st1"]]

    for g in range(8):
        p.dma("sync", q_sb[:, :].rearrange("p (c t) -> p c t", t=T)[:, g * 4:(g + 1) * 4, :],
              qT_d.rearrange("(c p) t -> p c t", p=128)[:, g * 4:(g + 1) * 4, :], s_ldq)
    v_ldq = s_ldq.n
    for eng in ("tensor", "vector", "scalar"):
        p.wait(eng, s_ldq, v_ldq)

    SBANKS = [ps_a, ps_b, ps_c, ps_d]
    last_exp = [0, 0, 0, 0]
    trans_done = {}
    corr_done = {}
    h_exp_done = {}
    osum_written = {}
    oev_prev = ("", 0)
    kld = {}
    kcons = {}
    vld = {}
    vcons = {}
    kq = [0]
    vq = [0]
    stv = {0: 0, 1: 0}
    mcons = {0: 0, 1: 0}

    def kpos(b):
        return (b % W) * T + (b // W) * 128

    def load_kgroup(j, h, gr):
        ring = kq[0] % 4
        kq[0] += 1
        par = ring
        b0 = gr * 4
        c0, j2 = b0 % W, b0 // W
        if kq[0] > 4:
            p.wait("gpsimd", s_pe, kcons.get(kq[0] - 4, 0))
        for ch in range(2):
            p.dma("gpsimd", kstg[ring][:, ch * 512:(ch + 1) * 512].rearrange("p (b t) -> p b t", t=128),
                  k_full[c0:c0 + 4, (2 * h + ch) * 128:(2 * h + ch + 1) * 128,
                         j2 * 128:(j2 + 1) * 128].rearrange("c p t -> p c t"), s_k[par])
        kld[kq[0]] = (ring, par, s_k[par].n)
        return kq[0]

    def load_vgroup(j, h, g4):
        ring = vq[0] % 4
        vq[0] += 1
        par = ring
        b0 = g4 * 4
        c0, j2 = b0 % W, b0 // W
        if vq[0] > 4:
            p.wait("sync", s_pe, vcons.get(vq[0] - 4, 0))
        p.dma("sync", vstg[ring][:, :].rearrange("p (b t) -> p b t", t=256),
              v_full[c0:c0 + 4, j2 * 128:(j2 + 1) * 128, h * 256:(h + 1) * 256].rearrange("c p t -> p c t"),
              s_v[par])
        vld[vq[0]] = (ring, par, s_v[par].n)
        return vq[0]

    for j in range(NBq):
        nkt = W * j + W
        ngr = nkt // 4
        NK8 = nkt // 8
        mpar = j % 2
        p.wait("gpsimd", s_dve, mcons[mpar])
        for s3 in range(3):
            vmsk = p.dma("gpsimd", msk_sb[mpar][:, s3 * 512:(s3 + 1) * 512], mask3[j, s3],
                         s_m[mpar])

        st_av = {"ptev": {}, "kid": {}, "gk": 0}
        av_q = []

        def score_group(h, gr):
            mcol = mcol2[h % 2]
            lcol = lcol2[h % 2]
            pss = SBANKS[gr % 4]
            kid = st_av["kid"].pop((h, gr))
            ring, par, vkl = kld[kid]
            if gr % 4 == 0:
                p.wait("tensor", s_act, max(last_exp))
            p.wait("tensor", s_k[par], vkl)
            p.op("tensor", lambda e, o=pss[:, :], w=q_sb[:, 2 * h * T + j * 128: 2 * h * T + j * 128 + 128],
                 x=kstg[ring][:, 0:512]: e.matmul(o, w, x, start=True, stop=False))
            vs = p.op("tensor", lambda e, o=pss[:, :],
                      w=q_sb[:, (2 * h + 1) * T + j * 128:(2 * h + 1) * T + j * 128 + 128],
                      x=kstg[ring][:, 512:1024]:
                      e.matmul(o, w, x, start=False, stop=True), sem=s_pe)
            kcons[kid] = vs
            p.wait("vector", s_pe, vs)
            s3 = gr - (ngr - 3)
            if s3 >= 0:
                if h == 0 and gr == max(0, ngr - 3):
                    p.wait("vector", s_m[mpar], vmsk)
                vs2 = p.op("vector", lambda e, o=pss[:, :], i0=pss[:, :],
                           i1=msk_sb[mpar][:, s3 * 512:(s3 + 1) * 512]:
                           e.tensor_tensor(o, i0, i1, ALU.add), sem=s_dve)
                p.wait("vector", s_dve, vs2)
                if h == NH1 - 1 and gr == ngr - 1:
                    mcons[mpar] = vs2
            vm = p.op("vector", lambda e, o=mcol[:, gr:gr + 1], i=pss[:, :]:
                      e.reduce_max(o, i, axis=AX.X, negate=True), sem=s_dve)
            p.wait("scalar", s_dve, vm)
            if gr == 0:
                p.wait("scalar", s_pe, trans_done.get(h % 3, 0))
            last_exp[gr % 4] = p.op("scalar", lambda e, o=pbuf[h % 3][:, gr * 512:(gr + 1) * 512],
                                    i=pss[:, :], b=mcol[:, gr:gr + 1], a=lcol[:, gr:gr + 1]:
                                    e.activation(o, i, AF.Exp, bias=b, scale=1.0, accum_out=a),
                                    sem=s_act)
            if gr == ngr - 1:
                h_exp_done[h] = last_exp[gr % 4]

        def emit_stats(h):
            mcol = mcol2[h % 2]
            lcol = lcol2[h % 2]
            p.wait("vector", s_act, h_exp_done[h])
            vC = p.op("vector", lambda e, o=negM[:, :], i=mcol[:, 0:ngr]:
                      e.tensor_reduce(o, i, axis=AX.X, op=ALU.min), sem=s_dve)
            p.wait("scalar", s_dve, vC)
            vCe = p.op("scalar", lambda e, o=Ccol[:, 0:ngr], i=mcol[:, 0:ngr], b=negM[:, :]:
                       e.activation(o, i, AF.Exp, bias=b, scale=-1.0), sem=s_act)
            p.wait("vector", s_act, vCe)
            v1 = p.op("vector", lambda e, o=CLc[:, 0:ngr], i0=Ccol[:, 0:ngr], i1=lcol[:, 0:ngr]:
                      e.tensor_tensor(o, i0, i1, ALU.mult), sem=s_dve)
            p.wait("vector", s_dve, v1)
            v2 = p.op("vector", lambda e, o=lsum[:, :], i=CLc[:, 0:ngr]:
                      e.reduce_sum(o, i, axis=AX.X), sem=s_dve)
            p.wait("vector", s_dve, v2)
            v3 = p.op("vector", lambda e, o=linv[:, :], i=lsum[:, :]: e.reciprocal(o, i), sem=s_dve)
            p.wait("vector", s_dve, v3)
            v4 = p.op("vector", lambda e, o=Fcol[:, 0:ngr], i=Ccol[:, 0:ngr], sc=linv[:, :]:
                      e.tensor_scalar(o, i, sc, None, ALU.mult), sem=s_dve)
            p.wait("vector", s_dve, v4)
            p.wait("vector", s_pe, trans_done.get(h % 3, 0))
            cv = 0
            for gr in range(ngr):
                cv = p.op("vector", lambda e, o=pbuf[h % 3][:, gr * 512:(gr + 1) * 512],
                          sc=Fcol[:, gr:gr + 1]: e.tensor_scalar(o, o, sc, None, ALU.mult),
                          sem=s_dve if gr == ngr - 1 else None)
            corr_done[h] = cv

        def av_step(h, k8):
            if k8 == 0:
                p.wait("tensor", s_dve, corr_done[h])
                eng0, val0 = oev_prev2[0]
                if eng0 == "scalar":
                    p.wait("tensor", s_act, val0)
                elif eng0 == "vector":
                    p.wait("tensor", s_dve, val0)
            if k8 > 0:
                kprev = k8 - 1
                eng0, val0, tbp = st_av[("pt", h, kprev)]
                p.wait("tensor", s_act if eng0 == "scalar" else s_dve, val0)
                for g4 in (2 * kprev, 2 * kprev + 1):
                    vid = st_av[("v", h, g4)]
                    ring, par, vvl = vld[vid]
                    p.wait("tensor", s_v[par], vvl)
                    avv = 0
                    for sub in range(4):
                        kt = 4 * g4 + sub
                        for ch in range(2):
                            avv = p.op("tensor", lambda e, o=ps_o[ch][:, :],
                                       w=vstg[ring][:, sub * 256 + ch * 128: sub * 256 + ch * 128 + 128],
                                       x=pt_sb[tbp][:, (kt - 8 * kprev) * 128:(kt - 8 * kprev + 1) * 128],
                                       st=(kt == 0), sp=(kt == nkt - 1):
                                       e.matmul(o, w, x, start=st, stop=sp),
                                       sem=s_pe if (sub == 3 and ch == 1) else None)
                    vcons[vid] = avv
            if k8 < NK8:
                gk = st_av["gk"]
                st_av["gk"] = gk + 1
                tb2 = gk % 2
                if gk >= 2:
                    eng0, val0, _ = st_av[("ptg", gk - 2)]
                    p.wait("tensor", s_act if eng0 == "scalar" else s_dve, val0)
                tv = 0
                for sub in range(8):
                    kt = 8 * k8 + sub
                    tv = p.op("tensor", lambda e, o=ps_t[tb2][:, sub * 128:(sub + 1) * 128],
                              i=pbuf[h % 3][:, kt * 128:(kt + 1) * 128], d=idb_sb[:, :]:
                              e.transpose(o, i, d), sem=s_pe if sub == 7 else None)
                eng = "vector" if gk % 2 == 0 else "scalar"
                p.wait(eng, s_pe, tv)
                if eng == "vector":
                    pv = p.op("vector", lambda e, o=pt_sb[tb2][:, :], i=ps_t[tb2][:, :]:
                              e.tensor_copy(o, i), sem=s_dve)
                else:
                    pv = p.op("scalar", lambda e, o=pt_sb[tb2][:, :], i=ps_t[tb2][:, :]:
                              e.copy(o, i), sem=s_act)
                st_av[("pt", h, k8)] = (eng, pv, tb2)
                st_av[("ptg", gk)] = (eng, pv, tb2)
            else:
                avf = s_pe.n
                trans_done[h % 3] = avf
                spar = h % 2
                p.wait("scalar", s_pe, avf)
                if stv[spar]:
                    p.wait("scalar", s_st[spar], stv[spar])
                p.op("scalar", lambda e, o=otst[spar][:, 0:128], i=ps_o[0][:, :]: e.copy(o, i))
                ov = p.op("scalar", lambda e, o=otst[spar][:, 128:256], i=ps_o[1][:, :]:
                          e.copy(o, i), sem=s_act)
                oev_prev2[0] = ("scalar", ov)
                p.wait("sync", s_act, ov)
                for ch in range(2):
                    stv[spar] = p.dma("sync", out[(2 * h + ch) * 128:(2 * h + ch + 1) * 128,
                                                  j * 128:(j + 1) * 128],
                                      otst[spar][:, ch * 128:(ch + 1) * 128], s_st[spar])

        oev_prev2 = [oev_prev]

        def pump():
            if av_q:
                h0, k8 = av_q[0]
                if k8 < NK8:
                    for g4 in (2 * k8, 2 * k8 + 1):
                        if ("v", h0, g4) not in st_av:
                            st_av[("v", h0, g4)] = load_vgroup(j, h0, g4)
                av_step(h0, k8)
                if k8 + 1 > NK8:
                    av_q.pop(0)
                else:
                    av_q[0][1] = k8 + 1

        slots = [(h, gr) for h in range(NH1) for gr in range(ngr)]
        for ci in range(min(2, len(slots))):
            hh, gg = slots[ci]
            st_av["kid"][(hh, gg)] = load_kgroup(j, hh, gg)
        for si, (h, gr) in enumerate(slots):
            if si + 2 < len(slots):
                hh, gg = slots[si + 2]
                st_av["kid"][(hh, gg)] = load_kgroup(j, hh, gg)
            score_group(h, gr)
            pump()
            if gr == 0:
                pump()
            if gr == ngr - 1:
                emit_stats(h)
                av_q.append([h, 0])
        while av_q:
            pump()
    p.wait("sync", s_st[0], s_st[0].n)
    p.wait("sync", s_st[1], s_st[1].n)


def build_l1_attn(SQ=S):
    nc = bass.Bass("TRN2", target_bir_lowering=False)
    io = dict(
        qT=nc.dram_tensor("qT", [D, T], BF16, kind="ExternalInput").ap(),
        k_full=nc.dram_tensor("k_full", [W, D, T], BF16, kind="ExternalInput").ap(),
        v_full=nc.dram_tensor("v_full", [W, T, D], BF16, kind="ExternalInput").ap(),
        mask3=nc.dram_tensor("mask3", [NB, 3, 128, 512], F32, kind="ExternalInput").ap(),
        idf=nc.dram_tensor("idf", [128, 128], F32, kind="ExternalInput").ap(),
        onesf=nc.dram_tensor("onesf", [1, 128], F32, kind="ExternalInput").ap(),
        onescf=nc.dram_tensor("onescf", [128, 1], F32, kind="ExternalInput").ap(),
        oT=nc.dram_tensor("oT", [D, T], BF16, kind="ExternalOutput").ap(),
    )
    with ExitStack() as ctx:
        sm = mk_sems(nc, ctx)
        block = ctx.enter_context(nc.Block())
        p = Prog(nc)
        sh = mk_shared(p, nc, ctx, sm, io)
        emit_e(p, nc, ctx, sm, sh, io, SQ=SQ)
        p.replay(block)
    return nc


def emit_e2(p, nc, ctx, sm, sh, io):
    NQB2 = S // 128
    q_full, k_full, vT_full = io["q_full"], io["k_full"], io["vT_full"]
    trimask, hix, out = io["trimask"], io["hix"], io["o_own"]
    E = ctx.enter_context
    idb_sb = sh["idb_sb"]
    qh = [E(nc.sbuf_tensor(f"qh{i}", [128, S], BF16)) for i in range(2)]
    kh = [E(nc.sbuf_tensor(f"kh{i}", [128, S], BF16)) for i in range(2)]
    vhf = [E(nc.sbuf_tensor(f"vhf{i}", [128, S], BF16)) for i in range(2)]
    vhT = E(nc.sbuf_tensor("vhT", [128, S], BF16))
    kstg2 = [E(nc.sbuf_tensor(f"kstg2{i}", [128, T], BF16)) for i in range(2)]
    pbuf = [E(nc.sbuf_tensor(f"pbufe{i}", [128, S], BF16)) for i in range(3)]
    pt_sb = [E(nc.sbuf_tensor(f"ptste{i}", [128, 1024], BF16)) for i in range(2)]
    msk_sb = E(nc.sbuf_tensor("msk_sb2", [128, 4 * 512], F32))
    hix_sb = E(nc.sbuf_tensor("hix_sb", [128, 32], U32))
    vix_sb = E(nc.sbuf_tensor("vix_sb", [128, 32], U32))
    otst = [E(nc.sbuf_tensor(f"otste{i}", [128, 256], BF16)) for i in range(2)]
    mcol2 = [E(nc.sbuf_tensor(f"mcole{i}", [128, 16], F32)) for i in range(2)]
    lcol2 = [E(nc.sbuf_tensor(f"lcole{i}", [128, 16], F32)) for i in range(2)]
    negM = E(nc.sbuf_tensor("negMe", [128, 1], F32))
    Ccol = E(nc.sbuf_tensor("Ccole", [128, 16], F32))
    CLc = E(nc.sbuf_tensor("CLce", [128, 16], F32))
    lsum = E(nc.sbuf_tensor("lsume", [128, 1], F32))
    linv = E(nc.sbuf_tensor("linve", [128, 1], F32))
    Fcol = E(nc.sbuf_tensor("Fcole", [128, 16], F32))
    ps_a = E(nc.psum_tensor("ps_ae", [128, 512], F32))
    ps_b = E(nc.psum_tensor("ps_be", [128, 512], F32))
    ps_c = E(nc.psum_tensor("ps_ce", [128, 512], F32))
    ps_d = E(nc.psum_tensor("ps_de", [128, 512], F32))
    ps_t = [E(nc.psum_tensor(f"ps_te{i}", [128, 1024], BF16)) for i in range(2)]
    ps_o2 = [E(nc.psum_tensor(f"ps_o2e{i}", [128, 128], F32)) for i in range(2)]

    s_ld, s_ldq = sm["ld_g"], sm["ld_s"]
    s_kg = [sm["m0"], sm["m1"]]
    s_rd = [sm["k0"], sm["k1"]]
    s_pe, s_act, s_dve = sm["pe"], sm["act"], sm["dve"]
    s_st = [sm["st0"], sm["st1"]]

    def kpos(b):
        return (b % W) * T + (b // W) * 128

    p.dma("gpsimd", msk_sb[:, :].rearrange("p (v t) -> p v t", t=512),
          trimask.rearrange("v p t -> p v t"), s_ld)
    v_hx = p.dma("gpsimd", hix_sb[:, :], hix[:, :], s_ld)
    vix_in = io.get("vix")
    if vix_in is not None:
        v_hx = p.dma("gpsimd", vix_sb[:, :], vix_in[:, :], s_ld)
    p.wait("gpsimd", s_ld, v_hx)

    qk_flat = [q_full.rearrange("c d t -> (c d) t"),
               k_full.rearrange("c d t -> (c d) t"),
               vT_full.rearrange("c d t -> (c d) t")]

    SBANKS = [ps_a, ps_b, ps_c, ps_d]
    last_exp = [0, 0, 0, 0]
    trans_done = {}
    stv = {0: 0, 1: 0}
    oev = [0]
    vCe_prev = [0]
    scnt = [0]
    ptfree = {0: ("scalar", 0), 1: ("scalar", 0)}

    cc_qkv = io.get("cc_qkv")
    post_head = io.get("post_head")
    for hh in range(2):
        if hh == 1:
            p.wait("gpsimd", s_pe, s_pe.n)
        if hh == 0 and cc_qkv is not None:
            p.wait("gpsimd", cc_qkv[0], cc_qkv[0].n)
        qv = 0
        for c in range(W):
            qv = p.dma_ind(qh[0][:, c * T:(c + 1) * T], qk_flat[0],
                           hix_sb[:, (hh * 16 + c):(hh * 16 + c) + 1], s_ld)
            qv = p.dma_ind(qh[1][:, c * T:(c + 1) * T], qk_flat[0],
                           hix_sb[:, (hh * 16 + 8 + c):(hh * 16 + 8 + c) + 1], s_ld)
        rd = {0: 0, 1: 0}
        kv = 0
        for ti in (1, 2):
            if hh == 0 and cc_qkv is not None:
                p.wait("gpsimd", cc_qkv[ti], cc_qkv[ti].n)
            for ch in range(2):
                if ti == 2 and ch == 1:
                    p.wait("sync", s_pe, s_pe.n)
                for c in range(W):
                    sl = c % 2
                    p.wait("gpsimd", s_rd[sl], rd[sl])
                    ixs = vix_sb if (ti == 2 and vix_in is not None) else hix_sb
                    gv = p.dma_ind(kstg2[sl][:, :], qk_flat[ti],
                                   ixs[:, (hh * 16 + ch * 8 + c):(hh * 16 + ch * 8 + c) + 1],
                                   s_kg[sl])
                    p.wait("sync", s_kg[sl], gv)
                    dst = kh[ch] if ti == 1 else vhT
                    kv = p.dma("sync",
                               dst[:, :].rearrange("p (m cu) -> p m cu", cu=1024)
                               [:, :, c * 128:(c + 1) * 128],
                               kstg2[sl][:, :].rearrange("p (m u) -> p m u", u=128),
                               s_rd[sl])
                    rd[sl] = kv
                if ti == 2:
                    p.wait("tensor", s_rd[0], rd[0])
                    p.wait("tensor", s_rd[1], rd[1])
                    for g8 in range(8):
                        tb2 = g8 % 2
                        e0, v0 = ptfree[tb2]
                        p.wait("tensor", s_act if e0 == "scalar" else s_dve, v0)
                        tv = 0
                        for sub in range(8):
                            bk = g8 * 8 + sub
                            tv = p.op("tensor", lambda e, o=ps_t[tb2][:, sub * 128:(sub + 1) * 128],
                                      i=vhT[:, bk * 128:(bk + 1) * 128], d=idb_sb[:, :]:
                                      e.transpose(o, i, d), sem=s_pe if sub == 7 else None)
                        eng = "vector" if g8 % 2 == 0 else "scalar"
                        p.wait(eng, s_pe, tv)
                        if eng == "vector":
                            pv = p.op("vector", lambda e, o=vhf[ch][:, g8 * 1024:(g8 + 1) * 1024],
                                      i=ps_t[tb2][:, :]: e.tensor_copy(o, i), sem=s_dve)
                        else:
                            pv = p.op("scalar", lambda e, o=vhf[ch][:, g8 * 1024:(g8 + 1) * 1024],
                                      i=ps_t[tb2][:, :]: e.copy(o, i), sem=s_act)
                        ptfree[tb2] = (eng, pv)
        v_gth = s_ld.n
        for eng in ("tensor", "vector", "scalar"):
            p.wait(eng, s_ld, v_gth)
            p.wait(eng, s_rd[0], rd[0])
            p.wait(eng, s_rd[1], rd[1])

        st_av = {"ptev": {}}
        av_q = []
        corr_done = {}
        qb_exp_done = {}

        def score_chunk(b, cc):
            mcol = mcol2[b % 2]
            lcol = lcol2[b % 2]
            pss = SBANKS[scnt[0] % 4]
            bank = scnt[0] % 4
            if scnt[0] % 4 == 0 and scnt[0] > 0:
                p.wait("tensor", s_act, last_exp[3])
            scnt[0] += 1
            p.op("tensor", lambda e, o=pss[:, :], w=qh[0][:, kpos(b):kpos(b) + 128],
                 x=kh[0][:, cc * 512:(cc + 1) * 512]: e.matmul(o, w, x, start=True, stop=False))
            vs = p.op("tensor", lambda e, o=pss[:, :], w=qh[1][:, kpos(b):kpos(b) + 128],
                      x=kh[1][:, cc * 512:(cc + 1) * 512]:
                      e.matmul(o, w, x, start=False, stop=True), sem=s_pe)
            if cc == b // 4:
                p.wait("vector", s_pe, vs)
                va = p.op("vector", lambda e, o=pss[:, :], i0=pss[:, :],
                          i1=msk_sb[:, (b % 4) * 512:(b % 4 + 1) * 512]:
                          e.tensor_tensor(o, i0, i1, ALU.add), sem=s_dve)
                p.wait("vector", s_dve, va)
                if cc > 0:
                    p.wait("scalar", s_dve, va)
            if cc == 0:
                p.wait("vector", s_pe, vs)
                vm = p.op("vector", lambda e, o=mcol[:, 0:1], i=pss[:, :]:
                          e.reduce_max(o, i, axis=AX.X, negate=True), sem=s_dve)
                p.wait("scalar", s_dve, vm)
            elif cc != b // 4:
                p.wait("scalar", s_pe, vs)
            last_exp[bank] = p.op("scalar", lambda e, o=pbuf[b % 3][:, cc * 512:(cc + 1) * 512],
                                  i=pss[:, :], b2=mcol[:, 0:1], a=lcol[:, cc:cc + 1]:
                                  e.activation(o, i, AF.Exp, bias=b2, scale=1.0, accum_out=a),
                                  sem=s_act)

        def emit_stats(b):
            ncc = b // 4 + 1
            lcol = lcol2[b % 2]
            p.wait("vector", s_act, qb_exp_done[b])
            v2 = p.op("vector", lambda e, o=lsum[:, :], i=lcol[:, 0:ncc]:
                      e.reduce_sum(o, i, axis=AX.X), sem=s_dve)
            p.wait("vector", s_dve, v2)
            v3 = p.op("vector", lambda e, o=linv[:, :], i=lsum[:, :]: e.reciprocal(o, i),
                      sem=s_dve)
            p.wait("vector", s_dve, v3)
            p.wait("vector", s_pe, trans_done.get(b % 3, 0))

        def emit_corr(b, cc):
            return p.op("vector", lambda e, o=pbuf[b % 3][:, cc * 512:(cc + 1) * 512],
                        sc=linv[:, :]: e.tensor_scalar(o, o, sc, None, ALU.mult),
                        sem=s_dve)

        def av_step(gq, k8):
            NKT_g = gq + 1
            NK8_g = (gq // 8) + 1
            if k8 == 0:
                p.wait("tensor", s_dve, corr_done[gq])
                p.wait("tensor", s_act, oev[0])
            if k8 > 0:
                kprev = k8 - 1
                e0, v0 = st_av["ptev"][kprev]
                p.wait("tensor", s_act if e0 == "scalar" else s_dve, v0)
                n_in = min(8, NKT_g - 8 * kprev)
                for sub in range(n_in):
                    kt = 8 * kprev + sub
                    for ch in range(2):
                        p.op("tensor", lambda e, o=ps_o2[ch][:, :],
                             w=vhf[ch][:, kt * 128:(kt + 1) * 128],
                             x=pt_sb[kprev % 2][:, sub * 128:(sub + 1) * 128],
                             st=(kt == 0), sp=(kt == NKT_g - 1):
                             e.matmul(o, w, x, start=st, stop=sp),
                             sem=s_pe if (kt == NKT_g - 1 and ch == 1) else None)
            if k8 < NK8_g:
                tb2 = k8 % 2
                e0, v0 = ptfree[tb2]
                p.wait("tensor", s_act if e0 == "scalar" else s_dve, v0)
                n_t = min(8, NKT_g - 8 * k8)
                tv = 0
                for sub in range(n_t):
                    kt = 8 * k8 + sub
                    tv = p.op("tensor", lambda e, o=ps_t[tb2][:, sub * 128:(sub + 1) * 128],
                              i=pbuf[gq % 3][:, kt * 128:(kt + 1) * 128], d=idb_sb[:, :]:
                              e.transpose(o, i, d), sem=s_pe if sub == n_t - 1 else None)
                eng = "vector" if k8 % 2 == 0 else "scalar"
                p.wait(eng, s_pe, tv)
                if eng == "vector":
                    pv = p.op("vector", lambda e, o=pt_sb[tb2][:, 0:n_t * 128],
                              i=ps_t[tb2][:, 0:n_t * 128]: e.tensor_copy(o, i), sem=s_dve)
                else:
                    pv = p.op("scalar", lambda e, o=pt_sb[tb2][:, 0:n_t * 128],
                              i=ps_t[tb2][:, 0:n_t * 128]: e.copy(o, i), sem=s_act)
                st_av["ptev"][k8] = (eng, pv)
                ptfree[tb2] = (eng, pv)
            else:
                avv_final = s_pe.n
                trans_done[gq % 3] = avv_final
                par = gq % 2
                p.wait("scalar", s_pe, avv_final)
                p.wait("scalar", s_st[par], stv[par])
                p.op("scalar", lambda e, o=otst[par][:, 0:128], i=ps_o2[0][:, :]: e.copy(o, i))
                ov = p.op("scalar", lambda e, o=otst[par][:, 128:256], i=ps_o2[1][:, :]:
                          e.copy(o, i), sem=s_act)
                oev[0] = ov
                p.wait("sync", s_act, ov)
                kp = kpos(gq)
                p.dma("sync", out[(hh * 2) * 128:(hh * 2 + 1) * 128, kp:kp + 128],
                      otst[par][:, 0:128], s_st[par])
                stv[par] = p.dma("sync", out[(hh * 2 + 1) * 128:(hh * 2 + 2) * 128, kp:kp + 128],
                                 otst[par][:, 128:256], s_st[par])

        slot = [0]

        def pump_av(force=False):
            if av_q and (force or av_q[0][2] <= slot[0]):
                g0, k8, _ = av_q[0]
                if k8 == 0:
                    st_av["ptev"] = {}
                av_step(g0, k8)
                if k8 + 1 > (g0 // 8) + 1:
                    av_q.pop(0)
                else:
                    av_q[0][1] = k8 + 1

        for b in range(NQB2):
            ncc = b // 4 + 1
            while av_q and av_q[0][0] <= b - 3:
                pump_av(force=True)
            for cc in range(ncc):
                if cc == 0:
                    p.wait("scalar", s_pe, trans_done.get(b % 3, 0))
                score_chunk(b, cc)
                if cc == ncc - 1:
                    qb_exp_done[b] = last_exp[(scnt[0] - 1) % 4]
                if b >= 1:
                    ncp = (b - 1) // 4 + 1
                    if ncc == 1:
                        if cc == 0:
                            emit_stats(b - 1)
                            cv = emit_corr(b - 1, 0)
                            corr_done[b - 1] = cv
                            av_q.append([b - 1, 0, slot[0] + 1])
                    else:
                        NCS = ceil_div(ncp, 2)
                        if cc == 0:
                            emit_stats(b - 1)
                        elif cc <= NCS:
                            for c2 in (2 * (cc - 1), 2 * (cc - 1) + 1):
                                if c2 < ncp:
                                    cv = emit_corr(b - 1, c2)
                            if cc == NCS:
                                corr_done[b - 1] = cv
                                av_q.append([b - 1, 0, slot[0] + 1])
                pump_av()
                slot[0] += 1
        emit_stats(NQB2 - 1)
        for cc in range(NQB2 // 4 + 0):
            cv = emit_corr(NQB2 - 1, cc)
        corr_done[NQB2 - 1] = cv
        av_q.append([NQB2 - 1, 0, 0])
        while av_q:
            pump_av(force=True)
        if post_head is not None:
            p.wait("gpsimd", s_st[0], s_st[0].n)
            p.wait("gpsimd", s_st[1], s_st[1].n)
            post_head[hh]()
    p.wait("sync", s_st[0], s_st[0].n)
    p.wait("sync", s_st[1], s_st[1].n)


def build_l1_attn2():
    nc = bass.Bass("TRN2", target_bir_lowering=False)
    io = dict(
        q_full=nc.dram_tensor("q_full", [W, D, T], BF16, kind="ExternalInput").ap(),
        k_full=nc.dram_tensor("k_full", [W, D, T], BF16, kind="ExternalInput").ap(),
        vT_full=nc.dram_tensor("vT_full", [W, D, T], BF16, kind="ExternalInput").ap(),
        trimask=nc.dram_tensor("trimask", [4, 128, 512], F32, kind="ExternalInput").ap(),
        hix=nc.dram_tensor("hix", [128, 32], U32, kind="ExternalInput").ap(),
        idf=nc.dram_tensor("idf", [128, 128], F32, kind="ExternalInput").ap(),
        onesf=nc.dram_tensor("onesf", [1, 128], F32, kind="ExternalInput").ap(),
        onescf=nc.dram_tensor("onescf", [128, 1], F32, kind="ExternalInput").ap(),
        o_own=nc.dram_tensor("o_own", [512, S], BF16, kind="ExternalOutput").ap(),
    )
    with ExitStack() as ctx:
        sm = mk_sems(nc, ctx)
        block = ctx.enter_context(nc.Block())
        p = Prog(nc)
        sh = mk_shared(p, nc, ctx, sm, io)
        emit_e2(p, nc, ctx, sm, sh, io)
        p.replay(block)
    return nc


def emit_f(p, nc, ctx, sm, sh, io):
    NL, RK1 = 16, 32
    ZC1 = NL * RK1 + RK1
    oT_d, x2T, out = io.get("oT"), io["x2T"], io["outT"]
    wo_pk, kn_pk, fb_d, az1, gate_exp = io["wo_pk"], io["kn_pk"], io["fbias"], io["az1"], io["gate_exp"]
    x1psp = nc.dram_tensor("x1psp_f", [32, 128, T], F32)
    x1nsp = nc.dram_tensor("x1nsp_f", [32, 128, T], F32)
    x2psp = nc.dram_tensor("x2psp_f", [32, 128, T], F32)
    if True:
        E = ctx.enter_context
        idf_sb = sh["idf_sb"]
        idb_sb = sh["idb_sb"]
        ones_r = sh["ones_r"]
        ones_cb = sh["ones_cb"]
        eps5 = sh["eps5"]
        big = E(nc.sbuf_tensor("bigf", [128, 32 * T], BF16))
        wbuf = [E(nc.sbuf_tensor(f"wbuff{i}", [128, 37 * 128], BF16)) for i in range(2)]
        xc = [E(nc.sbuf_tensor(f"xcf{i}", [128, T], F32)) for i in range(2)]
        t1 = [E(nc.sbuf_tensor(f"t1f{i}", [128, T], F32)) for i in range(2)]
        tbf = [E(nc.sbuf_tensor(f"tbff{i}", [128, T], BF16)) for i in range(2)]
        sqb = [E(nc.sbuf_tensor(f"sqbf{i}", [128, T], BF16)) for i in range(2)]
        mrow = E(nc.sbuf_tensor("mrowf", [1, T], F32))
        qrow = E(nc.sbuf_tensor("qrowf", [1, T], F32))
        vrow = E(nc.sbuf_tensor("vrowf", [1, T], F32))
        rrow = E(nc.sbuf_tensor("rrowf", [1, T], F32))
        mb = E(nc.sbuf_tensor("mbf", [128, T], F32))
        rb = E(nc.sbuf_tensor("rbf", [128, T], F32))
        zrhs = E(nc.sbuf_tensor("zrhsf", [128, 32 * ZC1], BF16))
        zwb = E(nc.sbuf_tensor("zwbf", [128, ZC1], BF16))
        zt128 = E(nc.sbuf_tensor("zt128", [128, 4 * T], BF16))
        zt32 = E(nc.sbuf_tensor("zt32", [RK1, T], BF16))
        gbc = E(nc.sbuf_tensor("gbcf", [128, ZC1], F32))
        grow = [E(nc.sbuf_tensor(f"growf{i}", [1, ZC1], F32)) for i in range(NB)]
        fb_c = None
        ostg = [E(nc.sbuf_tensor(f"ostgf{i}", [128, T], F32)) for i in range(2)]

        ps_1 = E(nc.psum_tensor("ps_1f", [128, 512], F32))
        ps_2 = E(nc.psum_tensor("ps_2f", [128, 512], F32))
        ps_bt = [E(nc.psum_tensor(f"ps_btf{i}", [128, 128], BF16)) for i in range(2)]
        ps_s = [E(nc.psum_tensor(f"ps_sf{i}", [1, 512], F32)) for i in range(2)]
        ps_q = [E(nc.psum_tensor(f"ps_qf{i}", [1, 512], F32)) for i in range(2)]

        s_ld, s_ld2 = sm["ld_g"], sm["ld_s"]
        s_w = [sm["w0"], sm["w1"]]
        s_x = [sm["x0"], sm["x1"]]
        s_pe, s_act, s_dve = sm["pe"], sm["act"], sm["dve"]
        s_sp = [sm["sp0"], sm["sp1"]]
        s_spl = [sm["k0"], sm["k1"]]
        s_st = [sm["st0"], sm["st1"]]

        p.dma("gpsimd", zrhs[:, :].rearrange("p (i z) -> p i z", z=ZC1),
              az1.rearrange("(i p) z -> p i z", p=128), s_ld)
        for jj in range(NB):
            p.dma("gpsimd", grow[jj][:, :], gate_exp[jj:jj + 1, :], s_ld)
        if "oix" in io:
            oix_sb = ctx.enter_context(nc.sbuf_tensor("oix_sb", [128, 32], U32))
            v_ox = p.dma("gpsimd", oix_sb[:, :], io["oix"][:, :], s_ld)
            p.wait("gpsimd", s_ld, v_ox)
            o_flats = [io["o_full_a"].rearrange("c hs (tb t) -> (c hs tb) t", t=T),
                       io["o_full_b"].rearrange("c hs (tb t) -> (c hs tb) t", t=T)]
            cc_o = io.get("cc_o")
            ics = [ic for ic in range(32) if ic % 4 < 2] + \
                  [ic for ic in range(32) if ic % 4 >= 2]
            for n_, ic in enumerate(ics):
                half = 0 if ic % 4 < 2 else 1
                if cc_o is not None and n_ in (0, 16):
                    p.wait("gpsimd", cc_o[half], cc_o[half].n)
                p.dma_ind(big[:, ic * T:(ic + 1) * T], o_flats[half],
                          oix_sb[:, ic:ic + 1], s_ld)
        else:
            for g in range(8):
                p.dma("sync", big[:, :].rearrange("p (c t) -> p c t", t=T)[:, g * 4:(g + 1) * 4, :],
                      oT_d.rearrange("(c p) t -> p c t", p=128)[:, g * 4:(g + 1) * 4, :], s_ld2)
        cv_fb = colvec_dma(p, nc, ctx, "fb_colf", fb_d, D, s_ld)
        v_ld, v_ld2 = s_ld.n, s_ld2.n
        for eng in ("tensor", "vector", "scalar"):
            p.wait(eng, s_ld, v_ld)
            p.wait(eng, s_ld2, v_ld2)
        fb_c = colvec_compute(p, cv_fb, idf_sb, ps_1, 0, s_pe, s_act)
        v_cv = s_act.n
        p.wait("scalar", s_act, v_cv)
        p.wait("tensor", s_act, v_cv)

        def pass1(emit_mms, bias_col, res_src, spill, wload=None, pre_wait=None):
            ev = {}
            addv = {}
            spv = {0: 0, 1: 0}
            stat_of = {}
            vsq_of = {}
            vtb_of = {}

            def emit_statmms(oc2):
                par2 = oc2 % 2
                p.wait("tensor", s_dve, vtb_of[oc2])
                p.wait("tensor", s_act, vsq_of[oc2])
                sm = 0
                for th2 in range(2):
                    p.op("tensor", lambda e, o=ps_s[th2][:, :], w=ones_cb[:, :],
                         x=tbf[par2][:, th2 * 512:(th2 + 1) * 512], st=(oc2 == 0), sp=(oc2 == 31):
                         e.matmul(o, w, x, start=st, stop=sp))
                    sm = p.op("tensor", lambda e, o=ps_q[th2][:, :], w=ones_cb[:, :],
                              x=sqb[par2][:, th2 * 512:(th2 + 1) * 512], st=(oc2 == 0), sp=(oc2 == 31):
                              e.matmul(o, w, x, start=st, stop=sp),
                              sem=s_pe if th2 == 1 else None)
                stat_of[oc2] = sm

            for oc in range(32):
                par = oc % 2
                if wload is not None:
                    wload(oc)
                if oc == 0 and pre_wait is not None:
                    for sm_, vv_ in pre_wait:
                        p.wait("sync", sm_, vv_)
                if oc >= 2:
                    p.wait("sync", s_dve, addv[oc - 2])
                p.dma("sync", xc[par][:, :], res_src(oc), s_x[par])
                vx = s_x[par].n
                for th in range(2):
                    k = oc * 2 + th
                    pst = ps_1 if k % 2 == 0 else ps_2
                    if k >= 2:
                        p.wait("tensor", s_act, ev[k - 2])
                    vk = emit_mms(oc, th, pst)
                    p.wait("scalar", s_pe, vk)
                    if oc >= 2 and th == 0:
                        p.wait("scalar", s_spl[par], spv[par])
                    if bias_col is None:
                        ev[k] = p.op("scalar", lambda e, o=t1[par][:, th * 512:(th + 1) * 512],
                                     i=pst[:, 0:512]: e.copy(o, i), sem=s_act)
                    else:
                        ev[k] = p.op("scalar", lambda e, o=t1[par][:, th * 512:(th + 1) * 512],
                                     i=pst[:, 0:512], b=bias_col[:, oc:oc + 1]:
                                     e.activation(o, i, AF.Identity, bias=b, scale=1.0), sem=s_act)
                p.wait("vector", s_act, ev[oc * 2 + 1])
                p.wait("vector", s_x[par], vx)
                addv[oc] = p.op("vector", lambda e, o=t1[par][:, :], i0=t1[par][:, :],
                                i1=xc[par][:, :]: e.tensor_tensor(o, i0, i1, ALU.add), sem=s_dve)
                p.wait("scalar", s_dve, addv[oc])
                if oc >= 2:
                    p.wait("scalar", s_pe, stat_of.get(oc - 2, 0))
                vsq_of[oc] = p.op("scalar", lambda e, o=sqb[par][:, :], i=t1[par][:, :]:
                                  e.square(o, i), sem=s_act)
                p.wait("vector", s_dve, addv[oc])
                if oc >= 2:
                    p.wait("vector", s_pe, stat_of.get(oc - 2, 0))
                vtb_of[oc] = p.op("vector", lambda e, o=tbf[par][:, :], i=t1[par][:, :]:
                                  e.tensor_copy(o, i), sem=s_dve)
                if oc >= 1:
                    emit_statmms(oc - 1)
                p.wait("sync", s_dve, addv[oc])
                spv[par] = p.dma("sync", spill[oc], t1[par][:, :], s_spl[par])
            emit_statmms(31)
            return spv, stat_of[31]

        def ln_finalize(stat_mark):
            p.wait("scalar", s_pe, stat_mark)
            c1 = p.op("scalar", lambda e, o=mrow[0:1, 0:512], i=ps_s[0][:, :]:
                      e.activation(o, i, AF.Copy, bias=0.0, scale=1.0 / D), sem=s_act)
            c2 = p.op("scalar", lambda e, o=mrow[0:1, 512:1024], i=ps_s[1][:, :]:
                      e.activation(o, i, AF.Copy, bias=0.0, scale=1.0 / D), sem=s_act)
            p.wait("scalar", s_act, c2)
            c3 = p.op("scalar", lambda e, o=qrow[:, :], i=mrow[:, :]: e.square(o, i), sem=s_act)
            p.wait("vector", s_act, c3)
            v1 = 0
            for th in range(2):
                v1 = p.op("vector", lambda e, o=vrow[0:1, th * 512:(th + 1) * 512],
                          i0=ps_q[th][:, :], i1=qrow[0:1, th * 512:(th + 1) * 512]:
                          e.scalar_tensor_tensor(o, i0, 1.0 / D, i1, ALU.mult, ALU.subtract),
                          sem=s_dve)
            p.wait("scalar", s_dve, v1)
            c4 = p.op("scalar", lambda e, o=vrow[:, :], i=vrow[:, :], b=eps5[:, :]:
                      e.activation(o, i, AF.Sqrt, bias=b, scale=1.0), sem=s_act)
            p.wait("vector", s_act, c4)
            v2 = p.op("vector", lambda e, o=rrow[:, :], i=vrow[:, :]: e.reciprocal(o, i), sem=s_dve)
            p.wait("tensor", s_dve, v2)
            p.wait("tensor", s_act, c2)
            vmb = vrb = 0
            for th in range(2):
                if th == 1:
                    p.wait("tensor", s_dve, vmb)
                vb1 = p.op("tensor", lambda e, o=ps_1[:, :], w=ones_r[:, :],
                           x=mrow[0:1, th * 512:(th + 1) * 512]:
                           e.matmul(o, w, x, start=True, stop=True), sem=s_pe)
                p.wait("vector", s_pe, vb1)
                vmb = p.op("vector", lambda e, o=mb[:, th * 512:(th + 1) * 512], i=ps_1[:, :]:
                           e.tensor_copy(o, i), sem=s_dve)
                if th == 1:
                    p.wait("tensor", s_dve, vrb)
                vb2 = p.op("tensor", lambda e, o=ps_2[:, :], w=ones_r[:, :],
                           x=rrow[0:1, th * 512:(th + 1) * 512]:
                           e.matmul(o, w, x, start=True, stop=True), sem=s_pe)
                p.wait("vector", s_pe, vb2)
                vrb = p.op("vector", lambda e, o=rb[:, th * 512:(th + 1) * 512], i=ps_2[:, :]:
                           e.tensor_copy(o, i), sem=s_dve)
            return vrb

        def pass2(spill_in, sink, vready, pre_wait=None):
            nv = {}
            for oc in range(32):
                par = oc % 2
                if oc == 0 and pre_wait is not None:
                    for sm_, vv_ in pre_wait:
                        p.wait("sync", sm_, vv_)
                if oc >= 2:
                    p.wait("sync", s_dve, nv[oc - 2])
                p.dma("sync", xc[par][:, :], spill_in[oc], s_x[par])
                vx = s_x[par].n
                p.wait("vector", s_x[par], vx)
                if oc == 0:
                    p.wait("vector", s_dve, vready)
                va1 = p.op("vector", lambda e, o=ostg[par][:, :], i0=xc[par][:, :], i1=mb[:, :]:
                           e.tensor_tensor(o, i0, i1, ALU.subtract), sem=s_dve)
                p.wait("vector", s_dve, va1)
                nv[oc] = p.op("vector", lambda e, o=ostg[par][:, :], i0=ostg[par][:, :], i1=rb[:, :]:
                              e.tensor_tensor(o, i0, i1, ALU.mult), sem=s_dve)
                sink(oc, par, nv[oc])
            return nv

        wv_of = {}

        def wload1(oc):
            par = oc % 2
            if oc >= 2:
                p.wait("gpsimd", s_pe, wv_of[oc - 2])
            p.dma("gpsimd", wbuf[par][:, 0:32 * 128], wo_pk[oc], s_w[par])

        def mm_outproj(oc, th, pst):
            par = oc % 2
            if th == 0:
                p.wait("tensor", s_w[par], s_w[par].n)
            vd = 0
            for ic in range(32):
                vd = p.op("tensor", lambda e, o=pst[:, 0:512], w=wbuf[par][:, ic * 128:(ic + 1) * 128],
                          x=big[:, ic * T + th * 512: ic * T + th * 512 + 512],
                          st=(ic == 0), sp=(ic == 31): e.matmul(o, w, x, start=st, stop=sp),
                          sem=s_pe if ic == 31 else None)
            wv_of[oc] = vd
            return vd

        spv1, stat1 = pass1(mm_outproj, None, lambda oc: x2T[oc * 128:(oc + 1) * 128, :], x1psp,
                            wload=wload1)
        vr1 = ln_finalize(stat1)
        pw1 = [(s_spl[0], s_spl[0].n), (s_spl[1], s_spl[1].n)]

        sp1n = {0: 0, 1: 0}
        v_oproj_done = stat1

        def sink1(oc, par, vn):
            p.wait("scalar", s_dve, vn)
            if oc == 0:
                p.wait("scalar", s_pe, v_oproj_done)
            vcp = p.op("scalar", lambda e, o=big[:, oc * T:(oc + 1) * T], i=ostg[par][:, :]:
                       e.copy(o, i), sem=s_act)
            p.wait("gpsimd", s_dve, vn)
            sp1n[par] = p.dma("gpsimd", x1nsp[oc], ostg[par][:, :], s_sp[par])
            if oc < 30:
                p.wait("vector", s_sp[par], sp1n[par])
                p.wait("vector", s_act, vcp)

        pass2(x1psp, sink1, vr1, pre_wait=pw1)
        v_x1bf = s_act.n
        pw2 = [(s_sp[0], s_sp[0].n), (s_sp[1], s_sp[1].n)]

        p.wait("tensor", s_act, v_x1bf)
        ztr = {}
        gb_of = {}
        for ts in range(NB):
            p.wait("tensor", s_dve, gb_of.get(ts - 1, 0))
            vg1 = p.op("tensor", lambda e, o=ps_1[:, 0:512], w=ones_r[:, :],
                       x=grow[ts][0:1, 0:512]: e.matmul(o, w, x, start=True, stop=True), sem=s_pe)
            p.wait("vector", s_pe, vg1)
            vgc = p.op("vector", lambda e, o=gbc[:, 0:512], i=ps_1[:, 0:512]:
                       e.tensor_copy(o, i), sem=s_dve)
            p.wait("tensor", s_dve, vgc)
            vg2 = p.op("tensor", lambda e, o=ps_1[:, 0:32], w=ones_r[:, :],
                       x=grow[ts][0:1, 512:544]: e.matmul(o, w, x, start=True, stop=True), sem=s_pe)
            p.wait("vector", s_pe, vg2)
            vgc2 = p.op("vector", lambda e, o=gbc[:, 512:544], i=ps_1[:, 0:32]:
                        e.tensor_copy(o, i), sem=s_dve)
            for grp, (z0, zn) in enumerate([(0, 512), (512, 32)]):
                if grp == 1:
                    p.wait("tensor", s_dve, zw_mark)
                vz = 0
                for ic in range(32):
                    vz = p.op("tensor", lambda e, o=ps_2[:, 0:zn], w=big[:, ic * T + ts * 128: ic * T + ts * 128 + 128],
                              x=zrhs[:, ic * ZC1 + z0: ic * ZC1 + z0 + zn], st=(ic == 0), sp=(ic == 31):
                              e.matmul(o, w, x, start=st, stop=sp), sem=s_pe if ic == 31 else None)
                p.wait("vector", s_pe, vz)
                p.wait("vector", s_dve, vgc2 if grp == 0 else vgc2)
                zw_mark = p.op("vector", lambda e, o=zwb[:, z0:z0 + zn], i0=ps_2[:, 0:zn],
                               i1=gbc[:, z0:z0 + zn]: e.tensor_tensor(o, i0, i1, ALU.mult),
                               sem=s_dve)
                p.wait("vector", s_dve, zw_mark)
            gb_of[ts] = zw_mark
            p.wait("tensor", s_dve, zw_mark)
            for kkk in range(5):
                if (ts * 5 + kkk) >= 2:
                    p.wait("tensor", s_act, ztr[ts * 5 + kkk - 2])
                tb2 = (ts * 5 + kkk) % 2
                if kkk < 4:
                    vt = p.op("tensor", lambda e, o=ps_bt[tb2][:, :],
                              i=zwb[:, kkk * 128:(kkk + 1) * 128], d=idb_sb[:, :]:
                              e.transpose(o, i, d), sem=s_pe)
                    p.wait("scalar", s_pe, vt)
                    ztr[ts * 5 + kkk] = p.op("scalar",
                        lambda e, o=zt128[:, kkk * T + ts * 128: kkk * T + ts * 128 + 128],
                        i=ps_bt[tb2][:, :]: e.copy(o, i), sem=s_act)
                else:
                    vt = p.op("tensor", lambda e, o=ps_bt[tb2][0:RK1, :],
                              i=zwb[:, 512:544], d=idb_sb[:, :]:
                              e.transpose(o, i, d), sem=s_pe)
                    p.wait("scalar", s_pe, vt)
                    ztr[ts * 5 + kkk] = p.op("scalar",
                        lambda e, o=zt32[:, ts * 128:(ts + 1) * 128],
                        i=ps_bt[tb2][0:RK1, :]: e.copy(o, i), sem=s_act)
        v_zwT = ztr[NB * 5 - 1]

        p.wait("tensor", s_act, v_zwT)
        wv2_of = {}

        def wload2(oc):
            par = oc % 2
            if oc >= 2:
                p.wait("gpsimd", s_pe, wv2_of[oc - 2])
            p.dma("gpsimd", wbuf[par][:, :], kn_pk[oc], s_w[par])

        def mm_ffn(oc, th, pst):
            par = oc % 2
            if th == 0:
                p.wait("tensor", s_w[par], s_w[par].n)
            for ic in range(32):
                p.op("tensor", lambda e, o=pst[:, 0:512], w=wbuf[par][:, ic * 128:(ic + 1) * 128],
                     x=big[:, ic * T + th * 512: ic * T + th * 512 + 512], st=(ic == 0):
                     e.matmul(o, w, x, start=st, stop=False))
            for kkk in range(4):
                p.op("tensor", lambda e, o=pst[:, 0:512], w=wbuf[par][:, (32 + kkk) * 128:(33 + kkk) * 128],
                     x=zt128[:, kkk * T + th * 512: kkk * T + th * 512 + 512]:
                     e.matmul(o, w, x, start=False, stop=False))
            vd = p.op("tensor", lambda e, o=pst[:, 0:512], w=wbuf[par][0:RK1, 36 * 128:37 * 128],
                      x=zt32[:, th * 512:(th + 1) * 512]:
                      e.matmul(o, w, x, start=False, stop=True), sem=s_pe)
            wv2_of[oc] = vd
            return vd

        spv2, stat2 = pass1(mm_ffn, fb_c, lambda oc: x1nsp[oc], x2psp, wload=wload2,
                            pre_wait=pw2)
        vr2 = ln_finalize(stat2)
        pw3 = [(s_spl[0], s_spl[0].n), (s_spl[1], s_spl[1].n)]

        stv = {0: 0, 1: 0}

        def sink2(oc, par, vn):
            p.wait("sync", s_dve, vn)
            stv[par] = p.dma("sync", out[oc * 128:(oc + 1) * 128, :], ostg[par][:, :], s_st[par])
            if oc < 30:
                p.wait("vector", s_st[par], stv[par])
        pass2(x2psp, sink2, vr2, pre_wait=pw3)
        p.wait("sync", s_st[0], s_st[0].n)
        p.wait("sync", s_st[1], s_st[1].n)


def build_l1_post():
    NL, RK1 = 16, 32
    ZC1 = NL * RK1 + RK1
    nc = bass.Bass("TRN2", target_bir_lowering=False)
    io = dict(
        oT=nc.dram_tensor("oT", [D, T], BF16, kind="ExternalInput").ap(),
        x2T=nc.dram_tensor("x2T", [D, T], F32, kind="ExternalInput").ap(),
        wo_pk=nc.dram_tensor("wo_pk", [32, 128, 32 * 128], F32, kind="ExternalInput").ap(),
        kn_pk=nc.dram_tensor("kn_pk", [32, 128, 37 * 128], F32, kind="ExternalInput").ap(),
        fbias=nc.dram_tensor("fbias", [D], F32, kind="ExternalInput").ap(),
        az1=nc.dram_tensor("az1", [D, ZC1], F32, kind="ExternalInput").ap(),
        gate_exp=nc.dram_tensor("gate_exp", [NB, ZC1], F32, kind="ExternalInput").ap(),
        idf=nc.dram_tensor("idf", [128, 128], F32, kind="ExternalInput").ap(),
        onesf=nc.dram_tensor("onesf", [1, 128], F32, kind="ExternalInput").ap(),
        onescf=nc.dram_tensor("onescf", [128, 1], F32, kind="ExternalInput").ap(),
        outT=nc.dram_tensor("outT", [D, T], F32, kind="ExternalOutput").ap(),
    )
    with ExitStack() as ctx:
        sm = mk_sems(nc, ctx)
        block = ctx.enter_context(nc.Block())
        p = Prog(nc)
        sh = mk_shared(p, nc, ctx, sm, io)
        emit_f(p, nc, ctx, sm, sh, io)
        p.replay(block)
    return nc


def build_fused():
    NE, RK = 4, 8
    ZC = NE * RK + NE
    NL, RK1 = 16, 32
    ZC1 = NL * RK1 + RK1
    nc = bass.Bass("TRN2", target_bir_lowering=False)

    def inp(name, shape, dt=F32):
        return nc.dram_tensor(name, shape, dt, kind="ExternalInput").ap()

    io = dict(
        xT=inp("xT", [D, T]),
        wpk_a=inp("wpk_a", [NOC_A, 128, 32 * 128]),
        bqc=inp("bqc", [QL]), bkvc=inp("bkvc", [KVL + ROPE0]),
        idf=inp("idf", [128, 128]), onesf=inp("onesf", [1, 128]), onescf=inp("onescf", [128, 1]),
        wqn=inp("wqn", [HPC0, 128, 12 * 128]), wqr=inp("wqr", [HPC0, 128, 12 * 64]),
        wkk=inp("wkk", [HPC0, 128, 4 * 128]), wkv=inp("wkv", [HPC0, 128, 4 * 128]),
        bqn_c=inp("bqn_c", [128, HPC0]), bqr_c=inp("bqr_c", [128, HPC0]),
        bkk_c=inp("bkk_c", [128, HPC0]), bkv_r=inp("bkv_r", [HPC0, 128]),
        woT=inp("woT", [VH0, D]), bo=inp("bo", [D]),
        wb_pk=inp("wb_pk", [32, 128, 32 * 128]), b_base=inp("b_base", [D]),
        az_rhs=inp("az_rhs", [D, ZC]), bcT=inp("bcT", [NE * RK, D]), lp_br=inp("lp_br", [2, NE]),
        wq_pk=inp("wq_pk", [32, 128, 32 * 128]), wk_pk=inp("wk_pk", [32, 128, 32 * 128]),
        wv_pk=inp("wv_pk", [32, 128, 32 * 128]),
        cosd=inp("cosd", [ROT1, T]), sins=inp("sins", [ROT1, T]),
        trimask=inp("trimask", [4, 128, 512]),
        hix=inp("hix", [128, 32], U32), oix=inp("oix", [128, 32], U32),
        vix=inp("vix", [128, 32], U32),
        wo_pk=inp("wo_pk", [32, 128, 32 * 128]), kn_pk=inp("kn_pk", [32, 128, 37 * 128]),
        fbias=inp("fbias", [D]), az1=inp("az1", [D, ZC1]), gate_exp=inp("gate_exp", [NB, ZC1]),
        outT=nc.dram_tensor("outT", [D, T], F32, kind="ExternalOutput").ap(),
    )
    qckv_own = nc.dram_tensor("i_qckv_own", [NOUT_A, T], BF16)
    qckv_full = nc.dram_tensor("i_qckv_full", [W, NOUT_A, T], BF16, addr_space="Shared")
    osum = nc.dram_tensor("i_osum", [S, VH0], F32)
    o_own = nc.dram_tensor("i_o_own", [T, VH0], F32)
    x2T_i = nc.dram_tensor("i_x2T", [D, T], F32)
    qT_i = nc.dram_tensor("i_qT", [D, T], BF16)
    kT_i = nc.dram_tensor("i_kT", [D, T], BF16)
    vT_i = nc.dram_tensor("i_vT", [D, T], BF16)
    q_full_i = nc.dram_tensor("i_q_full", [W, D, T], BF16, addr_space="Shared")
    k_full_i = nc.dram_tensor("i_k_full", [W, D, T], BF16, addr_space="Shared")
    vT_full_i = nc.dram_tensor("i_vT_full", [2, W, 2048, T], BF16, addr_space="Shared")
    o_own_i = nc.dram_tensor("i_oh_own", [512, S], BF16)
    o_full_a = nc.dram_tensor("i_o_full_a", [W, 256, S], BF16, addr_space="Shared")
    o_full_b = nc.dram_tensor("i_o_full_b", [W, 256, S], BF16, addr_space="Shared")
    rg = [list(range(W))]

    def coll(p, sm, kind, op, in_ap, out_ap, ccname):
        s = sm[ccname]
        v = s.inc(1)
        p.emit("gpsimd", lambda e, i=in_ap, o=out_ap, sh_=s.h:
               e.collective_compute(kind, op, replica_groups=rg, ins=[i], outs=[o]).then_inc(sh_, 1))
        for eng in Prog.ENGINES:
            p.wait(eng, s, v)

    with ExitStack() as octx:
        sm = mk_sems(nc, octx)
        block = octx.enter_context(nc.Block())
        p = Prog(nc)
        sh = mk_shared(p, nc, octx, sm, io)
        with ExitStack() as c1:
            emit_a(p, nc, c1, sm, sh, dict(io, qckvT=qckv_own.ap()))
            barrier(p, sm)
        coll(p, sm, "AllGather", ALU.bypass,
             qckv_own.ap(), qckv_full.ap().rearrange("w n t -> (w n) t"), "cc0")
        with ExitStack() as c2:
            emit_b(p, nc, c2, sm, sh, dict(io, qckv=qckv_full.ap(), osum_part=osum.ap()))
            barrier(p, sm)
        coll(p, sm, "ReduceScatter", ALU.add, osum.ap(), o_own.ap(), "cc1")
        with ExitStack() as c3:
            emit_c(p, nc, c3, sm, sh, dict(io, oparts=o_own.ap(), x2T=x2T_i.ap()), rs_mode=True)
            barrier(p, sm)
        def ag_async(in_ap, out_ap, ccname):
            def hook(i=in_ap, o=out_ap, n=ccname):
                s = sm[n]
                s.inc(1)
                p.emit("gpsimd", lambda e, i2=i, o2=o, sh_=s.h:
                       e.collective_compute("AllGather", ALU.bypass, replica_groups=rg,
                                            ins=[i2], outs=[o2]).then_inc(sh_, 1))
            return hook

        with ExitStack() as c4:
            emit_d(p, nc, c4, sm, sh,
                   dict(io, x2T=x2T_i.ap(), qT=qT_i.ap(), kT=kT_i.ap(), vT=vT_i.ap(),
                        post_pack=[
                            [ag_async(qT_i.ap(), q_full_i.ap().rearrange("w d t -> (w d) t"), "cc0")],
                            [ag_async(kT_i.ap(), k_full_i.ap().rearrange("w d t -> (w d) t"), "cc2")],
                            [ag_async(vT_i.ap()[0:2048, :],
                                      vT_full_i.ap()[0].rearrange("w d t -> (w d) t"), "cc3"),
                             ag_async(vT_i.ap()[2048:4096, :],
                                      vT_full_i.ap()[1].rearrange("w d t -> (w d) t"), "cc3")],
                        ]))
            barrier(p, sm, skip=("cc0", "cc2", "cc3"))
        with ExitStack() as c5:
            emit_e2(p, nc, c5, sm, sh,
                    dict(io, q_full=q_full_i.ap(), k_full=k_full_i.ap(),
                         vT_full=vT_full_i.ap().rearrange("c2 w d t -> (c2 w) d t"),
                         o_own=o_own_i.ap(),
                         cc_qkv=(sm["cc0"], sm["cc2"], sm["cc3"]),
                         post_head=[
                             ag_async(o_own_i.ap()[0:256, :],
                                      o_full_a.ap().rearrange("w h t -> (w h) t"), "stv0"),
                             ag_async(o_own_i.ap()[256:512, :],
                                      o_full_b.ap().rearrange("w h t -> (w h) t"), "stv1"),
                         ]))
            barrier(p, sm, skip=("stv0", "stv1"))
        with ExitStack() as c6:
            emit_f(p, nc, c6, sm, sh, dict(io, o_full_a=o_full_a.ap(), o_full_b=o_full_b.ap(),
                                           x2T=x2T_i.ap(), cc_o=(sm["stv0"], sm["stv1"])))
        p.replay(block)
    return nc


def host_inputs(inp):
    x = np.asarray(inp["L0_attn_hidden_size"])
    hp = {}
    hp["idf"] = np.eye(128, dtype=np.float32)
    hp["onesf"] = np.ones((1, 128), np.float32)
    hp["onescf"] = np.ones((128, 1), np.float32)
    wqcT = _f32(np.asarray(inp["L0_attn_Wq_c"]).T)
    wkvcT = _f32(np.asarray(inp["L0_attn_Wkv_c"]).T)
    hp["wpk_a"] = lhsT_pack(np.concatenate([wqcT, wkvcT], axis=1), pad_out=NOC_A * 128)
    hp["bqc"] = _f32(inp["L0_attn_bq_c"])
    hp["bkvc"] = _f32(inp["L0_attn_bkv_c"])
    act = np.asarray(inp["L0_ffn_active"])
    A = np.asarray(inp["L0_ffn_lora_A"])[act]
    Bm = np.asarray(inp["L0_ffn_lora_B"])[act]
    Wr = np.asarray(inp["L0_ffn_router_W"])[act]
    br = np.asarray(inp["L0_ffn_router_b"])[act]
    prior = np.asarray(inp["L0_ffn_domain_prior"])[act]
    hp["az_rhs"] = _f32(np.concatenate([A.reshape(-1, D).T, Wr.T], axis=1))
    hp["bcT"] = _f32(Bm.transpose(0, 2, 1).reshape(-1, D))
    hp["lp_br"] = _f32(np.stack([prior, br]))
    hp["wb_pk"] = lhsT_pack(_f32(np.asarray(inp["L0_ffn_W_base"]).T))
    hp["woT"] = _f32(np.asarray(inp["L0_attn_Wo"]).T)
    hp["bo"] = _f32(inp["L0_attn_bo"])
    hp["b_base"] = _f32(inp["L0_ffn_b_base"])
    hp["wq_pk"] = lhsT_pack(_f32(np.asarray(inp["L1_attn_w_q"]).T))
    hp["wk_pk"] = lhsT_pack(_f32(np.asarray(inp["L1_attn_w_k"]).T))
    hp["wv_pk"] = lhsT_pack(_f32(np.asarray(inp["L1_attn_w_v"]).T))
    lA = np.asarray(inp["L1_ffn_lora_A"])
    lB = np.asarray(inp["L1_ffn_lora_B"])
    lAs = np.asarray(inp["L1_ffn_lora_A_shared"])
    lBs = np.asarray(inp["L1_ffn_lora_B_shared"])
    gate = _f32(inp["L1_ffn_gate"])
    hp["az1"] = _f32(np.concatenate([lA.reshape(16 * 32, D).T, lAs.T], axis=1))
    bcat = np.concatenate([lB.transpose(0, 2, 1).reshape(16 * 32, D), lBs.T], axis=0)
    kn_pk = np.zeros((32, 128, 37 * 128), np.float32)
    kn_pk[:, :, 0:32 * 128] = lhsT_pack(_f32(np.asarray(inp["L1_ffn_kernel"])))
    for k3 in range(4):
        kn_pk[:, :, (32 + k3) * 128:(33 + k3) * 128] = \
            bcat[k3 * 128:(k3 + 1) * 128].reshape(128, 32, 128).transpose(1, 0, 2)
    kn_pk[:, 0:32, 36 * 128:] = bcat[512:544].reshape(32, 32, 128).transpose(1, 0, 2)
    hp["kn_pk"] = kn_pk
    hp["gate_exp"] = _f32(np.concatenate([np.repeat(gate, 32, axis=1), np.ones((NB, 32))], axis=1))
    hp["wo_pk"] = lhsT_pack(_f32(np.asarray(inp["L1_attn_w_out"]).T))
    hp["fbias"] = _f32(inp["L1_ffn_bias"])
    invf = 1.0 / 10000.0 ** (np.arange(0, ROT1, 2) / ROT1)
    percore = []
    for r in range(W):
        pos = own_token_idx(r).astype(np.float64)
        ang = pos[None, :] * invf[:, None]
        cosd = np.repeat(np.cos(ang), 2, axis=0).astype(np.float32)
        sins_ = np.zeros((ROT1, T), np.float32)
        sins_[0::2] = -np.sin(ang)
        sins_[1::2] = np.sin(ang)
        pk = pack_l0_heads(inp["L0_attn_Wq_up"], inp["L0_attn_bq_up"],
                           inp["L0_attn_Wkv_up"], inp["L0_attn_bkv_up"],
                           [HPC0 * r + h for h in range(HPC0)])
        pk.update(xT=_f32(x[0, own_token_idx(r), :].T), cosd=cosd, sins=sins_,
                  hix=build_hix(r), oix=build_oix(r), vix=build_vix(r))
        percore.append(pk)
    hp["trimask"] = build_trimask()
    return hp, percore


def kernel(**inp):
    hp, percore = host_inputs(inp)
    nc = build_fused()
    maps = [dict(hp, **percore[r]) for r in range(W)]
    res = run_bass_kernel_spmd(nc, maps, core_ids=list(range(W)))
    out = np.zeros((1, S, D), np.float32)
    for r in range(W):
        out[0, own_token_idx(r), :] = np.asarray(res.results[r]["outT"]).T
    return out


def build_mask3(r):
    tri = np.triu(np.full((128, 128), MASKVAL, np.float32), 1)
    m = np.zeros((NB, 3, 128, 512), np.float32)
    for j in range(NB):
        g = W * j + r
        ngr = 2 * j + 2
        for s3 in range(3):
            gs = ngr - 3 + s3
            if gs < 0:
                continue
            for bb in range(4):
                b = gs * 4 + bb
                if b == g:
                    m[j, s3, :, bb * 128:(bb + 1) * 128] = tri
                elif b > g:
                    m[j, s3, :, bb * 128:(bb + 1) * 128] = MASKVAL
    return m


def own_token_idx(r):
    return np.concatenate([np.arange((8 * j + r) * 128, (8 * j + r) * 128 + 128) for j in range(NB)])


def build_trimask():
    tri = np.triu(np.full((128, 128), MASKVAL, np.float32), 1)
    m = np.zeros((4, 128, 512), np.float32)
    for d in range(4):
        m[d, :, d * 128:(d + 1) * 128] = tri
        m[d, :, (d + 1) * 128:] = MASKVAL
    return m


def build_hix(r):
    p_ = np.arange(128, dtype=np.uint32)
    m = np.zeros((128, 32), np.uint32)
    for hh in range(2):
        for ch in range(2):
            for c in range(W):
                m[:, hh * 16 + ch * 8 + c] = c * D + (2 * r + hh) * 256 + ch * 128 + p_
    return m


def build_vix(r):
    p_ = np.arange(128, dtype=np.uint32)
    m = np.zeros((128, 32), np.uint32)
    for hh in range(2):
        for ch in range(2):
            base = (2 * r + hh) * 256 + ch * 128
            hi, lo = base // 2048, base % 2048
            for c in range(W):
                m[:, hh * 16 + ch * 8 + c] = hi * 16384 + c * 2048 + lo + p_
    return m


def build_oix(r):
    p_ = np.arange(128, dtype=np.uint32)
    m = np.zeros((128, 32), np.uint32)
    for ic in range(32):
        d = ic * 128 + p_
        m[:, ic] = ((d // 512) * 256 + (d % 256)) * 8 + r
    return m


def _f32(a):
    return np.ascontiguousarray(a, dtype=np.float32)


def lhsT_pack(WT, pad_out=None):
    in_d, out_d = WT.shape
    assert in_d % 128 == 0
    ic = in_d // 128
    od = out_d if pad_out is None else pad_out
    assert od % 128 == 0
    ocn = od // 128
    Wp = np.zeros((in_d, od), np.float32)
    Wp[:, :out_d] = WT
    P = Wp.reshape(ic, 128, ocn, 128).transpose(2, 1, 0, 3)
    return np.ascontiguousarray(P.reshape(ocn, 128, ic * 128))


def pack_l0_heads(wq_up, bq_up, wkv_up, bkv_up, heads):
    nh = len(heads)
    WqT = np.ascontiguousarray(np.asarray(wq_up).T, np.float32)
    WkT = np.ascontiguousarray(np.asarray(wkv_up).T, np.float32)
    Wq = WqT.reshape(12, 128, NH0, QH0)
    Wk = WkT.reshape(4, 128, NH0, NOPE0 + VH0)
    wqn = np.ascontiguousarray(Wq[:, :, heads, :NOPE0].transpose(2, 1, 0, 3).reshape(nh, 128, 12 * 128))
    wqr = np.ascontiguousarray(Wq[:, :, heads, NOPE0:].transpose(2, 1, 0, 3).reshape(nh, 128, 12 * 64))
    wkk = np.ascontiguousarray(Wk[:, :, heads, :NOPE0].transpose(2, 1, 0, 3).reshape(nh, 128, 4 * 128))
    wkv = np.ascontiguousarray(Wk[:, :, heads, NOPE0:].transpose(2, 1, 0, 3).reshape(nh, 128, 4 * 128))
    bq = np.asarray(bq_up).reshape(NH0, QH0)
    bk = np.asarray(bkv_up).reshape(NH0, NOPE0 + VH0)
    bqn_c = np.ascontiguousarray(bq[heads, :NOPE0].T, np.float32)
    bqr_c = np.zeros((128, nh), np.float32)
    bqr_c[0:ROPE0, :] = bq[heads, NOPE0:].T
    bkk_c = np.ascontiguousarray(bk[heads, :NOPE0].T, np.float32)
    bkv_r = np.ascontiguousarray(bk[heads, NOPE0:], np.float32)
    return dict(wqn=wqn, wqr=wqr, wkk=wkk, wkv=wkv,
                bqn_c=bqn_c, bqr_c=bqr_c, bkk_c=bkk_c, bkv_r=bkv_r)


def kernel_multi(**inp):
    x = np.asarray(inp["L0_attn_hidden_size"])
    idf = np.eye(128, dtype=np.float32)
    onesf = np.ones((1, 128), np.float32)
    onescf = np.ones((128, 1), np.float32)
    cores = list(range(W))

    wqcT = _f32(np.asarray(inp["L0_attn_Wq_c"]).T)
    wkvcT = _f32(np.asarray(inp["L0_attn_Wkv_c"]).T)
    wpk_a = lhsT_pack(np.concatenate([wqcT, wkvcT], axis=1), pad_out=NOC_A * 128)
    bqc = _f32(inp["L0_attn_bq_c"])
    bkvc = _f32(inp["L0_attn_bkv_c"])
    nc_a = build_l0_pre()
    xTs = [_f32(x[0, own_token_idx(r), :].T) for r in cores]
    maps = [dict(xT=xTs[r], wpk_a=wpk_a, bqc=bqc, bkvc=bkvc, idf=idf, onesf=onesf, onescf=onescf) for r in cores]
    res = run_bass_kernel_spmd(nc_a, maps, core_ids=cores)
    qckv_st = np.stack([np.asarray(res.results[r]["qckvT"]) for r in cores])

    nc_b = build_l0_attn()
    maps = []
    for r in cores:
        pk = pack_l0_heads(inp["L0_attn_Wq_up"], inp["L0_attn_bq_up"],
                           inp["L0_attn_Wkv_up"], inp["L0_attn_bkv_up"],
                           [HPC0 * r + h for h in range(HPC0)])
        pk.update(qckv=qckv_st, idf=idf, onesf=onesf, onescf=onescf)
        maps.append(pk)
    res = run_bass_kernel_spmd(nc_b, maps, core_ids=cores)
    osum_parts = [np.asarray(res.results[r]["osum_part"]) for r in cores]

    act = np.asarray(inp["L0_ffn_active"])
    A = np.asarray(inp["L0_ffn_lora_A"])[act]
    Bm = np.asarray(inp["L0_ffn_lora_B"])[act]
    Wr = np.asarray(inp["L0_ffn_router_W"])[act]
    br = np.asarray(inp["L0_ffn_router_b"])[act]
    prior = np.asarray(inp["L0_ffn_domain_prior"])[act]
    az = _f32(np.concatenate([A.reshape(-1, D).T, Wr.T], axis=1))
    bcT_h = _f32(Bm.transpose(0, 2, 1).reshape(-1, D))
    lp_br = _f32(np.stack([prior, br]))
    wb_pk = lhsT_pack(_f32(np.asarray(inp["L0_ffn_W_base"]).T))
    woT = _f32(np.asarray(inp["L0_attn_Wo"]).T)
    nc_c = build_l0_post()
    maps = []
    for r in cores:
        op_r = np.stack([osum_parts[c][r * T:(r + 1) * T, :] for c in cores])
        maps.append(dict(oparts=op_r, woT=woT, bo=_f32(inp["L0_attn_bo"]), xT=xTs[r],
                         wb_pk=wb_pk, b_base=_f32(inp["L0_ffn_b_base"]), az_rhs=az,
                         bcT=bcT_h, lp_br=lp_br, idf=idf, onesf=onesf, onescf=onescf))
    res = run_bass_kernel_spmd(nc_c, maps, core_ids=cores)
    x2Ts = [np.asarray(res.results[r]["x2T"]) for r in cores]

    wq_pk = lhsT_pack(_f32(np.asarray(inp["L1_attn_w_q"]).T))
    wk_pk = lhsT_pack(_f32(np.asarray(inp["L1_attn_w_k"]).T))
    wv_pk = lhsT_pack(_f32(np.asarray(inp["L1_attn_w_v"]).T))
    invf = 1.0 / 10000.0 ** (np.arange(0, ROT1, 2) / ROT1)
    nc_d = build_l1_qkv()
    maps = []
    for r in cores:
        pos = own_token_idx(r).astype(np.float64)
        ang = pos[None, :] * invf[:, None]
        cosd = np.repeat(np.cos(ang), 2, axis=0).astype(np.float32)
        sins_ = np.zeros((ROT1, T), np.float32)
        sins_[0::2] = -np.sin(ang)
        sins_[1::2] = np.sin(ang)
        maps.append(dict(x2T=x2Ts[r], wq_pk=wq_pk, wk_pk=wk_pk, wv_pk=wv_pk,
                         cosd=cosd, sins=sins_, idf=idf, onesf=onesf, onescf=onescf))
    res = run_bass_kernel_spmd(nc_d, maps, core_ids=cores)
    qTs = [np.asarray(res.results[r]["qT"]) for r in cores]
    k_full = np.stack([np.asarray(res.results[r]["kT"]) for r in cores])
    v_full = np.stack([np.asarray(res.results[r]["v_own"]) for r in cores])

    nc_e = build_l1_attn()
    maps = [dict(qT=qTs[r], k_full=k_full, v_full=v_full, mask3=build_mask3(r), idf=idf,
                 onesf=onesf, onescf=onescf) for r in cores]
    res = run_bass_kernel_spmd(nc_e, maps, core_ids=cores)
    oTs = [np.asarray(res.results[r]["oT"]) for r in cores]

    lA = np.asarray(inp["L1_ffn_lora_A"])
    lB = np.asarray(inp["L1_ffn_lora_B"])
    lAs = np.asarray(inp["L1_ffn_lora_A_shared"])
    lBs = np.asarray(inp["L1_ffn_lora_B_shared"])
    gate = _f32(inp["L1_ffn_gate"])
    az1 = _f32(np.concatenate([lA.reshape(16 * 32, D).T, lAs.T], axis=1))
    bcat = np.concatenate([lB.transpose(0, 2, 1).reshape(16 * 32, D), lBs.T], axis=0)
    kn_pk = np.zeros((32, 128, 37 * 128), np.float32)
    kn_pk[:, :, 0:32 * 128] = lhsT_pack(_f32(np.asarray(inp["L1_ffn_kernel"])))
    for k3 in range(4):
        kn_pk[:, :, (32 + k3) * 128:(33 + k3) * 128] = \
            bcat[k3 * 128:(k3 + 1) * 128].reshape(128, 32, 128).transpose(1, 0, 2)
    kn_pk[:, 0:32, 36 * 128:] = bcat[512:544].reshape(32, 32, 128).transpose(1, 0, 2)
    gate_exp = _f32(np.concatenate([np.repeat(gate, 32, axis=1), np.ones((NB, 32))], axis=1))
    wo_pk = lhsT_pack(_f32(np.asarray(inp["L1_attn_w_out"]).T))
    nc_f = build_l1_post()
    maps = [dict(oT=oTs[r], x2T=x2Ts[r], wo_pk=wo_pk, kn_pk=kn_pk,
                 fbias=_f32(inp["L1_ffn_bias"]), az1=az1, gate_exp=gate_exp, idf=idf,
                 onesf=onesf, onescf=onescf) for r in cores]
    res = run_bass_kernel_spmd(nc_f, maps, core_ids=cores)

    out = np.zeros((1, S, D), np.float32)
    for r in cores:
        out[0, own_token_idx(r), :] = np.asarray(res.results[r]["outT"]).T
    return out


if __name__ == "__main__":
    pass


# version 16
# speedup vs baseline: 1.2509x; 1.0035x over previous
import math
import numpy as np
import ml_dtypes
import concourse.bass as bass
import concourse.mybir as mybir
from concourse.bass_utils import run_bass_kernel_spmd
from contextlib import ExitStack

from concourse.bass import IndirectOffsetOnAxis

F32 = mybir.dt.float32
BF16 = mybir.dt.bfloat16
U32 = mybir.dt.uint32
AF = mybir.ActivationFunctionType
ALU = mybir.AluOpType
AX = mybir.AxisListType
BF = ml_dtypes.bfloat16

W = 8
S = 8192
D = 4096
T = S // W
NB = T // 128

QL, KVL = 1536, 512
NOPE0, ROPE0 = 128, 64
QH0 = NOPE0 + ROPE0
VH0 = 128
NH0, HPC0 = 32, 4
NH1, HD1, ROT1 = 16, 256, 64
MASKVAL = -1.0e9 / 16.0


class Sem:
    def __init__(self, handle):
        self.h = handle
        self.n = 0

    def inc(self, k=16):
        self.n += k
        return self.n


class Prog:
    ENGINES = ("gpsimd", "sync", "tensor", "vector", "scalar")

    def __init__(self, nc):
        self.nc = nc
        self.items = []

    def emit(self, eng, fn):
        self.items.append((eng, fn))

    def dma(self, eng, out, in_, sem):
        v = sem.inc(16)
        self.emit(eng, lambda e, o=out, i=in_, s=sem.h: e.dma_start(out=o, in_=i).then_inc(s, 16))
        return v

    def dma_ind(self, out, in_, idx, sem):
        v = sem.inc(16)
        self.emit("gpsimd", lambda e, o=out, i=in_, x=idx, s=sem.h:
                  e.indirect_dma_start(
                      out=o, in_=i,
                      in_offset=IndirectOffsetOnAxis(ap=x, axis=0),
                      out_offset=None).then_inc(s, 16))
        return v

    def wait(self, eng, sem, v):
        if v > 0:
            self.emit(eng, lambda e, s=sem.h, vv=v: e.wait_ge(s, vv))

    def op(self, eng, builder, sem=None):
        if sem is None:
            self.emit(eng, builder)
            return None
        v = sem.inc(1)
        self.emit(eng, lambda e, b=builder, s=sem.h: b(e).then_inc(s, 1))
        return v

    def replay(self, block):
        for key in self.ENGINES:
            deco = getattr(block, key)

            def make(k):
                def body(eng):
                    for ek, fn in self.items:
                        if ek == k:
                            fn(eng)
                return body
            deco(make(key))


def ceil_div(a, b):
    return (a + b - 1) // b


def colvec_dma(p, nc, ctx, name, dram_ap, n, s_ld):
    m = ceil_div(n, 128)
    nfull = n // 128
    tail = n - nfull * 128
    E = ctx.enter_context
    st = {}
    if nfull:
        st["stage"] = E(nc.sbuf_tensor(name + "_stg", [nfull, 128], F32))
        p.dma("gpsimd", st["stage"][:, :], dram_ap[0:nfull * 128].rearrange("(p f) -> p f", f=128), s_ld)
    if tail:
        st["stage2"] = E(nc.sbuf_tensor(name + "_stg2", [1, tail], F32))
        p.dma("gpsimd", st["stage2"][:, :], dram_ap[nfull * 128:n].rearrange("(p f) -> p f", f=tail), s_ld)
    st["col"] = E(nc.sbuf_tensor(name, [128, m], F32))
    st["n"] = n
    return st


def colvec_compute(p, st, idf_sb, ps_cv, col_off, s_pe, s_cv, scale=None):
    n = st["n"]
    m = ceil_div(n, 128)
    nfull = n // 128
    tail = n - nfull * 128
    col = st["col"]
    p.wait("tensor", s_cv, s_cv.n)
    if nfull:
        p.op("tensor", lambda e, o=ps_cv[0:128, col_off:col_off + nfull], i=st["stage"][:, :],
             d=idf_sb[0:nfull, 0:nfull]: e.transpose(o, i, d), sem=s_pe)
    if tail:
        p.op("tensor", lambda e, o=ps_cv[0:tail, col_off + nfull:col_off + nfull + 1], i=st["stage2"][:, :],
             d=idf_sb[0:1, 0:1]: e.transpose(o, i, d), sem=s_pe)
    p.wait("scalar", s_pe, s_pe.n)
    sc = 1.0 if scale is None else scale
    if nfull:
        p.op("scalar", lambda e, o=col[:, 0:nfull], i=ps_cv[:, col_off:col_off + nfull], s=sc:
             e.activation(o, i, AF.Copy, bias=0.0, scale=s), sem=None if tail else s_cv)
    if tail:
        p.op("scalar", lambda e, o=col[0:tail, nfull:nfull + 1],
             i=ps_cv[0:tail, col_off + nfull:col_off + nfull + 1], s=sc:
             e.activation(o, i, AF.Copy, bias=0.0, scale=s), sem=s_cv)
    return col



SEM_NAMES = ["ld_g", "ld_s", "w0", "w1", "x0", "x1", "sp0", "sp1", "st0", "st1",
             "stv0", "stv1", "k0", "k1", "k2", "k3", "v0", "v1", "v2", "v3", "m0", "m1",
             "pe", "act", "dve", "cv", "gp",
             "cc0", "cc1", "cc2", "cc3"]


def mk_sems(nc, ctx):
    return {n: Sem(ctx.enter_context(nc.semaphore("sm_" + n))) for n in SEM_NAMES}


def barrier(p, sm, skip=()):
    for eng in Prog.ENGINES:
        for name, s in sm.items():
            if name not in skip:
                p.wait(eng, s, s.n)


NOUT_A = QL + KVL + ROPE0
NOC_A = ceil_div(NOUT_A, 128)


def mk_shared(p, nc, ctx, sm, io):
    E = ctx.enter_context
    sh = {}
    sh["idf_sb"] = E(nc.sbuf_tensor("sh_idf", [128, 128], F32))
    sh["idb_sb"] = E(nc.sbuf_tensor("sh_idb", [128, 128], BF16))
    sh["ones_r"] = E(nc.sbuf_tensor("sh_onr", [1, 128], F32))
    sh["ones_rb"] = E(nc.sbuf_tensor("sh_onrb", [1, 128], BF16))
    sh["ones_cb"] = E(nc.sbuf_tensor("sh_oncb", [128, 1], BF16))
    sh["eps12"] = E(nc.sbuf_tensor("sh_eps12", [1, 1], F32))
    sh["eps5"] = E(nc.sbuf_tensor("sh_eps5", [1, 1], F32))
    s_ld = sm["ld_g"]
    p.dma("gpsimd", sh["idf_sb"][:, :], io["idf"][:, :], s_ld)
    p.dma("gpsimd", sh["idb_sb"][:, :], io["idf"][:, :], s_ld)
    p.dma("gpsimd", sh["ones_r"][:, :], io["onesf"][:, :], s_ld)
    p.dma("gpsimd", sh["ones_rb"][:, :], io["onesf"][:, :], s_ld)
    p.dma("gpsimd", sh["ones_cb"][:, :], io["onescf"][:, :], s_ld)
    p.op("vector", lambda e, o=sh["eps12"]: e.memset(o[:, :], 1e-12))
    v_eps = p.op("vector", lambda e, o=sh["eps5"]: e.memset(o[:, :], 1e-5), sem=sm["dve"])
    v_ld = s_ld.n
    for eng in ("tensor", "vector", "scalar", "sync"):
        p.wait(eng, s_ld, v_ld)
        p.wait(eng, sm["dve"], v_eps)
    p.wait("gpsimd", s_ld, v_ld)
    return sh


def emit_a(p, nc, ctx, sm, sh, io):
    E = ctx.enter_context
    xT, wpk, bqc, bkvc, out = io["xT"], io["wpk_a"], io["bqc"], io["bkvc"], io["qckvT"]
    idf_sb = sh["idf_sb"]
    xbf = E(nc.sbuf_tensor("a_xbf", [128, 32 * T], BF16))
    wbuf = [E(nc.sbuf_tensor(f"a_wbuf{i}", [128, 32 * 128], BF16)) for i in range(2)]
    evac = [E(nc.sbuf_tensor(f"a_evac{i}", [128, T], BF16)) for i in range(2)]
    ps_cv = E(nc.psum_tensor("a_ps_cv", [128, 32], F32))
    ps = [E(nc.psum_tensor(f"a_ps{i}", [128, 512], F32)) for i in range(4)]
    s_ld = sm["ld_g"]
    s_w = [sm["w0"], sm["w1"]]
    s_pe, s_cv, s_ev = sm["pe"], sm["cv"], sm["act"]
    s_st = [sm["st0"], sm["st1"]]

    for g in range(8):
        p.dma("gpsimd", xbf[:, :].rearrange("p (c t) -> p c t", t=T)[:, g * 4:(g + 1) * 4, :],
              xT.rearrange("(c p) t -> p c t", p=128)[:, g * 4:(g + 1) * 4, :], s_ld)
    cv_bqc = colvec_dma(p, nc, ctx, "a_bqc_col", bqc, QL, s_ld)
    cv_bkv = colvec_dma(p, nc, ctx, "a_bkv_col", bkvc, KVL + ROPE0, s_ld)
    v_ld = s_ld.n
    p.wait("tensor", s_ld, v_ld)
    bqc_col = colvec_compute(p, cv_bqc, idf_sb, ps_cv, 0, s_pe, s_cv)
    bkv_col = colvec_compute(p, cv_bkv, idf_sb, ps_cv, 12, s_pe, s_cv)
    v_cv0 = s_cv.n

    ev_of = {}
    st_of = {}
    pe_of = {}
    for oc in range(NOC_A):
        rows = min(128, NOUT_A - oc * 128)
        buf = wbuf[oc % 2]
        ebuf = evac[oc % 2]
        if oc >= 2:
            p.wait("gpsimd", s_pe, pe_of[(oc - 2, 1)])
        vw = p.dma("gpsimd", buf[:, :], wpk[oc], s_w[oc % 2])
        p.wait("tensor", s_w[oc % 2], vw)
        for th in range(2):
            k = oc * 2 + th
            pst = ps[k % 4]
            if k >= 4:
                p.wait("tensor", s_ev, ev_of[((k - 4) // 2, (k - 4) % 2)])
            for ic in range(32):
                last = ic == 31
                pe_v = p.op("tensor", lambda e, o=pst[:, :], w=buf[:, ic * 128:(ic + 1) * 128],
                            x=xbf[:, ic * T + th * 512: ic * T + th * 512 + 512],
                            st=(ic == 0), sp=last: e.matmul(o, w, x, start=st, stop=sp),
                            sem=s_pe if last else None)
            pe_of[(oc, th)] = pe_v
            col, ccol = (bqc_col, oc) if oc < 12 else (bkv_col, oc - 12)
            if oc >= 2 and th == 0:
                p.wait("scalar", s_st[oc % 2], st_of[(oc - 2, 1)])
            p.wait("scalar", s_pe, pe_of[(oc, th)])
            if oc == 0 and th == 0:
                p.wait("scalar", s_cv, v_cv0)
            ev_of[(oc, th)] = p.op("scalar",
                lambda e, o=ebuf[0:rows, th * 512:(th + 1) * 512], i=pst[0:rows, :],
                b=col[0:rows, ccol:ccol + 1]: e.activation(o, i, AF.Identity, bias=b, scale=1.0),
                sem=s_ev)
            p.wait("sync", s_ev, ev_of[(oc, th)])
            st_of[(oc, th)] = p.dma("sync", out[oc * 128: oc * 128 + rows, th * 512:(th + 1) * 512],
                                    ebuf[0:rows, th * 512:(th + 1) * 512], s_st[oc % 2])
    p.wait("sync", s_st[0], s_st[0].n)
    p.wait("sync", s_st[1], s_st[1].n)


def build_l0_pre():
    nc = bass.Bass("TRN2", target_bir_lowering=False)
    io = dict(
        xT=nc.dram_tensor("xT", [D, T], F32, kind="ExternalInput").ap(),
        wpk_a=nc.dram_tensor("wpk_a", [NOC_A, 128, 32 * 128], F32, kind="ExternalInput").ap(),
        bqc=nc.dram_tensor("bqc", [QL], F32, kind="ExternalInput").ap(),
        bkvc=nc.dram_tensor("bkvc", [KVL + ROPE0], F32, kind="ExternalInput").ap(),
        idf=nc.dram_tensor("idf", [128, 128], F32, kind="ExternalInput").ap(),
        onesf=nc.dram_tensor("onesf", [1, 128], F32, kind="ExternalInput").ap(),
        onescf=nc.dram_tensor("onescf", [128, 1], F32, kind="ExternalInput").ap(),
        qckvT=nc.dram_tensor("qckvT", [NOUT_A, T], BF16, kind="ExternalOutput").ap(),
    )
    with ExitStack() as ctx:
        sm = mk_sems(nc, ctx)
        block = ctx.enter_context(nc.Block())
        p = Prog(nc)
        sh = mk_shared(p, nc, ctx, sm, io)
        emit_a(p, nc, ctx, sm, sh, io)
        p.replay(block)
    return nc


def emit_b(p, nc, ctx, sm, sh, io, SQ=S, nheads=HPC0):
    TQ = SQ // W
    MT = SQ // 512
    NQB = SQ // 128
    NKT = SQ // 128
    QG = 2
    SC0 = 1.0 / math.sqrt(QH0)
    qckv, out = io["qckv"], io["osum_part"]
    wqn, wqr, wkk, wkv = io["wqn"], io["wqr"], io["wkk"], io["wkv"]
    bqn_c, bqr_c, bkk_c, bkv_r = io["bqn_c"], io["bqr_c"], io["bkk_c"], io["bkv_r"]
    if True:
        E = ctx.enter_context
        qn_sb = E(nc.sbuf_tensor("qn_sb", [128, SQ], BF16))
        kn_sb = E(nc.sbuf_tensor("kn_sb", [128, SQ], BF16))
        qr_sb = E(nc.sbuf_tensor("qr_sb", [128, SQ], BF16))
        kr_sb = E(nc.sbuf_tensor("kr_sb", [128, SQ], BF16))
        vbuf = E(nc.sbuf_tensor("vbuf", [128, SQ], BF16))
        pbuf = [E(nc.sbuf_tensor(f"pbuf{i}", [128, SQ], BF16)) for i in range(3)]
        osum = E(nc.sbuf_tensor("osum", [128, SQ], F32))
        qcsl = [E(nc.sbuf_tensor(f"qcsl{i}", [128, 12 * 512], BF16)) for i in range(2)]
        kvsl = [E(nc.sbuf_tensor(f"kvsl{i}", [128, 4 * 512], BF16)) for i in range(2)]
        pt_sb = [E(nc.sbuf_tensor(f"pt_sb{i}", [128, 1024], BF16)) for i in range(2)]
        whb = [dict(qn=E(nc.sbuf_tensor("wqnb", [128, 12 * 128], BF16)),
                    qr=E(nc.sbuf_tensor("wqrb", [128, 12 * 64], BF16)),
                    kk=E(nc.sbuf_tensor("wkkb", [128, 4 * 128], BF16)),
                    kv=E(nc.sbuf_tensor("wkvb", [128, 4 * 128], BF16)))]
        bqn = E(nc.sbuf_tensor("bqn", [128, nheads], F32))
        bqr = E(nc.sbuf_tensor("bqr", [128, nheads], F32))
        bkk = E(nc.sbuf_tensor("bkk", [128, nheads], F32))
        bvr = [E(nc.sbuf_tensor(f"bvr{h}", [1, 128], BF16)) for h in range(nheads)]
        mcol2 = [E(nc.sbuf_tensor(f"mcol{i}", [128, MT], F32)) for i in range(2)]
        lcol2 = [E(nc.sbuf_tensor(f"lcol{i}", [128, MT], F32)) for i in range(2)]
        negM = E(nc.sbuf_tensor("negM", [128, 1], F32))
        Ccol = E(nc.sbuf_tensor("Ccol", [128, MT], F32))
        CLc = E(nc.sbuf_tensor("CLc", [128, MT], F32))
        lsum = E(nc.sbuf_tensor("lsum", [128, 1], F32))
        linv = E(nc.sbuf_tensor("linv", [128, 1], F32))
        Fcol = E(nc.sbuf_tensor("Fcol", [128, MT], F32))
        ost = [E(nc.sbuf_tensor(f"ost{i}", [128, 128], F32)) for i in range(2)]

        ps_a = E(nc.psum_tensor("ps_a", [128, 512], F32))
        ps_b = E(nc.psum_tensor("ps_b", [128, 512], F32))
        ps_c = E(nc.psum_tensor("ps_c", [128, 512], F32))
        ps_d = E(nc.psum_tensor("ps_d", [128, 512], F32))
        ps_t = [E(nc.psum_tensor(f"ps_t{i}", [128, 1024], BF16)) for i in range(2)]
        ps_o = E(nc.psum_tensor("ps_o", [128, 128], F32))

        s_ld, s_ld2 = sm["ld_g"], sm["ld_s"]
        s_wh = [sm["w0"], sm["w1"]]
        s_sl = [sm["x0"], sm["x1"]]
        s_pe, s_act, s_dve = sm["pe"], sm["act"], sm["dve"]
        s_st = [sm["st0"], sm["st1"]]
        ones_sb = sh["ones_rb"]
        idf_sb = sh["idf_sb"]
        idb_sb = sh["idb_sb"]

        p.dma("gpsimd", bqn[:, :], bqn_c[:, :], s_ld)
        p.dma("gpsimd", bqr[:, :], bqr_c[:, :], s_ld)
        p.dma("gpsimd", bkk[:, :], bkk_c[:, :], s_ld)
        for h in range(nheads):
            p.dma("gpsimd", bvr[h][:, :], bkv_r[h:h + 1, :], s_ld)
        for c in range(W):
            p.dma("sync", kr_sb[0:64, c * TQ:(c + 1) * TQ], qckv[c, QL + KVL:NOUT_A, :], s_ld2)
        p.op("vector", lambda e: e.memset(qr_sb[64:128, :], 0.0))
        v_ms = p.op("vector", lambda e: e.memset(kr_sb[64:128, :], 0.0), sem=s_dve)
        p.wait("tensor", s_dve, v_ms)
        v_ld = s_ld.n
        v_ld2 = s_ld2.n
        p.wait("scalar", s_ld, v_ld)
        p.op("scalar", lambda e: e.mul(bqn[:, :], bqn[:, :], SC0), sem=s_act)
        v_bias = p.op("scalar", lambda e: e.mul(bqr[:, :], bqr[:, :], SC0), sem=s_act)
        p.wait("scalar", s_act, v_bias)
        p.wait("tensor", s_ld, v_ld)
        p.wait("tensor", s_ld2, v_ld2)
        p.wait("vector", s_ld, v_ld)

        vCe_prev = [0]
        last_evac = {"a": 0, "b": 0, "c": 0, "d": 0}
        last_exp = {"a": 0, "b": 0, "c": 0, "d": 0}
        s1_done_pe = {}
        sl_consumed = {}
        trans_last = 0
        trans_done = {}
        oev_prev = (None, 0)
        osum_written = {}
        st_of = {}

        for h in range(nheads):
            wb = whb[0]
            if h >= 1:
                p.wait("gpsimd", s_pe, s1_done_pe[h - 1])
            p.dma("gpsimd", wb["qn"][:, :], wqn[h], s_wh[0])
            p.dma("gpsimd", wb["qr"][:, :], wqr[h], s_wh[0])
            p.dma("gpsimd", wb["kk"][:, :], wkk[h], s_wh[0])
            vwh = p.dma("gpsimd", wb["kv"][:, :], wkv[h], s_wh[0])

            p.wait("tensor", s_wh[0], vwh)
            for m in range(MT):
                c, tb = m // (TQ // 512), m % (TQ // 512)
                mp = m % 2
                gm = h * MT + m
                if gm >= 2:
                    p.wait("sync", s_pe, sl_consumed[gm - 2])
                p.dma("sync", qcsl[mp][:, :].rearrange("p (i t) -> p i t", t=512),
                      qckv[c, 0:QL, tb * 512:(tb + 1) * 512].rearrange("(i p) t -> p i t", p=128),
                      s_sl[mp])
                vsl = p.dma("sync", kvsl[mp][:, :].rearrange("p (i t) -> p i t", t=512),
                            qckv[c, QL:QL + KVL, tb * 512:(tb + 1) * 512].rearrange("(i p) t -> p i t", p=128),
                            s_sl[mp])
                p.wait("tensor", s_sl[mp], vsl)
                p.wait("tensor", s_act, last_evac["a"])
                p.wait("tensor", s_act, last_exp["a"])
                for ic in range(12):
                    p.op("tensor", lambda e, o=ps_a[:, :], w=wb["qn"][:, ic * 128:(ic + 1) * 128],
                         x=qcsl[mp][:, ic * 512:(ic + 1) * 512], st=(ic == 0), sp=(ic == 11):
                         e.matmul(o, w, x, start=st, stop=sp))
                p.wait("tensor", s_act, last_evac["b"])
                p.wait("tensor", s_act, last_exp["b"])
                for ic in range(12):
                    p.op("tensor", lambda e, o=ps_b[0:64, :], w=wb["qr"][:, ic * 64:(ic + 1) * 64],
                         x=qcsl[mp][:, ic * 512:(ic + 1) * 512], st=(ic == 0), sp=(ic == 11):
                         e.matmul(o, w, x, start=st, stop=sp))
                p.wait("tensor", s_act, last_evac["c"])
                p.wait("tensor", s_act, last_exp["c"])
                for ic in range(4):
                    p.op("tensor", lambda e, o=ps_c[:, :], w=wb["kk"][:, ic * 128:(ic + 1) * 128],
                         x=kvsl[mp][:, ic * 512:(ic + 1) * 512], st=(ic == 0), sp=(ic == 3):
                         e.matmul(o, w, x, start=st, stop=sp))
                p.wait("tensor", s_act, last_evac["d"])
                vmm = 0
                for ts in range(4):
                    for ic in range(4):
                        p.op("tensor", lambda e, o=ps_d[:, ts * 128:(ts + 1) * 128],
                             w=kvsl[mp][:, ic * 512 + ts * 128: ic * 512 + ts * 128 + 128],
                             x=wb["kv"][:, ic * 128:(ic + 1) * 128], st=(ic == 0):
                             e.matmul(o, w, x, start=st, stop=False))
                    vmm = p.op("tensor", lambda e, o=ps_d[:, ts * 128:(ts + 1) * 128], w=ones_sb[:, :],
                               x=bvr[h][:, :]: e.matmul(o, w, x, start=False, stop=True),
                               sem=s_pe if ts == 3 else None)
                sl_consumed[gm] = vmm
                p.wait("scalar", s_pe, vmm)
                last_evac["a"] = p.op("scalar", lambda e, o=qn_sb[:, m * 512:(m + 1) * 512],
                                      i=ps_a[:, :], b=bqn[:, h:h + 1], s=SC0:
                                      e.activation(o, i, AF.Identity, bias=b, scale=s), sem=s_act)
                last_evac["b"] = p.op("scalar", lambda e, o=qr_sb[0:64, m * 512:(m + 1) * 512],
                                      i=ps_b[0:64, :], b=bqr[0:64, h:h + 1], s=SC0:
                                      e.activation(o, i, AF.Identity, bias=b, scale=s), sem=s_act)
                last_evac["c"] = p.op("scalar", lambda e, o=kn_sb[:, m * 512:(m + 1) * 512],
                                      i=ps_c[:, :], b=bkk[:, h:h + 1]:
                                      e.activation(o, i, AF.Identity, bias=b, scale=1.0), sem=s_act)
                last_evac["d"] = p.op("scalar", lambda e, o=vbuf[:, m * 512:(m + 1) * 512],
                                      i=ps_d[:, :]: e.copy(o, i), sem=s_act)
            s1_done_pe[h] = sl_consumed[h * MT + MT - 1]
            s1_evacs_done = last_evac["d"]

            p.wait("tensor", s_act, s1_evacs_done)
            SBANKS = [("a", ps_a), ("b", ps_b), ("c", ps_c), ("d", ps_d)]
            s_gp = sm["gp"]

            def score_chunk(qb, kc):
                pb = pbuf[qb % 3]
                mcol = mcol2[qb % 2]
                lcol = lcol2[qb % 2]
                bank, pss = SBANKS[kc % 4]
                if kc % 4 == 0:
                    p.wait("tensor", s_act, last_exp[SBANKS[3][0]])
                    if kc < 4:
                        p.wait("tensor", s_act, max(last_evac.values()))
                p.op("tensor", lambda e, o=pss[:, :], w=qn_sb[:, qb * 128:(qb + 1) * 128],
                     x=kn_sb[:, kc * 512:(kc + 1) * 512]: e.matmul(o, w, x, start=True, stop=False))
                vs = p.op("tensor", lambda e, o=pss[:, :], w=qr_sb[:, qb * 128:(qb + 1) * 128],
                          x=kr_sb[:, kc * 512:(kc + 1) * 512]:
                          e.matmul(o, w, x, start=False, stop=True), sem=s_pe)
                if kc == 0:
                    p.wait("vector", s_pe, vs)
                    vm = p.op("vector", lambda e, o=mcol[:, 0:1], i=pss[:, :]:
                              e.reduce_max(o, i, axis=AX.X, negate=True), sem=s_dve)
                    p.wait("scalar", s_dve, vm)
                else:
                    p.wait("scalar", s_pe, vs)
                last_exp[bank] = p.op("scalar", lambda e, o=pb[:, kc * 512:(kc + 1) * 512],
                                      i=pss[:, :], b=mcol[:, 0:1], a=lcol[:, kc:kc + 1]:
                                      e.activation(o, i, AF.Exp, bias=b, scale=1.0, accum_out=a),
                                      sem=s_act)

            def emit_stats(qb):
                lcol = lcol2[qb % 2]
                p.wait("vector", s_act, qb_exp_done[qb])
                v2 = p.op("vector", lambda e, o=lsum[:, :], i=lcol[:, :]:
                          e.reduce_sum(o, i, axis=AX.X), sem=s_dve)
                p.wait("vector", s_dve, v2)
                v3 = p.op("vector", lambda e, o=linv[:, :], i=lsum[:, :]: e.reciprocal(o, i),
                          sem=s_dve)
                p.wait("vector", s_dve, v3)
                p.wait("vector", s_pe, trans_done.get(qb % 3, 0))

            def emit_corr(qb, kc):
                pb = pbuf[qb % 3]
                return p.op("vector", lambda e, o=pb[:, kc * 512:(kc + 1) * 512],
                            sc=linv[:, :]: e.tensor_scalar(o, o, sc, None, ALU.mult),
                            sem=s_dve)

            def av_step(gq, k8, st_av):
                if k8 == 0:
                    p.wait("tensor", s_dve, corr_done[gq])
                    eng0, val0 = oev_prev2[0]
                    if eng0 == "scalar":
                        p.wait("tensor", s_act, val0)
                    elif eng0 == "vector":
                        p.wait("tensor", s_dve, val0)
                if k8 > 0:
                    kprev = k8 - 1
                    eng0, val0 = st_av["ptev"][kprev]
                    p.wait("tensor", s_act if eng0 == "scalar" else s_dve, val0)
                    for sub in range(8):
                        kt = 8 * kprev + sub
                        avv = p.op("tensor", lambda e, o=ps_o[:, :],
                                   w=vbuf[:, kt * 128:(kt + 1) * 128],
                                   x=pt_sb[kprev % 2][:, sub * 128:(sub + 1) * 128],
                                   st=(kt == 0), sp=(kt == NKT - 1):
                                   e.matmul(o, w, x, start=st, stop=sp),
                                   sem=s_pe if kt == NKT - 1 else None)
                if k8 < NK8:
                    tb2 = k8 % 2
                    if k8 >= 2:
                        eng0, val0 = st_av["ptev"][k8 - 2]
                        p.wait("tensor", s_act if eng0 == "scalar" else s_dve, val0)
                    tv = 0
                    for sub in range(8):
                        kt = 8 * k8 + sub
                        tv = p.op("tensor", lambda e,
                                  o=ps_t[tb2][:, sub * 128:(sub + 1) * 128],
                                  i=pbuf[gq % 3][:, kt * 128:(kt + 1) * 128], d=idb_sb[:, :]:
                                  e.transpose(o, i, d),
                                  sem=s_pe if sub == 7 else None)
                    eng = "vector" if k8 % 2 == 0 else "scalar"
                    p.wait(eng, s_pe, tv)
                    if eng == "vector":
                        pv = p.op("vector", lambda e, o=pt_sb[tb2][:, :], i=ps_t[tb2][:, :]:
                                  e.tensor_copy(o, i), sem=s_dve)
                    else:
                        pv = p.op("scalar", lambda e, o=pt_sb[tb2][:, :], i=ps_t[tb2][:, :]:
                                  e.copy(o, i), sem=s_act)
                    st_av["ptev"][k8] = (eng, pv)
                else:
                    avv_final = s_pe.n
                    trans_done[gq % 3] = avv_final
                    if h == 0:
                        p.wait("scalar", s_pe, avv_final)
                        ov = p.op("scalar", lambda e, o=osum[:, gq * 128:(gq + 1) * 128],
                                  i=ps_o[:, :]: e.copy(o, i), sem=s_act)
                        osum_written[gq] = ("scalar", ov)
                        oev_prev2[0] = ("scalar", ov)
                    else:
                        p.wait("vector", s_pe, avv_final)
                        peng, pval = osum_written[gq]
                        p.wait("vector", s_act if peng == "scalar" else s_dve, pval)
                        ov = p.op("vector", lambda e, o=osum[:, gq * 128:(gq + 1) * 128],
                                  i0=ps_o[:, :], i1=osum[:, gq * 128:(gq + 1) * 128]:
                                  e.tensor_tensor(o, i0, i1, ALU.add), sem=s_dve)
                        osum_written[gq] = ("vector", ov)
                        oev_prev2[0] = ("vector", ov)

            corr_done = {}
            qb_exp_done = {}
            oev_prev2 = [oev_prev]
            st_av = {"ptev": {}}
            av_q = []
            NK8 = NKT // 8
            slot = [0]

            def pump_av(force=False):
                if av_q and (force or av_q[0][2] <= slot[0]):
                    g0, k8, _ = av_q[0]
                    if k8 == 0:
                        st_av["ptev"] = {}
                    av_step(g0, k8, st_av)
                    if k8 + 1 > NK8:
                        av_q.pop(0)
                    else:
                        av_q[0][1] = k8 + 1

            NCS = (MT + 1) // 2
            for qb in range(NQB):
                pbq = qb % 3
                for kc in range(MT):
                    if kc == 0:
                        p.wait("scalar", s_pe, trans_done.get(pbq, 0))
                    score_chunk(qb, kc)
                    if kc == MT - 1:
                        qb_exp_done[qb] = last_exp[SBANKS[(MT - 1) % 4][0]]
                    if qb >= 1:
                        if kc == 0:
                            emit_stats(qb - 1)
                        elif kc <= NCS:
                            for c2 in (2 * (kc - 1), 2 * (kc - 1) + 1):
                                if c2 < MT:
                                    cv = emit_corr(qb - 1, c2)
                            if kc == NCS:
                                corr_done[qb - 1] = cv
                                av_q.append([qb - 1, 0, slot[0] + 1])
                    if kc % 2 == 1 or kc == 0:
                        pump_av()
                    slot[0] += 1
            emit_stats(NQB - 1)
            for kc in range(MT):
                cv = emit_corr(NQB - 1, kc)
            corr_done[NQB - 1] = cv
            av_q.append([NQB - 1, 0, 0])
            while av_q:
                pump_av(force=True)
            oev_prev = oev_prev2[0]
            trans_last = max(trans_done.values()) if trans_done else trans_last

        fin_eng, fin_val = osum_written[NQB // QG - 1]
        p.wait("tensor", s_dve if fin_eng == "vector" else s_act, fin_val)
        for g in sorted(osum_written):
            eng, val = osum_written[g]
            p.wait("tensor", s_dve if eng == "vector" else s_act, val)
        ev3 = {}
        for tt in range(NKT):
            pst = ps_c if tt % 2 == 0 else ps_d
            if tt >= 2:
                p.wait("tensor", s_act, ev3[tt - 2])
            tv = p.op("tensor", lambda e, o=pst[:, 0:128], i=osum[:, tt * 128:(tt + 1) * 128],
                      d=idf_sb[:, :]: e.transpose(o, i, d), sem=s_pe)
            p.wait("scalar", s_pe, tv)
            if tt >= 2:
                p.wait("scalar", s_st[tt % 2], st_of[tt - 2])
            ev3[tt] = p.op("scalar", lambda e, o=ost[tt % 2][:, :], i=pst[:, 0:128]: e.copy(o, i),
                           sem=s_act)
            p.wait("sync", s_act, ev3[tt])
            st_of[tt] = p.dma("sync", out[tt * 128:(tt + 1) * 128, :], ost[tt % 2][:, :], s_st[tt % 2])
        p.wait("sync", s_st[0], s_st[0].n)
        p.wait("sync", s_st[1], s_st[1].n)


def build_l0_attn(SQ=S, nheads=HPC0):
    TQ = SQ // W
    nc = bass.Bass("TRN2", target_bir_lowering=False)
    io = dict(
        qckv=nc.dram_tensor("qckv", [W, NOUT_A, TQ], BF16, kind="ExternalInput").ap(),
        wqn=nc.dram_tensor("wqn", [nheads, 128, 12 * 128], F32, kind="ExternalInput").ap(),
        wqr=nc.dram_tensor("wqr", [nheads, 128, 12 * 64], F32, kind="ExternalInput").ap(),
        wkk=nc.dram_tensor("wkk", [nheads, 128, 4 * 128], F32, kind="ExternalInput").ap(),
        wkv=nc.dram_tensor("wkv", [nheads, 128, 4 * 128], F32, kind="ExternalInput").ap(),
        bqn_c=nc.dram_tensor("bqn_c", [128, nheads], F32, kind="ExternalInput").ap(),
        bqr_c=nc.dram_tensor("bqr_c", [128, nheads], F32, kind="ExternalInput").ap(),
        bkk_c=nc.dram_tensor("bkk_c", [128, nheads], F32, kind="ExternalInput").ap(),
        bkv_r=nc.dram_tensor("bkv_r", [nheads, 128], F32, kind="ExternalInput").ap(),
        idf=nc.dram_tensor("idf", [128, 128], F32, kind="ExternalInput").ap(),
        onesf=nc.dram_tensor("onesf", [1, 128], F32, kind="ExternalInput").ap(),
        onescf=nc.dram_tensor("onescf", [128, 1], F32, kind="ExternalInput").ap(),
        osum_part=nc.dram_tensor("osum_part", [SQ, VH0], F32, kind="ExternalOutput").ap(),
    )
    with ExitStack() as ctx:
        sm = mk_sems(nc, ctx)
        block = ctx.enter_context(nc.Block())
        p = Prog(nc)
        sh = mk_shared(p, nc, ctx, sm, io)
        emit_b(p, nc, ctx, sm, sh, io, SQ=SQ, nheads=nheads)
        p.replay(block)
    return nc


def emit_c(p, nc, ctx, sm, sh, io, rs_mode=False):
    NE, RK = 4, 8
    ZC = NE * RK + NE
    oparts, out = io["oparts"], io["x2T"]
    woT, bo_d, xT = io["woT"], io["bo"], io["xT"]
    wb_pk, bb_d, az_rhs, bcT, lp_br = io["wb_pk"], io["b_base"], io["az_rhs"], io["bcT"], io["lp_br"]
    x1psp = nc.dram_tensor("x1psp", [32, 128, T], F32)
    x1nsp = nc.dram_tensor("x1nsp", [32, 128, T], F32)
    x2psp = nc.dram_tensor("x2psp", [32, 128, T], F32)
    if True:
        E = ctx.enter_context
        idf_sb = sh["idf_sb"]
        idb_sb = sh["idb_sb"]
        ones_r = sh["ones_r"]
        ones_cb = sh["ones_cb"]
        eps12 = sh["eps12"]
        eps5 = sh["eps5"]
        obuf = E(nc.sbuf_tensor("obuf", [128, (1 if rs_mode else W) * T], F32))
        oacc = E(nc.sbuf_tensor("oacc", [128, T], F32))
        oacb = E(nc.sbuf_tensor("oacb", [128, T], BF16))
        oT_bf = E(nc.sbuf_tensor("oT_bf", [128, T], BF16))
        wo_sb = E(nc.sbuf_tensor("wo_sb", [128, D], BF16))
        wbuf = [E(nc.sbuf_tensor(f"wbufc{i}", [128, 32 * 128], BF16)) for i in range(2)]
        xc = [E(nc.sbuf_tensor(f"xc{i}", [128, T], F32)) for i in range(2)]
        t1 = [E(nc.sbuf_tensor(f"t1_{i}", [128, T], F32)) for i in range(2)]
        tbf = [E(nc.sbuf_tensor(f"tbf{i}", [128, T], BF16)) for i in range(2)]
        sqb = [E(nc.sbuf_tensor(f"sqb{i}", [128, T], BF16)) for i in range(2)]
        x1bf = E(nc.sbuf_tensor("x1bf", [128, 32 * T], BF16))
        mrow = E(nc.sbuf_tensor("mrow", [1, T], F32))
        qrow = E(nc.sbuf_tensor("qrow", [1, T], F32))
        vrow = E(nc.sbuf_tensor("vrow", [1, T], F32))
        rrow = E(nc.sbuf_tensor("rrow", [1, T], F32))
        mb = E(nc.sbuf_tensor("mb", [128, T], F32))
        rb = E(nc.sbuf_tensor("rb", [128, T], F32))
        zb = E(nc.sbuf_tensor("zb", [128, NB * ZC], F32))
        zrhs = E(nc.sbuf_tensor("zrhs", [128, 32 * ZC], BF16))
        zwb = E(nc.sbuf_tensor("zwb", [128, NB * NE * RK], BF16))
        zwT = E(nc.sbuf_tensor("zwT", [NE * RK, T], BF16))
        bcT_sb = E(nc.sbuf_tensor("bcT_sb", [NE * RK, D], BF16))
        lpbr = E(nc.sbuf_tensor("lpbr", [1, NE], F32))
        brow = E(nc.sbuf_tensor("brow", [1, NE], F32))
        lrow = E(nc.sbuf_tensor("lrow", [1, NE], F32))
        lrow_b = E(nc.sbuf_tensor("lrow_b", [128, NE], F32))
        a4 = E(nc.sbuf_tensor("a4", [128, NE], F32))
        e4 = E(nc.sbuf_tensor("e4", [128, NE], F32))
        am = E(nc.sbuf_tensor("am", [128, 1], F32))
        al = E(nc.sbuf_tensor("al", [128, 1], F32))
        ali = E(nc.sbuf_tensor("ali", [128, 1], F32))
        ostg = [E(nc.sbuf_tensor(f"ostg{i}", [128, T], F32)) for i in range(2)]

        ps_1 = E(nc.psum_tensor("ps_1", [128, 512], F32))
        ps_2 = E(nc.psum_tensor("ps_2", [128, 512], F32))
        ps_bt = [E(nc.psum_tensor(f"ps_bt{i}", [128, 128], BF16)) for i in range(2)]
        ps_s = [E(nc.psum_tensor(f"ps_s{i}", [1, 512], F32)) for i in range(2)]
        ps_q = [E(nc.psum_tensor(f"ps_q{i}", [1, 512], F32)) for i in range(2)]

        s_ld, s_ld2 = sm["ld_g"], sm["ld_s"]
        s_w = [sm["w0"], sm["w1"]]
        s_x = [sm["x0"], sm["x1"]]
        s_pe, s_act, s_dve = sm["pe"], sm["act"], sm["dve"]
        s_sp = [sm["sp0"], sm["sp1"]]
        s_spl = [sm["k0"], sm["k1"]]
        s_st = [sm["st0"], sm["st1"]]

        p.dma("gpsimd", wo_sb[:, :], woT[:, :], s_ld)
        p.dma("gpsimd", zrhs[:, :].rearrange("p (i z) -> p i z", z=ZC),
              az_rhs.rearrange("(i p) z -> p i z", p=128), s_ld)
        p.dma("gpsimd", bcT_sb[:, :], bcT[:, :], s_ld)
        p.dma("gpsimd", lpbr[:, :], lp_br[0:1, :], s_ld)
        p.dma("gpsimd", brow[:, :], lp_br[1:2, :], s_ld)
        for c in range(1 if rs_mode else W):
            p.dma("sync", obuf[:, c * T:(c + 1) * T].rearrange("p (i v) -> p i v", v=VH0),
                  (oparts if rs_mode else oparts[c]).rearrange("(i p) v -> p i v", p=128), s_ld2)
        cv_bo = colvec_dma(p, nc, ctx, "bo_colc", bo_d, D, s_ld)
        cv_bb = colvec_dma(p, nc, ctx, "bb_colc", bb_d, D, s_ld)
        v_ld, v_ld2 = s_ld.n, s_ld2.n
        for eng in ("tensor", "vector", "scalar"):
            p.wait(eng, s_ld, v_ld)
            p.wait(eng, s_ld2, v_ld2)
        bo_c = colvec_compute(p, cv_bo, idf_sb, ps_1, 0, s_pe, s_act)
        bb_c = colvec_compute(p, cv_bb, idf_sb, ps_1, 32, s_pe, s_act)
        v_cv = s_act.n
        p.wait("scalar", s_act, v_cv)

        va = p.op("scalar", lambda e, o=lrow[:, :], i=lpbr[:, :], b=eps12[:, :]:
                  e.activation(o, i, AF.Ln, bias=b, scale=1.0), sem=s_act)
        p.wait("vector", s_act, va)
        vb = p.op("vector", lambda e, o=lrow[:, :], i0=lrow[:, :], i1=brow[:, :]:
                  e.tensor_tensor(o, i0, i1, ALU.add), sem=s_dve)
        p.wait("tensor", s_dve, vb)
        p.wait("tensor", s_act, v_cv)
        vbc = p.op("tensor", lambda e, o=ps_1[0:128, 0:NE], w=ones_r[:, :], x=lrow[:, :]:
                   e.matmul(o, w, x, start=True, stop=True), sem=s_pe)
        p.wait("vector", s_pe, vbc)
        v_lrb = p.op("vector", lambda e, o=lrow_b[:, :], i=ps_1[0:128, 0:NE]:
                     e.tensor_copy(o, i), sem=s_dve)

        if rs_mode:
            vcb = p.op("vector", lambda e, o=oacb[:, :], i=obuf[:, 0:T]: e.tensor_copy(o, i),
                       sem=s_dve)
        else:
            vsum = 0
            for c in range(1, W):
                src = obuf[:, 0:T] if c == 1 else oacc[:, :]
                if c > 1:
                    p.wait("vector", s_dve, vsum)
                vsum = p.op("vector", lambda e, o=oacc[:, :], i0=obuf[:, c * T:(c + 1) * T], i1=src:
                            e.tensor_tensor(o, i0, i1, ALU.add), sem=s_dve)
            p.wait("vector", s_dve, vsum)
            vcb = p.op("vector", lambda e, o=oacb[:, :], i=oacc[:, :]: e.tensor_copy(o, i), sem=s_dve)
        p.wait("tensor", s_dve, vcb)
        vet = 0
        for tt in range(NB):
            if tt >= 2:
                p.wait("tensor", s_act, ev_bt[tt - 2])
            vtr = p.op("tensor", lambda e, o=ps_bt[tt % 2][:, :],
                       i=oacb[:, tt * 128:(tt + 1) * 128], d=idb_sb[:, :]:
                       e.transpose(o, i, d), sem=s_pe)
            p.wait("scalar", s_pe, vtr)
            vet = p.op("scalar", lambda e, o=oT_bf[:, tt * 128:(tt + 1) * 128],
                       i=ps_bt[tt % 2][:, :]: e.copy(o, i), sem=s_act)
            ev_bt = ev_bt if tt > 0 else {}
            ev_bt[tt] = vet
        v_oT = vet

        def pass1(tag, emit_mms, bias_col, res_src, spill, wload=None, pre_wait=None):
            ev = {}
            addv = {}
            spv = {0: 0, 1: 0}
            stat_of = {}
            vsq_of = {}
            vtb_of = {}

            def emit_statmms(oc2):
                par2 = oc2 % 2
                p.wait("tensor", s_dve, vtb_of[oc2])
                p.wait("tensor", s_act, vsq_of[oc2])
                sm = 0
                for th2 in range(2):
                    p.op("tensor", lambda e, o=ps_s[th2][:, :], w=ones_cb[:, :],
                         x=tbf[par2][:, th2 * 512:(th2 + 1) * 512], st=(oc2 == 0), sp=(oc2 == 31):
                         e.matmul(o, w, x, start=st, stop=sp))
                    sm = p.op("tensor", lambda e, o=ps_q[th2][:, :], w=ones_cb[:, :],
                              x=sqb[par2][:, th2 * 512:(th2 + 1) * 512], st=(oc2 == 0), sp=(oc2 == 31):
                              e.matmul(o, w, x, start=st, stop=sp),
                              sem=s_pe if th2 == 1 else None)
                stat_of[oc2] = sm

            for oc in range(32):
                par = oc % 2
                if wload is not None:
                    wload(oc)
                if oc == 0 and pre_wait is not None:
                    for sm_, vv_ in pre_wait:
                        p.wait("sync", sm_, vv_)
                if oc >= 2:
                    p.wait("sync", s_dve, addv[oc - 2])
                p.dma("sync", xc[par][:, :], res_src(oc), s_x[par])
                vx = s_x[par].n
                for th in range(2):
                    k = oc * 2 + th
                    pst = ps_1 if k % 2 == 0 else ps_2
                    if k >= 2:
                        p.wait("tensor", s_act, ev[k - 2])
                    vk = emit_mms(oc, th, pst)
                    p.wait("scalar", s_pe, vk)
                    if oc >= 2 and th == 0:
                        p.wait("scalar", s_spl[par], spv[par])
                    ev[k] = p.op("scalar", lambda e, o=t1[par][:, th * 512:(th + 1) * 512],
                                 i=pst[:, 0:512], b=bias_col[:, oc:oc + 1]:
                                 e.activation(o, i, AF.Identity, bias=b, scale=1.0), sem=s_act)
                p.wait("vector", s_act, ev[oc * 2 + 1])
                p.wait("vector", s_x[par], vx)
                addv[oc] = p.op("vector", lambda e, o=t1[par][:, :], i0=t1[par][:, :],
                                i1=xc[par][:, :]: e.tensor_tensor(o, i0, i1, ALU.add), sem=s_dve)
                p.wait("scalar", s_dve, addv[oc])
                if oc >= 2:
                    p.wait("scalar", s_pe, stat_of.get(oc - 2, 0))
                vsq_of[oc] = p.op("scalar", lambda e, o=sqb[par][:, :], i=t1[par][:, :]:
                                  e.square(o, i), sem=s_act)
                p.wait("vector", s_dve, addv[oc])
                if oc >= 2:
                    p.wait("vector", s_pe, stat_of.get(oc - 2, 0))
                vtb_of[oc] = p.op("vector", lambda e, o=tbf[par][:, :], i=t1[par][:, :]:
                                  e.tensor_copy(o, i), sem=s_dve)
                if oc >= 1:
                    emit_statmms(oc - 1)
                p.wait("sync", s_dve, addv[oc])
                spv[par] = p.dma("sync", spill[oc], t1[par][:, :], s_spl[par])
            emit_statmms(31)
            return spv, stat_of[31]

        def ln_finalize(stat_mark):
            p.wait("scalar", s_pe, stat_mark)
            c1 = p.op("scalar", lambda e, o=mrow[0:1, 0:512], i=ps_s[0][:, :]:
                      e.activation(o, i, AF.Copy, bias=0.0, scale=1.0 / D), sem=s_act)
            c2 = p.op("scalar", lambda e, o=mrow[0:1, 512:1024], i=ps_s[1][:, :]:
                      e.activation(o, i, AF.Copy, bias=0.0, scale=1.0 / D), sem=s_act)
            p.wait("scalar", s_act, c2)
            c3 = p.op("scalar", lambda e, o=qrow[:, :], i=mrow[:, :]: e.square(o, i), sem=s_act)
            p.wait("vector", s_act, c3)
            v1 = 0
            for th in range(2):
                v1 = p.op("vector", lambda e, o=vrow[0:1, th * 512:(th + 1) * 512],
                          i0=ps_q[th][:, :], i1=qrow[0:1, th * 512:(th + 1) * 512]:
                          e.scalar_tensor_tensor(o, i0, 1.0 / D, i1, ALU.mult, ALU.subtract),
                          sem=s_dve)
            p.wait("scalar", s_dve, v1)
            c4 = p.op("scalar", lambda e, o=vrow[:, :], i=vrow[:, :], b=eps5[:, :]:
                      e.activation(o, i, AF.Sqrt, bias=b, scale=1.0), sem=s_act)
            p.wait("vector", s_act, c4)
            v2 = p.op("vector", lambda e, o=rrow[:, :], i=vrow[:, :]: e.reciprocal(o, i), sem=s_dve)
            p.wait("tensor", s_dve, v2)
            p.wait("tensor", s_act, c2)
            vmb = vrb = 0
            for th in range(2):
                if th == 1:
                    p.wait("tensor", s_dve, vmb)
                vb1 = p.op("tensor", lambda e, o=ps_1[:, :], w=ones_r[:, :],
                           x=mrow[0:1, th * 512:(th + 1) * 512]:
                           e.matmul(o, w, x, start=True, stop=True), sem=s_pe)
                p.wait("vector", s_pe, vb1)
                vmb = p.op("vector", lambda e, o=mb[:, th * 512:(th + 1) * 512], i=ps_1[:, :]:
                           e.tensor_copy(o, i), sem=s_dve)
                if th == 1:
                    p.wait("tensor", s_dve, vrb)
                vb2 = p.op("tensor", lambda e, o=ps_2[:, :], w=ones_r[:, :],
                           x=rrow[0:1, th * 512:(th + 1) * 512]:
                           e.matmul(o, w, x, start=True, stop=True), sem=s_pe)
                p.wait("vector", s_pe, vb2)
                vrb = p.op("vector", lambda e, o=rb[:, th * 512:(th + 1) * 512], i=ps_2[:, :]:
                           e.tensor_copy(o, i), sem=s_dve)
            return vrb

        def pass2(tag, spill_in, sink, vready, pre_wait=None):
            nv = {}
            for oc in range(32):
                par = oc % 2
                if oc == 0 and pre_wait is not None:
                    for sm_, vv_ in pre_wait:
                        p.wait("sync", sm_, vv_)
                if oc >= 2:
                    p.wait("sync", s_dve, nv[oc - 2])
                p.dma("sync", xc[par][:, :], spill_in[oc], s_x[par])
                vx = s_x[par].n
                p.wait("vector", s_x[par], vx)
                if oc == 0:
                    p.wait("vector", s_dve, vready)
                va1 = p.op("vector", lambda e, o=ostg[par][:, :], i0=xc[par][:, :], i1=mb[:, :]:
                           e.tensor_tensor(o, i0, i1, ALU.subtract), sem=s_dve)
                p.wait("vector", s_dve, va1)
                nv[oc] = p.op("vector", lambda e, o=ostg[par][:, :], i0=ostg[par][:, :], i1=rb[:, :]:
                              e.tensor_tensor(o, i0, i1, ALU.mult), sem=s_dve)
                sink(oc, par, nv[oc])
            return nv

        p.wait("tensor", s_act, v_oT)

        def mm_outproj(oc, th, pst):
            return p.op("tensor", lambda e, o=pst[:, 0:512],
                        w=wo_sb[:, oc * 128:(oc + 1) * 128],
                        x=oT_bf[:, th * 512:(th + 1) * 512]:
                        e.matmul(o, w, x, start=True, stop=True), sem=s_pe)

        spv1, stat1 = pass1("x1", mm_outproj, bo_c, lambda oc: xT[oc * 128:(oc + 1) * 128, :],
                            x1psp)
        vr1 = ln_finalize(stat1)
        pw1 = [(s_spl[0], s_spl[0].n), (s_spl[1], s_spl[1].n)]

        sp1n = {0: 0, 1: 0}

        def sink1(oc, par, vn):
            p.wait("scalar", s_dve, vn)
            vcp = p.op("scalar", lambda e, o=x1bf[:, oc * T:(oc + 1) * T], i=ostg[par][:, :]:
                       e.copy(o, i), sem=s_act)
            p.wait("gpsimd", s_dve, vn)
            sp1n[par] = p.dma("gpsimd", x1nsp[oc], ostg[par][:, :], s_sp[par])
            if oc < 30:
                p.wait("vector", s_sp[par], sp1n[par])
                p.wait("vector", s_act, vcp)

        nv1 = pass2("x1n", x1psp, sink1, vr1, pre_wait=pw1)
        v_x1bf = s_act.n
        pw2 = [(s_sp[0], s_sp[0].n), (s_sp[1], s_sp[1].n)]

        p.wait("tensor", s_act, v_x1bf)
        zev = {}
        ztr = {}
        for ts in range(NB):
            if ts >= 1:
                p.wait("tensor", s_act, zev[ts - 1])
            for ic in range(32):
                vz = p.op("tensor", lambda e, o=ps_2[:, 0:ZC], w=x1bf[:, ic * T + ts * 128: ic * T + ts * 128 + 128],
                          x=zrhs[:, ic * ZC:(ic + 1) * ZC], st=(ic == 0), sp=(ic == 31):
                          e.matmul(o, w, x, start=st, stop=sp), sem=s_pe if ic == 31 else None)
            p.wait("scalar", s_pe, vz)
            zev[ts] = p.op("scalar", lambda e, o=zb[:, ts * ZC:(ts + 1) * ZC], i=ps_2[:, 0:ZC]:
                           e.copy(o, i), sem=s_act)
            p.wait("vector", s_act, zev[ts])
            if ts == 0:
                p.wait("vector", s_dve, v_lrb)
            w1 = p.op("vector", lambda e, o=a4[:, :], i0=zb[:, ts * ZC + NE * RK:(ts + 1) * ZC],
                      i1=lrow_b[:, :]: e.tensor_tensor(o, i0, i1, ALU.add), sem=s_dve)
            p.wait("vector", s_dve, w1)
            w2 = p.op("vector", lambda e, o=am[:, :], i=a4[:, :]:
                      e.reduce_max(o, i, axis=AX.X, negate=True), sem=s_dve)
            p.wait("scalar", s_dve, w2)
            w3 = p.op("scalar", lambda e, o=e4[:, :], i=a4[:, :], b=am[:, :], a=al[:, :]:
                      e.activation(o, i, AF.Exp, bias=b, scale=1.0, accum_out=a), sem=s_act)
            p.wait("vector", s_act, w3)
            w4 = p.op("vector", lambda e, o=ali[:, :], i=al[:, :]: e.reciprocal(o, i), sem=s_dve)
            p.wait("scalar", s_dve, w4)
            w5 = p.op("scalar", lambda e, o=ali[:, :], i=ali[:, :]: e.mul(o, i, 2.0), sem=s_act)
            p.wait("vector", s_act, w5)
            w6 = p.op("vector", lambda e, o=e4[:, :], i=e4[:, :], sc=ali[:, :]:
                      e.tensor_scalar(o, i, sc, None, ALU.mult), sem=s_dve)
            p.wait("vector", s_dve, w6)
            w7 = 0
            for ee in range(NE):
                w7 = p.op("vector", lambda e, o=zwb[:, ts * NE * RK + ee * RK: ts * NE * RK + (ee + 1) * RK],
                          i=zb[:, ts * ZC + ee * RK: ts * ZC + (ee + 1) * RK], sc=e4[:, ee:ee + 1]:
                          e.tensor_scalar(o, i, sc, None, ALU.mult),
                          sem=s_dve if ee == NE - 1 else None)
            p.wait("tensor", s_dve, w7)
            if ts >= 2:
                p.wait("tensor", s_act, ztr[ts - 2])
            vt = p.op("tensor", lambda e, o=ps_bt[ts % 2][0:NE * RK, :],
                      i=zwb[:, ts * NE * RK:(ts + 1) * NE * RK], d=idb_sb[:, :]:
                      e.transpose(o, i, d), sem=s_pe)
            p.wait("scalar", s_pe, vt)
            ztr[ts] = p.op("scalar", lambda e, o=zwT[:, ts * 128:(ts + 1) * 128],
                           i=ps_bt[ts % 2][0:NE * RK, :]:
                           e.copy(o, i), sem=s_act)
        v_zwT = ztr[NB - 1]

        p.wait("tensor", s_act, v_zwT)
        wv_of = {}

        def wloadC(oc):
            par = oc % 2
            if oc >= 2:
                p.wait("gpsimd", s_pe, wv_of[oc - 2])
            p.dma("gpsimd", wbuf[par][:, :], wb_pk[oc], s_w[par])

        def mm_ffn(oc, th, pst):
            par = oc % 2
            if th == 0:
                p.wait("tensor", s_w[par], s_w[par].n)
            for ic in range(32):
                p.op("tensor", lambda e, o=pst[:, 0:512], w=wbuf[par][:, ic * 128:(ic + 1) * 128],
                     x=x1bf[:, ic * T + th * 512: ic * T + th * 512 + 512], st=(ic == 0):
                     e.matmul(o, w, x, start=st, stop=False))
            vd = p.op("tensor", lambda e, o=pst[:, 0:512], w=bcT_sb[:, oc * 128:(oc + 1) * 128],
                      x=zwT[:, th * 512:(th + 1) * 512]:
                      e.matmul(o, w, x, start=False, stop=True), sem=s_pe)
            wv_of[oc] = vd
            return vd

        spv2, stat2 = pass1("x2", mm_ffn, bb_c, lambda oc: x1nsp[oc], x2psp, wload=wloadC,
                            pre_wait=pw2)
        vr2 = ln_finalize(stat2)
        pw3 = [(s_spl[0], s_spl[0].n), (s_spl[1], s_spl[1].n)]

        stv = {0: 0, 1: 0}

        def sink2(oc, par, vn):
            p.wait("sync", s_dve, vn)
            stv[par] = p.dma("sync", out[oc * 128:(oc + 1) * 128, :], ostg[par][:, :], s_st[par])
            if oc < 30:
                p.wait("vector", s_st[par], stv[par])
        pass2("out", x2psp, sink2, vr2, pre_wait=pw3)
        p.wait("sync", s_st[0], s_st[0].n)
        p.wait("sync", s_st[1], s_st[1].n)


def build_l0_post():
    NE, RK = 4, 8
    ZC = NE * RK + NE
    nc = bass.Bass("TRN2", target_bir_lowering=False)
    io = dict(
        oparts=nc.dram_tensor("oparts", [W, T, VH0], F32, kind="ExternalInput").ap(),
        woT=nc.dram_tensor("woT", [VH0, D], F32, kind="ExternalInput").ap(),
        bo=nc.dram_tensor("bo", [D], F32, kind="ExternalInput").ap(),
        xT=nc.dram_tensor("xT", [D, T], F32, kind="ExternalInput").ap(),
        wb_pk=nc.dram_tensor("wb_pk", [32, 128, 32 * 128], F32, kind="ExternalInput").ap(),
        b_base=nc.dram_tensor("b_base", [D], F32, kind="ExternalInput").ap(),
        az_rhs=nc.dram_tensor("az_rhs", [D, ZC], F32, kind="ExternalInput").ap(),
        bcT=nc.dram_tensor("bcT", [NE * RK, D], F32, kind="ExternalInput").ap(),
        lp_br=nc.dram_tensor("lp_br", [2, NE], F32, kind="ExternalInput").ap(),
        idf=nc.dram_tensor("idf", [128, 128], F32, kind="ExternalInput").ap(),
        onesf=nc.dram_tensor("onesf", [1, 128], F32, kind="ExternalInput").ap(),
        onescf=nc.dram_tensor("onescf", [128, 1], F32, kind="ExternalInput").ap(),
        x2T=nc.dram_tensor("x2T", [D, T], F32, kind="ExternalOutput").ap(),
    )
    with ExitStack() as ctx:
        sm = mk_sems(nc, ctx)
        block = ctx.enter_context(nc.Block())
        p = Prog(nc)
        sh = mk_shared(p, nc, ctx, sm, io)
        emit_c(p, nc, ctx, sm, sh, io)
        p.replay(block)
    return nc


def emit_d(p, nc, ctx, sm, sh, io):
    x2T, cosd, sins = io["x2T"], io["cosd"], io["sins"]
    wq_pk, wk_pk, wv_pk = io["wq_pk"], io["wk_pk"], io["wv_pk"]
    qT, kT, vT = io["qT"], io["kT"], io["vT"]
    if True:
        E = ctx.enter_context
        xbf = E(nc.sbuf_tensor("xbfd", [128, 32 * T], BF16))
        idb_sb = sh["idb_sb"]
        cos_sb = E(nc.sbuf_tensor("cos_sb", [ROT1, T], F32))
        sin_sb = E(nc.sbuf_tensor("sin_sb", [ROT1, T], F32))
        wbuf = [E(nc.sbuf_tensor(f"wbufd{i}", [128, 32 * 128], BF16)) for i in range(2)]
        stg = [E(nc.sbuf_tensor(f"stgd{i}", [128, T], F32)) for i in range(2)]
        rtmp = E(nc.sbuf_tensor("rtmp", [ROT1, T], F32))
        obf = [E(nc.sbuf_tensor(f"obfd{i}", [128, T], BF16)) for i in range(2)]
        ps = [E(nc.psum_tensor(f"psd{i}", [128, 512], F32)) for i in range(4)]
        s_ld = sm["ld_g"]
        s_w = [sm["w0"], sm["w1"]]
        s_pe, s_act, s_dve = sm["pe"], sm["act"], sm["dve"]
        s_st = [sm["st0"], sm["st1"]]

        p.dma("gpsimd", cos_sb[:, :], cosd[:, :], s_ld)
        p.dma("gpsimd", sin_sb[:, :], sins[:, :], s_ld)
        for g in range(8):
            p.dma("gpsimd", xbf[:, :].rearrange("p (c t) -> p c t", t=T)[:, g * 4:(g + 1) * 4, :],
                  x2T.rearrange("(c p) t -> p c t", p=128)[:, g * 4:(g + 1) * 4, :], s_ld)
        v_ld = s_ld.n
        for eng in ("tensor", "vector", "scalar"):
            p.wait(eng, s_ld, v_ld)

        SHUF = [(i ^ 1) for i in range(32)]
        ev = {}
        st_of = {}
        rope_v = {}
        wld = {}
        kk = 0
        post_pack = io.get("post_pack")
        for wi, (wpk, outt) in enumerate([(wq_pk, qT), (wk_pk, kT), (wv_pk, vT)]):
            snaps = {}
            for oc in range(32):
                par = kk % 2
                if kk >= 2:
                    p.wait("gpsimd", s_pe, wld[kk - 2])
                p.dma("gpsimd", wbuf[par][:, :], wpk[oc], s_w[par])
                vw = s_w[par].n
                p.wait("tensor", s_w[par], vw)
                vmm = 0
                for th in range(2):
                    pst = ps[(kk * 2 + th) % 4]
                    if kk * 2 + th >= 4:
                        p.wait("tensor", s_act, ev[kk * 2 + th - 4])
                    for ic in range(32):
                        vmm = p.op("tensor", lambda e, o=pst[:, :], w=wbuf[par][:, ic * 128:(ic + 1) * 128],
                                   x=xbf[:, ic * T + th * 512: ic * T + th * 512 + 512],
                                   st=(ic == 0), sp=(ic == 31): e.matmul(o, w, x, start=st, stop=sp),
                                   sem=s_pe if ic == 31 else None)
                wld[kk] = vmm
                for th in range(2):
                    pst = ps[(kk * 2 + th) % 4]
                    p.wait("scalar", s_pe, wld[kk])
                    if kk >= 2 and th == 0:
                        if st_of.get(kk - 2, 0):
                            p.wait("scalar", s_st[par], st_of[kk - 2])
                        if (kk - 2) in rope_v:
                            p.wait("scalar", s_dve, rope_v[kk - 2])
                    sc = (1.0 / 16.0) if wi == 0 else 1.0
                    ev[kk * 2 + th] = p.op("scalar", lambda e, o=stg[par][:, th * 512:(th + 1) * 512],
                                           i=pst[:, :], s=sc:
                                           e.activation(o, i, AF.Copy, bias=0.0, scale=s), sem=s_act)
                if True:
                    p.wait("vector", s_act, ev[kk * 2 + 1])
                    vr = 0
                    if wi < 2 and oc % 2 == 0:
                        vr = p.op("vector", lambda e, o=rtmp[:, :], i=stg[par][0:ROT1, :], m=SHUF:
                                  e.stream_shuffle(o, i, m), sem=s_dve)
                        p.wait("vector", s_dve, vr)
                        vr = p.op("vector", lambda e, o=rtmp[:, :], i0=rtmp[:, :], i1=sin_sb[:, :]:
                                  e.tensor_tensor(o, i0, i1, ALU.mult), sem=s_dve)
                        p.wait("vector", s_dve, vr)
                        vr = p.op("vector", lambda e, o=stg[par][0:ROT1, :], i0=stg[par][0:ROT1, :],
                                  i1=cos_sb[:, :]: e.tensor_tensor(o, i0, i1, ALU.mult), sem=s_dve)
                        p.wait("vector", s_dve, vr)
                        vr = p.op("vector", lambda e, o=stg[par][0:ROT1, :], i0=stg[par][0:ROT1, :],
                                  i1=rtmp[:, :]: e.tensor_tensor(o, i0, i1, ALU.add), sem=s_dve)
                        p.wait("vector", s_dve, vr)
                    vcb = p.op("vector", lambda e, o=obf[par][:, :], i=stg[par][:, :]:
                               e.tensor_copy(o, i), sem=s_dve)
                    rope_v[kk] = vcb
                    p.wait("sync", s_dve, vcb)
                    st_of[kk] = p.dma("sync", outt[oc * 128:(oc + 1) * 128, :], obf[par][:, :],
                                      s_st[par])
                kk += 1
                nh = 1 if post_pack is None else len(post_pack[wi])
                if nh > 1:
                    per = 32 // nh
                    if oc % per == per - 1 and oc < 31:
                        snaps[oc] = (s_st[0].n, s_st[1].n)
                    if oc % per == 1 and oc > per:
                        j = oc // per - 1
                        sn = snaps[j * per + per - 1]
                        p.wait("gpsimd", s_st[0], sn[0])
                        p.wait("gpsimd", s_st[1], sn[1])
                        post_pack[wi][j]()
            if post_pack is not None:
                p.wait("gpsimd", s_st[0], s_st[0].n)
                p.wait("gpsimd", s_st[1], s_st[1].n)
                post_pack[wi][-1]()
        p.wait("sync", s_st[0], s_st[0].n)
        p.wait("sync", s_st[1], s_st[1].n)


def build_l1_qkv():
    nc = bass.Bass("TRN2", target_bir_lowering=False)
    io = dict(
        x2T=nc.dram_tensor("x2T", [D, T], F32, kind="ExternalInput").ap(),
        wq_pk=nc.dram_tensor("wq_pk", [32, 128, 32 * 128], F32, kind="ExternalInput").ap(),
        wk_pk=nc.dram_tensor("wk_pk", [32, 128, 32 * 128], F32, kind="ExternalInput").ap(),
        wv_pk=nc.dram_tensor("wv_pk", [32, 128, 32 * 128], F32, kind="ExternalInput").ap(),
        cosd=nc.dram_tensor("cosd", [ROT1, T], F32, kind="ExternalInput").ap(),
        sins=nc.dram_tensor("sins", [ROT1, T], F32, kind="ExternalInput").ap(),
        idf=nc.dram_tensor("idf", [128, 128], F32, kind="ExternalInput").ap(),
        onesf=nc.dram_tensor("onesf", [1, 128], F32, kind="ExternalInput").ap(),
        onescf=nc.dram_tensor("onescf", [128, 1], F32, kind="ExternalInput").ap(),
        qT=nc.dram_tensor("qT", [D, T], BF16, kind="ExternalOutput").ap(),
        kT=nc.dram_tensor("kT", [D, T], BF16, kind="ExternalOutput").ap(),
        vT=nc.dram_tensor("vT", [D, T], BF16, kind="ExternalOutput").ap(),
    )
    with ExitStack() as ctx:
        sm = mk_sems(nc, ctx)
        block = ctx.enter_context(nc.Block())
        p = Prog(nc)
        sh = mk_shared(p, nc, ctx, sm, io)
        emit_d(p, nc, ctx, sm, sh, io)
        p.replay(block)
    return nc


def emit_e(p, nc, ctx, sm, sh, io, SQ=S):
    NBq = T // 128
    qT_d, k_full, v_full, mask3, out = io["qT"], io["k_full"], io["v_full"], io["mask3"], io["oT"]
    E = ctx.enter_context
    idb_sb = sh["idb_sb"]
    q_sb = E(nc.sbuf_tensor("q_sbe", [128, 32 * T], BF16))
    msk_sb = [E(nc.sbuf_tensor(f"msk_sbe{i}", [128, 3 * 512], F32)) for i in range(2)]
    pbuf = [E(nc.sbuf_tensor(f"pbufe{i}", [128, SQ], BF16)) for i in range(3)]
    kstg = [E(nc.sbuf_tensor(f"kstg{i}", [128, 2 * 512], BF16)) for i in range(4)]
    vstg = [E(nc.sbuf_tensor(f"vstge{i}", [128, 1024], BF16)) for i in range(4)]
    pt_sb = [E(nc.sbuf_tensor(f"ptste{i}", [128, 1024], BF16)) for i in range(2)]
    otst = [E(nc.sbuf_tensor(f"otste{i}", [128, 256], BF16)) for i in range(2)]
    mcol2 = [E(nc.sbuf_tensor(f"mcole{i}", [128, 16], F32)) for i in range(2)]
    lcol2 = [E(nc.sbuf_tensor(f"lcole{i}", [128, 16], F32)) for i in range(2)]
    negM = E(nc.sbuf_tensor("negMe", [128, 1], F32))
    Ccol = E(nc.sbuf_tensor("Ccole", [128, 16], F32))
    CLc = E(nc.sbuf_tensor("CLce", [128, 16], F32))
    lsum = E(nc.sbuf_tensor("lsume", [128, 1], F32))
    linv = E(nc.sbuf_tensor("linve", [128, 1], F32))
    Fcol = E(nc.sbuf_tensor("Fcole", [128, 16], F32))
    ps_a = E(nc.psum_tensor("ps_ae", [128, 512], F32))
    ps_b = E(nc.psum_tensor("ps_be", [128, 512], F32))
    ps_c = E(nc.psum_tensor("ps_ce", [128, 512], F32))
    ps_d = E(nc.psum_tensor("ps_de", [128, 512], F32))
    ps_t = [E(nc.psum_tensor(f"ps_te{i}", [128, 1024], BF16)) for i in range(2)]
    ps_o = [E(nc.psum_tensor(f"ps_oe{i}", [128, 128], F32)) for i in range(2)]

    s_ld, s_ldq = sm["ld_g"], sm["ld_s"]
    s_m = [sm["m0"], sm["m1"]]
    s_k = [sm["w0"], sm["w1"], sm["sp0"], sm["sp1"]]
    s_v = [sm["v0"], sm["v1"], sm["v2"], sm["v3"]]
    s_pe, s_act, s_dve = sm["pe"], sm["act"], sm["dve"]
    s_st = [sm["st0"], sm["st1"]]

    for g in range(8):
        p.dma("sync", q_sb[:, :].rearrange("p (c t) -> p c t", t=T)[:, g * 4:(g + 1) * 4, :],
              qT_d.rearrange("(c p) t -> p c t", p=128)[:, g * 4:(g + 1) * 4, :], s_ldq)
    v_ldq = s_ldq.n
    for eng in ("tensor", "vector", "scalar"):
        p.wait(eng, s_ldq, v_ldq)

    SBANKS = [ps_a, ps_b, ps_c, ps_d]
    last_exp = [0, 0, 0, 0]
    trans_done = {}
    corr_done = {}
    h_exp_done = {}
    osum_written = {}
    oev_prev = ("", 0)
    kld = {}
    kcons = {}
    vld = {}
    vcons = {}
    kq = [0]
    vq = [0]
    stv = {0: 0, 1: 0}
    mcons = {0: 0, 1: 0}

    def kpos(b):
        return (b % W) * T + (b // W) * 128

    def load_kgroup(j, h, gr):
        ring = kq[0] % 4
        kq[0] += 1
        par = ring
        b0 = gr * 4
        c0, j2 = b0 % W, b0 // W
        if kq[0] > 4:
            p.wait("gpsimd", s_pe, kcons.get(kq[0] - 4, 0))
        for ch in range(2):
            p.dma("gpsimd", kstg[ring][:, ch * 512:(ch + 1) * 512].rearrange("p (b t) -> p b t", t=128),
                  k_full[c0:c0 + 4, (2 * h + ch) * 128:(2 * h + ch + 1) * 128,
                         j2 * 128:(j2 + 1) * 128].rearrange("c p t -> p c t"), s_k[par])
        kld[kq[0]] = (ring, par, s_k[par].n)
        return kq[0]

    def load_vgroup(j, h, g4):
        ring = vq[0] % 4
        vq[0] += 1
        par = ring
        b0 = g4 * 4
        c0, j2 = b0 % W, b0 // W
        if vq[0] > 4:
            p.wait("sync", s_pe, vcons.get(vq[0] - 4, 0))
        p.dma("sync", vstg[ring][:, :].rearrange("p (b t) -> p b t", t=256),
              v_full[c0:c0 + 4, j2 * 128:(j2 + 1) * 128, h * 256:(h + 1) * 256].rearrange("c p t -> p c t"),
              s_v[par])
        vld[vq[0]] = (ring, par, s_v[par].n)
        return vq[0]

    for j in range(NBq):
        nkt = W * j + W
        ngr = nkt // 4
        NK8 = nkt // 8
        mpar = j % 2
        p.wait("gpsimd", s_dve, mcons[mpar])
        for s3 in range(3):
            vmsk = p.dma("gpsimd", msk_sb[mpar][:, s3 * 512:(s3 + 1) * 512], mask3[j, s3],
                         s_m[mpar])

        st_av = {"ptev": {}, "kid": {}, "gk": 0}
        av_q = []

        def score_group(h, gr):
            mcol = mcol2[h % 2]
            lcol = lcol2[h % 2]
            pss = SBANKS[gr % 4]
            kid = st_av["kid"].pop((h, gr))
            ring, par, vkl = kld[kid]
            if gr % 4 == 0:
                p.wait("tensor", s_act, max(last_exp))
            p.wait("tensor", s_k[par], vkl)
            p.op("tensor", lambda e, o=pss[:, :], w=q_sb[:, 2 * h * T + j * 128: 2 * h * T + j * 128 + 128],
                 x=kstg[ring][:, 0:512]: e.matmul(o, w, x, start=True, stop=False))
            vs = p.op("tensor", lambda e, o=pss[:, :],
                      w=q_sb[:, (2 * h + 1) * T + j * 128:(2 * h + 1) * T + j * 128 + 128],
                      x=kstg[ring][:, 512:1024]:
                      e.matmul(o, w, x, start=False, stop=True), sem=s_pe)
            kcons[kid] = vs
            p.wait("vector", s_pe, vs)
            s3 = gr - (ngr - 3)
            if s3 >= 0:
                if h == 0 and gr == max(0, ngr - 3):
                    p.wait("vector", s_m[mpar], vmsk)
                vs2 = p.op("vector", lambda e, o=pss[:, :], i0=pss[:, :],
                           i1=msk_sb[mpar][:, s3 * 512:(s3 + 1) * 512]:
                           e.tensor_tensor(o, i0, i1, ALU.add), sem=s_dve)
                p.wait("vector", s_dve, vs2)
                if h == NH1 - 1 and gr == ngr - 1:
                    mcons[mpar] = vs2
            vm = p.op("vector", lambda e, o=mcol[:, gr:gr + 1], i=pss[:, :]:
                      e.reduce_max(o, i, axis=AX.X, negate=True), sem=s_dve)
            p.wait("scalar", s_dve, vm)
            if gr == 0:
                p.wait("scalar", s_pe, trans_done.get(h % 3, 0))
            last_exp[gr % 4] = p.op("scalar", lambda e, o=pbuf[h % 3][:, gr * 512:(gr + 1) * 512],
                                    i=pss[:, :], b=mcol[:, gr:gr + 1], a=lcol[:, gr:gr + 1]:
                                    e.activation(o, i, AF.Exp, bias=b, scale=1.0, accum_out=a),
                                    sem=s_act)
            if gr == ngr - 1:
                h_exp_done[h] = last_exp[gr % 4]

        def emit_stats(h):
            mcol = mcol2[h % 2]
            lcol = lcol2[h % 2]
            p.wait("vector", s_act, h_exp_done[h])
            vC = p.op("vector", lambda e, o=negM[:, :], i=mcol[:, 0:ngr]:
                      e.tensor_reduce(o, i, axis=AX.X, op=ALU.min), sem=s_dve)
            p.wait("scalar", s_dve, vC)
            vCe = p.op("scalar", lambda e, o=Ccol[:, 0:ngr], i=mcol[:, 0:ngr], b=negM[:, :]:
                       e.activation(o, i, AF.Exp, bias=b, scale=-1.0), sem=s_act)
            p.wait("vector", s_act, vCe)
            v1 = p.op("vector", lambda e, o=CLc[:, 0:ngr], i0=Ccol[:, 0:ngr], i1=lcol[:, 0:ngr]:
                      e.tensor_tensor(o, i0, i1, ALU.mult), sem=s_dve)
            p.wait("vector", s_dve, v1)
            v2 = p.op("vector", lambda e, o=lsum[:, :], i=CLc[:, 0:ngr]:
                      e.reduce_sum(o, i, axis=AX.X), sem=s_dve)
            p.wait("vector", s_dve, v2)
            v3 = p.op("vector", lambda e, o=linv[:, :], i=lsum[:, :]: e.reciprocal(o, i), sem=s_dve)
            p.wait("vector", s_dve, v3)
            v4 = p.op("vector", lambda e, o=Fcol[:, 0:ngr], i=Ccol[:, 0:ngr], sc=linv[:, :]:
                      e.tensor_scalar(o, i, sc, None, ALU.mult), sem=s_dve)
            p.wait("vector", s_dve, v4)
            p.wait("vector", s_pe, trans_done.get(h % 3, 0))
            cv = 0
            for gr in range(ngr):
                cv = p.op("vector", lambda e, o=pbuf[h % 3][:, gr * 512:(gr + 1) * 512],
                          sc=Fcol[:, gr:gr + 1]: e.tensor_scalar(o, o, sc, None, ALU.mult),
                          sem=s_dve if gr == ngr - 1 else None)
            corr_done[h] = cv

        def av_step(h, k8):
            if k8 == 0:
                p.wait("tensor", s_dve, corr_done[h])
                eng0, val0 = oev_prev2[0]
                if eng0 == "scalar":
                    p.wait("tensor", s_act, val0)
                elif eng0 == "vector":
                    p.wait("tensor", s_dve, val0)
            if k8 > 0:
                kprev = k8 - 1
                eng0, val0, tbp = st_av[("pt", h, kprev)]
                p.wait("tensor", s_act if eng0 == "scalar" else s_dve, val0)
                for g4 in (2 * kprev, 2 * kprev + 1):
                    vid = st_av[("v", h, g4)]
                    ring, par, vvl = vld[vid]
                    p.wait("tensor", s_v[par], vvl)
                    avv = 0
                    for sub in range(4):
                        kt = 4 * g4 + sub
                        for ch in range(2):
                            avv = p.op("tensor", lambda e, o=ps_o[ch][:, :],
                                       w=vstg[ring][:, sub * 256 + ch * 128: sub * 256 + ch * 128 + 128],
                                       x=pt_sb[tbp][:, (kt - 8 * kprev) * 128:(kt - 8 * kprev + 1) * 128],
                                       st=(kt == 0), sp=(kt == nkt - 1):
                                       e.matmul(o, w, x, start=st, stop=sp),
                                       sem=s_pe if (sub == 3 and ch == 1) else None)
                    vcons[vid] = avv
            if k8 < NK8:
                gk = st_av["gk"]
                st_av["gk"] = gk + 1
                tb2 = gk % 2
                if gk >= 2:
                    eng0, val0, _ = st_av[("ptg", gk - 2)]
                    p.wait("tensor", s_act if eng0 == "scalar" else s_dve, val0)
                tv = 0
                for sub in range(8):
                    kt = 8 * k8 + sub
                    tv = p.op("tensor", lambda e, o=ps_t[tb2][:, sub * 128:(sub + 1) * 128],
                              i=pbuf[h % 3][:, kt * 128:(kt + 1) * 128], d=idb_sb[:, :]:
                              e.transpose(o, i, d), sem=s_pe if sub == 7 else None)
                eng = "vector" if gk % 2 == 0 else "scalar"
                p.wait(eng, s_pe, tv)
                if eng == "vector":
                    pv = p.op("vector", lambda e, o=pt_sb[tb2][:, :], i=ps_t[tb2][:, :]:
                              e.tensor_copy(o, i), sem=s_dve)
                else:
                    pv = p.op("scalar", lambda e, o=pt_sb[tb2][:, :], i=ps_t[tb2][:, :]:
                              e.copy(o, i), sem=s_act)
                st_av[("pt", h, k8)] = (eng, pv, tb2)
                st_av[("ptg", gk)] = (eng, pv, tb2)
            else:
                avf = s_pe.n
                trans_done[h % 3] = avf
                spar = h % 2
                p.wait("scalar", s_pe, avf)
                if stv[spar]:
                    p.wait("scalar", s_st[spar], stv[spar])
                p.op("scalar", lambda e, o=otst[spar][:, 0:128], i=ps_o[0][:, :]: e.copy(o, i))
                ov = p.op("scalar", lambda e, o=otst[spar][:, 128:256], i=ps_o[1][:, :]:
                          e.copy(o, i), sem=s_act)
                oev_prev2[0] = ("scalar", ov)
                p.wait("sync", s_act, ov)
                for ch in range(2):
                    stv[spar] = p.dma("sync", out[(2 * h + ch) * 128:(2 * h + ch + 1) * 128,
                                                  j * 128:(j + 1) * 128],
                                      otst[spar][:, ch * 128:(ch + 1) * 128], s_st[spar])

        oev_prev2 = [oev_prev]

        def pump():
            if av_q:
                h0, k8 = av_q[0]
                if k8 < NK8:
                    for g4 in (2 * k8, 2 * k8 + 1):
                        if ("v", h0, g4) not in st_av:
                            st_av[("v", h0, g4)] = load_vgroup(j, h0, g4)
                av_step(h0, k8)
                if k8 + 1 > NK8:
                    av_q.pop(0)
                else:
                    av_q[0][1] = k8 + 1

        slots = [(h, gr) for h in range(NH1) for gr in range(ngr)]
        for ci in range(min(2, len(slots))):
            hh, gg = slots[ci]
            st_av["kid"][(hh, gg)] = load_kgroup(j, hh, gg)
        for si, (h, gr) in enumerate(slots):
            if si + 2 < len(slots):
                hh, gg = slots[si + 2]
                st_av["kid"][(hh, gg)] = load_kgroup(j, hh, gg)
            score_group(h, gr)
            pump()
            if gr == 0:
                pump()
            if gr == ngr - 1:
                emit_stats(h)
                av_q.append([h, 0])
        while av_q:
            pump()
    p.wait("sync", s_st[0], s_st[0].n)
    p.wait("sync", s_st[1], s_st[1].n)


def build_l1_attn(SQ=S):
    nc = bass.Bass("TRN2", target_bir_lowering=False)
    io = dict(
        qT=nc.dram_tensor("qT", [D, T], BF16, kind="ExternalInput").ap(),
        k_full=nc.dram_tensor("k_full", [W, D, T], BF16, kind="ExternalInput").ap(),
        v_full=nc.dram_tensor("v_full", [W, T, D], BF16, kind="ExternalInput").ap(),
        mask3=nc.dram_tensor("mask3", [NB, 3, 128, 512], F32, kind="ExternalInput").ap(),
        idf=nc.dram_tensor("idf", [128, 128], F32, kind="ExternalInput").ap(),
        onesf=nc.dram_tensor("onesf", [1, 128], F32, kind="ExternalInput").ap(),
        onescf=nc.dram_tensor("onescf", [128, 1], F32, kind="ExternalInput").ap(),
        oT=nc.dram_tensor("oT", [D, T], BF16, kind="ExternalOutput").ap(),
    )
    with ExitStack() as ctx:
        sm = mk_sems(nc, ctx)
        block = ctx.enter_context(nc.Block())
        p = Prog(nc)
        sh = mk_shared(p, nc, ctx, sm, io)
        emit_e(p, nc, ctx, sm, sh, io, SQ=SQ)
        p.replay(block)
    return nc


def emit_e2(p, nc, ctx, sm, sh, io):
    NQB2 = S // 128
    q_full, k_full, vT_full = io["q_full"], io["k_full"], io["vT_full"]
    trimask, hix, out = io["trimask"], io["hix"], io["o_own"]
    E = ctx.enter_context
    idb_sb = sh["idb_sb"]
    qh = [E(nc.sbuf_tensor(f"qh{i}", [128, S], BF16)) for i in range(2)]
    kh = [E(nc.sbuf_tensor(f"kh{i}", [128, S], BF16)) for i in range(2)]
    vhf = [E(nc.sbuf_tensor(f"vhf{i}", [128, S], BF16)) for i in range(2)]
    vhT = E(nc.sbuf_tensor("vhT", [128, S], BF16))
    kstg2 = [E(nc.sbuf_tensor(f"kstg2{i}", [128, T], BF16)) for i in range(2)]
    pbuf = [E(nc.sbuf_tensor(f"pbufe{i}", [128, S], BF16)) for i in range(3)]
    pt_sb = [E(nc.sbuf_tensor(f"ptste{i}", [128, 1024], BF16)) for i in range(2)]
    msk_sb = E(nc.sbuf_tensor("msk_sb2", [128, 4 * 512], F32))
    hix_sb = E(nc.sbuf_tensor("hix_sb", [128, 32], U32))
    vix_sb = E(nc.sbuf_tensor("vix_sb", [128, 32], U32))
    otst = [E(nc.sbuf_tensor(f"otste{i}", [128, 256], BF16)) for i in range(2)]
    mcol2 = [E(nc.sbuf_tensor(f"mcole{i}", [128, 16], F32)) for i in range(2)]
    lcol2 = [E(nc.sbuf_tensor(f"lcole{i}", [128, 16], F32)) for i in range(2)]
    negM = E(nc.sbuf_tensor("negMe", [128, 1], F32))
    Ccol = E(nc.sbuf_tensor("Ccole", [128, 16], F32))
    CLc = E(nc.sbuf_tensor("CLce", [128, 16], F32))
    lsum = E(nc.sbuf_tensor("lsume", [128, 1], F32))
    linv = E(nc.sbuf_tensor("linve", [128, 1], F32))
    Fcol = E(nc.sbuf_tensor("Fcole", [128, 16], F32))
    ps_a = E(nc.psum_tensor("ps_ae", [128, 512], F32))
    ps_b = E(nc.psum_tensor("ps_be", [128, 512], F32))
    ps_c = E(nc.psum_tensor("ps_ce", [128, 512], F32))
    ps_d = E(nc.psum_tensor("ps_de", [128, 512], F32))
    ps_t = [E(nc.psum_tensor(f"ps_te{i}", [128, 1024], BF16)) for i in range(2)]
    ps_o2 = [E(nc.psum_tensor(f"ps_o2e{i}", [128, 128], F32)) for i in range(2)]

    s_ld, s_ldq = sm["ld_g"], sm["ld_s"]
    s_kg = [sm["m0"], sm["m1"]]
    s_rd = [sm["k0"], sm["k1"]]
    s_pe, s_act, s_dve = sm["pe"], sm["act"], sm["dve"]
    s_st = [sm["st0"], sm["st1"]]

    def kpos(b):
        return (b % W) * T + (b // W) * 128

    p.dma("gpsimd", msk_sb[:, :].rearrange("p (v t) -> p v t", t=512),
          trimask.rearrange("v p t -> p v t"), s_ld)
    v_hx = p.dma("gpsimd", hix_sb[:, :], hix[:, :], s_ld)
    vix_in = io.get("vix")
    if vix_in is not None:
        v_hx = p.dma("gpsimd", vix_sb[:, :], vix_in[:, :], s_ld)
    p.wait("gpsimd", s_ld, v_hx)

    qk_flat = [q_full.rearrange("c d t -> (c d) t"),
               k_full.rearrange("c d t -> (c d) t"),
               vT_full.rearrange("c d t -> (c d) t")]

    SBANKS = [ps_a, ps_b, ps_c, ps_d]
    last_exp = [0, 0, 0, 0]
    trans_done = {}
    stv = {0: 0, 1: 0}
    oev = [0]
    vCe_prev = [0]
    scnt = [0]
    ptfree = {0: ("scalar", 0), 1: ("scalar", 0)}

    cc_qkv = io.get("cc_qkv")
    post_head = io.get("post_head")
    for hh in range(2):
        if hh == 1:
            p.wait("gpsimd", s_pe, s_pe.n)
        if hh == 0 and cc_qkv is not None:
            p.wait("gpsimd", cc_qkv[0], cc_qkv[0].n)
        qv = 0
        for c in range(W):
            qv = p.dma_ind(qh[0][:, c * T:(c + 1) * T], qk_flat[0],
                           hix_sb[:, (hh * 16 + c):(hh * 16 + c) + 1], s_ld)
            qv = p.dma_ind(qh[1][:, c * T:(c + 1) * T], qk_flat[0],
                           hix_sb[:, (hh * 16 + 8 + c):(hh * 16 + 8 + c) + 1], s_ld)
        rd = {0: 0, 1: 0}
        kv = 0
        for ti in (1, 2):
            if hh == 0 and cc_qkv is not None:
                p.wait("gpsimd", cc_qkv[ti], cc_qkv[ti].n)
            for ch in range(2):
                if ti == 2 and ch == 1:
                    p.wait("sync", s_pe, s_pe.n)
                for c in range(W):
                    sl = c % 2
                    p.wait("gpsimd", s_rd[sl], rd[sl])
                    ixs = vix_sb if (ti == 2 and vix_in is not None) else hix_sb
                    gv = p.dma_ind(kstg2[sl][:, :], qk_flat[ti],
                                   ixs[:, (hh * 16 + ch * 8 + c):(hh * 16 + ch * 8 + c) + 1],
                                   s_kg[sl])
                    p.wait("sync", s_kg[sl], gv)
                    dst = kh[ch] if ti == 1 else vhT
                    kv = p.dma("sync",
                               dst[:, :].rearrange("p (m cu) -> p m cu", cu=1024)
                               [:, :, c * 128:(c + 1) * 128],
                               kstg2[sl][:, :].rearrange("p (m u) -> p m u", u=128),
                               s_rd[sl])
                    rd[sl] = kv
                if ti == 2:
                    p.wait("tensor", s_rd[0], rd[0])
                    p.wait("tensor", s_rd[1], rd[1])
                    for g8 in range(8):
                        tb2 = g8 % 2
                        e0, v0 = ptfree[tb2]
                        p.wait("tensor", s_act if e0 == "scalar" else s_dve, v0)
                        tv = 0
                        for sub in range(8):
                            bk = g8 * 8 + sub
                            tv = p.op("tensor", lambda e, o=ps_t[tb2][:, sub * 128:(sub + 1) * 128],
                                      i=vhT[:, bk * 128:(bk + 1) * 128], d=idb_sb[:, :]:
                                      e.transpose(o, i, d), sem=s_pe if sub == 7 else None)
                        eng = "vector" if g8 % 2 == 0 else "scalar"
                        p.wait(eng, s_pe, tv)
                        if eng == "vector":
                            pv = p.op("vector", lambda e, o=vhf[ch][:, g8 * 1024:(g8 + 1) * 1024],
                                      i=ps_t[tb2][:, :]: e.tensor_copy(o, i), sem=s_dve)
                        else:
                            pv = p.op("scalar", lambda e, o=vhf[ch][:, g8 * 1024:(g8 + 1) * 1024],
                                      i=ps_t[tb2][:, :]: e.copy(o, i), sem=s_act)
                        ptfree[tb2] = (eng, pv)
        v_gth = s_ld.n
        for eng in ("tensor", "vector", "scalar"):
            p.wait(eng, s_ld, v_gth)
            p.wait(eng, s_rd[0], rd[0])
            p.wait(eng, s_rd[1], rd[1])

        st_av = {"ptev": {}}
        av_q = []
        corr_done = {}
        qb_exp_done = {}

        def score_chunk(b, cc):
            mcol = mcol2[b % 2]
            lcol = lcol2[b % 2]
            pss = SBANKS[scnt[0] % 4]
            bank = scnt[0] % 4
            if scnt[0] % 4 == 0 and scnt[0] > 0:
                p.wait("tensor", s_act, last_exp[3])
            scnt[0] += 1
            p.op("tensor", lambda e, o=pss[:, :], w=qh[0][:, kpos(b):kpos(b) + 128],
                 x=kh[0][:, cc * 512:(cc + 1) * 512]: e.matmul(o, w, x, start=True, stop=False))
            vs = p.op("tensor", lambda e, o=pss[:, :], w=qh[1][:, kpos(b):kpos(b) + 128],
                      x=kh[1][:, cc * 512:(cc + 1) * 512]:
                      e.matmul(o, w, x, start=False, stop=True), sem=s_pe)
            if cc == b // 4:
                p.wait("vector", s_pe, vs)
                va = p.op("vector", lambda e, o=pss[:, :], i0=pss[:, :],
                          i1=msk_sb[:, (b % 4) * 512:(b % 4 + 1) * 512]:
                          e.tensor_tensor(o, i0, i1, ALU.add), sem=s_dve)
                p.wait("vector", s_dve, va)
                if cc > 0:
                    p.wait("scalar", s_dve, va)
            if cc == 0:
                p.wait("vector", s_pe, vs)
                vm = p.op("vector", lambda e, o=mcol[:, 0:1], i=pss[:, :]:
                          e.reduce_max(o, i, axis=AX.X, negate=True), sem=s_dve)
                p.wait("scalar", s_dve, vm)
            elif cc != b // 4:
                p.wait("scalar", s_pe, vs)
            last_exp[bank] = p.op("scalar", lambda e, o=pbuf[b % 3][:, cc * 512:(cc + 1) * 512],
                                  i=pss[:, :], b2=mcol[:, 0:1], a=lcol[:, cc:cc + 1]:
                                  e.activation(o, i, AF.Exp, bias=b2, scale=1.0, accum_out=a),
                                  sem=s_act)

        def emit_stats(b):
            ncc = b // 4 + 1
            lcol = lcol2[b % 2]
            p.wait("vector", s_act, qb_exp_done[b])
            v2 = p.op("vector", lambda e, o=lsum[:, :], i=lcol[:, 0:ncc]:
                      e.reduce_sum(o, i, axis=AX.X), sem=s_dve)
            p.wait("vector", s_dve, v2)
            v3 = p.op("vector", lambda e, o=linv[:, :], i=lsum[:, :]: e.reciprocal(o, i),
                      sem=s_dve)
            p.wait("vector", s_dve, v3)
            p.wait("vector", s_pe, trans_done.get(b % 3, 0))

        def emit_corr(b, cc):
            return p.op("vector", lambda e, o=pbuf[b % 3][:, cc * 512:(cc + 1) * 512],
                        sc=linv[:, :]: e.tensor_scalar(o, o, sc, None, ALU.mult),
                        sem=s_dve)

        def av_step(gq, k8):
            NKT_g = gq + 1
            NK8_g = (gq // 8) + 1
            if k8 == 0:
                p.wait("tensor", s_dve, corr_done[gq])
                p.wait("tensor", s_act, oev[0])
            if k8 > 0:
                kprev = k8 - 1
                e0, v0 = st_av["ptev"][kprev]
                p.wait("tensor", s_act if e0 == "scalar" else s_dve, v0)
                n_in = min(8, NKT_g - 8 * kprev)
                for sub in range(n_in):
                    kt = 8 * kprev + sub
                    for ch in range(2):
                        p.op("tensor", lambda e, o=ps_o2[ch][:, :],
                             w=vhf[ch][:, kt * 128:(kt + 1) * 128],
                             x=pt_sb[kprev % 2][:, sub * 128:(sub + 1) * 128],
                             st=(kt == 0), sp=(kt == NKT_g - 1):
                             e.matmul(o, w, x, start=st, stop=sp),
                             sem=s_pe if (kt == NKT_g - 1 and ch == 1) else None)
            if k8 < NK8_g:
                tb2 = k8 % 2
                e0, v0 = ptfree[tb2]
                p.wait("tensor", s_act if e0 == "scalar" else s_dve, v0)
                n_t = min(8, NKT_g - 8 * k8)
                tv = 0
                for sub in range(n_t):
                    kt = 8 * k8 + sub
                    tv = p.op("tensor", lambda e, o=ps_t[tb2][:, sub * 128:(sub + 1) * 128],
                              i=pbuf[gq % 3][:, kt * 128:(kt + 1) * 128], d=idb_sb[:, :]:
                              e.transpose(o, i, d), sem=s_pe if sub == n_t - 1 else None)
                eng = "vector" if k8 % 2 == 0 else "scalar"
                p.wait(eng, s_pe, tv)
                if eng == "vector":
                    pv = p.op("vector", lambda e, o=pt_sb[tb2][:, 0:n_t * 128],
                              i=ps_t[tb2][:, 0:n_t * 128]: e.tensor_copy(o, i), sem=s_dve)
                else:
                    pv = p.op("scalar", lambda e, o=pt_sb[tb2][:, 0:n_t * 128],
                              i=ps_t[tb2][:, 0:n_t * 128]: e.copy(o, i), sem=s_act)
                st_av["ptev"][k8] = (eng, pv)
                ptfree[tb2] = (eng, pv)
            else:
                avv_final = s_pe.n
                trans_done[gq % 3] = avv_final
                par = gq % 2
                p.wait("scalar", s_pe, avv_final)
                p.wait("scalar", s_st[par], stv[par])
                p.op("scalar", lambda e, o=otst[par][:, 0:128], i=ps_o2[0][:, :]: e.copy(o, i))
                ov = p.op("scalar", lambda e, o=otst[par][:, 128:256], i=ps_o2[1][:, :]:
                          e.copy(o, i), sem=s_act)
                oev[0] = ov
                p.wait("sync", s_act, ov)
                kp = kpos(gq)
                p.dma("sync", out[(hh * 2) * 128:(hh * 2 + 1) * 128, kp:kp + 128],
                      otst[par][:, 0:128], s_st[par])
                stv[par] = p.dma("sync", out[(hh * 2 + 1) * 128:(hh * 2 + 2) * 128, kp:kp + 128],
                                 otst[par][:, 128:256], s_st[par])

        slot = [0]

        def pump_av(force=False):
            if av_q and (force or av_q[0][2] <= slot[0]):
                g0, k8, _ = av_q[0]
                if k8 == 0:
                    st_av["ptev"] = {}
                av_step(g0, k8)
                if k8 + 1 > (g0 // 8) + 1:
                    av_q.pop(0)
                else:
                    av_q[0][1] = k8 + 1

        for b in range(NQB2):
            ncc = b // 4 + 1
            while av_q and av_q[0][0] <= b - 3:
                pump_av(force=True)
            for cc in range(ncc):
                if cc == 0:
                    p.wait("scalar", s_pe, trans_done.get(b % 3, 0))
                score_chunk(b, cc)
                if cc == ncc - 1:
                    qb_exp_done[b] = last_exp[(scnt[0] - 1) % 4]
                if b >= 1:
                    ncp = (b - 1) // 4 + 1
                    if ncc == 1:
                        if cc == 0:
                            emit_stats(b - 1)
                            cv = emit_corr(b - 1, 0)
                            corr_done[b - 1] = cv
                            av_q.append([b - 1, 0, slot[0] + 1])
                    else:
                        NCS = ceil_div(ncp, 2)
                        if cc == 0:
                            emit_stats(b - 1)
                        elif cc <= NCS:
                            for c2 in (2 * (cc - 1), 2 * (cc - 1) + 1):
                                if c2 < ncp:
                                    cv = emit_corr(b - 1, c2)
                            if cc == NCS:
                                corr_done[b - 1] = cv
                                av_q.append([b - 1, 0, slot[0] + 1])
                pump_av()
                slot[0] += 1
        emit_stats(NQB2 - 1)
        for cc in range(NQB2 // 4 + 0):
            cv = emit_corr(NQB2 - 1, cc)
        corr_done[NQB2 - 1] = cv
        av_q.append([NQB2 - 1, 0, 0])
        while av_q:
            pump_av(force=True)
        if post_head is not None:
            p.wait("gpsimd", s_st[0], s_st[0].n)
            p.wait("gpsimd", s_st[1], s_st[1].n)
            post_head[hh]()
    p.wait("sync", s_st[0], s_st[0].n)
    p.wait("sync", s_st[1], s_st[1].n)


def build_l1_attn2():
    nc = bass.Bass("TRN2", target_bir_lowering=False)
    io = dict(
        q_full=nc.dram_tensor("q_full", [W, D, T], BF16, kind="ExternalInput").ap(),
        k_full=nc.dram_tensor("k_full", [W, D, T], BF16, kind="ExternalInput").ap(),
        vT_full=nc.dram_tensor("vT_full", [W, D, T], BF16, kind="ExternalInput").ap(),
        trimask=nc.dram_tensor("trimask", [4, 128, 512], F32, kind="ExternalInput").ap(),
        hix=nc.dram_tensor("hix", [128, 32], U32, kind="ExternalInput").ap(),
        idf=nc.dram_tensor("idf", [128, 128], F32, kind="ExternalInput").ap(),
        onesf=nc.dram_tensor("onesf", [1, 128], F32, kind="ExternalInput").ap(),
        onescf=nc.dram_tensor("onescf", [128, 1], F32, kind="ExternalInput").ap(),
        o_own=nc.dram_tensor("o_own", [512, S], BF16, kind="ExternalOutput").ap(),
    )
    with ExitStack() as ctx:
        sm = mk_sems(nc, ctx)
        block = ctx.enter_context(nc.Block())
        p = Prog(nc)
        sh = mk_shared(p, nc, ctx, sm, io)
        emit_e2(p, nc, ctx, sm, sh, io)
        p.replay(block)
    return nc


def emit_f(p, nc, ctx, sm, sh, io):
    NL, RK1 = 16, 32
    ZC1 = NL * RK1 + RK1
    oT_d, x2T, out = io.get("oT"), io["x2T"], io["outT"]
    wo_pk, kn_pk, fb_d, az1, gate_exp = io["wo_pk"], io["kn_pk"], io["fbias"], io["az1"], io["gate_exp"]
    x1psp = nc.dram_tensor("x1psp_f", [32, 128, T], F32)
    x1nsp = nc.dram_tensor("x1nsp_f", [32, 128, T], F32)
    x2psp = nc.dram_tensor("x2psp_f", [32, 128, T], F32)
    if True:
        E = ctx.enter_context
        idf_sb = sh["idf_sb"]
        idb_sb = sh["idb_sb"]
        ones_r = sh["ones_r"]
        ones_cb = sh["ones_cb"]
        eps5 = sh["eps5"]
        big = E(nc.sbuf_tensor("bigf", [128, 32 * T], BF16))
        wbuf = [E(nc.sbuf_tensor(f"wbuff{i}", [128, 37 * 128], BF16)) for i in range(2)]
        xc = [E(nc.sbuf_tensor(f"xcf{i}", [128, T], F32)) for i in range(2)]
        t1 = [E(nc.sbuf_tensor(f"t1f{i}", [128, T], F32)) for i in range(2)]
        tbf = [E(nc.sbuf_tensor(f"tbff{i}", [128, T], BF16)) for i in range(2)]
        sqb = [E(nc.sbuf_tensor(f"sqbf{i}", [128, T], BF16)) for i in range(2)]
        mrow = E(nc.sbuf_tensor("mrowf", [1, T], F32))
        qrow = E(nc.sbuf_tensor("qrowf", [1, T], F32))
        vrow = E(nc.sbuf_tensor("vrowf", [1, T], F32))
        rrow = E(nc.sbuf_tensor("rrowf", [1, T], F32))
        mb = E(nc.sbuf_tensor("mbf", [128, T], F32))
        rb = E(nc.sbuf_tensor("rbf", [128, T], F32))
        zrhs = E(nc.sbuf_tensor("zrhsf", [128, 32 * ZC1], BF16))
        zwb = E(nc.sbuf_tensor("zwbf", [128, ZC1], BF16))
        zt128 = E(nc.sbuf_tensor("zt128", [128, 4 * T], BF16))
        zt32 = E(nc.sbuf_tensor("zt32", [RK1, T], BF16))
        gbc = E(nc.sbuf_tensor("gbcf", [128, ZC1], F32))
        grow = [E(nc.sbuf_tensor(f"growf{i}", [1, ZC1], F32)) for i in range(NB)]
        fb_c = None
        ostg = [E(nc.sbuf_tensor(f"ostgf{i}", [128, T], F32)) for i in range(2)]

        ps_1 = E(nc.psum_tensor("ps_1f", [128, 512], F32))
        ps_2 = E(nc.psum_tensor("ps_2f", [128, 512], F32))
        ps_bt = [E(nc.psum_tensor(f"ps_btf{i}", [128, 128], BF16)) for i in range(2)]
        ps_s = [E(nc.psum_tensor(f"ps_sf{i}", [1, 512], F32)) for i in range(2)]
        ps_q = [E(nc.psum_tensor(f"ps_qf{i}", [1, 512], F32)) for i in range(2)]

        s_ld, s_ld2 = sm["ld_g"], sm["ld_s"]
        s_w = [sm["w0"], sm["w1"]]
        s_x = [sm["x0"], sm["x1"]]
        s_pe, s_act, s_dve = sm["pe"], sm["act"], sm["dve"]
        s_sp = [sm["sp0"], sm["sp1"]]
        s_spl = [sm["k0"], sm["k1"]]
        s_st = [sm["st0"], sm["st1"]]

        p.dma("gpsimd", zrhs[:, :].rearrange("p (i z) -> p i z", z=ZC1),
              az1.rearrange("(i p) z -> p i z", p=128), s_ld)
        for jj in range(NB):
            p.dma("gpsimd", grow[jj][:, :], gate_exp[jj:jj + 1, :], s_ld)
        if "oix" in io:
            oix_sb = ctx.enter_context(nc.sbuf_tensor("oix_sb", [128, 32], U32))
            v_ox = p.dma("gpsimd", oix_sb[:, :], io["oix"][:, :], s_ld)
            p.wait("gpsimd", s_ld, v_ox)
            o_flats = [io["o_full_a"].rearrange("c hs (tb t) -> (c hs tb) t", t=T),
                       io["o_full_b"].rearrange("c hs (tb t) -> (c hs tb) t", t=T)]
            cc_o = io.get("cc_o")
            ics = [ic for ic in range(32) if ic % 4 < 2] + \
                  [ic for ic in range(32) if ic % 4 >= 2]
            for n_, ic in enumerate(ics):
                half = 0 if ic % 4 < 2 else 1
                if cc_o is not None and n_ in (0, 16):
                    p.wait("gpsimd", cc_o[half], cc_o[half].n)
                p.dma_ind(big[:, ic * T:(ic + 1) * T], o_flats[half],
                          oix_sb[:, ic:ic + 1], s_ld)
        else:
            for g in range(8):
                p.dma("sync", big[:, :].rearrange("p (c t) -> p c t", t=T)[:, g * 4:(g + 1) * 4, :],
                      oT_d.rearrange("(c p) t -> p c t", p=128)[:, g * 4:(g + 1) * 4, :], s_ld2)
        cv_fb = colvec_dma(p, nc, ctx, "fb_colf", fb_d, D, s_ld)
        v_ld, v_ld2 = s_ld.n, s_ld2.n
        for eng in ("tensor", "vector", "scalar"):
            p.wait(eng, s_ld, v_ld)
            p.wait(eng, s_ld2, v_ld2)
        fb_c = colvec_compute(p, cv_fb, idf_sb, ps_1, 0, s_pe, s_act)
        v_cv = s_act.n
        p.wait("scalar", s_act, v_cv)
        p.wait("tensor", s_act, v_cv)

        def pass1(emit_mms, bias_col, res_src, spill, wload=None, pre_wait=None):
            ev = {}
            addv = {}
            spv = {0: 0, 1: 0}
            stat_of = {}
            vsq_of = {}
            vtb_of = {}

            def emit_statmms(oc2):
                par2 = oc2 % 2
                p.wait("tensor", s_dve, vtb_of[oc2])
                p.wait("tensor", s_act, vsq_of[oc2])
                sm = 0
                for th2 in range(2):
                    p.op("tensor", lambda e, o=ps_s[th2][:, :], w=ones_cb[:, :],
                         x=tbf[par2][:, th2 * 512:(th2 + 1) * 512], st=(oc2 == 0), sp=(oc2 == 31):
                         e.matmul(o, w, x, start=st, stop=sp))
                    sm = p.op("tensor", lambda e, o=ps_q[th2][:, :], w=ones_cb[:, :],
                              x=sqb[par2][:, th2 * 512:(th2 + 1) * 512], st=(oc2 == 0), sp=(oc2 == 31):
                              e.matmul(o, w, x, start=st, stop=sp),
                              sem=s_pe if th2 == 1 else None)
                stat_of[oc2] = sm

            for oc in range(32):
                par = oc % 2
                if wload is not None:
                    wload(oc)
                if oc == 0 and pre_wait is not None:
                    for sm_, vv_ in pre_wait:
                        p.wait("sync", sm_, vv_)
                if oc >= 2:
                    p.wait("sync", s_dve, addv[oc - 2])
                p.dma("sync", xc[par][:, :], res_src(oc), s_x[par])
                vx = s_x[par].n
                for th in range(2):
                    k = oc * 2 + th
                    pst = ps_1 if k % 2 == 0 else ps_2
                    if k >= 2:
                        p.wait("tensor", s_act, ev[k - 2])
                    vk = emit_mms(oc, th, pst)
                    p.wait("scalar", s_pe, vk)
                    if oc >= 2 and th == 0:
                        p.wait("scalar", s_spl[par], spv[par])
                    if bias_col is None:
                        ev[k] = p.op("scalar", lambda e, o=t1[par][:, th * 512:(th + 1) * 512],
                                     i=pst[:, 0:512]: e.copy(o, i), sem=s_act)
                    else:
                        ev[k] = p.op("scalar", lambda e, o=t1[par][:, th * 512:(th + 1) * 512],
                                     i=pst[:, 0:512], b=bias_col[:, oc:oc + 1]:
                                     e.activation(o, i, AF.Identity, bias=b, scale=1.0), sem=s_act)
                p.wait("vector", s_act, ev[oc * 2 + 1])
                p.wait("vector", s_x[par], vx)
                addv[oc] = p.op("vector", lambda e, o=t1[par][:, :], i0=t1[par][:, :],
                                i1=xc[par][:, :]: e.tensor_tensor(o, i0, i1, ALU.add), sem=s_dve)
                p.wait("scalar", s_dve, addv[oc])
                if oc >= 2:
                    p.wait("scalar", s_pe, stat_of.get(oc - 2, 0))
                vsq_of[oc] = p.op("scalar", lambda e, o=sqb[par][:, :], i=t1[par][:, :]:
                                  e.square(o, i), sem=s_act)
                p.wait("vector", s_dve, addv[oc])
                if oc >= 2:
                    p.wait("vector", s_pe, stat_of.get(oc - 2, 0))
                vtb_of[oc] = p.op("vector", lambda e, o=tbf[par][:, :], i=t1[par][:, :]:
                                  e.tensor_copy(o, i), sem=s_dve)
                if oc >= 1:
                    emit_statmms(oc - 1)
                p.wait("sync", s_dve, addv[oc])
                spv[par] = p.dma("sync", spill[oc], t1[par][:, :], s_spl[par])
            emit_statmms(31)
            return spv, stat_of[31]

        def ln_finalize(stat_mark):
            p.wait("scalar", s_pe, stat_mark)
            c1 = p.op("scalar", lambda e, o=mrow[0:1, 0:512], i=ps_s[0][:, :]:
                      e.activation(o, i, AF.Copy, bias=0.0, scale=1.0 / D), sem=s_act)
            c2 = p.op("scalar", lambda e, o=mrow[0:1, 512:1024], i=ps_s[1][:, :]:
                      e.activation(o, i, AF.Copy, bias=0.0, scale=1.0 / D), sem=s_act)
            p.wait("scalar", s_act, c2)
            c3 = p.op("scalar", lambda e, o=qrow[:, :], i=mrow[:, :]: e.square(o, i), sem=s_act)
            p.wait("vector", s_act, c3)
            v1 = 0
            for th in range(2):
                v1 = p.op("vector", lambda e, o=vrow[0:1, th * 512:(th + 1) * 512],
                          i0=ps_q[th][:, :], i1=qrow[0:1, th * 512:(th + 1) * 512]:
                          e.scalar_tensor_tensor(o, i0, 1.0 / D, i1, ALU.mult, ALU.subtract),
                          sem=s_dve)
            p.wait("scalar", s_dve, v1)
            c4 = p.op("scalar", lambda e, o=vrow[:, :], i=vrow[:, :], b=eps5[:, :]:
                      e.activation(o, i, AF.Sqrt, bias=b, scale=1.0), sem=s_act)
            p.wait("vector", s_act, c4)
            v2 = p.op("vector", lambda e, o=rrow[:, :], i=vrow[:, :]: e.reciprocal(o, i), sem=s_dve)
            p.wait("tensor", s_dve, v2)
            p.wait("tensor", s_act, c2)
            vmb = vrb = 0
            for th in range(2):
                if th == 1:
                    p.wait("tensor", s_dve, vmb)
                vb1 = p.op("tensor", lambda e, o=ps_1[:, :], w=ones_r[:, :],
                           x=mrow[0:1, th * 512:(th + 1) * 512]:
                           e.matmul(o, w, x, start=True, stop=True), sem=s_pe)
                p.wait("vector", s_pe, vb1)
                vmb = p.op("vector", lambda e, o=mb[:, th * 512:(th + 1) * 512], i=ps_1[:, :]:
                           e.tensor_copy(o, i), sem=s_dve)
                if th == 1:
                    p.wait("tensor", s_dve, vrb)
                vb2 = p.op("tensor", lambda e, o=ps_2[:, :], w=ones_r[:, :],
                           x=rrow[0:1, th * 512:(th + 1) * 512]:
                           e.matmul(o, w, x, start=True, stop=True), sem=s_pe)
                p.wait("vector", s_pe, vb2)
                vrb = p.op("vector", lambda e, o=rb[:, th * 512:(th + 1) * 512], i=ps_2[:, :]:
                           e.tensor_copy(o, i), sem=s_dve)
            return vrb

        def pass2(spill_in, sink, vready, pre_wait=None):
            nv = {}
            for oc in range(32):
                par = oc % 2
                if oc == 0 and pre_wait is not None:
                    for sm_, vv_ in pre_wait:
                        p.wait("sync", sm_, vv_)
                if oc >= 2:
                    p.wait("sync", s_dve, nv[oc - 2])
                p.dma("sync", xc[par][:, :], spill_in[oc], s_x[par])
                vx = s_x[par].n
                p.wait("vector", s_x[par], vx)
                if oc == 0:
                    p.wait("vector", s_dve, vready)
                va1 = p.op("vector", lambda e, o=ostg[par][:, :], i0=xc[par][:, :], i1=mb[:, :]:
                           e.tensor_tensor(o, i0, i1, ALU.subtract), sem=s_dve)
                p.wait("vector", s_dve, va1)
                nv[oc] = p.op("vector", lambda e, o=ostg[par][:, :], i0=ostg[par][:, :], i1=rb[:, :]:
                              e.tensor_tensor(o, i0, i1, ALU.mult), sem=s_dve)
                sink(oc, par, nv[oc])
            return nv

        wv_of = {}

        def wload1(oc):
            par = oc % 2
            if oc >= 2:
                p.wait("gpsimd", s_pe, wv_of[oc - 2])
            p.dma("gpsimd", wbuf[par][:, 0:32 * 128], wo_pk[oc], s_w[par])

        def mm_outproj(oc, th, pst):
            par = oc % 2
            if th == 0:
                p.wait("tensor", s_w[par], s_w[par].n)
            vd = 0
            for ic in range(32):
                vd = p.op("tensor", lambda e, o=pst[:, 0:512], w=wbuf[par][:, ic * 128:(ic + 1) * 128],
                          x=big[:, ic * T + th * 512: ic * T + th * 512 + 512],
                          st=(ic == 0), sp=(ic == 31): e.matmul(o, w, x, start=st, stop=sp),
                          sem=s_pe if ic == 31 else None)
            wv_of[oc] = vd
            return vd

        spv1, stat1 = pass1(mm_outproj, None, lambda oc: x2T[oc * 128:(oc + 1) * 128, :], x1psp,
                            wload=wload1)
        vr1 = ln_finalize(stat1)
        pw1 = [(s_spl[0], s_spl[0].n), (s_spl[1], s_spl[1].n)]

        sp1n = {0: 0, 1: 0}
        v_oproj_done = stat1

        def sink1(oc, par, vn):
            p.wait("scalar", s_dve, vn)
            if oc == 0:
                p.wait("scalar", s_pe, v_oproj_done)
            vcp = p.op("scalar", lambda e, o=big[:, oc * T:(oc + 1) * T], i=ostg[par][:, :]:
                       e.copy(o, i), sem=s_act)
            p.wait("gpsimd", s_dve, vn)
            sp1n[par] = p.dma("gpsimd", x1nsp[oc], ostg[par][:, :], s_sp[par])
            if oc < 30:
                p.wait("vector", s_sp[par], sp1n[par])
                p.wait("vector", s_act, vcp)

        pass2(x1psp, sink1, vr1, pre_wait=pw1)
        v_x1bf = s_act.n
        pw2 = [(s_sp[0], s_sp[0].n), (s_sp[1], s_sp[1].n)]

        p.wait("tensor", s_act, v_x1bf)
        ztr = {}
        gb_of = {}
        for ts in range(NB):
            p.wait("tensor", s_dve, gb_of.get(ts - 1, 0))
            vg1 = p.op("tensor", lambda e, o=ps_1[:, 0:512], w=ones_r[:, :],
                       x=grow[ts][0:1, 0:512]: e.matmul(o, w, x, start=True, stop=True), sem=s_pe)
            p.wait("vector", s_pe, vg1)
            vgc = p.op("vector", lambda e, o=gbc[:, 0:512], i=ps_1[:, 0:512]:
                       e.tensor_copy(o, i), sem=s_dve)
            p.wait("tensor", s_dve, vgc)
            vg2 = p.op("tensor", lambda e, o=ps_1[:, 0:32], w=ones_r[:, :],
                       x=grow[ts][0:1, 512:544]: e.matmul(o, w, x, start=True, stop=True), sem=s_pe)
            p.wait("vector", s_pe, vg2)
            vgc2 = p.op("vector", lambda e, o=gbc[:, 512:544], i=ps_1[:, 0:32]:
                        e.tensor_copy(o, i), sem=s_dve)
            for grp, (z0, zn) in enumerate([(0, 512), (512, 32)]):
                if grp == 1:
                    p.wait("tensor", s_dve, zw_mark)
                vz = 0
                for ic in range(32):
                    vz = p.op("tensor", lambda e, o=ps_2[:, 0:zn], w=big[:, ic * T + ts * 128: ic * T + ts * 128 + 128],
                              x=zrhs[:, ic * ZC1 + z0: ic * ZC1 + z0 + zn], st=(ic == 0), sp=(ic == 31):
                              e.matmul(o, w, x, start=st, stop=sp), sem=s_pe if ic == 31 else None)
                p.wait("vector", s_pe, vz)
                p.wait("vector", s_dve, vgc2 if grp == 0 else vgc2)
                zw_mark = p.op("vector", lambda e, o=zwb[:, z0:z0 + zn], i0=ps_2[:, 0:zn],
                               i1=gbc[:, z0:z0 + zn]: e.tensor_tensor(o, i0, i1, ALU.mult),
                               sem=s_dve)
                p.wait("vector", s_dve, zw_mark)
            gb_of[ts] = zw_mark
            p.wait("tensor", s_dve, zw_mark)
            for kkk in range(5):
                if (ts * 5 + kkk) >= 2:
                    p.wait("tensor", s_act, ztr[ts * 5 + kkk - 2])
                tb2 = (ts * 5 + kkk) % 2
                if kkk < 4:
                    vt = p.op("tensor", lambda e, o=ps_bt[tb2][:, :],
                              i=zwb[:, kkk * 128:(kkk + 1) * 128], d=idb_sb[:, :]:
                              e.transpose(o, i, d), sem=s_pe)
                    p.wait("scalar", s_pe, vt)
                    ztr[ts * 5 + kkk] = p.op("scalar",
                        lambda e, o=zt128[:, kkk * T + ts * 128: kkk * T + ts * 128 + 128],
                        i=ps_bt[tb2][:, :]: e.copy(o, i), sem=s_act)
                else:
                    vt = p.op("tensor", lambda e, o=ps_bt[tb2][0:RK1, :],
                              i=zwb[:, 512:544], d=idb_sb[:, :]:
                              e.transpose(o, i, d), sem=s_pe)
                    p.wait("scalar", s_pe, vt)
                    ztr[ts * 5 + kkk] = p.op("scalar",
                        lambda e, o=zt32[:, ts * 128:(ts + 1) * 128],
                        i=ps_bt[tb2][0:RK1, :]: e.copy(o, i), sem=s_act)
        v_zwT = ztr[NB * 5 - 1]

        p.wait("tensor", s_act, v_zwT)
        wv2_of = {}

        def wload2(oc):
            par = oc % 2
            if oc >= 2:
                p.wait("gpsimd", s_pe, wv2_of[oc - 2])
            p.dma("gpsimd", wbuf[par][:, :], kn_pk[oc], s_w[par])

        def mm_ffn(oc, th, pst):
            par = oc % 2
            if th == 0:
                p.wait("tensor", s_w[par], s_w[par].n)
            for ic in range(32):
                p.op("tensor", lambda e, o=pst[:, 0:512], w=wbuf[par][:, ic * 128:(ic + 1) * 128],
                     x=big[:, ic * T + th * 512: ic * T + th * 512 + 512], st=(ic == 0):
                     e.matmul(o, w, x, start=st, stop=False))
            for kkk in range(4):
                p.op("tensor", lambda e, o=pst[:, 0:512], w=wbuf[par][:, (32 + kkk) * 128:(33 + kkk) * 128],
                     x=zt128[:, kkk * T + th * 512: kkk * T + th * 512 + 512]:
                     e.matmul(o, w, x, start=False, stop=False))
            vd = p.op("tensor", lambda e, o=pst[:, 0:512], w=wbuf[par][0:RK1, 36 * 128:37 * 128],
                      x=zt32[:, th * 512:(th + 1) * 512]:
                      e.matmul(o, w, x, start=False, stop=True), sem=s_pe)
            wv2_of[oc] = vd
            return vd

        spv2, stat2 = pass1(mm_ffn, fb_c, lambda oc: x1nsp[oc], x2psp, wload=wload2,
                            pre_wait=pw2)
        vr2 = ln_finalize(stat2)
        pw3 = [(s_spl[0], s_spl[0].n), (s_spl[1], s_spl[1].n)]

        stv = {0: 0, 1: 0}

        def sink2(oc, par, vn):
            p.wait("sync", s_dve, vn)
            stv[par] = p.dma("sync", out[oc * 128:(oc + 1) * 128, :], ostg[par][:, :], s_st[par])
            if oc < 30:
                p.wait("vector", s_st[par], stv[par])
        pass2(x2psp, sink2, vr2, pre_wait=pw3)
        p.wait("sync", s_st[0], s_st[0].n)
        p.wait("sync", s_st[1], s_st[1].n)


def build_l1_post():
    NL, RK1 = 16, 32
    ZC1 = NL * RK1 + RK1
    nc = bass.Bass("TRN2", target_bir_lowering=False)
    io = dict(
        oT=nc.dram_tensor("oT", [D, T], BF16, kind="ExternalInput").ap(),
        x2T=nc.dram_tensor("x2T", [D, T], F32, kind="ExternalInput").ap(),
        wo_pk=nc.dram_tensor("wo_pk", [32, 128, 32 * 128], F32, kind="ExternalInput").ap(),
        kn_pk=nc.dram_tensor("kn_pk", [32, 128, 37 * 128], F32, kind="ExternalInput").ap(),
        fbias=nc.dram_tensor("fbias", [D], F32, kind="ExternalInput").ap(),
        az1=nc.dram_tensor("az1", [D, ZC1], F32, kind="ExternalInput").ap(),
        gate_exp=nc.dram_tensor("gate_exp", [NB, ZC1], F32, kind="ExternalInput").ap(),
        idf=nc.dram_tensor("idf", [128, 128], F32, kind="ExternalInput").ap(),
        onesf=nc.dram_tensor("onesf", [1, 128], F32, kind="ExternalInput").ap(),
        onescf=nc.dram_tensor("onescf", [128, 1], F32, kind="ExternalInput").ap(),
        outT=nc.dram_tensor("outT", [D, T], F32, kind="ExternalOutput").ap(),
    )
    with ExitStack() as ctx:
        sm = mk_sems(nc, ctx)
        block = ctx.enter_context(nc.Block())
        p = Prog(nc)
        sh = mk_shared(p, nc, ctx, sm, io)
        emit_f(p, nc, ctx, sm, sh, io)
        p.replay(block)
    return nc


def build_fused():
    NE, RK = 4, 8
    ZC = NE * RK + NE
    NL, RK1 = 16, 32
    ZC1 = NL * RK1 + RK1
    nc = bass.Bass("TRN2", target_bir_lowering=False)

    def inp(name, shape, dt=F32):
        return nc.dram_tensor(name, shape, dt, kind="ExternalInput").ap()

    io = dict(
        xT=inp("xT", [D, T]),
        wpk_a=inp("wpk_a", [NOC_A, 128, 32 * 128]),
        bqc=inp("bqc", [QL]), bkvc=inp("bkvc", [KVL + ROPE0]),
        idf=inp("idf", [128, 128]), onesf=inp("onesf", [1, 128]), onescf=inp("onescf", [128, 1]),
        wqn=inp("wqn", [HPC0, 128, 12 * 128]), wqr=inp("wqr", [HPC0, 128, 12 * 64]),
        wkk=inp("wkk", [HPC0, 128, 4 * 128]), wkv=inp("wkv", [HPC0, 128, 4 * 128]),
        bqn_c=inp("bqn_c", [128, HPC0]), bqr_c=inp("bqr_c", [128, HPC0]),
        bkk_c=inp("bkk_c", [128, HPC0]), bkv_r=inp("bkv_r", [HPC0, 128]),
        woT=inp("woT", [VH0, D]), bo=inp("bo", [D]),
        wb_pk=inp("wb_pk", [32, 128, 32 * 128]), b_base=inp("b_base", [D]),
        az_rhs=inp("az_rhs", [D, ZC]), bcT=inp("bcT", [NE * RK, D]), lp_br=inp("lp_br", [2, NE]),
        wq_pk=inp("wq_pk", [32, 128, 32 * 128]), wk_pk=inp("wk_pk", [32, 128, 32 * 128]),
        wv_pk=inp("wv_pk", [32, 128, 32 * 128]),
        cosd=inp("cosd", [ROT1, T]), sins=inp("sins", [ROT1, T]),
        trimask=inp("trimask", [4, 128, 512]),
        hix=inp("hix", [128, 32], U32), oix=inp("oix", [128, 32], U32),
        vix=inp("vix", [128, 32], U32),
        wo_pk=inp("wo_pk", [32, 128, 32 * 128]), kn_pk=inp("kn_pk", [32, 128, 37 * 128]),
        fbias=inp("fbias", [D]), az1=inp("az1", [D, ZC1]), gate_exp=inp("gate_exp", [NB, ZC1]),
        outT=nc.dram_tensor("outT", [D, T], F32, kind="ExternalOutput").ap(),
    )
    qckv_own = nc.dram_tensor("i_qckv_own", [NOUT_A, T], BF16)
    qckv_full = nc.dram_tensor("i_qckv_full", [W, NOUT_A, T], BF16, addr_space="Shared")
    osum = nc.dram_tensor("i_osum", [S, VH0], F32)
    o_own = nc.dram_tensor("i_o_own", [T, VH0], F32)
    x2T_i = nc.dram_tensor("i_x2T", [D, T], F32)
    qT_i = nc.dram_tensor("i_qT", [D, T], BF16)
    kT_i = nc.dram_tensor("i_kT", [D, T], BF16)
    vT_i = nc.dram_tensor("i_vT", [D, T], BF16)
    q_full_i = nc.dram_tensor("i_q_full", [W, D, T], BF16, addr_space="Shared")
    k_full_i = nc.dram_tensor("i_k_full", [W, D, T], BF16, addr_space="Shared")
    vT_full_i = nc.dram_tensor("i_vT_full", [4, W, 1024, T], BF16, addr_space="Shared")
    o_own_i = nc.dram_tensor("i_oh_own", [512, S], BF16)
    o_full_a = nc.dram_tensor("i_o_full_a", [W, 256, S], BF16, addr_space="Shared")
    o_full_b = nc.dram_tensor("i_o_full_b", [W, 256, S], BF16, addr_space="Shared")
    rg = [list(range(W))]

    def coll(p, sm, kind, op, in_ap, out_ap, ccname):
        s = sm[ccname]
        v = s.inc(1)
        p.emit("gpsimd", lambda e, i=in_ap, o=out_ap, sh_=s.h:
               e.collective_compute(kind, op, replica_groups=rg, ins=[i], outs=[o]).then_inc(sh_, 1))
        for eng in Prog.ENGINES:
            p.wait(eng, s, v)

    with ExitStack() as octx:
        sm = mk_sems(nc, octx)
        block = octx.enter_context(nc.Block())
        p = Prog(nc)
        sh = mk_shared(p, nc, octx, sm, io)
        with ExitStack() as c1:
            emit_a(p, nc, c1, sm, sh, dict(io, qckvT=qckv_own.ap()))
            barrier(p, sm)
        coll(p, sm, "AllGather", ALU.bypass,
             qckv_own.ap(), qckv_full.ap().rearrange("w n t -> (w n) t"), "cc0")
        with ExitStack() as c2:
            emit_b(p, nc, c2, sm, sh, dict(io, qckv=qckv_full.ap(), osum_part=osum.ap()))
            barrier(p, sm)
        coll(p, sm, "ReduceScatter", ALU.add, osum.ap(), o_own.ap(), "cc1")
        with ExitStack() as c3:
            emit_c(p, nc, c3, sm, sh, dict(io, oparts=o_own.ap(), x2T=x2T_i.ap()), rs_mode=True)
            barrier(p, sm)
        def ag_async(in_ap, out_ap, ccname):
            def hook(i=in_ap, o=out_ap, n=ccname):
                s = sm[n]
                s.inc(1)
                p.emit("gpsimd", lambda e, i2=i, o2=o, sh_=s.h:
                       e.collective_compute("AllGather", ALU.bypass, replica_groups=rg,
                                            ins=[i2], outs=[o2]).then_inc(sh_, 1))
            return hook

        with ExitStack() as c4:
            emit_d(p, nc, c4, sm, sh,
                   dict(io, x2T=x2T_i.ap(), qT=qT_i.ap(), kT=kT_i.ap(), vT=vT_i.ap(),
                        post_pack=[
                            [ag_async(qT_i.ap(), q_full_i.ap().rearrange("w d t -> (w d) t"), "cc0")],
                            [ag_async(kT_i.ap(), k_full_i.ap().rearrange("w d t -> (w d) t"), "cc2")],
                            [ag_async(vT_i.ap()[j * 1024:(j + 1) * 1024, :],
                                      vT_full_i.ap()[j].rearrange("w d t -> (w d) t"), "cc3")
                             for j in range(4)],
                        ]))
            barrier(p, sm, skip=("cc0", "cc2", "cc3"))
        with ExitStack() as c5:
            emit_e2(p, nc, c5, sm, sh,
                    dict(io, q_full=q_full_i.ap(), k_full=k_full_i.ap(),
                         vT_full=vT_full_i.ap().rearrange("c2 w d t -> (c2 w) d t"),
                         o_own=o_own_i.ap(),
                         cc_qkv=(sm["cc0"], sm["cc2"], sm["cc3"]),
                         post_head=[
                             ag_async(o_own_i.ap()[0:256, :],
                                      o_full_a.ap().rearrange("w h t -> (w h) t"), "stv0"),
                             ag_async(o_own_i.ap()[256:512, :],
                                      o_full_b.ap().rearrange("w h t -> (w h) t"), "stv1"),
                         ]))
            barrier(p, sm, skip=("stv0", "stv1"))
        with ExitStack() as c6:
            emit_f(p, nc, c6, sm, sh, dict(io, o_full_a=o_full_a.ap(), o_full_b=o_full_b.ap(),
                                           x2T=x2T_i.ap(), cc_o=(sm["stv0"], sm["stv1"])))
        p.replay(block)
    return nc


def host_inputs(inp):
    x = np.asarray(inp["L0_attn_hidden_size"])
    hp = {}
    hp["idf"] = np.eye(128, dtype=np.float32)
    hp["onesf"] = np.ones((1, 128), np.float32)
    hp["onescf"] = np.ones((128, 1), np.float32)
    wqcT = _f32(np.asarray(inp["L0_attn_Wq_c"]).T)
    wkvcT = _f32(np.asarray(inp["L0_attn_Wkv_c"]).T)
    hp["wpk_a"] = lhsT_pack(np.concatenate([wqcT, wkvcT], axis=1), pad_out=NOC_A * 128)
    hp["bqc"] = _f32(inp["L0_attn_bq_c"])
    hp["bkvc"] = _f32(inp["L0_attn_bkv_c"])
    act = np.asarray(inp["L0_ffn_active"])
    A = np.asarray(inp["L0_ffn_lora_A"])[act]
    Bm = np.asarray(inp["L0_ffn_lora_B"])[act]
    Wr = np.asarray(inp["L0_ffn_router_W"])[act]
    br = np.asarray(inp["L0_ffn_router_b"])[act]
    prior = np.asarray(inp["L0_ffn_domain_prior"])[act]
    hp["az_rhs"] = _f32(np.concatenate([A.reshape(-1, D).T, Wr.T], axis=1))
    hp["bcT"] = _f32(Bm.transpose(0, 2, 1).reshape(-1, D))
    hp["lp_br"] = _f32(np.stack([prior, br]))
    hp["wb_pk"] = lhsT_pack(_f32(np.asarray(inp["L0_ffn_W_base"]).T))
    hp["woT"] = _f32(np.asarray(inp["L0_attn_Wo"]).T)
    hp["bo"] = _f32(inp["L0_attn_bo"])
    hp["b_base"] = _f32(inp["L0_ffn_b_base"])
    hp["wq_pk"] = lhsT_pack(_f32(np.asarray(inp["L1_attn_w_q"]).T))
    hp["wk_pk"] = lhsT_pack(_f32(np.asarray(inp["L1_attn_w_k"]).T))
    hp["wv_pk"] = lhsT_pack(_f32(np.asarray(inp["L1_attn_w_v"]).T))
    lA = np.asarray(inp["L1_ffn_lora_A"])
    lB = np.asarray(inp["L1_ffn_lora_B"])
    lAs = np.asarray(inp["L1_ffn_lora_A_shared"])
    lBs = np.asarray(inp["L1_ffn_lora_B_shared"])
    gate = _f32(inp["L1_ffn_gate"])
    hp["az1"] = _f32(np.concatenate([lA.reshape(16 * 32, D).T, lAs.T], axis=1))
    bcat = np.concatenate([lB.transpose(0, 2, 1).reshape(16 * 32, D), lBs.T], axis=0)
    kn_pk = np.zeros((32, 128, 37 * 128), np.float32)
    kn_pk[:, :, 0:32 * 128] = lhsT_pack(_f32(np.asarray(inp["L1_ffn_kernel"])))
    for k3 in range(4):
        kn_pk[:, :, (32 + k3) * 128:(33 + k3) * 128] = \
            bcat[k3 * 128:(k3 + 1) * 128].reshape(128, 32, 128).transpose(1, 0, 2)
    kn_pk[:, 0:32, 36 * 128:] = bcat[512:544].reshape(32, 32, 128).transpose(1, 0, 2)
    hp["kn_pk"] = kn_pk
    hp["gate_exp"] = _f32(np.concatenate([np.repeat(gate, 32, axis=1), np.ones((NB, 32))], axis=1))
    hp["wo_pk"] = lhsT_pack(_f32(np.asarray(inp["L1_attn_w_out"]).T))
    hp["fbias"] = _f32(inp["L1_ffn_bias"])
    invf = 1.0 / 10000.0 ** (np.arange(0, ROT1, 2) / ROT1)
    percore = []
    for r in range(W):
        pos = own_token_idx(r).astype(np.float64)
        ang = pos[None, :] * invf[:, None]
        cosd = np.repeat(np.cos(ang), 2, axis=0).astype(np.float32)
        sins_ = np.zeros((ROT1, T), np.float32)
        sins_[0::2] = -np.sin(ang)
        sins_[1::2] = np.sin(ang)
        pk = pack_l0_heads(inp["L0_attn_Wq_up"], inp["L0_attn_bq_up"],
                           inp["L0_attn_Wkv_up"], inp["L0_attn_bkv_up"],
                           [HPC0 * r + h for h in range(HPC0)])
        pk.update(xT=_f32(x[0, own_token_idx(r), :].T), cosd=cosd, sins=sins_,
                  hix=build_hix(r), oix=build_oix(r), vix=build_vix(r))
        percore.append(pk)
    hp["trimask"] = build_trimask()
    return hp, percore


def kernel(**inp):
    hp, percore = host_inputs(inp)
    nc = build_fused()
    maps = [dict(hp, **percore[r]) for r in range(W)]
    res = run_bass_kernel_spmd(nc, maps, core_ids=list(range(W)))
    out = np.zeros((1, S, D), np.float32)
    for r in range(W):
        out[0, own_token_idx(r), :] = np.asarray(res.results[r]["outT"]).T
    return out


def build_mask3(r):
    tri = np.triu(np.full((128, 128), MASKVAL, np.float32), 1)
    m = np.zeros((NB, 3, 128, 512), np.float32)
    for j in range(NB):
        g = W * j + r
        ngr = 2 * j + 2
        for s3 in range(3):
            gs = ngr - 3 + s3
            if gs < 0:
                continue
            for bb in range(4):
                b = gs * 4 + bb
                if b == g:
                    m[j, s3, :, bb * 128:(bb + 1) * 128] = tri
                elif b > g:
                    m[j, s3, :, bb * 128:(bb + 1) * 128] = MASKVAL
    return m


def own_token_idx(r):
    return np.concatenate([np.arange((8 * j + r) * 128, (8 * j + r) * 128 + 128) for j in range(NB)])


def build_trimask():
    tri = np.triu(np.full((128, 128), MASKVAL, np.float32), 1)
    m = np.zeros((4, 128, 512), np.float32)
    for d in range(4):
        m[d, :, d * 128:(d + 1) * 128] = tri
        m[d, :, (d + 1) * 128:] = MASKVAL
    return m


def build_hix(r):
    p_ = np.arange(128, dtype=np.uint32)
    m = np.zeros((128, 32), np.uint32)
    for hh in range(2):
        for ch in range(2):
            for c in range(W):
                m[:, hh * 16 + ch * 8 + c] = c * D + (2 * r + hh) * 256 + ch * 128 + p_
    return m


def build_vix(r):
    p_ = np.arange(128, dtype=np.uint32)
    m = np.zeros((128, 32), np.uint32)
    for hh in range(2):
        for ch in range(2):
            base = (2 * r + hh) * 256 + ch * 128
            hi, lo = base // 1024, base % 1024
            for c in range(W):
                m[:, hh * 16 + ch * 8 + c] = hi * 8192 + c * 1024 + lo + p_
    return m


def build_oix(r):
    p_ = np.arange(128, dtype=np.uint32)
    m = np.zeros((128, 32), np.uint32)
    for ic in range(32):
        d = ic * 128 + p_
        m[:, ic] = ((d // 512) * 256 + (d % 256)) * 8 + r
    return m


def _f32(a):
    return np.ascontiguousarray(a, dtype=np.float32)


def lhsT_pack(WT, pad_out=None):
    in_d, out_d = WT.shape
    assert in_d % 128 == 0
    ic = in_d // 128
    od = out_d if pad_out is None else pad_out
    assert od % 128 == 0
    ocn = od // 128
    Wp = np.zeros((in_d, od), np.float32)
    Wp[:, :out_d] = WT
    P = Wp.reshape(ic, 128, ocn, 128).transpose(2, 1, 0, 3)
    return np.ascontiguousarray(P.reshape(ocn, 128, ic * 128))


def pack_l0_heads(wq_up, bq_up, wkv_up, bkv_up, heads):
    nh = len(heads)
    WqT = np.ascontiguousarray(np.asarray(wq_up).T, np.float32)
    WkT = np.ascontiguousarray(np.asarray(wkv_up).T, np.float32)
    Wq = WqT.reshape(12, 128, NH0, QH0)
    Wk = WkT.reshape(4, 128, NH0, NOPE0 + VH0)
    wqn = np.ascontiguousarray(Wq[:, :, heads, :NOPE0].transpose(2, 1, 0, 3).reshape(nh, 128, 12 * 128))
    wqr = np.ascontiguousarray(Wq[:, :, heads, NOPE0:].transpose(2, 1, 0, 3).reshape(nh, 128, 12 * 64))
    wkk = np.ascontiguousarray(Wk[:, :, heads, :NOPE0].transpose(2, 1, 0, 3).reshape(nh, 128, 4 * 128))
    wkv = np.ascontiguousarray(Wk[:, :, heads, NOPE0:].transpose(2, 1, 0, 3).reshape(nh, 128, 4 * 128))
    bq = np.asarray(bq_up).reshape(NH0, QH0)
    bk = np.asarray(bkv_up).reshape(NH0, NOPE0 + VH0)
    bqn_c = np.ascontiguousarray(bq[heads, :NOPE0].T, np.float32)
    bqr_c = np.zeros((128, nh), np.float32)
    bqr_c[0:ROPE0, :] = bq[heads, NOPE0:].T
    bkk_c = np.ascontiguousarray(bk[heads, :NOPE0].T, np.float32)
    bkv_r = np.ascontiguousarray(bk[heads, NOPE0:], np.float32)
    return dict(wqn=wqn, wqr=wqr, wkk=wkk, wkv=wkv,
                bqn_c=bqn_c, bqr_c=bqr_c, bkk_c=bkk_c, bkv_r=bkv_r)


def kernel_multi(**inp):
    x = np.asarray(inp["L0_attn_hidden_size"])
    idf = np.eye(128, dtype=np.float32)
    onesf = np.ones((1, 128), np.float32)
    onescf = np.ones((128, 1), np.float32)
    cores = list(range(W))

    wqcT = _f32(np.asarray(inp["L0_attn_Wq_c"]).T)
    wkvcT = _f32(np.asarray(inp["L0_attn_Wkv_c"]).T)
    wpk_a = lhsT_pack(np.concatenate([wqcT, wkvcT], axis=1), pad_out=NOC_A * 128)
    bqc = _f32(inp["L0_attn_bq_c"])
    bkvc = _f32(inp["L0_attn_bkv_c"])
    nc_a = build_l0_pre()
    xTs = [_f32(x[0, own_token_idx(r), :].T) for r in cores]
    maps = [dict(xT=xTs[r], wpk_a=wpk_a, bqc=bqc, bkvc=bkvc, idf=idf, onesf=onesf, onescf=onescf) for r in cores]
    res = run_bass_kernel_spmd(nc_a, maps, core_ids=cores)
    qckv_st = np.stack([np.asarray(res.results[r]["qckvT"]) for r in cores])

    nc_b = build_l0_attn()
    maps = []
    for r in cores:
        pk = pack_l0_heads(inp["L0_attn_Wq_up"], inp["L0_attn_bq_up"],
                           inp["L0_attn_Wkv_up"], inp["L0_attn_bkv_up"],
                           [HPC0 * r + h for h in range(HPC0)])
        pk.update(qckv=qckv_st, idf=idf, onesf=onesf, onescf=onescf)
        maps.append(pk)
    res = run_bass_kernel_spmd(nc_b, maps, core_ids=cores)
    osum_parts = [np.asarray(res.results[r]["osum_part"]) for r in cores]

    act = np.asarray(inp["L0_ffn_active"])
    A = np.asarray(inp["L0_ffn_lora_A"])[act]
    Bm = np.asarray(inp["L0_ffn_lora_B"])[act]
    Wr = np.asarray(inp["L0_ffn_router_W"])[act]
    br = np.asarray(inp["L0_ffn_router_b"])[act]
    prior = np.asarray(inp["L0_ffn_domain_prior"])[act]
    az = _f32(np.concatenate([A.reshape(-1, D).T, Wr.T], axis=1))
    bcT_h = _f32(Bm.transpose(0, 2, 1).reshape(-1, D))
    lp_br = _f32(np.stack([prior, br]))
    wb_pk = lhsT_pack(_f32(np.asarray(inp["L0_ffn_W_base"]).T))
    woT = _f32(np.asarray(inp["L0_attn_Wo"]).T)
    nc_c = build_l0_post()
    maps = []
    for r in cores:
        op_r = np.stack([osum_parts[c][r * T:(r + 1) * T, :] for c in cores])
        maps.append(dict(oparts=op_r, woT=woT, bo=_f32(inp["L0_attn_bo"]), xT=xTs[r],
                         wb_pk=wb_pk, b_base=_f32(inp["L0_ffn_b_base"]), az_rhs=az,
                         bcT=bcT_h, lp_br=lp_br, idf=idf, onesf=onesf, onescf=onescf))
    res = run_bass_kernel_spmd(nc_c, maps, core_ids=cores)
    x2Ts = [np.asarray(res.results[r]["x2T"]) for r in cores]

    wq_pk = lhsT_pack(_f32(np.asarray(inp["L1_attn_w_q"]).T))
    wk_pk = lhsT_pack(_f32(np.asarray(inp["L1_attn_w_k"]).T))
    wv_pk = lhsT_pack(_f32(np.asarray(inp["L1_attn_w_v"]).T))
    invf = 1.0 / 10000.0 ** (np.arange(0, ROT1, 2) / ROT1)
    nc_d = build_l1_qkv()
    maps = []
    for r in cores:
        pos = own_token_idx(r).astype(np.float64)
        ang = pos[None, :] * invf[:, None]
        cosd = np.repeat(np.cos(ang), 2, axis=0).astype(np.float32)
        sins_ = np.zeros((ROT1, T), np.float32)
        sins_[0::2] = -np.sin(ang)
        sins_[1::2] = np.sin(ang)
        maps.append(dict(x2T=x2Ts[r], wq_pk=wq_pk, wk_pk=wk_pk, wv_pk=wv_pk,
                         cosd=cosd, sins=sins_, idf=idf, onesf=onesf, onescf=onescf))
    res = run_bass_kernel_spmd(nc_d, maps, core_ids=cores)
    qTs = [np.asarray(res.results[r]["qT"]) for r in cores]
    k_full = np.stack([np.asarray(res.results[r]["kT"]) for r in cores])
    v_full = np.stack([np.asarray(res.results[r]["v_own"]) for r in cores])

    nc_e = build_l1_attn()
    maps = [dict(qT=qTs[r], k_full=k_full, v_full=v_full, mask3=build_mask3(r), idf=idf,
                 onesf=onesf, onescf=onescf) for r in cores]
    res = run_bass_kernel_spmd(nc_e, maps, core_ids=cores)
    oTs = [np.asarray(res.results[r]["oT"]) for r in cores]

    lA = np.asarray(inp["L1_ffn_lora_A"])
    lB = np.asarray(inp["L1_ffn_lora_B"])
    lAs = np.asarray(inp["L1_ffn_lora_A_shared"])
    lBs = np.asarray(inp["L1_ffn_lora_B_shared"])
    gate = _f32(inp["L1_ffn_gate"])
    az1 = _f32(np.concatenate([lA.reshape(16 * 32, D).T, lAs.T], axis=1))
    bcat = np.concatenate([lB.transpose(0, 2, 1).reshape(16 * 32, D), lBs.T], axis=0)
    kn_pk = np.zeros((32, 128, 37 * 128), np.float32)
    kn_pk[:, :, 0:32 * 128] = lhsT_pack(_f32(np.asarray(inp["L1_ffn_kernel"])))
    for k3 in range(4):
        kn_pk[:, :, (32 + k3) * 128:(33 + k3) * 128] = \
            bcat[k3 * 128:(k3 + 1) * 128].reshape(128, 32, 128).transpose(1, 0, 2)
    kn_pk[:, 0:32, 36 * 128:] = bcat[512:544].reshape(32, 32, 128).transpose(1, 0, 2)
    gate_exp = _f32(np.concatenate([np.repeat(gate, 32, axis=1), np.ones((NB, 32))], axis=1))
    wo_pk = lhsT_pack(_f32(np.asarray(inp["L1_attn_w_out"]).T))
    nc_f = build_l1_post()
    maps = [dict(oT=oTs[r], x2T=x2Ts[r], wo_pk=wo_pk, kn_pk=kn_pk,
                 fbias=_f32(inp["L1_ffn_bias"]), az1=az1, gate_exp=gate_exp, idf=idf,
                 onesf=onesf, onescf=onescf) for r in cores]
    res = run_bass_kernel_spmd(nc_f, maps, core_ids=cores)

    out = np.zeros((1, S, D), np.float32)
    for r in cores:
        out[0, own_token_idx(r), :] = np.asarray(res.results[r]["outT"]).T
    return out


if __name__ == "__main__":
    pass
